# version 05
# baseline (speedup 1.0000x reference)
import os
import numpy as np
import ml_dtypes
from contextlib import ExitStack

os.environ.setdefault("GAUGE_NTFF_IGNORE_DMA", "1")

import concourse.bass as bass
import concourse.mybir as mybir
from concourse.ap import AP
from concourse.bass_utils import run_bass_kernel_spmd

F32 = mybir.dt.float32
BF16 = mybir.dt.bfloat16
AF = mybir.ActivationFunctionType
ALU = mybir.AluOpType
AX = mybir.AxisListType
BF = ml_dtypes.bfloat16

T, CM, H, DH, WIN = 2048, 1024, 16, 64, 256
NCH, TC, NL, DI, NJ = 8, 512, 6, 2730, 22
NTC = T // TC
WJL = DI - 21 * 128
NSEQ = 2
NTT = T // 128
DBG_NW = 2
DBG_NH = H
DBG_SNAP = None
DBG_CUTOP = None



def host_constants():
    c = {}
    half = np.arange(0, DH, 2, dtype=np.float64)
    inv_freq = 1.0 / 10000.0 ** (half / DH)
    t = np.arange(T, dtype=np.float64)
    freqs = t[:, None] * inv_freq[None, :]
    freqs = np.concatenate([freqs, freqs], -1)
    base_scale = (half + 0.4 * DH) / (1.4 * DH)
    power = (t - T // 2) / (WIN // 2)
    scale = base_scale[None, :] ** power[:, None]
    scale = np.concatenate([scale, scale], -1)

    def fm(x):
        x = np.ascontiguousarray(x.T.astype(np.float32))
        return np.concatenate([x, x], 0).astype(BF)

    c["c_tabs"] = np.concatenate([fm(np.cos(freqs) * scale * 8.0), fm(np.sin(freqs) * scale * 8.0),
                                  fm(np.cos(freqs) / scale), fm(np.sin(freqs) / scale)], axis=1)
    c["c_identb"] = np.eye(128, dtype=BF)
    c["c_identf"] = np.eye(128, dtype=np.float32)
    pr = np.zeros((128, 128), np.float32)
    for b0 in (0, 64):
        for d in range(64):
            if d < 32:
                pr[b0 + d + 32, b0 + d] = -1.0
            else:
                pr[b0 + d - 32, b0 + d] = 1.0
    c["c_prot"] = pr.astype(BF)
    bq = np.zeros((128, NCH * 16), np.float32)
    bk = np.zeros((128, NCH * 48), np.float32)
    for ch in range(NCH):
        bq[0:64, ch * 16 + 2 * ch] = 1.0
        bq[64:128, ch * 16 + 2 * ch + 1] = 1.0
        bk[0:64, ch * 48 + 32 + 2 * ch] = 1.0
        bk[64:128, ch * 48 + 32 + 2 * ch + 1] = 1.0
    c["c_bselq"] = bq.astype(BF)
    c["c_bselk"] = bk.astype(BF)
    sp = np.zeros((16, NCH * 128), np.float32)
    for ch in range(NCH):
        sp[2 * ch, ch * 128:ch * 128 + 64] = 1.0
        sp[2 * ch + 1, ch * 128 + 64:ch * 128 + 128] = 1.0
    c["c_selpat"] = sp.astype(BF)
    u = np.zeros((1, 16 * 16), np.float32)
    for h in range(16):
        u[0, h * 16 + h] = 1.0
    c["c_unit16"] = u.astype(BF)
    c["c_ones1"] = np.ones((1, 128), np.float32).astype(BF)
    c["c_onescol"] = np.ones((128, 1), np.float32).astype(BF)
    c["c_onesrow"] = np.ones((1, TC), np.float32).astype(BF)
    p = np.arange(128)[:, None]
    i = np.arange(256)[None, :]

    def keep(jc):
        d = (i + 256) - (jc * 128 + p)
        return ((d >= 0) & (d <= 256)).astype(np.float32)

    k4 = np.hstack([keep(0), keep(1), keep(2), keep(3)])
    c["c_maskb"] = ((1.0 - k4) * -300.0).astype(BF)
    k0 = np.hstack([keep(2), keep(3)])
    c["c_maskb0"] = ((1.0 - k0) * -300.0).astype(BF)
    return c


def host_layouts(ins):
    d = {}
    qs = np.asarray(ins["L0_ffn_q_scale"], np.float32)
    ks = np.asarray(ins["L0_ffn_k_scale"], np.float32)
    d["qs_pp"] = np.ascontiguousarray(np.concatenate([qs.T, qs.T], 0))
    d["ks_pp"] = np.ascontiguousarray(np.concatenate([ks.T, ks.T], 0))
    g = np.asarray(ins["L0_ffn_ln_g"], np.float32)
    b = np.asarray(ins["L0_ffn_ln_b"], np.float32)
    d["g_pp"] = np.ascontiguousarray(g.reshape(NL, NCH, 128).transpose(2, 0, 1).reshape(128, NL * NCH))
    d["gb_host"] = np.ascontiguousarray(np.stack([g.reshape(-1), b.reshape(-1)], 0))
    b1 = np.asarray(ins["L0_ffn_b1"], np.float32)
    b1a = np.zeros((128, NL * NJ), np.float32)
    b1g = np.zeros((128, NL * NJ), np.float32)
    for l in range(NL):
        for j in range(NJ):
            w = 128 if j < NJ - 1 else WJL
            b1a[0:w, l * NJ + j] = b1[l, j * 128:j * 128 + w]
            b1g[0:w, l * NJ + j] = b1[l, DI + j * 128:DI + j * 128 + w]
    d["b1a_pp"] = b1a
    d["b1g_pp"] = b1g
    b2 = np.asarray(ins["L0_ffn_b2"], np.float32)
    d["b2_pp"] = np.ascontiguousarray(b2.reshape(NL, NCH, 128).transpose(2, 0, 1).reshape(128, NL * NCH))
    d["bq_pp"] = np.ascontiguousarray(np.asarray(ins["L0_attn_bq"], np.float32).reshape(NCH, 128).T)
    d["bk_r"] = np.asarray(ins["L0_attn_bk"], np.float32).reshape(1, CM).copy()
    d["bv_r"] = np.asarray(ins["L0_attn_bv"], np.float32).reshape(1, CM).copy()
    d["bp_r"] = np.asarray(ins["L0_attn_bp"], np.float32).reshape(1, CM).copy()
    return d



ENG2ATTR = {"pe": "tensor", "act": "scalar", "dve": "vector", "pool": "gpsimd", "sp": "sync"}


AUTO_SEM = {"dve": "c_dve", "act": "c_act", "pool": "c_gp"}
DMA_LOOKAHEAD = 8


class Em:
    def __init__(self):
        self.ops = {e: [] for e in ENG2ATTR}
        self.sems = {}
        self.cnt = {}
        self.waited = {e: {} for e in ENG2ATTR}
        self.nops = 0
        self.pe_tags = []

    def add_sem(self, name, handle):
        self.sems[name] = handle
        self.cnt[name] = 0

    def _dummy(self, eng, amt):
        sb = self.dummy_sb
        ps = self.dummy_ps
        bb = self.dummy_bb
        if amt == 16:
            return lambda ep: ep.dma_start(sb[0:1, 4:8], sb[0:1, 0:4])
        if eng == "pe":
            return lambda ep: ep.matmul(ps[0:1, 0:1], bb[0:1, 0:1], bb[0:1, 0:1],
                                        start=True, stop=True, skip_group_check=True)
        if eng == "act":
            return lambda ep: ep.activation(sb[0:1, 0:1], sb[0:1, 1:2], AF.Copy)
        return lambda ep: ep.engine_nop()

    def op(self, eng, fn, inc=None, amt=1):
        if eng == "pe":
            import sys as _s
            self.pe_tags.append(_s._getframe(1).f_lineno)
        if DBG_CUTOP is not None and self.nops >= DBG_CUTOP:
            fn = self._dummy(eng, amt)
        if eng in AUTO_SEM and amt == 1:
            own = AUTO_SEM[eng]
            if inc is None:
                inc = own
            if inc == own:
                self.wait(eng, (own, self.cnt[own]))
        self.nops += 1
        if inc is not None:
            self.cnt[inc] += amt
            tok = (inc, self.cnt[inc])
            sem_h = self.sems[inc]

            def wrapped(ep, fn=fn, sem_h=sem_h, amt=amt):
                fn(ep).then_inc(sem_h, amt)

            self.ops[eng].append(wrapped)
            return tok
        self.ops[eng].append(fn)
        return None

    def wait(self, eng, tok):
        if tok is None:
            return
        s, v = tok
        if v <= 0 or self.waited[eng].get(s, 0) >= v:
            return
        self.waited[eng][s] = v
        sem_h = self.sems[s]
        self.ops[eng].append(lambda ep, sem_h=sem_h, v=v: ep.wait_ge(sem_h, v))

    def waits(self, eng, toks):
        for t in toks:
            if t is not None:
                self.wait(eng, t)

    def flush(self, blk):
        for eng, attr in ENG2ATTR.items():
            ops = self.ops[eng]
            self.ops[eng] = []

            def run(ep, ops=ops):
                for f in ops:
                    f(ep)

            getattr(blk, attr)(run)


def tmax(*toks):
    best = {}
    for t in toks:
        if t is None:
            continue
        for u in (t if isinstance(t, list) else [t]):
            if u is not None:
                best[u[0]] = max(best.get(u[0], 0), u[1])
    return [(s, v) for s, v in best.items()]



def reap(src_ap, dims):
    return AP(src_ap.tensor, src_ap.offset, [list(d) for d in dims])



class KB:
    def __init__(self, nc, dbg_cut=None):
        self.nc = nc
        self.dbg_cut = dbg_cut
        self.es = ExitStack()
        self.y_tok = None
        self.dbg_done = False
        self.rd = {}
        self.wr = {}

    def rreg(self, key, tok):
        if tok is not None:
            self.rd.setdefault(key, []).append(tok)

    def rgate(self, *keys):
        out = []
        for k in keys:
            out += self.rd.pop(k, [])
        return tmax(*out)

    def sb(self, name, shape, dt):
        return self.es.enter_context(self.nc.sbuf_tensor(name, shape, dt))

    def sem(self, name):
        h = self.es.enter_context(self.nc.semaphore(name))
        self.em.add_sem(name, h)

    def build(self):
        nc = self.nc
        self.x = nc.dram_tensor("x", [NSEQ * T, CM], F32, kind="ExternalInput").ap()
        self.out = nc.dram_tensor("out", [NSEQ * T, CM], F32, kind="ExternalOutput").ap()
        for n, s in [("wqkv", [NL, CM, 3 * CM]), ("wo3", [NL, CM, CM]),
                     ("w1", [NL, CM, 2 * DI]), ("w2", [NL, DI, CM]),
                     ("awq", [CM, CM]), ("awk", [CM, CM]), ("awv", [CM, CM]), ("awp", [CM, CM]),
                     ("qs_pp", [128, NL]), ("ks_pp", [128, NL]), ("g_pp", [128, NL * NCH]),
                     ("gb_host", [2, NL * CM]), ("b1a_pp", [128, NL * NJ]), ("b1g_pp", [128, NL * NJ]),
                     ("b2_pp", [128, NL * NCH]), ("bq_pp", [128, NCH]),
                     ("bk_r", [1, CM]), ("bv_r", [1, CM]), ("bp_r", [1, CM]),
                     ("c_identf", [128, 128])]:
            setattr(self, "d_" + n, nc.dram_tensor(n, s, F32, kind="ExternalInput").ap())
        for n, s in [("c_tabs", [128, 4 * T]), ("c_identb", [128, 128]), ("c_prot", [128, 128]),
                     ("c_bselq", [128, NCH * 16]), ("c_bselk", [128, NCH * 48]),
                     ("c_selpat", [16, NCH * 128]), ("c_unit16", [1, 256]),
                     ("c_ones1", [1, 128]), ("c_onescol", [128, 1]), ("c_onesrow", [1, TC]),
                     ("c_maskb", [128, 1024]), ("c_maskb0", [128, 512])]:
            setattr(self, "d_" + n, nc.dram_tensor(n, s, BF16, kind="ExternalInput").ap())
        self.s_wqkv = nc.dram_tensor("s_wqkv", [NL, 24, NCH, 128, 128], BF16, kind="Internal").ap()
        self.s_wo = nc.dram_tensor("s_wo", [NL, NCH, NCH, 128, 128], BF16, kind="Internal").ap()
        self.s_w1 = nc.dram_tensor("s_w1", [NL, NJ, 2, NCH, 128, 128], BF16, kind="Internal").ap()
        self.s_w2 = nc.dram_tensor("s_w2", [NL, DI, CM], BF16, kind="Internal").ap()
        self.s_wT = nc.dram_tensor("s_wT", [4, CM, CM], BF16, kind="Internal").ap()
        self.s_x2 = nc.dram_tensor("s_x2", [NSEQ * T, CM], F32, kind="Internal").ap()
        if self.dbg_cut is not None:
            self.dbg = nc.dram_tensor("dbg", [128, NCH * (T + 1)], F32, kind="ExternalOutput").ap()
            self.dbg_snap = nc.dram_tensor("dbg_snap", [128, 1536], BF16, kind="ExternalOutput").ap()
            self.dbg_snap2 = nc.dram_tensor("dbg_snap2", [128, 6 * 1040], BF16, kind="ExternalOutput").ap()
            self.dbg_snap3 = nc.dram_tensor("dbg_snap3", [65, 256], F32, kind="ExternalOutput").ap()

        with self.es:
            self.em = Em()
            for s in (["d_sp", "d_gp", "c_pe", "c_dve", "c_act", "c_gp", "cbr", "cws",
                       "ccst", "cvw", "cx0", "cpk", "cwt1", "cwt2"]
                      + [f"cv{i}" for i in range(NL)] + [f"cw{i}" for i in range(4)]
                      + [f"cgb{i}" for i in range(4)] + [f"qkp{i}" for i in range(4)]
                      + [f"w1p{i}" for i in range(4)] + [f"w2p{i}" for i in range(4)]
                      + [f"wop{i}" for i in range(2)] + [f"tab{i}" for i in range(2)]):
                self.sem(s)
            self._alloc()
            self._emit()
            blk = self.es.enter_context(nc.Block())
            self.em.flush(blk)
        return self.em.nops

    def _alloc(self):
        sb = self.sb
        self.big1 = sb("big1", [128, NCH * (T + 1)], F32)
        self.big2 = sb("big2", [128, 16 * CM], F32)
        self.big3 = sb("big3", [128, 20480], BF16)
        self.tabs = sb("tabs", [128, 2 * 4 * TC], BF16)
        self.identb = sb("identb", [128, 128], BF16)
        self.identf = sb("identf", [128, 128], F32)
        self.prot = sb("prot", [128, 128], BF16)
        self.bselq = sb("bselq", [128, NCH * 16], BF16)
        self.bselk = sb("bselk", [128, NCH * 48], BF16)
        self.selpat = sb("selpat", [16, NCH * 128], BF16)
        self.unit16 = sb("unit16", [1, 256], BF16)
        self.ones1 = sb("ones1", [1, 128], BF16)
        self.onescol = sb("onescol", [128, 1], BF16)
        self.maskb = sb("maskb", [128, 1024], BF16)
        self.maskb0 = sb("maskb0", [128, 512], BF16)
        self.qs_pp = sb("qs_pp_s", [128, NL], F32)
        self.ks_pp = sb("ks_pp_s", [128, NL], F32)
        self.g_pp = sb("g_pp_s", [128, NL * NCH], F32)
        self.b1a_pp = sb("b1a_s", [128, NL * NJ], F32)
        self.b1g_pp = sb("b1g_s", [128, NL * NJ], F32)
        self.b2_pp = sb("b2_s", [128, NL * NCH], F32)
        self.bq_pp = sb("bq_s", [128, NCH], F32)
        self.gb2 = sb("gb2", [2, NCH * 128], BF16)
        self.x0t = sb("x0t", [128, CM], F32)
        self.tmu = sb("tmu", [128, CM], F32)
        self.sqtm = self.tmu
        self.ytm = self.tmu
        self.ytmb = self.tmu[:, :].bitcast(BF16)[:, 0:CM]

        self.arhs = sb("arhs", [1, TC], BF16)
        self.c2rhs = sb("c2rhs", [2, TC], BF16)
        self.col1 = sb("col1", [128, 8], F32)
        self.nstat_s = sb("nstat_s", [16, TC], F32)
        self.qss = self.nstat_s
        self.invden16 = sb("invden16", [16, TC], BF16)
        self.denrow = sb("denrow", [1, 2 * 256], BF16)
        self.bcq_sb = sb("bcq_sb", [128, 2 * TC], BF16)
        self.scrA = sb("scrA", [128, TC], BF16)
        self.scrB = sb("scrB", [128, TC], BF16)
        self.t1s = sb("t1s", [128, TC], F32)

        self.ksumc = sb("ksumc", [128, 8], F32)
        self.kwsel = sb("kwsel", [128, NCH * 16], BF16)
        self.denk = sb("denk", [128, 16], F32)
        self.bsave = sb("bsave", [128, NCH], BF16)
        self.epsc = sb("epsc", [128, 1], F32)
        self.knraw_v = self.tmu[:, :].bitcast(BF16)[:, 0:CM]
        self.ctxacc = self.tmu[:, 512:1024]
        self.mus = self.x0t[32:33, 0:TC]
        self.statv = self.x0t[0:1, 0:TC]
        self.statl = self.x0t[0:1, TC:2 * TC]
        self.qinv16 = self.big3[0:16, 19456:19968]
        self.kinv16 = self.big3[0:16, 19968:20480]
        psum = lambda n, s: self.es.enter_context(self.nc.psum_tensor(n, s, F32))
        self.PA = psum("PA", [128, 1024])
        self.PB = psum("PB", [128, 1024])
        self.PC = psum("PC", [128, 1024])
        self.PD = psum("PD", [128, 512])
        self.PE_ = psum("PE", [128, 512])
        self.em.dummy_sb = self.col1
        self.em.dummy_bb = self.onescol
        self.em.dummy_ps = self.PD

    def y3(self):
        return self.big1[:, :].rearrange("p (c n) -> p c n", n=T + 1)

    def xt3(self):
        return self.big2[:, :].rearrange("p (c n) -> p c n", n=CM)

    def b2bf(self, lo, hi):
        return self.big2[:, :].bitcast(BF16)[:, lo:hi]

    def b1bf(self, lo, hi):
        return self.big1[:, :].bitcast(BF16)[:, lo:hi]

    def _emit(self):
        em = self.em
        if self.dbg_cut is not None:
            em.op("dve", lambda ep: ep.memset(self.big2[:, :], 0.125))
            dm = em.op("dve", lambda ep: ep.memset(self.big3[:, :], 0.125), inc="c_dve")
            em.op("dve", lambda ep: ep.memset(self.denrow[:, :], 0.125))
            self.rreg("big2", dm)
            self.rreg("big3", dm)
        self._const_loads()
        self._conversion()
        for p in [self.PA, self.PB, self.PC]:
            em.op("dve", lambda ep, p=p: ep.memset(p[:, :], 0.0))
        em.op("dve", lambda ep: ep.memset(self.PD[:, :], 0.0))
        em.op("dve", lambda ep: ep.memset(self.PE_[:, :], 0.0))
        z = em.op("dve", lambda ep: ep.memset(self.epsc[:, :], 1e-5), inc="c_dve")
        self.eps_tok = z
        for k in ["PA0", "PA1", "PB0", "PB1", "PC0", "PC1", "PD", "PE"]:
            self.rreg(k, z)
        for s in range(NSEQ):
            self.seq = s
            self._seq(s)
            if self.dbg_done:
                break
        em.waits("sp", [(n, em.cnt[n]) for n in em.cnt if em.cnt[n] > 0 and not n.startswith("c_")])

    def _const_loads(self):
        em = self.em
        for name in ["c_identb identb", "c_identf identf", "c_prot prot",
                     "c_bselq bselq", "c_bselk bselk", "c_selpat selpat", "c_unit16 unit16",
                     "c_ones1 ones1", "c_onescol onescol", "c_maskb maskb", "c_maskb0 maskb0",
                     "qs_pp qs_pp", "ks_pp ks_pp", "g_pp g_pp", "b1a_pp b1a_pp",
                     "b1g_pp b1g_pp", "b2_pp b2_pp", "bq_pp bq_pp"]:
            dn, sn = name.split()
            em.op("sp", lambda ep, d=getattr(self, sn), s=getattr(self, "d_" + dn):
                  ep.dma_start(d[:, :], s), inc="ccst", amt=16)
        em.op("sp", lambda ep: ep.dma_start(self.c2rhs[1:2, :], self.d_c_onesrow), inc="ccst", amt=16)
        self.const_tok = ("ccst", em.cnt["ccst"])
        self.const_gp = ("d_gp", em.cnt["d_gp"])

    def _conversion(self):
        em = self.em
        self.conv_tok = {}
        for l in range(NL):
            cs = f"cv{l}"
            src = self.d_wqkv[l].rearrange("(s p) (o c) -> o s p c", p=128, c=128)
            for oc in range(24):
                em.op("pool", lambda ep, s=src[oc], d=self.s_wqkv[l, oc]:
                      ep.dma_start(d, s), inc=cs, amt=16)
            srco = self.d_wo3[l].rearrange("(s p) (o c) -> o s p c", p=128, c=128)
            for oc in range(NCH):
                em.op("pool", lambda ep, s=srco[oc], d=self.s_wo[l, oc]:
                      ep.dma_start(d, s), inc=cs, amt=16)
            for ag in range(2):
                src1 = self.d_w1[l][:, ag * DI:(ag + 1) * DI]
                for j in range(NJ):
                    w = 128 if j < NJ - 1 else WJL
                    sj = src1[:, j * 128:j * 128 + w].rearrange("(s p) c -> s p c", p=128)
                    em.op("pool", lambda ep, s=sj, d=self.s_w1[l, j, ag][:, :, 0:w]:
                          ep.dma_start(d, s), inc=cs, amt=16)
            em.op("pool", lambda ep, s=self.d_w2[l], d=self.s_w2[l]: ep.dma_start(d, s),
                  inc=cs, amt=16)
            self.conv_tok[l] = (cs, em.cnt[cs])
        em.waits("sp", [self.const_tok])
        em.waits("pool", tmax(self.rgate("big2"), self.rgate("big3")))
        wstage = self.b2bf(0, 8 * CM)
        wtst = self.big3[:, 0:NCH * 128]
        trpb = self.PC[:, :].bitcast(BF16)
        gate = None
        ev = None
        for widx, name in enumerate(["awq", "awk", "awv", "awp"]):
            src = reap(getattr(self, "d_" + name), [(CM, 128), (128 * CM, NCH), (1, CM)])
            em.waits("pool", tmax(gate))
            ld = em.op("pool", lambda ep, s=src, d=wstage: ep.dma_start(d, s), inc=f"cw{widx}", amt=16)
            for ic in range(NCH):
                em.waits("pe", tmax(ld, ev, self.const_tok))
                tk = None
                for oc in range(NCH):
                    tk = em.op("pe", lambda ep, oc=oc, ic=ic:
                               ep.transpose(trpb[:, oc * 128:oc * 128 + 128],
                                            wstage[:, oc * CM + ic * 128: oc * CM + ic * 128 + 128],
                                            self.identb[:, :]),
                               inc="c_pe" if oc == NCH - 1 else None)
                em.waits("dve", tmax(tk, gate))
                ev = em.op("dve", lambda ep: ep.tensor_copy(wtst[:, :], trpb[:, 0:1024]), inc="c_dve")
                em.waits("sp", [ev])
                gate = em.op("sp", lambda ep, d=self.s_wT[widx, ic * 128:ic * 128 + 128, :]:
                             ep.dma_start(d, wtst[:, :]), inc="cws", amt=16)
        self.wT_tok = ("cws", em.cnt["cws"])
        self.rreg("big2", self.wT_tok)
        self.rreg("big3", self.wT_tok)
        self.rreg("PC0", ev)
        self.rreg("PC1", ev)

    def _seq(self, s):
        self._zero_guards()
        self._ln_stream(self.x[s * T:(s + 1) * T, :], None)
        if self._dbg(0):
            return
        for l in range(NL):
            self._attn(l)
            if self._dbg(1 + 2 * l):
                return
            self._ffn(l)
            if self._dbg(2 + 2 * l):
                return
        self._assemble_x(self.x[s * T:(s + 1) * T, :], True)
        self._outer()
        if self._dbg(100):
            return
        self._park_x2()
        self._zero_guards()
        self._ln_stream(None, self.xt3())
        if self._dbg(101):
            return
        for l in range(NL):
            self._attn(l)
            if self._dbg(102 + 2 * l):
                return
            self._ffn(l)
            if self._dbg(103 + 2 * l):
                return
        self._assemble_x(self.s_x2[s * T:(s + 1) * T, :], False)

    def _dbg(self, sid):
        if self.dbg_cut is None:
            return False
        if self.seq == self.dbg_cut[0] and sid == self.dbg_cut[1]:
            em = self.em
            if sid == 100:
                em.waits("sp", tmax(self.xt_tok))
                em.op("sp", lambda ep: ep.dma_start(self.dbg[:, 0:16 * CM], self.big2[:, :]),
                      inc="d_sp", amt=16)
            else:
                em.waits("sp", tmax(self.y_tok))
                em.op("sp", lambda ep: ep.dma_start(self.dbg, self.big1[:, :]), inc="d_sp", amt=16)
            self.dbg_done = True
            return True
        return False

    def _zero_guards(self):
        em = self.em
        y3 = self.y3()
        em.waits("dve", tmax(self.y_tok, self.rgate("big1")))
        self.y_tok = [em.op("dve", lambda ep: ep.memset(y3[:, :, 0:1], 0.0), inc="c_dve")]

    def _ln_stream(self, hbm_src, xt3):
        em = self.em
        y3 = self.y3()
        prev = None
        for tt in range(NTT):
            if hbm_src is not None:
                em.waits("sp", self.rgate("x0t"))
                ld = em.op("sp", lambda ep, s=hbm_src[tt * 128:(tt + 1) * 128, :]:
                           ep.dma_start(self.x0t[:, :], s), inc="cx0", amt=16)
                xin = self.x0t[:, :]
                gate = [ld]
            else:
                xin = xt3[:, tt, :]
                gate = tmax(self.xt_tok)
            rstd, nm = self._tm_stats(xin, gate)
            t7 = em.op("dve", lambda ep, xin=xin, rstd=rstd, nm=nm: ep.tensor_scalar(
                self.ytm[:, :], xin, rstd, nm, ALU.mult, ALU.add), inc="c_dve")
            self.rreg("col1", t7)
            if hbm_src is not None:
                self.rreg("x0t", t7)
            em.waits("pe", tmax(t7, self.rgate("PC0", "PC1")))
            tk = None
            for ch in range(NCH):
                tk = em.op("pe", lambda ep, ch=ch: ep.transpose(
                    self.PC[:, ch * 128:(ch + 1) * 128],
                    self.ytm[:, ch * 128:(ch + 1) * 128], self.identf[:, :]),
                    inc="c_pe" if ch == NCH - 1 else None)
            self.rreg("ytm", tk)
            self.rreg("big1", tk)
            em.waits("dve", [tk])
            dst = y3[:, :, 1 + tt * 128: 1 + (tt + 1) * 128]
            prev = em.op("dve", lambda ep, dst=dst: ep.tensor_copy(
                dst, self.PC[:, 0:1024].rearrange("p (c n) -> p c n", n=128)), inc="c_dve")
            self.rreg("PC0", prev)
            self.rreg("PC1", prev)
        self.y_tok = tmax(self.y_tok, prev)

    def _tm_stats(self, xin, gate):
        em = self.em
        c = self.col1
        yg = self.rgate("ytm")
        em.waits("dve", tmax(gate, yg, self.rgate("col1", "sqtm")))
        em.waits("act", tmax(gate, yg))
        rs = em.op("dve", lambda ep: ep.reduce_sum(c[:, 0:1], xin, axis=AX.X))
        t_sq = em.op("act", lambda ep: ep.activation(self.sqtm[:, :], xin, AF.Square), inc="c_act")
        em.waits("dve", [t_sq])
        rr = em.op("dve", lambda ep: ep.reduce_sum(c[:, 1:2], self.sqtm[:, :], axis=AX.X), inc="c_dve")
        self.rreg("sqtm", rr)
        em.op("dve", lambda ep: ep.tensor_scalar_mul(c[:, 2:3], c[:, 0:1], 1.0 / CM))
        em.op("dve", lambda ep: ep.tensor_tensor(c[:, 3:4], c[:, 2:3], c[:, 2:3], ALU.mult))
        t3 = em.op("dve", lambda ep: ep.scalar_tensor_tensor(c[:, 4:5], c[:, 1:2], 1.0 / CM, c[:, 3:4],
                                                             ALU.mult, ALU.subtract), inc="c_dve")
        em.waits("act", [t3])
        em.waits("act", [self.eps_tok])
        em.op("act", lambda ep: ep.activation(c[:, 5:6], c[:, 4:5], AF.Ln, bias=self.epsc[:, 0:1]))
        t5 = em.op("act", lambda ep: ep.activation(c[:, 6:7], c[:, 5:6], AF.Exp, scale=-0.5), inc="c_act")
        em.waits("dve", [t5])
        em.op("dve", lambda ep: ep.tensor_scalar(c[:, 7:8], c[:, 2:3], c[:, 6:7], -1.0,
                                                 ALU.mult, ALU.mult))
        return c[:, 6:7], c[:, 7:8]

    def _assemble_x(self, base_hbm, into_xt):
        em = self.em
        y3 = self.y3()
        s = self.seq
        xt3 = self.xt3()
        self.xt_tok = []
        if not into_xt:
            em.waits("sp", [("cpk", em.cnt["cpk"])])
        for tt in range(NTT):
            em.waits("sp", self.rgate("x0t"))
            ld = em.op("sp", lambda ep, src=base_hbm[tt * 128:(tt + 1) * 128, :]:
                       ep.dma_start(self.x0t[:, :], src), inc="cx0", amt=16)
            em.waits("pe", tmax(self.y_tok, self.rgate("PC0", "PC1")))
            tk = None
            for ch in range(NCH):
                src = y3[:, ch, 1 + tt * 128:1 + (tt + 1) * 128]
                tk = em.op("pe", lambda ep, src=src, ch=ch: ep.transpose(
                    self.PC[:, ch * 128:(ch + 1) * 128], src, self.identf[:, :]),
                    inc="c_pe" if ch == NCH - 1 else None)
            self.rreg("big1", tk)
            em.waits("dve", [tk, ld])
            if into_xt:
                dst = xt3[:, tt, :].rearrange("p (c n) -> p c n", n=128)
                em.waits("dve", self.rgate(("xt", tt)))
            else:
                dst = self.x0t[:, :].rearrange("p (c n) -> p c n", n=128)
            tadd = em.op("dve", lambda ep, dst=dst: ep.scalar_tensor_tensor(
                dst, self.PC[:, 0:1024].rearrange("p (c n) -> p c n", n=128), 0.5,
                self.x0t[:, :].rearrange("p (c n) -> p c n", n=128), ALU.mult, ALU.add), inc="c_dve")
            self.rreg("PC0", tadd)
            self.rreg("PC1", tadd)
            self.rreg("x0t", tadd)
            if into_xt:
                self.xt_tok.append(tadd)
            else:
                em.waits("sp", [tadd])
                st = em.op("sp", lambda ep, d=self.out[s * T + tt * 128:s * T + (tt + 1) * 128, :]:
                           ep.dma_start(d, self.x0t[:, :]), inc="d_sp", amt=16)
                self.rreg("x0t", st)

    def _park_x2(self):
        em = self.em
        s = self.seq
        xt3 = self.xt3()
        em.waits("sp", tmax(self.xt_tok))
        self.park_tok = []
        for tt in range(NTT):
            src = xt3[:, tt, :]
            st = em.op("sp", lambda ep, src=src, d=self.s_x2[s * T + tt * 128:s * T + (tt + 1) * 128, :]:
                       ep.dma_start(d, src), inc="cpk", amt=16)
            self.park_tok.append(st)
            self.rreg(("xt", tt), st)
            self.rreg("big2", st)

    def _attn(self, l):
        em = self.em
        y3 = self.y3()
        zb3 = self.b2bf(0, NCH * 513).rearrange("p (c n) -> p c n", n=513)
        o = NCH * 513
        k2r = self.b2bf(o, o + NCH * 1024); o += NCH * 1024
        q2r = self.b2bf(o, o + NCH * 512); o += NCH * 512
        qraw = self.b2bf(o, o + NCH * 512); o += NCH * 512
        kraw = self.b2bf(o, o + NCH * 512); o += NCH * 512
        vtm = self.b2bf(o, o + 6 * 1040); o += 6 * 1040
        qp = self.b2bf(o, o + 512); o += 512
        kp = self.b2bf(o, o + 512); o += 512
        sqs = self.b2bf(o, o + 512); o += 512
        assert o <= 32768, o
        v65 = vtm.rearrange("p (sh n) -> p sh n", n=65)
        qkvp = self.big3[:, 0:4 * NCH * 128]
        wop = self.big3[:, 4 * NCH * 128:6 * NCH * 128]
        vw = self.big3[:, 6 * NCH * 128:6 * NCH * 128 + NCH * CM]
        PT = self.big3[:, 14 * 1024:15 * 1024]
        attnU = self.big3[:, 15 * 1024:15 * 1024 + NCH * 512]
        assert 15 * 1024 + NCH * 512 <= 20480

        big2_gate = self.rgate("big2")
        big3_gate = self.rgate("big3")
        em.waits("sp", tmax(self.conv_tok[l], big3_gate, self.rgate("vw")))
        for vcc in range(NCH):
            vsrc = reap(self.s_wqkv[l, 16 + vcc], [(128, 128), (16384, NCH), (1, 128)])
            vwdst = AP(vw.tensor, vw.offset + vcc * 128, [[vw.ap[0][0], 128], [CM, NCH], [1, 128]])
            vw_tok = em.op("sp", lambda ep, vsrc=vsrc, vwdst=vwdst: ep.dma_start(vwdst, vsrc),
                           inc="cvw", amt=16)
        em.waits("dve", tmax(big2_gate, self.rgate("vtm")))
        ones_tok = em.op("dve", lambda ep: ep.memset(v65[:, :, 64:65], 1.0), inc="c_dve")

        y_upd = []
        pidx = 0
        for tc in range(NTC):
            t0 = tc * TC
            em.waits("dve", tmax(self.y_tok, self.rgate("zb"), big2_gate))
            big2_gate = None
            if tc == 0:
                zb_tok = em.op("dve", lambda ep: ep.tensor_copy(
                    zb3[:, :, 0:513], y3[:, :, 0:513]), inc="c_dve")
            else:
                em.op("dve", lambda ep, t0=t0: ep.tensor_copy(
                    zb3[:, :, 1:513], y3[:, :, t0 + 1:t0 + 513]))
                zb_tok = em.op("dve", lambda ep: ep.tensor_copy(
                    zb3[:, :, 0:1], self.bsave[:, :].rearrange("p (c n) -> p c n", n=1)), inc="c_dve")
            if tc < NTC - 1:
                bs = em.op("dve", lambda ep, t0=t0: ep.tensor_copy(
                    self.bsave[:, :].rearrange("p (c n) -> p c n", n=1),
                    y3[:, :, t0 + TC:t0 + TC + 1]), inc="c_dve")

            em.waits("sp", self.rgate(("tabr", tc % 2)))
            tsrc = self.d_c_tabs.rearrange("p (k t) -> p k t", t=T)[:, :, t0:t0 + TC]
            tslot = (tc % 2) * 4 * TC
            tab_tok = em.op("sp", lambda ep, tsrc=tsrc, tslot=tslot: ep.dma_start(
                self.tabs[:, tslot:tslot + 4 * TC].rearrange("p (k t) -> p k t", t=TC), tsrc),
                inc=f"tab{tc % 2}", amt=16)

            def zsl(c):
                off = 1 if c < 4 else 0
                return zb3[:, c, off:off + TC]

            def ztile(c, ti):
                off = 1 if c < 4 else 0
                return zb3[:, c, off + ti * 128:off + ti * 128 + 128]

            qk_done = {}
            sq_by = {}
            sqslots = [sqs, qp]
            for oc in range(17):
                if oc < 16:
                    slot = pidx % 4
                    pidx += 1
                    em.waits("sp", self.rgate(("pc", slot)))
                    src = reap(self.s_wqkv[l, oc], [(128, 128), (128 * 128, NCH), (1, 128)])
                    dst = qkvp[:, slot * NCH * 128:(slot + 1) * NCH * 128]
                    ptok = em.op("sp", lambda ep, s=src, d=dst: ep.dma_start(d, s),
                                 inc=f"qkp{slot}", amt=16)
                    reg = oc % 2
                    pk = "PA" + str(reg)
                    em.waits("pe", tmax(ptok, zb_tok, self.rgate(pk)))
                    mm = None
                    for sc in range(NCH):
                        mm = em.op("pe", lambda ep, sc=sc, dst=dst, reg=reg: ep.matmul(
                            self.PA[:, reg * 512:reg * 512 + 512], dst[:, sc * 128:(sc + 1) * 128],
                            zsl(sc), start=(sc == 0), stop=(sc == NCH - 1), skip_group_check=True),
                            inc="c_pe" if sc == NCH - 1 else None)
                    self.rreg(("pc", slot), mm)
                    self.rreg("zb", mm)
                    raw = qraw if oc < 8 else kraw
                    c = oc % 8
                    em.waits("dve", tmax(mm, self.rgate(("raw", oc < 8, c))))
                    ev = em.op("dve", lambda ep, raw=raw, c=c, reg=reg: ep.tensor_copy(
                        raw[:, c * 512:(c + 1) * 512], self.PA[:, reg * 512:reg * 512 + 512]), inc="c_dve")
                    self.rreg(pk, ev)
                    sslot = oc % 2
                    em.waits("act", tmax(ev, self.rgate(("sqs", sslot))))
                    sq = em.op("act", lambda ep, raw=raw, c=c, sslot=sslot: ep.activation(
                        sqslots[sslot][:, 0:512], raw[:, c * 512:(c + 1) * 512],
                        AF.Square), inc="c_act")
                    self.rreg(("raw", oc < 8, c), sq)
                    sq_by[oc] = (sq, sslot)
                po = oc - 1
                if po >= 0:
                    sqtok, sslot = sq_by.pop(po)
                    if po == 0:
                        em.waits("pe", tmax(sqtok, self.rgate("PD")))
                    elif po == 8:
                        em.waits("pe", tmax(sqtok, self.rgate("PE")))
                    else:
                        em.waits("pe", [sqtok])
                    pdst = self.PD if po < 8 else self.PE_
                    g = em.op("pe", lambda ep, po=po, pdst=pdst, sslot=sslot: ep.matmul(
                        pdst[0:16, :], self.bselq[:, (po % 8) * 16:(po % 8 + 1) * 16],
                        sqslots[sslot][:, 0:512],
                        start=(po % 8 == 0), stop=(po % 8 == 7), skip_group_check=True),
                        inc="c_pe")
                    self.rreg(("sqs", sslot), g)
                    if po == 7:
                        qk_done["q"] = g
                    if po == 15:
                        qk_done["k"] = g
            em.waits("dve", tmax(qk_done["q"], qk_done["k"], self.rgate("nss")))
            c1 = em.op("dve", lambda ep: ep.tensor_scalar_max(self.nstat_s[:, :], self.PD[0:16, :],
                                                              1e-24), inc="c_dve")
            em.waits("act", [c1])
            em.op("act", lambda ep: ep.activation(self.nstat_s[:, :], self.nstat_s[:, :], AF.Ln))
            c4 = em.op("act", lambda ep: ep.activation(self.qinv16, self.nstat_s[:, :], AF.Exp,
                                                       scale=-0.5), inc="c_act")
            em.waits("dve", [c4])
            c2 = em.op("dve", lambda ep: ep.tensor_scalar_max(self.nstat_s[:, :], self.PE_[0:16, :],
                                                              1e-24), inc="c_dve")
            self.rreg("PD", c1)
            self.rreg("PE", c2)
            em.waits("act", [c2])
            em.op("act", lambda ep: ep.activation(self.nstat_s[:, :], self.nstat_s[:, :], AF.Ln))
            c6 = em.op("act", lambda ep: ep.activation(self.kinv16, self.nstat_s[:, :], AF.Exp,
                                                       scale=-0.5), inc="c_act")
            em.waits("dve", self.rgate(("sqs", 0), ("sqs", 1)))
            for c in range(NCH):
                for qk in range(2):
                    raw = qraw if qk == 0 else kraw
                    inv16 = self.qinv16 if qk == 0 else self.kinv16
                    gain = self.qs_pp if qk == 0 else self.ks_pp
                    tab_c = tslot + (0 if qk == 0 else 2 * TC)
                    tab_s = tslot + (TC if qk == 0 else 3 * TC)
                    ntok = c4 if qk == 0 else c6
                    pbuf = qp if qk == 0 else kp
                    bcdst = self.PE_ if qk == 0 else self.PD
                    bkey = "PE" if qk == 0 else "PD"
                    em.waits("pe", tmax(ntok, self.rgate(bkey)))
                    bc = em.op("pe", lambda ep, c=c, inv16=inv16, bcdst=bcdst: ep.matmul(
                        bcdst[:, :], self.selpat[:, c * 128:(c + 1) * 128], inv16,
                        start=True, stop=True), inc="c_pe")
                    em.waits("act", tmax(bc, self.rgate(("bcs", qk))))
                    bce = em.op("act", lambda ep, b=qk, bcdst=bcdst: ep.activation(
                        self.bcq_sb[:, b * TC:(b + 1) * TC], bcdst[:, :], AF.Identity), inc="c_act")
                    self.rreg(bkey, bce)
                    self.rreg("nss", bce)
                    em.waits("dve", tmax(bce, self.rgate(("pp", qk)),
                                         self.rgate(("raw", qk == 0, c))))
                    st = em.op("dve", lambda ep, c=c, raw=raw, gain=gain, pbuf=pbuf, b=qk:
                               ep.scalar_tensor_tensor(pbuf[:, 0:512],
                                                       raw[:, c * 512:(c + 1) * 512],
                                                       gain[:, l:l + 1],
                                                       self.bcq_sb[:, b * TC:(b + 1) * TC],
                                                       ALU.mult, ALU.mult), inc="c_dve")
                    self.rreg(("bcs", qk), st)
                    pbb = qk * 512
                    em.waits("pe", tmax(st, self.rgate("PB" + str(qk))))
                    pm = em.op("pe", lambda ep, pbuf=pbuf, pbb=pbb: ep.matmul(
                        self.PB[:, pbb:pbb + 512], self.prot[:, :], pbuf[:, 0:512],
                        start=True, stop=True), inc="c_pe")
                    self.rreg(("pp", qk), pm)
                    if qk == 0:
                        dsl = q2r[:, c * 512:(c + 1) * 512]
                        wkey = ("q2", c)
                        scr, skey = self.scrA, "scrA"
                    else:
                        dsl = k2r[:, c * 1024 + (tc % 2) * 512: c * 1024 + (tc % 2) * 512 + 512]
                        wkey = ("k2", c, tc % 2)
                        scr, skey = self.scrB, "scrB"
                    em.waits("dve", tmax(pm, tab_tok, self.rgate(wkey), self.rgate(skey)))
                    tt1 = em.op("dve", lambda ep, dsl=dsl, pbuf=pbuf, tab_c=tab_c:
                          ep.tensor_tensor(dsl, pbuf[:, 0:512],
                                           self.tabs[:, tab_c:tab_c + TC], ALU.mult))
                    em.op("dve", lambda ep, tab_s=tab_s, scr=scr, pbb=pbb: ep.scalar_tensor_tensor(
                        scr[:, :], self.PB[:, pbb:pbb + 512], 1.0,
                        self.tabs[:, tab_s:tab_s + TC], ALU.mult, ALU.mult))
                    a3 = em.op("dve", lambda ep, dsl=dsl, scr=scr: ep.tensor_tensor(
                        dsl, dsl, scr[:, :], ALU.add), inc="c_dve")
                    self.rreg(("tabr", tc % 2), a3)
                    self.rreg(("pp", qk), a3)
                    self.rreg("PB" + str(qk), a3)
                    self.rreg(skey, a3)
                    if qk == 0:
                        self.rreg(("sqs", 1), tt1)
                        self.rreg(("sqs", 1), pm)
                    self.wr[wkey] = a3
            for ti in range(4):
                tile = tc * 4 + ti
                slot = tile % 6
                em.waits("pe", tmax(vw_tok, zb_tok, ones_tok, self.rgate("PB0", "PB1")))
                mm = None
                for sc in range(NCH):
                    for g2 in range(2):
                        mm = em.op("pe", lambda ep, sc=sc, g2=g2, ti=ti: ep.matmul(
                            self.PB[:, g2 * 512:(g2 + 1) * 512], ztile(sc, ti),
                            vw[:, sc * CM + g2 * 512: sc * CM + (g2 + 1) * 512],
                            start=(sc == 0), stop=(sc == NCH - 1), skip_group_check=True),
                            inc="c_pe" if (sc == NCH - 1 and g2 == 1) else None)
                self.rreg("zb", mm)
                self.rreg("vw", mm)
                em.waits("dve", tmax(mm, self.rgate(("vt", slot))))
                dstv = v65[:, slot * 16:(slot + 1) * 16, 0:64]
                ev = em.op("dve", lambda ep, dstv=dstv: ep.tensor_copy(
                    dstv, self.PB[:, 0:1024].rearrange("p (h n) -> p h n", n=64)), inc="c_dve")
                self.rreg("PB0", ev)
                self.rreg("PB1", ev)
                self.wr[("vt", slot)] = ev
            em.waits("pe", self.rgate("PB0", "PB1"))
            PT1 = self.tmu[:, :].bitcast(BF16)[:, 0:1024]
            ptbufs = [PT, PT1]
            if tc == 0:
                self.tmu_gate = tmax(self.rgate("sqtm"), self.rgate("ytm"))
            its = []
            for wloc in range(DBG_NW):
                w = 2 * tc + wloc
                w0 = (w == 0)
                jts = [2 * w - 2, 2 * w - 1, 2 * w, 2 * w + 1]
                live = [(idx, jt) for idx, jt in enumerate(jts) if not (w0 and idx < 2)]
                for h in range(DBG_NH):
                    its.append((wloc, w0, live, h))
            st_mk = {}
            st_dr = {}
            den_done = None

            def emit_sim(i):
                wloc, w0, live, h = its[i]
                par = i % 2
                ptb = ptbufs[par]
                ch, hf = h // 2, h % 2
                simp = self.PC if par == 0 else self.PA
                sk0, sk1 = ("PC0", "PC1") if par == 0 else ("PA0", "PA1")
                em.waits("pe", tmax(self.wr.get(("q2", ch)), self.const_tok,
                                    *(self.wr.get(("k2", ch, (jt // 4) % 2)) for _, jt in live)))
                em.waits("pe", self.rgate(sk0, sk1))
                qrhs = q2r[hf * 64:hf * 64 + 64, ch * 512 + wloc * 256: ch * 512 + (wloc + 1) * 256]
                mmt = None
                for idx, jt in live:
                    ksl = k2r[hf * 64:hf * 64 + 64,
                              ch * 1024 + ((jt // 4) % 2) * 512 + (jt % 4) * 128:
                              ch * 1024 + ((jt // 4) % 2) * 512 + (jt % 4) * 128 + 128]
                    if idx == 0:
                        i0, i1 = 0, 128
                    elif idx == 3:
                        i0, i1 = 128, 256
                    else:
                        i0, i1 = 0, 256
                    mb = (self.maskb0[:, (idx - 2) * 256:(idx - 1) * 256] if w0
                          else self.maskb[:, idx * 256:(idx + 1) * 256])
                    em.op("pe", lambda ep, idx=idx, simp=simp, mb=mb: ep.matmul(
                        simp[:, idx * 256:(idx + 1) * 256], self.identb[:, :], mb,
                        start=True, stop=False, skip_group_check=True))
                    mmt = em.op("pe", lambda ep, ksl=ksl, qrhs=qrhs, idx=idx, i0=i0, i1=i1,
                                simp=simp: ep.matmul(
                                    simp[:, idx * 256 + i0: idx * 256 + i1],
                                    ksl, qrhs[:, i0:i1], start=False, stop=True,
                                    skip_group_check=True),
                                inc="c_pe" if (idx, jt) == live[-1] else None)
                    self.rreg(("q2", ch), mmt)
                    self.rreg(("k2", ch, (jt // 4) % 2), mmt)
                em.waits("act", tmax(mmt, self.rgate(("PTb", par)), self.tmu_gate))
                self.tmu_gate = None
                if w0:
                    ex = em.op("act", lambda ep, simp=simp, ptb=ptb: ep.activation(
                        ptb[:, 512:1024], simp[:, 512:1024], AF.Exp), inc="c_act")
                else:
                    ex = em.op("act", lambda ep, simp=simp, ptb=ptb: ep.activation(
                        ptb[:, :], simp[:, :], AF.Exp), inc="c_act")
                self.rreg(sk0, ex)
                self.rreg(sk1, ex)
                st_mk[i] = ex

            def emit_pv(i):
                wloc, w0, live, h = its[i]
                par = i % 2
                ptb = ptbufs[par]
                ch, hf = h // 2, h % 2
                mk = st_mk.pop(i)
                em.waits("pe", tmax(mk, self.rgate(("pv", par)),
                                    *(self.wr.get(("vt", jt % 6)) for _, jt in live)))
                pvr = par * 256
                mmv = None
                for n, (idx, jt) in enumerate(live):
                    mmv = em.op("pe", lambda ep, jt=jt, idx=idx, pvr=pvr, h=h, ptb=ptb,
                                st=(n == 0), sp=(n == len(live) - 1): ep.matmul(
                        self.PB[0:65, pvr:pvr + 256], v65[:, (jt % 6) * 16 + h, :],
                        ptb[:, idx * 256:(idx + 1) * 256],
                        start=st, stop=sp, skip_group_check=True),
                        inc="c_pe" if n == len(live) - 1 else None)
                    self.rreg(("vt", jt % 6), mmv)
                self.rreg(("PTb", par), mmv)
                if par == 1:
                    self.rreg("ytm", mmv)
                em.waits("dve", tmax(mmv, self.rgate(("aU", ch, hf, wloc))))
                du = em.op("dve", lambda ep, hf=hf, ch=ch, wloc=wloc, pvr=pvr: ep.tensor_copy(
                    attnU[hf * 64:hf * 64 + 64,
                          ch * 512 + wloc * 256: ch * 512 + (wloc + 1) * 256],
                    self.PB[0:64, pvr:pvr + 256]), inc="c_dve")
                em.waits("dve", self.rgate(("dr", par)))
                dr = em.op("dve", lambda ep, pvr=pvr, drslot=par: ep.tensor_copy(
                    self.denrow[0:1, drslot * 256:(drslot + 1) * 256],
                    self.PB[64:65, pvr:pvr + 256]), inc="c_dve")
                self.rreg(("pv", par), du)
                self.rreg(("pv", par), dr)
                self.wr[("aU", ch, hf, wloc)] = du
                st_dr[i] = dr

            def emit_den(i):
                wloc, w0, live, h = its[i]
                par = i % 2
                dr = st_dr.pop(i)
                em.waits("pe", [dr])
                dn = em.op("pe", lambda ep, h=h, wloc=wloc, drslot=par: ep.matmul(
                    self.PB[0:16, 512 + wloc * 256:512 + (wloc + 1) * 256],
                    self.unit16[:, h * 16:(h + 1) * 16],
                    self.denrow[0:1, drslot * 256:(drslot + 1) * 256],
                    start=(h == 0), stop=(i == len(its) - 1), skip_group_check=True),
                    inc="c_pe")
                self.rreg(("dr", par), dn)
                return dn

            for i in range(len(its) + 1):
                if i < len(its):
                    emit_sim(i)
                if i >= 1:
                    den_done = emit_den(i - 1)
                if i < len(its):
                    emit_pv(i)
            em.waits("act", [den_done])
            em.op("act", lambda ep: ep.activation(self.nstat_s[:, :], self.PB[0:16, 512:1024], AF.Ln))
            iv = em.op("act", lambda ep: ep.activation(self.invden16[:, :], self.nstat_s[:, :],
                                                       AF.Exp, scale=-1.0), inc="c_act")
            self.rreg("PB1", iv)
            last_nm = None
            for c in range(NCH):
                bdst = self.PE_ if c % 2 == 0 else self.PD
                bkey = "PE" if c % 2 == 0 else "PD"
                em.waits("pe", tmax(iv, self.rgate(bkey)))
                bc = em.op("pe", lambda ep, c=c, bdst=bdst: ep.matmul(
                    bdst[:, :], self.selpat[:, c * 128:(c + 1) * 128], self.invden16[:, :],
                    start=True, stop=True), inc="c_pe")
                em.waits("dve", tmax(bc, self.wr.get(("aU", c, 0, 0)), self.wr.get(("aU", c, 0, 1)),
                                     self.wr.get(("aU", c, 1, 0)), self.wr.get(("aU", c, 1, 1))))
                nm = em.op("dve", lambda ep, c=c, bdst=bdst: ep.tensor_tensor(
                    attnU[:, c * 512:(c + 1) * 512], attnU[:, c * 512:(c + 1) * 512],
                    bdst[:, :], ALU.mult), inc="c_dve")
                self.rreg(bkey, nm)
                self.wr[("aN", c)] = nm
                last_nm = nm
            for oc in range(NCH):
                wslot = oc % 2
                em.waits("sp", tmax(self.conv_tok[l], self.rgate(("wo", wslot))))
                wsrc = reap(self.s_wo[l, oc], [(128, 128), (128 * 128, NCH), (1, 128)])
                wdst = wop[:, wslot * NCH * 128:(wslot + 1) * NCH * 128]
                wtok = em.op("sp", lambda ep, s=wsrc, d=wdst: ep.dma_start(d, s),
                             inc=f"wop{wslot}", amt=16)
                reg = oc % 2
                pk = "PA" + str(reg)
                em.waits("pe", tmax(wtok, self.rgate(pk),
                                    *(self.wr.get(("aN", sc)) for sc in range(NCH))))
                mm = None
                for sc in range(NCH):
                    mm = em.op("pe", lambda ep, sc=sc, wdst=wdst, reg=reg: ep.matmul(
                        self.PA[:, reg * 512:reg * 512 + 512], wdst[:, sc * 128:(sc + 1) * 128],
                        attnU[:, sc * 512:(sc + 1) * 512],
                        start=(sc == 0), stop=(sc == NCH - 1), skip_group_check=True),
                        inc="c_pe" if sc == NCH - 1 else None)
                self.rreg(("wo", wslot), mm)
                for sc in range(NCH):
                    self.rreg(("aU", sc, 0, 0), mm)
                em.waits("dve", [mm])
                dsty = y3[:, oc, 1 + t0:1 + t0 + TC]
                up = em.op("dve", lambda ep, dsty=dsty, reg=reg: ep.tensor_tensor(
                    dsty, dsty, self.PA[:, reg * 512:reg * 512 + 512], ALU.add), inc="c_dve")
                self.rreg(pk, up)
                y_upd.append(up)
        self.y_tok = tmax(y_upd)
        self.rreg("big2", tmax(y_upd))
        self.rreg("big3", tmax(y_upd))
        for c in range(NCH):
            for hf in range(2):
                for wl in range(2):
                    self.rd.pop(("aU", c, hf, wl), None)
            self.rd.pop(("q2", c), None)
            for sl in range(2):
                self.rd.pop(("k2", c, sl), None)
        for s2 in range(6):
            for t in self.rd.pop(("vt", s2), []):
                self.rreg("vtm", t)
        self.rreg("vtm", tmax(y_upd))

    def _ffn(self, l):
        em = self.em
        y3 = self.y3()
        zb3 = self.b2bf(0, NCH * 513).rearrange("p (c n) -> p c n", n=513)
        o = NCH * 513
        xn = self.b2bf(o, o + NCH * 512); o += NCH * 512
        hsb = self.b2bf(o, o + NJ * 512); o += NJ * 512
        gel = self.b2bf(o, o + 2 * 512); o += 2 * 512
        assert o <= 32768
        w1p = self.big3[:, 0:4 * 2 * NCH * 128]
        w2p = self.big3[:, 4 * 2 * NCH * 128:4 * 2 * NCH * 128 + 4 * CM]

        big2_gate = self.rgate("big2", "vtm")
        big3_gate = self.rgate("big3", "vw")
        y_upd = []
        self._gbi = getattr(self, "_gbi", 0)
        gbsem = f"cgb{self._gbi % 4}"
        self._gbi += 1
        em.waits("pool", tmax(self.rgate("gb2"), self.const_gp, self.conv_tok[NL - 1]))
        gb = em.op("pool", lambda ep: ep.dma_start(
            self.gb2[:, :], self.d_gb_host[:, l * CM:(l + 1) * CM]), inc=gbsem, amt=16)

        for tc in range(NTC):
            t0 = tc * TC
            em.waits("dve", tmax(self.y_tok, self.rgate("zb"), big2_gate))
            big2_gate = None
            if tc == 0:
                zb_tok = em.op("dve", lambda ep: ep.tensor_copy(
                    zb3[:, :, 0:513], y3[:, :, 0:513]), inc="c_dve")
            else:
                em.op("dve", lambda ep, t0=t0: ep.tensor_copy(
                    zb3[:, :, 1:513], y3[:, :, t0 + 1:t0 + 513]))
                zb_tok = em.op("dve", lambda ep: ep.tensor_copy(
                    zb3[:, :, 0:1], self.bsave[:, :].rearrange("p (c n) -> p c n", n=1)), inc="c_dve")
            if tc < NTC - 1:
                em.op("dve", lambda ep, t0=t0: ep.tensor_copy(
                    self.bsave[:, :].rearrange("p (c n) -> p c n", n=1),
                    y3[:, :, t0 + TC:t0 + TC + 1]))

            def zsl(c):
                off = 1 if c < 4 else 0
                return zb3[:, c, off:off + TC]

            em.waits("pe", tmax(zb_tok, self.rgate("PD")))
            sq_t = None
            for c in range(NCH):
                em.op("pe", lambda ep, c=c: ep.matmul(
                    self.PD[0:1, :], self.onescol[:, :], zsl(c),
                    start=(c == 0), stop=(c == NCH - 1), skip_group_check=True))
                em.waits("act", tmax(zb_tok, self.rgate(("bcs", c % 2))))
                sq = em.op("act", lambda ep, c=c: ep.activation(
                    self.bcq_sb[:, (c % 2) * TC:(c % 2 + 1) * TC], zsl(c), AF.Square), inc="c_act")
                self.rreg("zb", sq)
                if c == 0:
                    em.waits("pe", tmax(sq, self.rgate("PE")))
                else:
                    em.waits("pe", [sq])
                sq_t = em.op("pe", lambda ep, c=c: ep.matmul(
                    self.PE_[0:1, :], self.onescol[:, :],
                    self.bcq_sb[:, (c % 2) * TC:(c % 2 + 1) * TC],
                    start=(c == 0), stop=(c == NCH - 1), skip_group_check=True),
                    inc="c_pe")
                self.rreg(("bcs", c % 2), sq_t)
                self.rreg("zb", sq_t)
            em.waits("dve", tmax(sq_t, self.rgate("x0t")))
            em.op("dve", lambda ep: ep.tensor_scalar_mul(self.mus, self.PD[0:1, :], 1.0 / CM))
            t_sv = em.op("dve", lambda ep: ep.tensor_tensor(self.statv, self.mus, self.mus,
                                                            ALU.mult), inc="c_dve")
            v1 = em.op("dve", lambda ep: ep.scalar_tensor_tensor(
                self.statv, self.PE_[0:1, :], 1.0 / CM, self.statv,
                ALU.mult, ALU.subtract), inc="c_dve")
            self.rreg("PD", v1)
            self.rreg("PE", v1)
            em.waits("act", [v1])
            em.waits("act", [self.eps_tok])
            em.op("act", lambda ep: ep.activation(self.statl, self.statv, AF.Ln,
                                                  bias=self.epsc[0:1, 0:1]))
            r1 = em.op("act", lambda ep: ep.activation(self.arhs[:, :], self.statl,
                                                       AF.Exp, scale=-0.5), inc="c_act")
            em.waits("dve", [r1])
            cr = em.op("dve", lambda ep: ep.scalar_tensor_tensor(
                self.c2rhs[0:1, :], self.PD[0:1, :], -1.0 / CM, self.arhs[:, :],
                ALU.mult, ALU.mult), inc="c_dve")
            self.rreg("PD", cr)
            self.rreg("x0t", t_sv)
            em.waits("pe", tmax(r1, cr, gb, self.rgate("PE")))
            at = em.op("pe", lambda ep: ep.matmul(self.PE_[:, :], self.ones1[:, :],
                                                  self.arhs[:, :], start=True, stop=True), inc="c_pe")
            xn_tok = []
            for c in range(NCH):
                pk = "PB" + str(c % 2)
                em.waits("pe", self.rgate(pk))
                c2 = em.op("pe", lambda ep, c=c: ep.matmul(
                    self.PB[:, (c % 2) * 512:(c % 2) * 512 + 512],
                    self.gb2[:, c * 128:(c + 1) * 128], self.c2rhs[:, :],
                    start=True, stop=True), inc="c_pe")
                em.waits("dve", tmax(at, c2, self.rgate(("xn", c))))
                p1 = em.op("dve", lambda ep, c=c: ep.tensor_tensor(
                    self.t1s[:, :], zsl(c), self.PE_[:, :], ALU.mult),
                    inc="c_dve")
                self.rreg("zb", p1)
                p2 = em.op("dve", lambda ep, c=c: ep.scalar_tensor_tensor(
                    xn[:, c * 512:(c + 1) * 512], self.t1s[:, :],
                    self.g_pp[:, l * NCH + c:l * NCH + c + 1],
                    self.PB[:, (c % 2) * 512:(c % 2) * 512 + 512],
                    ALU.mult, ALU.add), inc="c_dve")
                self.rreg(pk, p2)
                self.rreg("PE", p2)
                xn_tok.append(p2)
                self.wr[("xn", c)] = p2
            self.rreg("gb2", xn_tok[-1])
            h_tok = [None] * NJ
            for j in range(NJ):
                wj = 128 if j < NJ - 1 else WJL
                slot = j % 4
                em.waits("sp", tmax(self.conv_tok[l], self.rgate(("w1", slot)), big3_gate))
                big3_gate = None
                wsrc = reap(self.s_w1[l, j], [(128, 128), (NCH * 128 * 128, 2), (128 * 128, NCH), (1, wj)])
                wdst = w1p[:, slot * 2 * NCH * 128:(slot + 1) * 2 * NCH * 128]
                wldst = reap(wdst, [(wdst.ap[0][0], 128), (NCH * 128, 2), (128, NCH), (1, wj)])
                wtok = em.op("sp", lambda ep, s=wsrc, d=wldst: ep.dma_start(d, s),
                             inc=f"w1p{slot}", amt=16)
                em.waits("pe", tmax(wtok, *xn_tok))
                em.waits("pe", self.rgate("PC" + str(j % 2), "PA" + str(j % 2)))
                mma = mmg = None
                for sc in range(NCH):
                    mma = em.op("pe", lambda ep, sc=sc, wdst=wdst, wj=wj, j=j: ep.matmul(
                        self.PC[0:wj, (j % 2) * 512:(j % 2) * 512 + 512],
                        wdst[:, sc * 128:sc * 128 + wj], xn[:, sc * 512:(sc + 1) * 512],
                        start=(sc == 0), stop=(sc == NCH - 1), skip_group_check=True))
                for sc in range(NCH):
                    mmg = em.op("pe", lambda ep, sc=sc, wdst=wdst, wj=wj, j=j: ep.matmul(
                        self.PA[0:wj, (j % 2) * 512:(j % 2) * 512 + 512],
                        wdst[:, NCH * 128 + sc * 128: NCH * 128 + sc * 128 + wj],
                        xn[:, sc * 512:(sc + 1) * 512],
                        start=(sc == 0), stop=(sc == NCH - 1), skip_group_check=True),
                        inc="c_pe" if sc == NCH - 1 else None)
                self.rreg(("w1", slot), mmg)
                for c in range(NCH):
                    self.rreg(("xn", c), mmg)
                em.waits("act", tmax(mmg, self.rgate(("gel", j % 2))))
                gl = em.op("act", lambda ep, j=j, wj=wj: ep.activation(
                    gel[0:wj, (j % 2) * 512:(j % 2) * 512 + 512],
                    self.PA[0:wj, (j % 2) * 512:(j % 2) * 512 + 512],
                    AF.Gelu, bias=self.b1g_pp[0:wj, l * NJ + j:l * NJ + j + 1]), inc="c_act")
                self.rreg("PA" + str(j % 2), gl)
                em.waits("dve", tmax(gl, self.rgate(("h", j))))
                ht = em.op("dve", lambda ep, j=j, wj=wj: ep.scalar_tensor_tensor(
                    hsb[0:wj, j * 512:j * 512 + 512],
                    self.PC[0:wj, (j % 2) * 512:(j % 2) * 512 + 512],
                    self.b1a_pp[0:wj, l * NJ + j:l * NJ + j + 1],
                    gel[0:wj, (j % 2) * 512:(j % 2) * 512 + 512], ALU.add, ALU.mult), inc="c_dve")
                h_tok[j] = ht
                self.rreg("PC" + str(j % 2), ht)
                self.rreg(("gel", j % 2), ht)
                self.wr[("h", j)] = ht
            em.waits("pe", tmax(h_tok))
            for grp in range(2):
                em.waits("pe", self.rgate("PA0", "PA1", "PB0", "PB1"))
                mmlast = None
                for j in range(NJ):
                    wj = 128 if j < NJ - 1 else WJL
                    slot = j % 4
                    em.waits("sp", tmax(self.conv_tok[l], self.rgate(("w2", slot))))
                    wsrc = self.s_w2[l, j * 128:j * 128 + wj, :]
                    wdst = w2p[:, slot * CM:(slot + 1) * CM]
                    wtok = em.op("sp", lambda ep, s=wsrc, d=wdst[0:wj, :]: ep.dma_start(d, s),
                                 inc=f"w2p{slot}", amt=16)
                    em.waits("pe", tmax(wtok))
                    for oc4 in range(4):
                        oc = grp * 4 + oc4
                        pdst = self.PA if oc4 < 2 else self.PB
                        mmlast = em.op("pe", lambda ep, j=j, wj=wj, oc=oc, oc4=oc4, wdst=wdst,
                                       pdst=pdst: ep.matmul(
                                           pdst[:, (oc4 % 2) * 512:(oc4 % 2) * 512 + 512],
                                           wdst[0:wj, (oc % 8) * 128:(oc % 8) * 128 + 128],
                                           hsb[0:wj, j * 512:j * 512 + 512],
                                           start=(j == 0), stop=(j == NJ - 1),
                                           skip_group_check=True),
                                       inc="c_pe" if oc4 == 3 else None)
                    self.rreg(("w2", slot), mmlast)
                    if grp == 1 and j == NJ - 1:
                        for jj in range(NJ):
                            self.rreg(("h", jj), mmlast)
                for oc4 in range(4):
                    oc = grp * 4 + oc4
                    pdst = self.PA if oc4 < 2 else self.PB
                    pk = ("PA" if oc4 < 2 else "PB") + str(oc4 % 2)
                    em.waits("dve", [mmlast])
                    dsty = y3[:, oc, 1 + t0:1 + t0 + TC]
                    up = em.op("dve", lambda ep, dsty=dsty, pdst=pdst, oc4=oc4, oc=oc:
                               ep.scalar_tensor_tensor(dsty, pdst[:, (oc4 % 2) * 512:(oc4 % 2) * 512 + 512],
                                                       self.b2_pp[:, l * NCH + oc:l * NCH + oc + 1],
                                                       dsty, ALU.add, ALU.add), inc="c_dve")
                    self.rreg(pk, up)
                    y_upd.append(up)
        self.y_tok = tmax(y_upd)
        self.rreg("big2", tmax(y_upd))
        self.rreg("big3", tmax(y_upd))
        for c in range(NCH):
            self.rd.pop(("xn", c), None)
        for j in range(NJ):
            self.rd.pop(("h", j), None)

    def _outer(self):
        em = self.em
        xt3 = self.xt3()
        vFM = self.b1bf(16384, 32768)
        E = self.b1bf(0, 16384)
        wT1 = self.big3[:, 0:NCH * CM]
        wT2 = self.big3[:, NCH * CM:2 * NCH * CM]
        knb = self.big3[:, 2 * NCH * CM:2 * NCH * CM + 2 * CM]
        vtb = self.big3[:, 2 * NCH * CM + 2 * CM:2 * NCH * CM + 4 * CM]
        ctxs = self.t1s[:, :].bitcast(BF16)[:, 0:512]
        knraw = self.knraw_v
        ksumr = self.knraw_v[0:1, :]
        bk_row = self.tabs[0:1, 0:CM]
        bv_row = self.tabs[0:1, CM:2 * CM]
        bp_row = self.tabs[0:1, 2 * CM:3 * CM]
        em.waits("pool", tmax(self.rgate(("tabr", 0)), self.rgate(("tabr", 1)), self.const_gp))
        em.op("pool", lambda ep: ep.dma_start(bk_row, self.d_bk_r), inc="cbr", amt=16)
        em.op("pool", lambda ep: ep.dma_start(bv_row, self.d_bv_r), inc="cbr", amt=16)
        em.op("pool", lambda ep: ep.dma_start(bp_row, self.d_bp_r), inc="cbr", amt=16)
        br_tok = ("cbr", em.cnt["cbr"])

        big3_gate = self.rgate("big3", "vw", "vtm")
        big1_gate = tmax(self.rgate("big1"), self.y_tok)

        trpb = self.PC[:, :].bitcast(BF16)
        vfm_tok = None
        for tt in range(NTT):
            xin = xt3[:, tt, :]
            rstd, nm = self._tm_stats(xin, tmax(self.xt_tok))
            tv = em.op("dve", lambda ep, xin=xin, rstd=rstd, nm=nm: ep.tensor_scalar(
                self.ytmb, xin, rstd, nm, ALU.mult, ALU.add), inc="c_dve")
            self.rreg("col1", tv)
            em.waits("pe", tmax(tv, self.rgate("PC0", "PC1")))
            tk = None
            for ch in range(NCH):
                tk = em.op("pe", lambda ep, ch=ch: ep.transpose(
                    trpb[:, ch * 128:(ch + 1) * 128], self.ytmb[:, ch * 128:(ch + 1) * 128],
                    self.identb[:, :]), inc="c_pe" if ch == NCH - 1 else None)
            self.rreg("ytm", tk)
            em.waits("dve", tmax(tk, big1_gate))
            big1_gate = None
            dst = AP(vFM.tensor, vFM.offset + tt * 128,
                     [[vFM.ap[0][0], 128], [T, NCH], [1, 128]])
            vfm_tok = em.op("dve", lambda ep, dst=dst: ep.tensor_copy(
                dst, trpb[:, 0:1024].rearrange("p (c n) -> p c n", n=128)), inc="c_dve")
            self.rreg("PC0", vfm_tok)
            self.rreg("PC1", vfm_tok)
        em.waits("sp", tmax(self.wT_tok, big3_gate))
        w1tok = em.op("sp", lambda ep: ep.dma_start(
            wT1, reap(self.s_wT[0], [(CM, 128), (128 * CM, NCH), (1, CM)])), inc="cwt1", amt=16)
        em.waits("act", tmax(big1_gate, vfm_tok))
        e_toks = []
        for tc in range(NTC):
            for oc in range(NCH):
                pq = self.PA if oc % 4 < 2 else self.PB
                pk = ("PA" if oc % 4 < 2 else "PB") + str(oc % 2)
                em.waits("pe", tmax(w1tok, vfm_tok, self.rgate(pk)))
                mm = None
                for sc in range(NCH):
                    mm = em.op("pe", lambda ep, sc=sc, oc=oc, tc=tc, pq=pq: ep.matmul(
                        pq[:, (oc % 2) * 512:(oc % 2) * 512 + 512],
                        wT1[:, sc * CM + oc * 128: sc * CM + oc * 128 + 128],
                        vFM[:, sc * T + tc * 512: sc * T + (tc + 1) * 512],
                        start=(sc == 0), stop=(sc == NCH - 1), skip_group_check=True),
                        inc="c_pe" if sc == NCH - 1 else None)
                em.waits("act", [mm])
                ex = em.op("act", lambda ep, oc=oc, tc=tc, pq=pq: ep.activation(
                    E[:, oc * T + tc * 512: oc * T + (tc + 1) * 512],
                    pq[:, (oc % 2) * 512:(oc % 2) * 512 + 512],
                    AF.Exp, bias=self.bq_pp[:, oc:oc + 1]), inc="c_act")
                self.rreg(pk, ex)
                e_toks.append(ex)
        em.waits("sp", tmax(e_toks[-1], self.rgate("wT1rd")))
        w2tok = em.op("sp", lambda ep: ep.dma_start(
            wT2, reap(self.s_wT[1], [(CM, 128), (128 * CM, NCH), (1, CM)])), inc="cwt2", amt=16)
        w3tok = em.op("sp", lambda ep: ep.dma_start(
            wT1, reap(self.s_wT[2], [(CM, 128), (128 * CM, NCH), (1, CM)])), inc="cwt1", amt=16)
        em.waits("pe", self.rgate("PC0", "PC1", "PD"))
        ctx_done = None
        ks_done = None
        kn_tok = vt_tok = None
        for tt in range(NTT):
            for which in range(2):
                wt = wT2 if which == 0 else wT1
                wtk = w2tok if which == 0 else w3tok
                brow = bk_row if which == 0 else bv_row
                pq = self.PA if which == 0 else self.PB
                pk0 = ("PA0", "PA1") if which == 0 else ("PB0", "PB1")
                em.waits("pe", tmax(wtk, vfm_tok, br_tok))
                em.waits("pe", self.rgate(*pk0))
                mm = None
                for g2 in range(2):
                    for sc in range(NCH):
                        em.op("pe", lambda ep, sc=sc, g2=g2, tt=tt, wt=wt, pq=pq: ep.matmul(
                            pq[:, g2 * 512:(g2 + 1) * 512],
                            vFM[:, sc * T + tt * 128: sc * T + tt * 128 + 128],
                            wt[:, sc * CM + g2 * 512: sc * CM + (g2 + 1) * 512],
                            start=(sc == 0), stop=False, skip_group_check=True))
                    mm = em.op("pe", lambda ep, g2=g2, brow=brow, pq=pq: ep.matmul(
                        pq[:, g2 * 512:(g2 + 1) * 512], self.ones1[:, :],
                        brow[:, g2 * 512:(g2 + 1) * 512],
                        start=False, stop=True, skip_group_check=True),
                        inc="c_pe" if g2 == 1 else None)
                if which == 0:
                    em.waits("act", tmax(mm, self.rgate("knraw")))
                    ek = em.op("act", lambda ep: ep.activation(knraw[:, :], self.PA[:, 0:1024],
                                                               AF.Exp), inc="c_act")
                    self.rreg("PA0", ek)
                    self.rreg("PA1", ek)
                    em.waits("dve", [ek])
                    dk = em.op("dve", lambda ep: ep.reduce_sum(
                        self.denk[:, :], knraw.rearrange("p (h n) -> p h n", n=64), axis=AX.X),
                        inc="c_dve")
                    em.waits("act", [dk])
                    em.op("act", lambda ep: ep.activation(self.denk[:, :], self.denk[:, :], AF.Ln))
                    ik = em.op("act", lambda ep: ep.activation(self.denk[:, :], self.denk[:, :],
                                                               AF.Exp, scale=-1.0), inc="c_act")
                    em.waits("dve", tmax(ik, self.rgate(("knb", tt % 2))))
                    kt = None
                    for h in range(H):
                        kt = em.op("dve", lambda ep, h=h, tt=tt: ep.tensor_scalar_mul(
                            knb[:, (tt % 2) * CM + h * 64:(tt % 2) * CM + (h + 1) * 64],
                            knraw[:, h * 64:(h + 1) * 64], self.denk[:, h:h + 1]),
                            inc="c_dve" if h == H - 1 else None)
                    self.rreg("knraw", kt)
                    kn_tok = kt
                else:
                    em.waits("dve", tmax(mm, self.rgate(("vtb", tt % 2))))
                    vt_tok = em.op("dve", lambda ep, tt=tt: ep.tensor_copy(
                        vtb[:, (tt % 2) * CM:(tt % 2 + 1) * CM], self.PB[:, 0:1024]), inc="c_dve")
                    self.rreg("PB0", vt_tok)
                    self.rreg("PB1", vt_tok)
            em.waits("pe", tmax(kn_tok, vt_tok, self.rgate("PC0", "PC1")))
            for h in range(H):
                ctx_done = em.op("pe", lambda ep, h=h, tt=tt: ep.matmul(
                    self.PC[(h % 2) * 64:(h % 2) * 64 + 64, (h // 2) * 64:(h // 2) * 64 + 64],
                    knb[:, (tt % 2) * CM + h * 64:(tt % 2) * CM + (h + 1) * 64],
                    vtb[:, (tt % 2) * CM + h * 64:(tt % 2) * CM + (h + 1) * 64],
                    start=True, stop=True, skip_group_check=True),
                    inc="c_pe" if h == H - 1 else None)
            em.waits("dve", [ctx_done])
            if tt == 0:
                ctx_done = em.op("dve", lambda ep: ep.tensor_copy(
                    self.ctxacc[:, :], self.PC[:, 0:512]), inc="c_dve")
            else:
                ctx_done = em.op("dve", lambda ep: ep.tensor_tensor(
                    self.ctxacc[:, :], self.ctxacc[:, :], self.PC[:, 0:512], ALU.add), inc="c_dve")
            self.rreg("PC0", ctx_done)
            for g2 in range(2):
                ks_done = em.op("pe", lambda ep, g2=g2, tt=tt: ep.matmul(
                    self.PD[0:1, :] if g2 == 0 else self.PE_[0:1, :],
                    self.onescol[:, :],
                    knb[:, (tt % 2) * CM + g2 * 512:(tt % 2) * CM + (g2 + 1) * 512],
                    start=(tt == 0), stop=(tt == NTT - 1), skip_group_check=True),
                    inc="c_pe" if (g2 == 1 and tt == NTT - 1) else None)
            self.rreg(("knb", tt % 2), ctx_done)
            self.rreg(("vtb", tt % 2), ctx_done)
        em.waits("dve", [ctx_done, ks_done])
        ctx_ev = em.op("dve", lambda ep: ep.tensor_copy(ctxs[:, :], self.ctxacc[:, :]), inc="c_dve")
        em.op("dve", lambda ep: ep.tensor_copy(ksumr[0:1, 0:512], self.PD[0:1, :]))
        ks_ev = em.op("dve", lambda ep: ep.tensor_copy(ksumr[0:1, 512:1024], self.PE_[0:1, :]),
                      inc="c_dve")
        self.rreg("knraw", ks_ev)
        self.rreg("PD", ks_ev)
        self.rreg("PE", ks_ev)
        em.waits("pe", tmax(ks_ev, self.rgate("PE")))
        tk = None
        for c in range(NCH):
            tk = em.op("pe", lambda ep, c=c, ksumr=ksumr: ep.matmul(
                self.PE_[:, c:c + 1], ksumr[0:1, c * 128:(c + 1) * 128],
                self.onescol[0:1, 0:1], start=True, stop=True, skip_group_check=True),
                inc="c_pe" if c == NCH - 1 else None)
        em.waits("dve", [tk])
        kc = em.op("dve", lambda ep: ep.tensor_copy(self.ksumc[:, :], self.PE_[:, 0:8]), inc="c_dve")
        self.rreg("PE", kc)
        em.waits("dve", [kc])
        kw = None
        for c in range(NCH):
            kw = em.op("dve", lambda ep, c=c: ep.tensor_scalar_mul(
                self.kwsel[:, c * 16:(c + 1) * 16], self.bselq[:, c * 16:(c + 1) * 16],
                self.ksumc[:, c:c + 1]), inc="c_dve" if c == NCH - 1 else None)
        collq = AP(vFM.tensor, vFM.offset, [[vFM.ap[0][0], 16], [1, 2048]])
        collD = AP(vFM.tensor, vFM.offset + 2048, [[vFM.ap[0][0], 16], [1, 2048]])
        dq_toks = []
        for tc in range(NTC):
            for which in range(2):
                sel = self.bselq if which == 0 else self.kwsel
                coll = collq if which == 0 else collD
                em.waits("pe", tmax(kw, e_toks[-1], self.rgate("PD")))
                mm = None
                for c in range(NCH):
                    mm = em.op("pe", lambda ep, c=c, sel=sel, tc=tc: ep.matmul(
                        self.PD[0:16, :], sel[:, c * 16:(c + 1) * 16],
                        E[:, c * T + tc * 512:c * T + (tc + 1) * 512],
                        start=(c == 0), stop=(c == NCH - 1), skip_group_check=True),
                        inc="c_pe" if c == NCH - 1 else None)
                em.waits("dve", [mm])
                ev = em.op("dve", lambda ep, coll=coll, tc=tc: ep.tensor_copy(
                    coll[:, tc * 512:(tc + 1) * 512], self.PD[0:16, :]), inc="c_dve")
                self.rreg("PD", ev)
                dq_toks.append(ev)
        em.waits("act", tmax(dq_toks))
        em.op("act", lambda ep: ep.activation(collq, collq, AF.Ln))
        iq = em.op("act", lambda ep: ep.activation(collq, collq, AF.Exp, scale=-1.0), inc="c_act")
        em.op("act", lambda ep: ep.activation(collD, collD, AF.Ln))
        iD = em.op("act", lambda ep: ep.activation(collD, collD, AF.Exp, scale=-1.0), inc="c_act")
        asm_toks = []
        for tc in range(NTC):
            for c in range(NCH):
                em.waits("pe", tmax(ctx_ev, iq, iD, self.rgate("PB0"), self.rgate("PE")))
                mp = None
                for hh in range(2):
                    h = 2 * c + hh
                    mp = em.op("pe", lambda ep, h=h, hh=hh, c=c, tc=tc: ep.matmul(
                        self.PB[hh * 64:hh * 64 + 64, 0:512],
                        ctxs[hh * 64:hh * 64 + 64, (h // 2) * 64:(h // 2) * 64 + 64],
                        E[hh * 64:hh * 64 + 64, c * T + tc * 512:c * T + (tc + 1) * 512],
                        start=True, stop=True), inc="c_pe" if hh == 1 else None)
                bq1 = em.op("pe", lambda ep, c=c, tc=tc: ep.matmul(
                    self.PE_[:, :], self.selpat[:, c * 128:(c + 1) * 128],
                    collq[:, tc * 512:(tc + 1) * 512],
                    start=True, stop=True), inc="c_pe")
                em.waits("dve", tmax(bq1, self.rgate("scrA")))
                t1 = em.op("dve", lambda ep, c=c, tc=tc: ep.tensor_tensor(
                    self.scrA[:, :], E[:, c * T + tc * 512:c * T + (tc + 1) * 512],
                    self.PE_[:, :], ALU.mult), inc="c_dve")
                em.waits("pe", [t1])
                bD = em.op("pe", lambda ep, c=c, tc=tc: ep.matmul(
                    self.PE_[:, :], self.selpat[:, c * 128:(c + 1) * 128],
                    collD[:, tc * 512:(tc + 1) * 512],
                    start=True, stop=True), inc="c_pe")
                em.waits("act", tmax(bD, self.rgate("bcsb")))
                bDs = em.op("act", lambda ep: ep.activation(self.bcq_sb[:, 0:TC], self.PE_[:, :],
                                                            AF.Identity), inc="c_act")
                self.rreg("PE", bDs)
                em.waits("dve", tmax(bDs, mp, self.rgate("scrB")))
                t2 = em.op("dve", lambda ep: ep.tensor_tensor(
                    self.scrB[:, :], self.PB[:, 0:512], self.bcq_sb[:, 0:TC], ALU.mult), inc="c_dve")
                self.rreg("PB0", t2)
                self.rreg("bcsb", t2)
                fin = em.op("dve", lambda ep, c=c, tc=tc: ep.tensor_tensor(
                    E[:, c * T + tc * 512:c * T + (tc + 1) * 512], self.scrA[:, :],
                    self.scrB[:, :], ALU.add), inc="c_dve")
                self.rreg("scrA", fin)
                self.rreg("scrB", fin)
                asm_toks.append(fin)
        em.waits("sp", tmax(asm_toks))
        self.rreg(("tabr", 0), ks_done)
        self.rreg(("tabr", 1), ks_done)
        wp_tok = em.op("sp", lambda ep: ep.dma_start(
            wT2, reap(self.s_wT[3], [(CM, 128), (128 * CM, NCH), (1, CM)])), inc="cwt2", amt=16)
        self.xt_tok = []
        for tt in range(NTT):
            em.waits("pe", tmax(wp_tok, asm_toks))
            em.waits("pe", self.rgate("PA0", "PA1"))
            mm = None
            for g2 in range(2):
                for sc in range(NCH):
                    em.op("pe", lambda ep, sc=sc, g2=g2, tt=tt: ep.matmul(
                        self.PA[:, g2 * 512:(g2 + 1) * 512],
                        E[:, sc * T + tt * 128: sc * T + tt * 128 + 128],
                        wT2[:, sc * CM + g2 * 512: sc * CM + (g2 + 1) * 512],
                        start=(sc == 0), stop=False, skip_group_check=True))
                mm = em.op("pe", lambda ep, g2=g2: ep.matmul(
                    self.PA[:, g2 * 512:(g2 + 1) * 512], self.ones1[:, :],
                    bp_row[:, g2 * 512:(g2 + 1) * 512],
                    start=False, stop=True, skip_group_check=True),
                    inc="c_pe" if g2 == 1 else None)
            em.waits("dve", tmax(mm, self.rgate(("xt", tt))))
            up = em.op("dve", lambda ep, tt=tt: ep.tensor_tensor(
                self.xt3()[:, tt, :].rearrange("p (c n) -> p c n", n=128),
                self.xt3()[:, tt, :].rearrange("p (c n) -> p c n", n=128),
                self.PA[:, 0:1024].rearrange("p (c n) -> p c n", n=128), ALU.add), inc="c_dve")
            self.rreg("PA0", up)
            self.rreg("PA1", up)
            self.xt_tok.append(up)
        self.rreg("big1", tmax(asm_toks, self.xt_tok))
        self.rreg("big3", tmax(asm_toks, self.xt_tok))



_CACHE = {}


def _build(dbg_cut=None):
    key = ("nc", dbg_cut)
    if key not in _CACHE:
        nc = bass.Bass("TRN2", target_bir_lowering=False)
        kb = KB(nc, dbg_cut=dbg_cut)
        n = kb.build()
        nc._pe_tags = kb.em.pe_tags
        _CACHE[key] = (nc, n)
    return _CACHE[key]


def make_inputs(inputs):
    consts = host_constants()
    lay = host_layouts(inputs)
    x = np.ascontiguousarray(np.asarray(inputs["L0_attn_x"], np.float32))
    base = {
        "wqkv": np.ascontiguousarray(np.asarray(inputs["L0_ffn_Wqkv"], np.float32)),
        "wo3": np.ascontiguousarray(np.asarray(inputs["L0_ffn_Wo"], np.float32)),
        "w1": np.ascontiguousarray(np.asarray(inputs["L0_ffn_W1"], np.float32)),
        "w2": np.ascontiguousarray(np.asarray(inputs["L0_ffn_W2"], np.float32)),
        "awq": np.ascontiguousarray(np.asarray(inputs["L0_attn_Wq"], np.float32)),
        "awk": np.ascontiguousarray(np.asarray(inputs["L0_attn_Wk"], np.float32)),
        "awv": np.ascontiguousarray(np.asarray(inputs["L0_attn_Wv"], np.float32)),
        "awp": np.ascontiguousarray(np.asarray(inputs["L0_attn_Wp"], np.float32)),
    }
    base.update(consts)
    base.update(lay)
    in_maps = []
    for core in range(8):
        m = dict(base)
        m["x"] = np.ascontiguousarray(x[2 * core:2 * core + 2].reshape(NSEQ * T, CM))
        in_maps.append(m)
    return in_maps


def kernel(**inputs):
    nc, _ = _build()
    in_maps = make_inputs(inputs)
    res = run_bass_kernel_spmd(nc, in_maps, core_ids=list(range(8)))
    out = np.empty((16, T, CM), np.float32)
    for core in range(8):
        out[2 * core:2 * core + 2] = res.results[core]["out"].reshape(NSEQ, T, CM)
    return out


# version 06
# speedup vs baseline: 1.0418x; 1.0418x over previous
import os
import numpy as np
import ml_dtypes
from contextlib import ExitStack

os.environ.setdefault("GAUGE_NTFF_IGNORE_DMA", "1")

import concourse.bass as bass
import concourse.mybir as mybir
from concourse.ap import AP
from concourse.bass_utils import run_bass_kernel_spmd

F32 = mybir.dt.float32
BF16 = mybir.dt.bfloat16
AF = mybir.ActivationFunctionType
ALU = mybir.AluOpType
AX = mybir.AxisListType
BF = ml_dtypes.bfloat16

T, CM, H, DH, WIN = 2048, 1024, 16, 64, 256
NCH, TC, NL, DI, NJ = 8, 512, 6, 2730, 22
NTC = T // TC
WJL = DI - 21 * 128
NSEQ = 2
NTT = T // 128
DBG_NW = 2
DBG_NH = H
DBG_SNAP = None
DBG_CUTOP = None



def host_constants():
    c = {}
    half = np.arange(0, DH, 2, dtype=np.float64)
    inv_freq = 1.0 / 10000.0 ** (half / DH)
    t = np.arange(T, dtype=np.float64)
    freqs = t[:, None] * inv_freq[None, :]
    freqs = np.concatenate([freqs, freqs], -1)
    base_scale = (half + 0.4 * DH) / (1.4 * DH)
    power = (t - T // 2) / (WIN // 2)
    scale = base_scale[None, :] ** power[:, None]
    scale = np.concatenate([scale, scale], -1)

    def fm(x):
        x = np.ascontiguousarray(x.T.astype(np.float32))
        return np.concatenate([x, x], 0).astype(BF)

    c["c_tabs"] = np.concatenate([fm(np.cos(freqs) * scale * 8.0), fm(np.sin(freqs) * scale * 8.0),
                                  fm(np.cos(freqs) / scale), fm(np.sin(freqs) / scale)], axis=1)
    c["c_identb"] = np.eye(128, dtype=BF)
    c["c_identf"] = np.eye(128, dtype=np.float32)
    pr = np.zeros((128, 128), np.float32)
    for b0 in (0, 64):
        for d in range(64):
            if d < 32:
                pr[b0 + d + 32, b0 + d] = -1.0
            else:
                pr[b0 + d - 32, b0 + d] = 1.0
    c["c_prot"] = pr.astype(BF)
    bq = np.zeros((128, NCH * 16), np.float32)
    bk = np.zeros((128, NCH * 48), np.float32)
    for ch in range(NCH):
        bq[0:64, ch * 16 + 2 * ch] = 1.0
        bq[64:128, ch * 16 + 2 * ch + 1] = 1.0
        bk[0:64, ch * 48 + 32 + 2 * ch] = 1.0
        bk[64:128, ch * 48 + 32 + 2 * ch + 1] = 1.0
    c["c_bselq"] = bq.astype(BF)
    c["c_bselk"] = bk.astype(BF)
    sp = np.zeros((16, NCH * 128), np.float32)
    for ch in range(NCH):
        sp[2 * ch, ch * 128:ch * 128 + 64] = 1.0
        sp[2 * ch + 1, ch * 128 + 64:ch * 128 + 128] = 1.0
    c["c_selpat"] = sp.astype(BF)
    u = np.zeros((1, 16 * 16), np.float32)
    for h in range(16):
        u[0, h * 16 + h] = 1.0
    c["c_unit16"] = u.astype(BF)
    c["c_ones1"] = np.ones((1, 128), np.float32).astype(BF)
    c["c_onescol"] = np.ones((128, 1), np.float32).astype(BF)
    c["c_onesrow"] = np.ones((1, TC), np.float32).astype(BF)
    p = np.arange(128)[:, None]
    i = np.arange(256)[None, :]

    def keep(jc):
        d = (i + 256) - (jc * 128 + p)
        return ((d >= 0) & (d <= 256)).astype(np.float32)

    k4 = np.hstack([keep(0), keep(1), keep(2), keep(3)])
    c["c_maskb"] = ((1.0 - k4) * -300.0).astype(BF)
    k0 = np.hstack([keep(2), keep(3)])
    c["c_maskb0"] = ((1.0 - k0) * -300.0).astype(BF)
    return c


def host_layouts(ins):
    d = {}
    qs = np.asarray(ins["L0_ffn_q_scale"], np.float32)
    ks = np.asarray(ins["L0_ffn_k_scale"], np.float32)
    d["qs_pp"] = np.ascontiguousarray(np.concatenate([qs.T, qs.T], 0))
    d["ks_pp"] = np.ascontiguousarray(np.concatenate([ks.T, ks.T], 0))
    g = np.asarray(ins["L0_ffn_ln_g"], np.float32)
    b = np.asarray(ins["L0_ffn_ln_b"], np.float32)
    d["g_pp"] = np.ascontiguousarray(g.reshape(NL, NCH, 128).transpose(2, 0, 1).reshape(128, NL * NCH))
    d["gb_host"] = np.ascontiguousarray(np.stack([g.reshape(-1), b.reshape(-1)], 0))
    b1 = np.asarray(ins["L0_ffn_b1"], np.float32)
    b1a = np.zeros((128, NL * NJ), np.float32)
    b1g = np.zeros((128, NL * NJ), np.float32)
    for l in range(NL):
        for j in range(NJ):
            w = 128 if j < NJ - 1 else WJL
            b1a[0:w, l * NJ + j] = b1[l, j * 128:j * 128 + w]
            b1g[0:w, l * NJ + j] = b1[l, DI + j * 128:DI + j * 128 + w]
    d["b1a_pp"] = b1a
    d["b1g_pp"] = b1g
    b2 = np.asarray(ins["L0_ffn_b2"], np.float32)
    d["b2_pp"] = np.ascontiguousarray(b2.reshape(NL, NCH, 128).transpose(2, 0, 1).reshape(128, NL * NCH))
    d["bq_pp"] = np.ascontiguousarray(np.asarray(ins["L0_attn_bq"], np.float32).reshape(NCH, 128).T)
    d["bk_r"] = np.asarray(ins["L0_attn_bk"], np.float32).reshape(1, CM).copy()
    d["bv_r"] = np.asarray(ins["L0_attn_bv"], np.float32).reshape(1, CM).copy()
    d["bp_r"] = np.asarray(ins["L0_attn_bp"], np.float32).reshape(1, CM).copy()
    return d



ENG2ATTR = {"pe": "tensor", "act": "scalar", "dve": "vector", "pool": "gpsimd", "sp": "sync"}


AUTO_SEM = {"dve": "c_dve", "act": "c_act", "pool": "c_gp"}
DMA_LOOKAHEAD = 8


class Em:
    def __init__(self):
        self.ops = {e: [] for e in ENG2ATTR}
        self.sems = {}
        self.cnt = {}
        self.waited = {e: {} for e in ENG2ATTR}
        self.nops = 0
        self.pe_tags = []

    def add_sem(self, name, handle):
        self.sems[name] = handle
        self.cnt[name] = 0

    def _dummy(self, eng, amt):
        sb = self.dummy_sb
        ps = self.dummy_ps
        bb = self.dummy_bb
        if amt == 16:
            return lambda ep: ep.dma_start(sb[0:1, 4:8], sb[0:1, 0:4])
        if eng == "pe":
            return lambda ep: ep.matmul(ps[0:1, 0:1], bb[0:1, 0:1], bb[0:1, 0:1],
                                        start=True, stop=True, skip_group_check=True)
        if eng == "act":
            return lambda ep: ep.activation(sb[0:1, 0:1], sb[0:1, 1:2], AF.Copy)
        return lambda ep: ep.engine_nop()

    def op(self, eng, fn, inc=None, amt=1):
        if eng == "pe":
            import sys as _s
            self.pe_tags.append(_s._getframe(1).f_lineno)
        if DBG_CUTOP is not None and self.nops >= DBG_CUTOP:
            fn = self._dummy(eng, amt)
        if eng in AUTO_SEM and amt == 1:
            own = AUTO_SEM[eng]
            if inc is None:
                inc = own
            if inc == own:
                self.wait(eng, (own, self.cnt[own]))
        self.nops += 1
        if inc is not None:
            self.cnt[inc] += amt
            tok = (inc, self.cnt[inc])
            sem_h = self.sems[inc]

            def wrapped(ep, fn=fn, sem_h=sem_h, amt=amt):
                fn(ep).then_inc(sem_h, amt)

            self.ops[eng].append(wrapped)
            return tok
        self.ops[eng].append(fn)
        return None

    def wait(self, eng, tok):
        if tok is None:
            return
        s, v = tok
        if v <= 0 or self.waited[eng].get(s, 0) >= v:
            return
        self.waited[eng][s] = v
        sem_h = self.sems[s]
        self.ops[eng].append(lambda ep, sem_h=sem_h, v=v: ep.wait_ge(sem_h, v))

    def waits(self, eng, toks):
        for t in toks:
            if t is not None:
                self.wait(eng, t)

    def flush(self, blk):
        for eng, attr in ENG2ATTR.items():
            ops = self.ops[eng]
            self.ops[eng] = []

            def run(ep, ops=ops):
                for f in ops:
                    f(ep)

            getattr(blk, attr)(run)


def tmax(*toks):
    best = {}
    for t in toks:
        if t is None:
            continue
        for u in (t if isinstance(t, list) else [t]):
            if u is not None:
                best[u[0]] = max(best.get(u[0], 0), u[1])
    return [(s, v) for s, v in best.items()]



def reap(src_ap, dims):
    return AP(src_ap.tensor, src_ap.offset, [list(d) for d in dims])



class KB:
    def __init__(self, nc, dbg_cut=None):
        self.nc = nc
        self.dbg_cut = dbg_cut
        self.es = ExitStack()
        self.y_tok = None
        self.dbg_done = False
        self.rd = {}
        self.wr = {}

    def rreg(self, key, tok):
        if tok is not None:
            self.rd.setdefault(key, []).append(tok)

    def rgate(self, *keys):
        out = []
        for k in keys:
            out += self.rd.pop(k, [])
        return tmax(*out)

    def sb(self, name, shape, dt):
        return self.es.enter_context(self.nc.sbuf_tensor(name, shape, dt))

    def sem(self, name):
        h = self.es.enter_context(self.nc.semaphore(name))
        self.em.add_sem(name, h)

    def build(self):
        nc = self.nc
        self.x = nc.dram_tensor("x", [NSEQ * T, CM], F32, kind="ExternalInput").ap()
        self.out = nc.dram_tensor("out", [NSEQ * T, CM], F32, kind="ExternalOutput").ap()
        for n, s in [("wqkv", [NL, CM, 3 * CM]), ("wo3", [NL, CM, CM]),
                     ("w1", [NL, CM, 2 * DI]), ("w2", [NL, DI, CM]),
                     ("awq", [CM, CM]), ("awk", [CM, CM]), ("awv", [CM, CM]), ("awp", [CM, CM]),
                     ("qs_pp", [128, NL]), ("ks_pp", [128, NL]), ("g_pp", [128, NL * NCH]),
                     ("gb_host", [2, NL * CM]), ("b1a_pp", [128, NL * NJ]), ("b1g_pp", [128, NL * NJ]),
                     ("b2_pp", [128, NL * NCH]), ("bq_pp", [128, NCH]),
                     ("bk_r", [1, CM]), ("bv_r", [1, CM]), ("bp_r", [1, CM]),
                     ("c_identf", [128, 128])]:
            setattr(self, "d_" + n, nc.dram_tensor(n, s, F32, kind="ExternalInput").ap())
        for n, s in [("c_tabs", [128, 4 * T]), ("c_identb", [128, 128]), ("c_prot", [128, 128]),
                     ("c_bselq", [128, NCH * 16]), ("c_bselk", [128, NCH * 48]),
                     ("c_selpat", [16, NCH * 128]), ("c_unit16", [1, 256]),
                     ("c_ones1", [1, 128]), ("c_onescol", [128, 1]), ("c_onesrow", [1, TC]),
                     ("c_maskb", [128, 1024]), ("c_maskb0", [128, 512])]:
            setattr(self, "d_" + n, nc.dram_tensor(n, s, BF16, kind="ExternalInput").ap())
        self.s_wqkv = nc.dram_tensor("s_wqkv", [NL, 24, NCH, 128, 128], BF16, kind="Internal").ap()
        self.s_wo = nc.dram_tensor("s_wo", [NL, NCH, NCH, 128, 128], BF16, kind="Internal").ap()
        self.s_w1 = nc.dram_tensor("s_w1", [NL, NJ, 2, NCH, 128, 128], BF16, kind="Internal").ap()
        self.s_w2 = nc.dram_tensor("s_w2", [NL, DI, CM], BF16, kind="Internal").ap()
        self.s_wT = nc.dram_tensor("s_wT", [4, CM, CM], BF16, kind="Internal").ap()
        self.s_x2 = nc.dram_tensor("s_x2", [NSEQ * T, CM], F32, kind="Internal").ap()
        if self.dbg_cut is not None:
            self.dbg = nc.dram_tensor("dbg", [128, NCH * (T + 1)], F32, kind="ExternalOutput").ap()
            self.dbg_snap = nc.dram_tensor("dbg_snap", [128, 1536], BF16, kind="ExternalOutput").ap()
            self.dbg_snap2 = nc.dram_tensor("dbg_snap2", [128, 6 * 1040], BF16, kind="ExternalOutput").ap()
            self.dbg_snap3 = nc.dram_tensor("dbg_snap3", [65, 256], F32, kind="ExternalOutput").ap()

        with self.es:
            self.em = Em()
            for s in (["d_sp", "d_gp", "c_pe", "c_dve", "c_act", "c_gp", "cbr", "cws",
                       "ccst", "cvw", "cx0", "cpk", "cwt1", "cwt2"]
                      + [f"cv{i}" for i in range(NL)] + [f"cw{i}" for i in range(4)]
                      + [f"cgb{i}" for i in range(4)] + [f"qkp{i}" for i in range(4)]
                      + [f"w1p{i}" for i in range(4)] + [f"w2p{i}" for i in range(4)]
                      + [f"wop{i}" for i in range(2)] + [f"tab{i}" for i in range(2)]):
                self.sem(s)
            self._alloc()
            self._emit()
            blk = self.es.enter_context(nc.Block())
            self.em.flush(blk)
        return self.em.nops

    def _alloc(self):
        sb = self.sb
        self.big1 = sb("big1", [128, NCH * (T + 1)], F32)
        self.big2 = sb("big2", [128, 16 * CM], F32)
        self.big3 = sb("big3", [128, 20480], BF16)
        self.tabs = sb("tabs", [128, 2 * 4 * TC], BF16)
        self.identb = sb("identb", [128, 128], BF16)
        self.identf = sb("identf", [128, 128], F32)
        self.prot = sb("prot", [128, 128], BF16)
        self.bselq = sb("bselq", [128, NCH * 16], BF16)
        self.bselk = sb("bselk", [128, NCH * 48], BF16)
        self.selpat = sb("selpat", [16, NCH * 128], BF16)
        self.unit16 = sb("unit16", [1, 256], BF16)
        self.ones1 = sb("ones1", [1, 128], BF16)
        self.onescol = sb("onescol", [128, 1], BF16)
        self.maskb = sb("maskb", [128, 1024], BF16)
        self.maskb0 = sb("maskb0", [128, 512], BF16)
        self.qs_pp = sb("qs_pp_s", [128, NL], F32)
        self.ks_pp = sb("ks_pp_s", [128, NL], F32)
        self.g_pp = sb("g_pp_s", [128, NL * NCH], F32)
        self.b1a_pp = sb("b1a_s", [128, NL * NJ], F32)
        self.b1g_pp = sb("b1g_s", [128, NL * NJ], F32)
        self.b2_pp = sb("b2_s", [128, NL * NCH], F32)
        self.bq_pp = sb("bq_s", [128, NCH], F32)
        self.gb2 = sb("gb2", [2, NCH * 128], BF16)
        self.x0t = sb("x0t", [128, CM], F32)
        self.tmu = sb("tmu", [128, CM], F32)
        self.sqtm = self.tmu
        self.ytm = self.tmu
        self.ytmb = self.tmu[:, :].bitcast(BF16)[:, 0:CM]

        self.arhs = sb("arhs", [1, TC], BF16)
        self.c2rhs = sb("c2rhs", [2, TC], BF16)
        self.col1 = sb("col1", [128, 8], F32)
        self.nstat_s = sb("nstat_s", [16, TC], F32)
        self.qss = self.nstat_s
        self.invden16 = sb("invden16", [16, TC], BF16)
        self.denrow = sb("denrow", [1, 2 * 256], BF16)
        self.bcq_sb = sb("bcq_sb", [128, 2 * TC], BF16)
        self.scrA = sb("scrA", [128, TC], BF16)
        self.scrB = sb("scrB", [128, TC], BF16)
        self.t1s = sb("t1s", [128, TC], F32)

        self.ksumc = sb("ksumc", [128, 8], F32)
        self.kwsel = sb("kwsel", [128, NCH * 16], BF16)
        self.denk = sb("denk", [128, 16], F32)
        self.bsave = sb("bsave", [128, NCH], BF16)
        self.epsc = sb("epsc", [128, 1], F32)
        self.knraw_v = self.tmu[:, :].bitcast(BF16)[:, 0:CM]
        self.ctxacc = self.tmu[:, 512:1024]
        self.mus = self.x0t[32:33, 0:TC]
        self.statv = self.x0t[0:1, 0:TC]
        self.statl = self.x0t[0:1, TC:2 * TC]
        self.qinv16 = self.big3[0:16, 19456:19968]
        self.kinv16 = self.big3[0:16, 19968:20480]
        psum = lambda n, s: self.es.enter_context(self.nc.psum_tensor(n, s, F32))
        self.PA = psum("PA", [128, 1024])
        self.PB = psum("PB", [128, 1024])
        self.PC = psum("PC", [128, 1024])
        self.PD = psum("PD", [128, 512])
        self.PE_ = psum("PE", [128, 512])
        self.em.dummy_sb = self.col1
        self.em.dummy_bb = self.onescol
        self.em.dummy_ps = self.PD

    def y3(self):
        return self.big1[:, :].rearrange("p (c n) -> p c n", n=T + 1)

    def xt3(self):
        return self.big2[:, :].rearrange("p (c n) -> p c n", n=CM)

    def b2bf(self, lo, hi):
        return self.big2[:, :].bitcast(BF16)[:, lo:hi]

    def b1bf(self, lo, hi):
        return self.big1[:, :].bitcast(BF16)[:, lo:hi]

    def _emit(self):
        em = self.em
        if self.dbg_cut is not None:
            em.op("dve", lambda ep: ep.memset(self.big2[:, :], 0.125))
            dm = em.op("dve", lambda ep: ep.memset(self.big3[:, :], 0.125), inc="c_dve")
            em.op("dve", lambda ep: ep.memset(self.denrow[:, :], 0.125))
            self.rreg("big2", dm)
            self.rreg("big3", dm)
        self._const_loads()
        self._conversion()
        for p in [self.PA, self.PB, self.PC]:
            em.op("dve", lambda ep, p=p: ep.memset(p[:, :], 0.0))
        em.op("dve", lambda ep: ep.memset(self.PD[:, :], 0.0))
        em.op("dve", lambda ep: ep.memset(self.PE_[:, :], 0.0))
        z = em.op("dve", lambda ep: ep.memset(self.epsc[:, :], 1e-5), inc="c_dve")
        self.eps_tok = z
        for k in ["PA0", "PA1", "PB0", "PB1", "PC0", "PC1", "PD", "PE"]:
            self.rreg(k, z)
        for s in range(NSEQ):
            self.seq = s
            self._seq(s)
            if self.dbg_done:
                break
        em.waits("sp", [(n, em.cnt[n]) for n in em.cnt if em.cnt[n] > 0 and not n.startswith("c_")])

    def _const_loads(self):
        em = self.em
        for name in ["c_identb identb", "c_identf identf", "c_prot prot",
                     "c_bselq bselq", "c_bselk bselk", "c_selpat selpat", "c_unit16 unit16",
                     "c_ones1 ones1", "c_onescol onescol", "c_maskb maskb", "c_maskb0 maskb0",
                     "qs_pp qs_pp", "ks_pp ks_pp", "g_pp g_pp", "b1a_pp b1a_pp",
                     "b1g_pp b1g_pp", "b2_pp b2_pp", "bq_pp bq_pp"]:
            dn, sn = name.split()
            em.op("sp", lambda ep, d=getattr(self, sn), s=getattr(self, "d_" + dn):
                  ep.dma_start(d[:, :], s), inc="ccst", amt=16)
        em.op("sp", lambda ep: ep.dma_start(self.c2rhs[1:2, :], self.d_c_onesrow), inc="ccst", amt=16)
        self.const_tok = ("ccst", em.cnt["ccst"])
        self.const_gp = ("d_gp", em.cnt["d_gp"])

    def _conversion(self):
        em = self.em
        self.conv_tok = {}
        for l in range(NL):
            cs = f"cv{l}"
            src = self.d_wqkv[l].rearrange("(s p) (o c) -> o s p c", p=128, c=128)
            for oc in range(24):
                em.op("pool", lambda ep, s=src[oc], d=self.s_wqkv[l, oc]:
                      ep.dma_start(d, s), inc=cs, amt=16)
            srco = self.d_wo3[l].rearrange("(s p) (o c) -> o s p c", p=128, c=128)
            for oc in range(NCH):
                em.op("pool", lambda ep, s=srco[oc], d=self.s_wo[l, oc]:
                      ep.dma_start(d, s), inc=cs, amt=16)
            for ag in range(2):
                src1 = self.d_w1[l][:, ag * DI:(ag + 1) * DI]
                for j in range(NJ):
                    w = 128 if j < NJ - 1 else WJL
                    sj = src1[:, j * 128:j * 128 + w].rearrange("(s p) c -> s p c", p=128)
                    em.op("pool", lambda ep, s=sj, d=self.s_w1[l, j, ag][:, :, 0:w]:
                          ep.dma_start(d, s), inc=cs, amt=16)
            em.op("pool", lambda ep, s=self.d_w2[l], d=self.s_w2[l]: ep.dma_start(d, s),
                  inc=cs, amt=16)
            self.conv_tok[l] = (cs, em.cnt[cs])
        em.waits("sp", [self.const_tok])
        em.waits("pool", tmax(self.rgate("big2"), self.rgate("big3")))
        wstage = self.b2bf(0, 8 * CM)
        wtst = self.big3[:, 0:NCH * 128]
        trpb = self.PC[:, :].bitcast(BF16)
        gate = None
        ev = None
        for widx, name in enumerate(["awq", "awk", "awv", "awp"]):
            src = reap(getattr(self, "d_" + name), [(CM, 128), (128 * CM, NCH), (1, CM)])
            em.waits("pool", tmax(gate))
            ld = em.op("pool", lambda ep, s=src, d=wstage: ep.dma_start(d, s), inc=f"cw{widx}", amt=16)
            for ic in range(NCH):
                em.waits("pe", tmax(ld, ev, self.const_tok))
                tk = None
                for oc in range(NCH):
                    tk = em.op("pe", lambda ep, oc=oc, ic=ic:
                               ep.transpose(trpb[:, oc * 128:oc * 128 + 128],
                                            wstage[:, oc * CM + ic * 128: oc * CM + ic * 128 + 128],
                                            self.identb[:, :]),
                               inc="c_pe" if oc == NCH - 1 else None)
                em.waits("dve", tmax(tk, gate))
                ev = em.op("dve", lambda ep: ep.tensor_copy(wtst[:, :], trpb[:, 0:1024]), inc="c_dve")
                em.waits("sp", [ev])
                gate = em.op("sp", lambda ep, d=self.s_wT[widx, ic * 128:ic * 128 + 128, :]:
                             ep.dma_start(d, wtst[:, :]), inc="cws", amt=16)
        self.wT_tok = ("cws", em.cnt["cws"])
        self.rreg("big2", self.wT_tok)
        self.rreg("big3", self.wT_tok)
        self.rreg("PC0", ev)
        self.rreg("PC1", ev)

    def _seq(self, s):
        self._zero_guards()
        self._ln_stream(self.x[s * T:(s + 1) * T, :], None)
        if self._dbg(0):
            return
        for l in range(NL):
            self._attn(l)
            if self._dbg(1 + 2 * l):
                return
            self._ffn(l)
            if self._dbg(2 + 2 * l):
                return
        self._assemble_x(self.x[s * T:(s + 1) * T, :], True)
        self._outer()
        if self._dbg(100):
            return
        self._park_x2()
        self._zero_guards()
        self._ln_stream(None, self.xt3())
        if self._dbg(101):
            return
        for l in range(NL):
            self._attn(l)
            if self._dbg(102 + 2 * l):
                return
            self._ffn(l)
            if self._dbg(103 + 2 * l):
                return
        self._assemble_x(self.s_x2[s * T:(s + 1) * T, :], False)

    def _dbg(self, sid):
        if self.dbg_cut is None:
            return False
        if self.seq == self.dbg_cut[0] and sid == self.dbg_cut[1]:
            em = self.em
            if sid == 100:
                em.waits("sp", tmax(self.xt_tok))
                em.op("sp", lambda ep: ep.dma_start(self.dbg[:, 0:16 * CM], self.big2[:, :]),
                      inc="d_sp", amt=16)
            else:
                em.waits("sp", tmax(self.y_tok))
                em.op("sp", lambda ep: ep.dma_start(self.dbg, self.big1[:, :]), inc="d_sp", amt=16)
            self.dbg_done = True
            return True
        return False

    def _zero_guards(self):
        em = self.em
        y3 = self.y3()
        em.waits("dve", tmax(self.y_tok, self.rgate("big1")))
        self.y_tok = [em.op("dve", lambda ep: ep.memset(y3[:, :, 0:1], 0.0), inc="c_dve")]

    def _ln_stream(self, hbm_src, xt3):
        em = self.em
        y3 = self.y3()
        prev = None
        for tt in range(NTT):
            if hbm_src is not None:
                em.waits("sp", self.rgate("x0t"))
                ld = em.op("sp", lambda ep, s=hbm_src[tt * 128:(tt + 1) * 128, :]:
                           ep.dma_start(self.x0t[:, :], s), inc="cx0", amt=16)
                xin = self.x0t[:, :]
                gate = [ld]
            else:
                xin = xt3[:, tt, :]
                gate = tmax(self.xt_tok)
            rstd, nm = self._tm_stats(xin, gate)
            t7 = em.op("dve", lambda ep, xin=xin, rstd=rstd, nm=nm: ep.tensor_scalar(
                self.ytm[:, :], xin, rstd, nm, ALU.mult, ALU.add), inc="c_dve")
            self.rreg("col1", t7)
            if hbm_src is not None:
                self.rreg("x0t", t7)
            em.waits("pe", tmax(t7, self.rgate("PC0", "PC1")))
            tk = None
            for ch in range(NCH):
                tk = em.op("pe", lambda ep, ch=ch: ep.transpose(
                    self.PC[:, ch * 128:(ch + 1) * 128],
                    self.ytm[:, ch * 128:(ch + 1) * 128], self.identf[:, :]),
                    inc="c_pe" if ch == NCH - 1 else None)
            self.rreg("ytm", tk)
            self.rreg("big1", tk)
            em.waits("dve", [tk])
            dst = y3[:, :, 1 + tt * 128: 1 + (tt + 1) * 128]
            prev = em.op("dve", lambda ep, dst=dst: ep.tensor_copy(
                dst, self.PC[:, 0:1024].rearrange("p (c n) -> p c n", n=128)), inc="c_dve")
            self.rreg("PC0", prev)
            self.rreg("PC1", prev)
        self.y_tok = tmax(self.y_tok, prev)

    def _tm_stats(self, xin, gate):
        em = self.em
        c = self.col1
        yg = self.rgate("ytm")
        em.waits("dve", tmax(gate, yg, self.rgate("col1", "sqtm")))
        em.waits("act", tmax(gate, yg))
        rs = em.op("dve", lambda ep: ep.reduce_sum(c[:, 0:1], xin, axis=AX.X))
        t_sq = em.op("act", lambda ep: ep.activation(self.sqtm[:, :], xin, AF.Square), inc="c_act")
        em.waits("dve", [t_sq])
        rr = em.op("dve", lambda ep: ep.reduce_sum(c[:, 1:2], self.sqtm[:, :], axis=AX.X), inc="c_dve")
        self.rreg("sqtm", rr)
        em.op("dve", lambda ep: ep.tensor_scalar_mul(c[:, 2:3], c[:, 0:1], 1.0 / CM))
        em.op("dve", lambda ep: ep.tensor_tensor(c[:, 3:4], c[:, 2:3], c[:, 2:3], ALU.mult))
        t3 = em.op("dve", lambda ep: ep.scalar_tensor_tensor(c[:, 4:5], c[:, 1:2], 1.0 / CM, c[:, 3:4],
                                                             ALU.mult, ALU.subtract), inc="c_dve")
        em.waits("act", [t3])
        em.waits("act", [self.eps_tok])
        em.op("act", lambda ep: ep.activation(c[:, 5:6], c[:, 4:5], AF.Ln, bias=self.epsc[:, 0:1]))
        t5 = em.op("act", lambda ep: ep.activation(c[:, 6:7], c[:, 5:6], AF.Exp, scale=-0.5), inc="c_act")
        em.waits("dve", [t5])
        em.op("dve", lambda ep: ep.tensor_scalar(c[:, 7:8], c[:, 2:3], c[:, 6:7], -1.0,
                                                 ALU.mult, ALU.mult))
        return c[:, 6:7], c[:, 7:8]

    def _assemble_x(self, base_hbm, into_xt):
        em = self.em
        y3 = self.y3()
        s = self.seq
        xt3 = self.xt3()
        self.xt_tok = []
        if not into_xt:
            em.waits("sp", [("cpk", em.cnt["cpk"])])
        for tt in range(NTT):
            em.waits("sp", self.rgate("x0t"))
            ld = em.op("sp", lambda ep, src=base_hbm[tt * 128:(tt + 1) * 128, :]:
                       ep.dma_start(self.x0t[:, :], src), inc="cx0", amt=16)
            em.waits("pe", tmax(self.y_tok, self.rgate("PC0", "PC1")))
            tk = None
            for ch in range(NCH):
                src = y3[:, ch, 1 + tt * 128:1 + (tt + 1) * 128]
                tk = em.op("pe", lambda ep, src=src, ch=ch: ep.transpose(
                    self.PC[:, ch * 128:(ch + 1) * 128], src, self.identf[:, :]),
                    inc="c_pe" if ch == NCH - 1 else None)
            self.rreg("big1", tk)
            em.waits("dve", [tk, ld])
            if into_xt:
                dst = xt3[:, tt, :].rearrange("p (c n) -> p c n", n=128)
                em.waits("dve", self.rgate(("xt", tt)))
            else:
                dst = self.x0t[:, :].rearrange("p (c n) -> p c n", n=128)
            tadd = em.op("dve", lambda ep, dst=dst: ep.scalar_tensor_tensor(
                dst, self.PC[:, 0:1024].rearrange("p (c n) -> p c n", n=128), 0.5,
                self.x0t[:, :].rearrange("p (c n) -> p c n", n=128), ALU.mult, ALU.add), inc="c_dve")
            self.rreg("PC0", tadd)
            self.rreg("PC1", tadd)
            self.rreg("x0t", tadd)
            if into_xt:
                self.xt_tok.append(tadd)
            else:
                em.waits("sp", [tadd])
                st = em.op("sp", lambda ep, d=self.out[s * T + tt * 128:s * T + (tt + 1) * 128, :]:
                           ep.dma_start(d, self.x0t[:, :]), inc="d_sp", amt=16)
                self.rreg("x0t", st)

    def _park_x2(self):
        em = self.em
        s = self.seq
        xt3 = self.xt3()
        em.waits("sp", tmax(self.xt_tok))
        self.park_tok = []
        for tt in range(NTT):
            src = xt3[:, tt, :]
            st = em.op("sp", lambda ep, src=src, d=self.s_x2[s * T + tt * 128:s * T + (tt + 1) * 128, :]:
                       ep.dma_start(d, src), inc="cpk", amt=16)
            self.park_tok.append(st)
            self.rreg(("xt", tt), st)
            self.rreg("big2", st)

    def _attn(self, l):
        em = self.em
        y3 = self.y3()
        zb3 = self.b2bf(0, NCH * 513).rearrange("p (c n) -> p c n", n=513)
        o = NCH * 513
        k2r = self.b2bf(o, o + NCH * 1024); o += NCH * 1024
        q2r = self.b2bf(o, o + NCH * 512); o += NCH * 512
        qraw = self.b2bf(o, o + NCH * 512); o += NCH * 512
        kraw = self.b2bf(o, o + NCH * 512); o += NCH * 512
        vtm = self.b2bf(o, o + 6 * 1040); o += 6 * 1040
        qp = self.b2bf(o, o + 512); o += 512
        kp = self.b2bf(o, o + 512); o += 512
        sqs = self.b2bf(o, o + 512); o += 512
        assert o <= 32768, o
        v65 = vtm.rearrange("p (sh n) -> p sh n", n=65)
        qkvp = self.big3[:, 0:4 * NCH * 128]
        wop = self.big3[:, 4 * NCH * 128:6 * NCH * 128]
        vw = self.big3[:, 6 * NCH * 128:6 * NCH * 128 + NCH * CM]
        PT = self.big3[:, 14 * 1024:15 * 1024]
        attnU = self.big3[:, 15 * 1024:15 * 1024 + NCH * 512]
        assert 15 * 1024 + NCH * 512 <= 20480

        big2_gate = self.rgate("big2")
        big3_gate = self.rgate("big3")
        em.waits("sp", tmax(self.conv_tok[l], big3_gate, self.rgate("vw")))
        for vcc in range(NCH):
            vsrc = reap(self.s_wqkv[l, 16 + vcc], [(128, 128), (16384, NCH), (1, 128)])
            vwdst = AP(vw.tensor, vw.offset + vcc * 128, [[vw.ap[0][0], 128], [CM, NCH], [1, 128]])
            vw_tok = em.op("sp", lambda ep, vsrc=vsrc, vwdst=vwdst: ep.dma_start(vwdst, vsrc),
                           inc="cvw", amt=16)
        em.waits("dve", tmax(big2_gate, self.rgate("vtm")))
        ones_tok = em.op("dve", lambda ep: ep.memset(v65[:, :, 64:65], 1.0), inc="c_dve")

        y_upd = []
        pidx = 0
        for tc in range(NTC):
            t0 = tc * TC
            em.waits("dve", tmax(self.y_tok, self.rgate("zb"), big2_gate))
            big2_gate = None
            if tc == 0:
                zb_tok = em.op("dve", lambda ep: ep.tensor_copy(
                    zb3[:, :, 0:513], y3[:, :, 0:513]), inc="c_dve")
            else:
                em.op("dve", lambda ep, t0=t0: ep.tensor_copy(
                    zb3[:, :, 1:513], y3[:, :, t0 + 1:t0 + 513]))
                zb_tok = em.op("dve", lambda ep: ep.tensor_copy(
                    zb3[:, :, 0:1], self.bsave[:, :].rearrange("p (c n) -> p c n", n=1)), inc="c_dve")
            if tc < NTC - 1:
                bs = em.op("dve", lambda ep, t0=t0: ep.tensor_copy(
                    self.bsave[:, :].rearrange("p (c n) -> p c n", n=1),
                    y3[:, :, t0 + TC:t0 + TC + 1]), inc="c_dve")

            em.waits("sp", self.rgate(("tabr", tc % 2)))
            tsrc = self.d_c_tabs.rearrange("p (k t) -> p k t", t=T)[:, :, t0:t0 + TC]
            tslot = (tc % 2) * 4 * TC
            tab_tok = em.op("sp", lambda ep, tsrc=tsrc, tslot=tslot: ep.dma_start(
                self.tabs[:, tslot:tslot + 4 * TC].rearrange("p (k t) -> p k t", t=TC), tsrc),
                inc=f"tab{tc % 2}", amt=16)

            def zsl(c):
                off = 1 if c < 4 else 0
                return zb3[:, c, off:off + TC]

            def ztile(c, ti):
                off = 1 if c < 4 else 0
                return zb3[:, c, off + ti * 128:off + ti * 128 + 128]

            qk_done = {}
            sq_by = {}
            sqslots = [sqs, qp]
            for oc in range(17):
                if oc < 16:
                    slot = pidx % 4
                    pidx += 1
                    em.waits("sp", self.rgate(("pc", slot)))
                    src = reap(self.s_wqkv[l, oc], [(128, 128), (128 * 128, NCH), (1, 128)])
                    dst = qkvp[:, slot * NCH * 128:(slot + 1) * NCH * 128]
                    ptok = em.op("sp", lambda ep, s=src, d=dst: ep.dma_start(d, s),
                                 inc=f"qkp{slot}", amt=16)
                    reg = oc % 2
                    pk = "PA" + str(reg)
                    em.waits("pe", tmax(ptok, zb_tok, self.rgate(pk)))
                    mm = None
                    for sc in range(NCH):
                        mm = em.op("pe", lambda ep, sc=sc, dst=dst, reg=reg: ep.matmul(
                            self.PA[:, reg * 512:reg * 512 + 512], dst[:, sc * 128:(sc + 1) * 128],
                            zsl(sc), start=(sc == 0), stop=(sc == NCH - 1), skip_group_check=True),
                            inc="c_pe" if sc == NCH - 1 else None)
                    self.rreg(("pc", slot), mm)
                    self.rreg("zb", mm)
                    raw = qraw if oc < 8 else kraw
                    c = oc % 8
                    em.waits("dve", tmax(mm, self.rgate(("raw", oc < 8, c))))
                    ev = em.op("dve", lambda ep, raw=raw, c=c, reg=reg: ep.tensor_copy(
                        raw[:, c * 512:(c + 1) * 512], self.PA[:, reg * 512:reg * 512 + 512]), inc="c_dve")
                    self.rreg(pk, ev)
                    sslot = oc % 2
                    em.waits("act", tmax(ev, self.rgate(("sqs", sslot))))
                    sq = em.op("act", lambda ep, raw=raw, c=c, sslot=sslot: ep.activation(
                        sqslots[sslot][:, 0:512], raw[:, c * 512:(c + 1) * 512],
                        AF.Square), inc="c_act")
                    self.rreg(("raw", oc < 8, c), sq)
                    sq_by[oc] = (sq, sslot)
                po = oc - 1
                if po >= 0:
                    sqtok, sslot = sq_by.pop(po)
                    if po == 0:
                        em.waits("pe", tmax(sqtok, self.rgate("PD")))
                    elif po == 8:
                        em.waits("pe", tmax(sqtok, self.rgate("PE")))
                    else:
                        em.waits("pe", [sqtok])
                    pdst = self.PD if po < 8 else self.PE_
                    g = em.op("pe", lambda ep, po=po, pdst=pdst, sslot=sslot: ep.matmul(
                        pdst[0:16, :], self.bselq[:, (po % 8) * 16:(po % 8 + 1) * 16],
                        sqslots[sslot][:, 0:512],
                        start=(po % 8 == 0), stop=(po % 8 == 7), skip_group_check=True),
                        inc="c_pe")
                    self.rreg(("sqs", sslot), g)
                    if po == 7:
                        qk_done["q"] = g
                    if po == 15:
                        qk_done["k"] = g
            em.waits("dve", tmax(qk_done["q"], qk_done["k"], self.rgate("nss")))
            c1 = em.op("dve", lambda ep: ep.tensor_scalar_max(self.nstat_s[:, :], self.PD[0:16, :],
                                                              1e-24), inc="c_dve")
            em.waits("act", [c1])
            em.op("act", lambda ep: ep.activation(self.nstat_s[:, :], self.nstat_s[:, :], AF.Ln))
            c4 = em.op("act", lambda ep: ep.activation(self.qinv16, self.nstat_s[:, :], AF.Exp,
                                                       scale=-0.5), inc="c_act")
            em.waits("dve", [c4])
            c2 = em.op("dve", lambda ep: ep.tensor_scalar_max(self.nstat_s[:, :], self.PE_[0:16, :],
                                                              1e-24), inc="c_dve")
            self.rreg("PD", c1)
            self.rreg("PE", c2)
            em.waits("act", [c2])
            em.op("act", lambda ep: ep.activation(self.nstat_s[:, :], self.nstat_s[:, :], AF.Ln))
            c6 = em.op("act", lambda ep: ep.activation(self.kinv16, self.nstat_s[:, :], AF.Exp,
                                                       scale=-0.5), inc="c_act")
            em.waits("dve", self.rgate(("sqs", 0), ("sqs", 1)))
            for c in range(NCH):
                for qk in range(2):
                    raw = qraw if qk == 0 else kraw
                    inv16 = self.qinv16 if qk == 0 else self.kinv16
                    gain = self.qs_pp if qk == 0 else self.ks_pp
                    tab_c = tslot + (0 if qk == 0 else 2 * TC)
                    tab_s = tslot + (TC if qk == 0 else 3 * TC)
                    ntok = c4 if qk == 0 else c6
                    pbuf = qp if qk == 0 else kp
                    bcdst = self.PE_ if qk == 0 else self.PD
                    bkey = "PE" if qk == 0 else "PD"
                    em.waits("pe", tmax(ntok, self.rgate(bkey)))
                    bc = em.op("pe", lambda ep, c=c, inv16=inv16, bcdst=bcdst: ep.matmul(
                        bcdst[:, :], self.selpat[:, c * 128:(c + 1) * 128], inv16,
                        start=True, stop=True), inc="c_pe")
                    em.waits("act", tmax(bc, self.rgate(("bcs", qk))))
                    bce = em.op("act", lambda ep, b=qk, bcdst=bcdst: ep.activation(
                        self.bcq_sb[:, b * TC:(b + 1) * TC], bcdst[:, :], AF.Identity), inc="c_act")
                    self.rreg(bkey, bce)
                    self.rreg("nss", bce)
                    em.waits("dve", tmax(bce, self.rgate(("pp", qk)),
                                         self.rgate(("raw", qk == 0, c))))
                    st = em.op("dve", lambda ep, c=c, raw=raw, gain=gain, pbuf=pbuf, b=qk:
                               ep.scalar_tensor_tensor(pbuf[:, 0:512],
                                                       raw[:, c * 512:(c + 1) * 512],
                                                       gain[:, l:l + 1],
                                                       self.bcq_sb[:, b * TC:(b + 1) * TC],
                                                       ALU.mult, ALU.mult), inc="c_dve")
                    self.rreg(("bcs", qk), st)
                    pbb = qk * 512
                    em.waits("pe", tmax(st, self.rgate("PB" + str(qk))))
                    pm = em.op("pe", lambda ep, pbuf=pbuf, pbb=pbb: ep.matmul(
                        self.PB[:, pbb:pbb + 512], self.prot[:, :], pbuf[:, 0:512],
                        start=True, stop=True), inc="c_pe")
                    self.rreg(("pp", qk), pm)
                    if qk == 0:
                        dsl = q2r[:, c * 512:(c + 1) * 512]
                        wkey = ("q2", c)
                        scr, skey = self.scrA, "scrA"
                    else:
                        dsl = k2r[:, c * 1024 + (tc % 2) * 512: c * 1024 + (tc % 2) * 512 + 512]
                        wkey = ("k2", c, tc % 2)
                        scr, skey = self.scrB, "scrB"
                    em.waits("pool", tmax(st, tab_tok, self.rgate(wkey)))
                    tt1 = em.op("pool", lambda ep, dsl=dsl, pbuf=pbuf, tab_c=tab_c:
                          ep.tensor_tensor(dsl, pbuf[:, 0:512],
                                           self.tabs[:, tab_c:tab_c + TC], ALU.mult))
                    em.waits("dve", tmax(pm, tab_tok, self.rgate(skey)))
                    stt = em.op("dve", lambda ep, tab_s=tab_s, scr=scr, pbb=pbb: ep.scalar_tensor_tensor(
                        scr[:, :], self.PB[:, pbb:pbb + 512], 1.0,
                        self.tabs[:, tab_s:tab_s + TC], ALU.mult, ALU.mult))
                    em.waits("pool", [stt])
                    a3 = em.op("pool", lambda ep, dsl=dsl, scr=scr: ep.tensor_tensor(
                        dsl, dsl, scr[:, :], ALU.add), inc="c_gp")
                    self.rreg(("tabr", tc % 2), a3)
                    self.rreg(("pp", qk), a3)
                    self.rreg("PB" + str(qk), stt)
                    self.rreg(skey, a3)
                    if qk == 0:
                        self.rreg(("sqs", 1), tt1)
                        self.rreg(("sqs", 1), pm)
                    self.wr[wkey] = a3
            for ti in range(4):
                tile = tc * 4 + ti
                slot = tile % 6
                em.waits("pe", tmax(vw_tok, zb_tok, ones_tok, self.rgate("PB0", "PB1")))
                mm = None
                for sc in range(NCH):
                    for g2 in range(2):
                        mm = em.op("pe", lambda ep, sc=sc, g2=g2, ti=ti: ep.matmul(
                            self.PB[:, g2 * 512:(g2 + 1) * 512], ztile(sc, ti),
                            vw[:, sc * CM + g2 * 512: sc * CM + (g2 + 1) * 512],
                            start=(sc == 0), stop=(sc == NCH - 1), skip_group_check=True),
                            inc="c_pe" if (sc == NCH - 1 and g2 == 1) else None)
                self.rreg("zb", mm)
                self.rreg("vw", mm)
                em.waits("dve", tmax(mm, self.rgate(("vt", slot))))
                dstv = v65[:, slot * 16:(slot + 1) * 16, 0:64]
                ev = em.op("dve", lambda ep, dstv=dstv: ep.tensor_copy(
                    dstv, self.PB[:, 0:1024].rearrange("p (h n) -> p h n", n=64)), inc="c_dve")
                self.rreg("PB0", ev)
                self.rreg("PB1", ev)
                self.wr[("vt", slot)] = ev
            em.waits("pe", self.rgate("PB0", "PB1"))
            PT1 = self.tmu[:, :].bitcast(BF16)[:, 0:1024]
            ptbufs = [PT, PT1]
            if tc == 0:
                self.tmu_gate = tmax(self.rgate("sqtm"), self.rgate("ytm"))
            its = []
            for wloc in range(DBG_NW):
                w = 2 * tc + wloc
                w0 = (w == 0)
                jts = [2 * w - 2, 2 * w - 1, 2 * w, 2 * w + 1]
                live = [(idx, jt) for idx, jt in enumerate(jts) if not (w0 and idx < 2)]
                for h in range(DBG_NH):
                    its.append((wloc, w0, live, h))
            st_mk = {}
            st_dr = {}
            den_done = None

            def emit_sim(i):
                wloc, w0, live, h = its[i]
                par = i % 2
                ptb = ptbufs[par]
                ch, hf = h // 2, h % 2
                simp = self.PC if par == 0 else self.PA
                sk0, sk1 = ("PC0", "PC1") if par == 0 else ("PA0", "PA1")
                em.waits("pe", tmax(self.wr.get(("q2", ch)), self.const_tok,
                                    *(self.wr.get(("k2", ch, (jt // 4) % 2)) for _, jt in live)))
                em.waits("pe", self.rgate(sk0, sk1))
                qrhs = q2r[hf * 64:hf * 64 + 64, ch * 512 + wloc * 256: ch * 512 + (wloc + 1) * 256]
                if w0:
                    em.op("pe", lambda ep, simp=simp: ep.matmul(
                        simp[:, 512:1024], self.identb[:, :], self.maskb0[:, :],
                        start=True, stop=False, skip_group_check=True))
                else:
                    for half in range(2):
                        em.op("pe", lambda ep, simp=simp, half=half: ep.matmul(
                            simp[:, half * 512:half * 512 + 512], self.identb[:, :],
                            self.maskb[:, half * 512:half * 512 + 512],
                            start=True, stop=False, skip_group_check=True))
                mmt = None
                for idx, jt in live:
                    ksl = k2r[hf * 64:hf * 64 + 64,
                              ch * 1024 + ((jt // 4) % 2) * 512 + (jt % 4) * 128:
                              ch * 1024 + ((jt // 4) % 2) * 512 + (jt % 4) * 128 + 128]
                    if idx == 0:
                        i0, i1 = 0, 128
                    elif idx == 3:
                        i0, i1 = 128, 256
                    else:
                        i0, i1 = 0, 256
                    mmt = em.op("pe", lambda ep, ksl=ksl, qrhs=qrhs, idx=idx, i0=i0, i1=i1,
                                simp=simp: ep.matmul(
                                    simp[:, idx * 256 + i0: idx * 256 + i1],
                                    ksl, qrhs[:, i0:i1], start=False, stop=True,
                                    skip_group_check=True),
                                inc="c_pe" if (idx, jt) == live[-1] else None)
                    self.rreg(("q2", ch), mmt)
                    self.rreg(("k2", ch, (jt // 4) % 2), mmt)
                em.waits("act", tmax(mmt, self.rgate(("PTb", par)), self.tmu_gate))
                self.tmu_gate = None
                if w0:
                    ex = em.op("act", lambda ep, simp=simp, ptb=ptb: ep.activation(
                        ptb[:, 512:1024], simp[:, 512:1024], AF.Exp), inc="c_act")
                else:
                    ex = em.op("act", lambda ep, simp=simp, ptb=ptb: ep.activation(
                        ptb[:, :], simp[:, :], AF.Exp), inc="c_act")
                self.rreg(sk0, ex)
                self.rreg(sk1, ex)
                st_mk[i] = ex

            def emit_pv(i):
                wloc, w0, live, h = its[i]
                par = i % 2
                ptb = ptbufs[par]
                ch, hf = h // 2, h % 2
                mk = st_mk.pop(i)
                em.waits("pe", tmax(mk, self.rgate(("pv", par)),
                                    *(self.wr.get(("vt", jt % 6)) for _, jt in live)))
                pvr = par * 256
                mmv = None
                for n, (idx, jt) in enumerate(live):
                    mmv = em.op("pe", lambda ep, jt=jt, idx=idx, pvr=pvr, h=h, ptb=ptb,
                                st=(n == 0), sp=(n == len(live) - 1): ep.matmul(
                        self.PB[0:65, pvr:pvr + 256], v65[:, (jt % 6) * 16 + h, :],
                        ptb[:, idx * 256:(idx + 1) * 256],
                        start=st, stop=sp, skip_group_check=True),
                        inc="c_pe" if n == len(live) - 1 else None)
                    self.rreg(("vt", jt % 6), mmv)
                self.rreg(("PTb", par), mmv)
                if par == 1:
                    self.rreg("ytm", mmv)
                em.waits("dve", tmax(mmv, self.rgate(("aU", ch, hf, wloc))))
                du = em.op("dve", lambda ep, hf=hf, ch=ch, wloc=wloc, pvr=pvr: ep.tensor_copy(
                    attnU[hf * 64:hf * 64 + 64,
                          ch * 512 + wloc * 256: ch * 512 + (wloc + 1) * 256],
                    self.PB[0:64, pvr:pvr + 256]), inc="c_dve")
                em.waits("dve", self.rgate(("dr", par)))
                dr = em.op("dve", lambda ep, pvr=pvr, drslot=par: ep.tensor_copy(
                    self.denrow[0:1, drslot * 256:(drslot + 1) * 256],
                    self.PB[64:65, pvr:pvr + 256]), inc="c_dve")
                self.rreg(("pv", par), du)
                self.rreg(("pv", par), dr)
                self.wr[("aU", ch, hf, wloc)] = du
                st_dr[i] = dr

            def emit_den(i):
                wloc, w0, live, h = its[i]
                par = i % 2
                dr = st_dr.pop(i)
                em.waits("pe", [dr])
                dn = em.op("pe", lambda ep, h=h, wloc=wloc, drslot=par: ep.matmul(
                    self.PB[0:16, 512 + wloc * 256:512 + (wloc + 1) * 256],
                    self.unit16[:, h * 16:(h + 1) * 16],
                    self.denrow[0:1, drslot * 256:(drslot + 1) * 256],
                    start=(h == 0), stop=(i == len(its) - 1), skip_group_check=True),
                    inc="c_pe")
                self.rreg(("dr", par), dn)
                return dn

            for i in range(len(its) + 1):
                if i < len(its):
                    emit_sim(i)
                if i >= 1:
                    den_done = emit_den(i - 1)
                if i < len(its):
                    emit_pv(i)
            em.waits("act", [den_done])
            em.op("act", lambda ep: ep.activation(self.nstat_s[:, :], self.PB[0:16, 512:1024], AF.Ln))
            iv = em.op("act", lambda ep: ep.activation(self.invden16[:, :], self.nstat_s[:, :],
                                                       AF.Exp, scale=-1.0), inc="c_act")
            self.rreg("PB1", iv)
            last_nm = None
            for c in range(NCH):
                bdst = self.PE_ if c % 2 == 0 else self.PD
                bkey = "PE" if c % 2 == 0 else "PD"
                em.waits("pe", tmax(iv, self.rgate(bkey)))
                bc = em.op("pe", lambda ep, c=c, bdst=bdst: ep.matmul(
                    bdst[:, :], self.selpat[:, c * 128:(c + 1) * 128], self.invden16[:, :],
                    start=True, stop=True), inc="c_pe")
                em.waits("dve", tmax(bc, self.wr.get(("aU", c, 0, 0)), self.wr.get(("aU", c, 0, 1)),
                                     self.wr.get(("aU", c, 1, 0)), self.wr.get(("aU", c, 1, 1))))
                nm = em.op("dve", lambda ep, c=c, bdst=bdst: ep.tensor_tensor(
                    attnU[:, c * 512:(c + 1) * 512], attnU[:, c * 512:(c + 1) * 512],
                    bdst[:, :], ALU.mult), inc="c_dve")
                self.rreg(bkey, nm)
                self.wr[("aN", c)] = nm
                last_nm = nm
            for oc in range(NCH):
                wslot = oc % 2
                em.waits("sp", tmax(self.conv_tok[l], self.rgate(("wo", wslot))))
                wsrc = reap(self.s_wo[l, oc], [(128, 128), (128 * 128, NCH), (1, 128)])
                wdst = wop[:, wslot * NCH * 128:(wslot + 1) * NCH * 128]
                wtok = em.op("sp", lambda ep, s=wsrc, d=wdst: ep.dma_start(d, s),
                             inc=f"wop{wslot}", amt=16)
                reg = oc % 2
                pk = "PA" + str(reg)
                em.waits("pe", tmax(wtok, self.rgate(pk),
                                    *(self.wr.get(("aN", sc)) for sc in range(NCH))))
                mm = None
                for sc in range(NCH):
                    mm = em.op("pe", lambda ep, sc=sc, wdst=wdst, reg=reg: ep.matmul(
                        self.PA[:, reg * 512:reg * 512 + 512], wdst[:, sc * 128:(sc + 1) * 128],
                        attnU[:, sc * 512:(sc + 1) * 512],
                        start=(sc == 0), stop=(sc == NCH - 1), skip_group_check=True),
                        inc="c_pe" if sc == NCH - 1 else None)
                self.rreg(("wo", wslot), mm)
                for sc in range(NCH):
                    self.rreg(("aU", sc, 0, 0), mm)
                em.waits("dve", [mm])
                dsty = y3[:, oc, 1 + t0:1 + t0 + TC]
                up = em.op("dve", lambda ep, dsty=dsty, reg=reg: ep.tensor_tensor(
                    dsty, dsty, self.PA[:, reg * 512:reg * 512 + 512], ALU.add), inc="c_dve")
                self.rreg(pk, up)
                y_upd.append(up)
        self.y_tok = tmax(y_upd)
        self.rreg("big2", tmax(y_upd))
        self.rreg("big3", tmax(y_upd))
        for c in range(NCH):
            for hf in range(2):
                for wl in range(2):
                    self.rd.pop(("aU", c, hf, wl), None)
            self.rd.pop(("q2", c), None)
            for sl in range(2):
                self.rd.pop(("k2", c, sl), None)
        for s2 in range(6):
            for t in self.rd.pop(("vt", s2), []):
                self.rreg("vtm", t)
        self.rreg("vtm", tmax(y_upd))

    def _ffn(self, l):
        em = self.em
        y3 = self.y3()
        zb3 = self.b2bf(0, NCH * 513).rearrange("p (c n) -> p c n", n=513)
        o = NCH * 513
        xn = self.b2bf(o, o + NCH * 512); o += NCH * 512
        hsb = self.b2bf(o, o + NJ * 512); o += NJ * 512
        gel = self.b2bf(o, o + 2 * 512); o += 2 * 512
        assert o <= 32768
        w1p = self.big3[:, 0:4 * 2 * NCH * 128]
        w2p = self.big3[:, 4 * 2 * NCH * 128:4 * 2 * NCH * 128 + 4 * CM]

        big2_gate = self.rgate("big2", "vtm")
        big3_gate = self.rgate("big3", "vw")
        y_upd = []
        self._gbi = getattr(self, "_gbi", 0)
        gbsem = f"cgb{self._gbi % 4}"
        self._gbi += 1
        em.waits("pool", tmax(self.rgate("gb2"), self.const_gp, self.conv_tok[NL - 1]))
        gb = em.op("pool", lambda ep: ep.dma_start(
            self.gb2[:, :], self.d_gb_host[:, l * CM:(l + 1) * CM]), inc=gbsem, amt=16)

        for tc in range(NTC):
            t0 = tc * TC
            em.waits("dve", tmax(self.y_tok, self.rgate("zb"), big2_gate))
            big2_gate = None
            if tc == 0:
                zb_tok = em.op("dve", lambda ep: ep.tensor_copy(
                    zb3[:, :, 0:513], y3[:, :, 0:513]), inc="c_dve")
            else:
                em.op("dve", lambda ep, t0=t0: ep.tensor_copy(
                    zb3[:, :, 1:513], y3[:, :, t0 + 1:t0 + 513]))
                zb_tok = em.op("dve", lambda ep: ep.tensor_copy(
                    zb3[:, :, 0:1], self.bsave[:, :].rearrange("p (c n) -> p c n", n=1)), inc="c_dve")
            if tc < NTC - 1:
                em.op("dve", lambda ep, t0=t0: ep.tensor_copy(
                    self.bsave[:, :].rearrange("p (c n) -> p c n", n=1),
                    y3[:, :, t0 + TC:t0 + TC + 1]))

            def zsl(c):
                off = 1 if c < 4 else 0
                return zb3[:, c, off:off + TC]

            em.waits("pe", tmax(zb_tok, self.rgate("PD")))
            sq_t = None
            for c in range(NCH):
                em.op("pe", lambda ep, c=c: ep.matmul(
                    self.PD[0:1, :], self.onescol[:, :], zsl(c),
                    start=(c == 0), stop=(c == NCH - 1), skip_group_check=True))
                em.waits("act", tmax(zb_tok, self.rgate(("bcs", c % 2))))
                sq = em.op("act", lambda ep, c=c: ep.activation(
                    self.bcq_sb[:, (c % 2) * TC:(c % 2 + 1) * TC], zsl(c), AF.Square), inc="c_act")
                self.rreg("zb", sq)
                if c == 0:
                    em.waits("pe", tmax(sq, self.rgate("PE")))
                else:
                    em.waits("pe", [sq])
                sq_t = em.op("pe", lambda ep, c=c: ep.matmul(
                    self.PE_[0:1, :], self.onescol[:, :],
                    self.bcq_sb[:, (c % 2) * TC:(c % 2 + 1) * TC],
                    start=(c == 0), stop=(c == NCH - 1), skip_group_check=True),
                    inc="c_pe")
                self.rreg(("bcs", c % 2), sq_t)
                self.rreg("zb", sq_t)
            em.waits("dve", tmax(sq_t, self.rgate("x0t")))
            em.op("dve", lambda ep: ep.tensor_scalar_mul(self.mus, self.PD[0:1, :], 1.0 / CM))
            t_sv = em.op("dve", lambda ep: ep.tensor_tensor(self.statv, self.mus, self.mus,
                                                            ALU.mult), inc="c_dve")
            v1 = em.op("dve", lambda ep: ep.scalar_tensor_tensor(
                self.statv, self.PE_[0:1, :], 1.0 / CM, self.statv,
                ALU.mult, ALU.subtract), inc="c_dve")
            self.rreg("PD", v1)
            self.rreg("PE", v1)
            em.waits("act", [v1])
            em.waits("act", [self.eps_tok])
            em.op("act", lambda ep: ep.activation(self.statl, self.statv, AF.Ln,
                                                  bias=self.epsc[0:1, 0:1]))
            r1 = em.op("act", lambda ep: ep.activation(self.arhs[:, :], self.statl,
                                                       AF.Exp, scale=-0.5), inc="c_act")
            em.waits("dve", [r1])
            cr = em.op("dve", lambda ep: ep.scalar_tensor_tensor(
                self.c2rhs[0:1, :], self.PD[0:1, :], -1.0 / CM, self.arhs[:, :],
                ALU.mult, ALU.mult), inc="c_dve")
            self.rreg("PD", cr)
            self.rreg("x0t", t_sv)
            em.waits("pe", tmax(r1, cr, gb, self.rgate("PE")))
            at = em.op("pe", lambda ep: ep.matmul(self.PE_[:, :], self.ones1[:, :],
                                                  self.arhs[:, :], start=True, stop=True), inc="c_pe")
            xn_tok = []
            for c in range(NCH):
                pk = "PB" + str(c % 2)
                em.waits("pe", self.rgate(pk))
                c2 = em.op("pe", lambda ep, c=c: ep.matmul(
                    self.PB[:, (c % 2) * 512:(c % 2) * 512 + 512],
                    self.gb2[:, c * 128:(c + 1) * 128], self.c2rhs[:, :],
                    start=True, stop=True), inc="c_pe")
                em.waits("dve", tmax(at, c2, self.rgate(("xn", c))))
                p1 = em.op("dve", lambda ep, c=c: ep.tensor_tensor(
                    self.t1s[:, :], zsl(c), self.PE_[:, :], ALU.mult),
                    inc="c_dve")
                self.rreg("zb", p1)
                p2 = em.op("dve", lambda ep, c=c: ep.scalar_tensor_tensor(
                    xn[:, c * 512:(c + 1) * 512], self.t1s[:, :],
                    self.g_pp[:, l * NCH + c:l * NCH + c + 1],
                    self.PB[:, (c % 2) * 512:(c % 2) * 512 + 512],
                    ALU.mult, ALU.add), inc="c_dve")
                self.rreg(pk, p2)
                self.rreg("PE", p2)
                xn_tok.append(p2)
                self.wr[("xn", c)] = p2
            self.rreg("gb2", xn_tok[-1])
            h_tok = [None] * NJ
            for j in range(NJ):
                wj = 128 if j < NJ - 1 else WJL
                slot = j % 4
                em.waits("sp", tmax(self.conv_tok[l], self.rgate(("w1", slot)), big3_gate))
                big3_gate = None
                wsrc = reap(self.s_w1[l, j], [(128, 128), (NCH * 128 * 128, 2), (128 * 128, NCH), (1, wj)])
                wdst = w1p[:, slot * 2 * NCH * 128:(slot + 1) * 2 * NCH * 128]
                wldst = reap(wdst, [(wdst.ap[0][0], 128), (NCH * 128, 2), (128, NCH), (1, wj)])
                wtok = em.op("sp", lambda ep, s=wsrc, d=wldst: ep.dma_start(d, s),
                             inc=f"w1p{slot}", amt=16)
                em.waits("pe", tmax(wtok, *xn_tok))
                em.waits("pe", self.rgate("PC" + str(j % 2), "PA" + str(j % 2)))
                mma = mmg = None
                for sc in range(NCH):
                    mma = em.op("pe", lambda ep, sc=sc, wdst=wdst, wj=wj, j=j: ep.matmul(
                        self.PC[0:wj, (j % 2) * 512:(j % 2) * 512 + 512],
                        wdst[:, sc * 128:sc * 128 + wj], xn[:, sc * 512:(sc + 1) * 512],
                        start=(sc == 0), stop=(sc == NCH - 1), skip_group_check=True))
                for sc in range(NCH):
                    mmg = em.op("pe", lambda ep, sc=sc, wdst=wdst, wj=wj, j=j: ep.matmul(
                        self.PA[0:wj, (j % 2) * 512:(j % 2) * 512 + 512],
                        wdst[:, NCH * 128 + sc * 128: NCH * 128 + sc * 128 + wj],
                        xn[:, sc * 512:(sc + 1) * 512],
                        start=(sc == 0), stop=(sc == NCH - 1), skip_group_check=True),
                        inc="c_pe" if sc == NCH - 1 else None)
                self.rreg(("w1", slot), mmg)
                for c in range(NCH):
                    self.rreg(("xn", c), mmg)
                em.waits("act", tmax(mmg, self.rgate(("gel", j % 2))))
                gl = em.op("act", lambda ep, j=j, wj=wj: ep.activation(
                    gel[0:wj, (j % 2) * 512:(j % 2) * 512 + 512],
                    self.PA[0:wj, (j % 2) * 512:(j % 2) * 512 + 512],
                    AF.Gelu, bias=self.b1g_pp[0:wj, l * NJ + j:l * NJ + j + 1]), inc="c_act")
                self.rreg("PA" + str(j % 2), gl)
                em.waits("dve", tmax(gl, self.rgate(("h", j))))
                ht = em.op("dve", lambda ep, j=j, wj=wj: ep.scalar_tensor_tensor(
                    hsb[0:wj, j * 512:j * 512 + 512],
                    self.PC[0:wj, (j % 2) * 512:(j % 2) * 512 + 512],
                    self.b1a_pp[0:wj, l * NJ + j:l * NJ + j + 1],
                    gel[0:wj, (j % 2) * 512:(j % 2) * 512 + 512], ALU.add, ALU.mult), inc="c_dve")
                h_tok[j] = ht
                self.rreg("PC" + str(j % 2), ht)
                self.rreg(("gel", j % 2), ht)
                self.wr[("h", j)] = ht
            em.waits("pe", tmax(h_tok))
            for grp in range(2):
                em.waits("pe", self.rgate("PA0", "PA1", "PB0", "PB1"))
                mmlast = None
                for j in range(NJ):
                    wj = 128 if j < NJ - 1 else WJL
                    slot = j % 4
                    em.waits("sp", tmax(self.conv_tok[l], self.rgate(("w2", slot))))
                    wsrc = self.s_w2[l, j * 128:j * 128 + wj, :]
                    wdst = w2p[:, slot * CM:(slot + 1) * CM]
                    wtok = em.op("sp", lambda ep, s=wsrc, d=wdst[0:wj, :]: ep.dma_start(d, s),
                                 inc=f"w2p{slot}", amt=16)
                    em.waits("pe", tmax(wtok))
                    for oc4 in range(4):
                        oc = grp * 4 + oc4
                        pdst = self.PA if oc4 < 2 else self.PB
                        mmlast = em.op("pe", lambda ep, j=j, wj=wj, oc=oc, oc4=oc4, wdst=wdst,
                                       pdst=pdst: ep.matmul(
                                           pdst[:, (oc4 % 2) * 512:(oc4 % 2) * 512 + 512],
                                           wdst[0:wj, (oc % 8) * 128:(oc % 8) * 128 + 128],
                                           hsb[0:wj, j * 512:j * 512 + 512],
                                           start=(j == 0), stop=(j == NJ - 1),
                                           skip_group_check=True),
                                       inc="c_pe" if oc4 == 3 else None)
                    self.rreg(("w2", slot), mmlast)
                    if grp == 1 and j == NJ - 1:
                        for jj in range(NJ):
                            self.rreg(("h", jj), mmlast)
                for oc4 in range(4):
                    oc = grp * 4 + oc4
                    pdst = self.PA if oc4 < 2 else self.PB
                    pk = ("PA" if oc4 < 2 else "PB") + str(oc4 % 2)
                    em.waits("dve", [mmlast])
                    dsty = y3[:, oc, 1 + t0:1 + t0 + TC]
                    up = em.op("dve", lambda ep, dsty=dsty, pdst=pdst, oc4=oc4, oc=oc:
                               ep.scalar_tensor_tensor(dsty, pdst[:, (oc4 % 2) * 512:(oc4 % 2) * 512 + 512],
                                                       self.b2_pp[:, l * NCH + oc:l * NCH + oc + 1],
                                                       dsty, ALU.add, ALU.add), inc="c_dve")
                    self.rreg(pk, up)
                    y_upd.append(up)
        self.y_tok = tmax(y_upd)
        self.rreg("big2", tmax(y_upd))
        self.rreg("big3", tmax(y_upd))
        for c in range(NCH):
            self.rd.pop(("xn", c), None)
        for j in range(NJ):
            self.rd.pop(("h", j), None)

    def _outer(self):
        em = self.em
        xt3 = self.xt3()
        vFM = self.b1bf(16384, 32768)
        E = self.b1bf(0, 16384)
        wT1 = self.big3[:, 0:NCH * CM]
        wT2 = self.big3[:, NCH * CM:2 * NCH * CM]
        knb = self.big3[:, 2 * NCH * CM:2 * NCH * CM + 2 * CM]
        vtb = self.big3[:, 2 * NCH * CM + 2 * CM:2 * NCH * CM + 4 * CM]
        ctxs = self.t1s[:, :].bitcast(BF16)[:, 0:512]
        knraw = self.knraw_v
        ksumr = self.knraw_v[0:1, :]
        bk_row = self.tabs[0:1, 0:CM]
        bv_row = self.tabs[0:1, CM:2 * CM]
        bp_row = self.tabs[0:1, 2 * CM:3 * CM]
        em.waits("pool", tmax(self.rgate(("tabr", 0)), self.rgate(("tabr", 1)), self.const_gp))
        em.op("pool", lambda ep: ep.dma_start(bk_row, self.d_bk_r), inc="cbr", amt=16)
        em.op("pool", lambda ep: ep.dma_start(bv_row, self.d_bv_r), inc="cbr", amt=16)
        em.op("pool", lambda ep: ep.dma_start(bp_row, self.d_bp_r), inc="cbr", amt=16)
        br_tok = ("cbr", em.cnt["cbr"])

        big3_gate = self.rgate("big3", "vw", "vtm")
        big1_gate = tmax(self.rgate("big1"), self.y_tok)

        trpb = self.PC[:, :].bitcast(BF16)
        vfm_tok = None
        for tt in range(NTT):
            xin = xt3[:, tt, :]
            rstd, nm = self._tm_stats(xin, tmax(self.xt_tok))
            tv = em.op("dve", lambda ep, xin=xin, rstd=rstd, nm=nm: ep.tensor_scalar(
                self.ytmb, xin, rstd, nm, ALU.mult, ALU.add), inc="c_dve")
            self.rreg("col1", tv)
            em.waits("pe", tmax(tv, self.rgate("PC0", "PC1")))
            tk = None
            for ch in range(NCH):
                tk = em.op("pe", lambda ep, ch=ch: ep.transpose(
                    trpb[:, ch * 128:(ch + 1) * 128], self.ytmb[:, ch * 128:(ch + 1) * 128],
                    self.identb[:, :]), inc="c_pe" if ch == NCH - 1 else None)
            self.rreg("ytm", tk)
            em.waits("dve", tmax(tk, big1_gate))
            big1_gate = None
            dst = AP(vFM.tensor, vFM.offset + tt * 128,
                     [[vFM.ap[0][0], 128], [T, NCH], [1, 128]])
            vfm_tok = em.op("dve", lambda ep, dst=dst: ep.tensor_copy(
                dst, trpb[:, 0:1024].rearrange("p (c n) -> p c n", n=128)), inc="c_dve")
            self.rreg("PC0", vfm_tok)
            self.rreg("PC1", vfm_tok)
        em.waits("sp", tmax(self.wT_tok, big3_gate))
        w1tok = em.op("sp", lambda ep: ep.dma_start(
            wT1, reap(self.s_wT[0], [(CM, 128), (128 * CM, NCH), (1, CM)])), inc="cwt1", amt=16)
        em.waits("act", tmax(big1_gate, vfm_tok))
        e_toks = []
        for tc in range(NTC):
            for oc in range(NCH):
                pq = self.PA if oc % 4 < 2 else self.PB
                pk = ("PA" if oc % 4 < 2 else "PB") + str(oc % 2)
                em.waits("pe", tmax(w1tok, vfm_tok, self.rgate(pk)))
                mm = None
                for sc in range(NCH):
                    mm = em.op("pe", lambda ep, sc=sc, oc=oc, tc=tc, pq=pq: ep.matmul(
                        pq[:, (oc % 2) * 512:(oc % 2) * 512 + 512],
                        wT1[:, sc * CM + oc * 128: sc * CM + oc * 128 + 128],
                        vFM[:, sc * T + tc * 512: sc * T + (tc + 1) * 512],
                        start=(sc == 0), stop=(sc == NCH - 1), skip_group_check=True),
                        inc="c_pe" if sc == NCH - 1 else None)
                em.waits("act", [mm])
                ex = em.op("act", lambda ep, oc=oc, tc=tc, pq=pq: ep.activation(
                    E[:, oc * T + tc * 512: oc * T + (tc + 1) * 512],
                    pq[:, (oc % 2) * 512:(oc % 2) * 512 + 512],
                    AF.Exp, bias=self.bq_pp[:, oc:oc + 1]), inc="c_act")
                self.rreg(pk, ex)
                e_toks.append(ex)
        em.waits("sp", tmax(e_toks[-1], self.rgate("wT1rd")))
        w2tok = em.op("sp", lambda ep: ep.dma_start(
            wT2, reap(self.s_wT[1], [(CM, 128), (128 * CM, NCH), (1, CM)])), inc="cwt2", amt=16)
        w3tok = em.op("sp", lambda ep: ep.dma_start(
            wT1, reap(self.s_wT[2], [(CM, 128), (128 * CM, NCH), (1, CM)])), inc="cwt1", amt=16)
        em.waits("pe", self.rgate("PC0", "PC1", "PD"))
        ctx_done = None
        ks_done = None
        kn_tok = vt_tok = None
        for tt in range(NTT):
            for which in range(2):
                wt = wT2 if which == 0 else wT1
                wtk = w2tok if which == 0 else w3tok
                brow = bk_row if which == 0 else bv_row
                pq = self.PA if which == 0 else self.PB
                pk0 = ("PA0", "PA1") if which == 0 else ("PB0", "PB1")
                em.waits("pe", tmax(wtk, vfm_tok, br_tok))
                em.waits("pe", self.rgate(*pk0))
                mm = None
                for g2 in range(2):
                    for sc in range(NCH):
                        em.op("pe", lambda ep, sc=sc, g2=g2, tt=tt, wt=wt, pq=pq: ep.matmul(
                            pq[:, g2 * 512:(g2 + 1) * 512],
                            vFM[:, sc * T + tt * 128: sc * T + tt * 128 + 128],
                            wt[:, sc * CM + g2 * 512: sc * CM + (g2 + 1) * 512],
                            start=(sc == 0), stop=False, skip_group_check=True))
                    mm = em.op("pe", lambda ep, g2=g2, brow=brow, pq=pq: ep.matmul(
                        pq[:, g2 * 512:(g2 + 1) * 512], self.ones1[:, :],
                        brow[:, g2 * 512:(g2 + 1) * 512],
                        start=False, stop=True, skip_group_check=True),
                        inc="c_pe" if g2 == 1 else None)
                if which == 0:
                    em.waits("act", tmax(mm, self.rgate("knraw")))
                    ek = em.op("act", lambda ep: ep.activation(knraw[:, :], self.PA[:, 0:1024],
                                                               AF.Exp), inc="c_act")
                    self.rreg("PA0", ek)
                    self.rreg("PA1", ek)
                    em.waits("dve", [ek])
                    dk = em.op("dve", lambda ep: ep.reduce_sum(
                        self.denk[:, :], knraw.rearrange("p (h n) -> p h n", n=64), axis=AX.X),
                        inc="c_dve")
                    em.waits("act", [dk])
                    em.op("act", lambda ep: ep.activation(self.denk[:, :], self.denk[:, :], AF.Ln))
                    ik = em.op("act", lambda ep: ep.activation(self.denk[:, :], self.denk[:, :],
                                                               AF.Exp, scale=-1.0), inc="c_act")
                    em.waits("dve", tmax(ik, self.rgate(("knb", tt % 2))))
                    kt = None
                    for h in range(H):
                        kt = em.op("dve", lambda ep, h=h, tt=tt: ep.tensor_scalar_mul(
                            knb[:, (tt % 2) * CM + h * 64:(tt % 2) * CM + (h + 1) * 64],
                            knraw[:, h * 64:(h + 1) * 64], self.denk[:, h:h + 1]),
                            inc="c_dve" if h == H - 1 else None)
                    self.rreg("knraw", kt)
                    kn_tok = kt
                else:
                    em.waits("dve", tmax(mm, self.rgate(("vtb", tt % 2))))
                    vt_tok = em.op("dve", lambda ep, tt=tt: ep.tensor_copy(
                        vtb[:, (tt % 2) * CM:(tt % 2 + 1) * CM], self.PB[:, 0:1024]), inc="c_dve")
                    self.rreg("PB0", vt_tok)
                    self.rreg("PB1", vt_tok)
            em.waits("pe", tmax(kn_tok, vt_tok, self.rgate("PC0", "PC1")))
            for h in range(H):
                ctx_done = em.op("pe", lambda ep, h=h, tt=tt: ep.matmul(
                    self.PC[(h % 2) * 64:(h % 2) * 64 + 64, (h // 2) * 64:(h // 2) * 64 + 64],
                    knb[:, (tt % 2) * CM + h * 64:(tt % 2) * CM + (h + 1) * 64],
                    vtb[:, (tt % 2) * CM + h * 64:(tt % 2) * CM + (h + 1) * 64],
                    start=True, stop=True, skip_group_check=True),
                    inc="c_pe" if h == H - 1 else None)
            em.waits("dve", [ctx_done])
            if tt == 0:
                ctx_done = em.op("dve", lambda ep: ep.tensor_copy(
                    self.ctxacc[:, :], self.PC[:, 0:512]), inc="c_dve")
            else:
                ctx_done = em.op("dve", lambda ep: ep.tensor_tensor(
                    self.ctxacc[:, :], self.ctxacc[:, :], self.PC[:, 0:512], ALU.add), inc="c_dve")
            self.rreg("PC0", ctx_done)
            for g2 in range(2):
                ks_done = em.op("pe", lambda ep, g2=g2, tt=tt: ep.matmul(
                    self.PD[0:1, :] if g2 == 0 else self.PE_[0:1, :],
                    self.onescol[:, :],
                    knb[:, (tt % 2) * CM + g2 * 512:(tt % 2) * CM + (g2 + 1) * 512],
                    start=(tt == 0), stop=(tt == NTT - 1), skip_group_check=True),
                    inc="c_pe" if (g2 == 1 and tt == NTT - 1) else None)
            self.rreg(("knb", tt % 2), ctx_done)
            self.rreg(("vtb", tt % 2), ctx_done)
        em.waits("dve", [ctx_done, ks_done])
        ctx_ev = em.op("dve", lambda ep: ep.tensor_copy(ctxs[:, :], self.ctxacc[:, :]), inc="c_dve")
        em.op("dve", lambda ep: ep.tensor_copy(ksumr[0:1, 0:512], self.PD[0:1, :]))
        ks_ev = em.op("dve", lambda ep: ep.tensor_copy(ksumr[0:1, 512:1024], self.PE_[0:1, :]),
                      inc="c_dve")
        self.rreg("knraw", ks_ev)
        self.rreg("PD", ks_ev)
        self.rreg("PE", ks_ev)
        em.waits("pe", tmax(ks_ev, self.rgate("PE")))
        tk = None
        for c in range(NCH):
            tk = em.op("pe", lambda ep, c=c, ksumr=ksumr: ep.matmul(
                self.PE_[:, c:c + 1], ksumr[0:1, c * 128:(c + 1) * 128],
                self.onescol[0:1, 0:1], start=True, stop=True, skip_group_check=True),
                inc="c_pe" if c == NCH - 1 else None)
        em.waits("dve", [tk])
        kc = em.op("dve", lambda ep: ep.tensor_copy(self.ksumc[:, :], self.PE_[:, 0:8]), inc="c_dve")
        self.rreg("PE", kc)
        em.waits("dve", [kc])
        kw = None
        for c in range(NCH):
            kw = em.op("dve", lambda ep, c=c: ep.tensor_scalar_mul(
                self.kwsel[:, c * 16:(c + 1) * 16], self.bselq[:, c * 16:(c + 1) * 16],
                self.ksumc[:, c:c + 1]), inc="c_dve" if c == NCH - 1 else None)
        collq = AP(vFM.tensor, vFM.offset, [[vFM.ap[0][0], 16], [1, 2048]])
        collD = AP(vFM.tensor, vFM.offset + 2048, [[vFM.ap[0][0], 16], [1, 2048]])
        dq_toks = []
        for tc in range(NTC):
            for which in range(2):
                sel = self.bselq if which == 0 else self.kwsel
                coll = collq if which == 0 else collD
                em.waits("pe", tmax(kw, e_toks[-1], self.rgate("PD")))
                mm = None
                for c in range(NCH):
                    mm = em.op("pe", lambda ep, c=c, sel=sel, tc=tc: ep.matmul(
                        self.PD[0:16, :], sel[:, c * 16:(c + 1) * 16],
                        E[:, c * T + tc * 512:c * T + (tc + 1) * 512],
                        start=(c == 0), stop=(c == NCH - 1), skip_group_check=True),
                        inc="c_pe" if c == NCH - 1 else None)
                em.waits("dve", [mm])
                ev = em.op("dve", lambda ep, coll=coll, tc=tc: ep.tensor_copy(
                    coll[:, tc * 512:(tc + 1) * 512], self.PD[0:16, :]), inc="c_dve")
                self.rreg("PD", ev)
                dq_toks.append(ev)
        em.waits("act", tmax(dq_toks))
        em.op("act", lambda ep: ep.activation(collq, collq, AF.Ln))
        iq = em.op("act", lambda ep: ep.activation(collq, collq, AF.Exp, scale=-1.0), inc="c_act")
        em.op("act", lambda ep: ep.activation(collD, collD, AF.Ln))
        iD = em.op("act", lambda ep: ep.activation(collD, collD, AF.Exp, scale=-1.0), inc="c_act")
        asm_toks = []
        for tc in range(NTC):
            for c in range(NCH):
                em.waits("pe", tmax(ctx_ev, iq, iD, self.rgate("PB0"), self.rgate("PE")))
                mp = None
                for hh in range(2):
                    h = 2 * c + hh
                    mp = em.op("pe", lambda ep, h=h, hh=hh, c=c, tc=tc: ep.matmul(
                        self.PB[hh * 64:hh * 64 + 64, 0:512],
                        ctxs[hh * 64:hh * 64 + 64, (h // 2) * 64:(h // 2) * 64 + 64],
                        E[hh * 64:hh * 64 + 64, c * T + tc * 512:c * T + (tc + 1) * 512],
                        start=True, stop=True), inc="c_pe" if hh == 1 else None)
                bq1 = em.op("pe", lambda ep, c=c, tc=tc: ep.matmul(
                    self.PE_[:, :], self.selpat[:, c * 128:(c + 1) * 128],
                    collq[:, tc * 512:(tc + 1) * 512],
                    start=True, stop=True), inc="c_pe")
                em.waits("dve", tmax(bq1, self.rgate("scrA")))
                t1 = em.op("dve", lambda ep, c=c, tc=tc: ep.tensor_tensor(
                    self.scrA[:, :], E[:, c * T + tc * 512:c * T + (tc + 1) * 512],
                    self.PE_[:, :], ALU.mult), inc="c_dve")
                em.waits("pe", [t1])
                bD = em.op("pe", lambda ep, c=c, tc=tc: ep.matmul(
                    self.PE_[:, :], self.selpat[:, c * 128:(c + 1) * 128],
                    collD[:, tc * 512:(tc + 1) * 512],
                    start=True, stop=True), inc="c_pe")
                em.waits("act", tmax(bD, self.rgate("bcsb")))
                bDs = em.op("act", lambda ep: ep.activation(self.bcq_sb[:, 0:TC], self.PE_[:, :],
                                                            AF.Identity), inc="c_act")
                self.rreg("PE", bDs)
                em.waits("dve", tmax(bDs, mp, self.rgate("scrB")))
                t2 = em.op("dve", lambda ep: ep.tensor_tensor(
                    self.scrB[:, :], self.PB[:, 0:512], self.bcq_sb[:, 0:TC], ALU.mult), inc="c_dve")
                self.rreg("PB0", t2)
                self.rreg("bcsb", t2)
                fin = em.op("dve", lambda ep, c=c, tc=tc: ep.tensor_tensor(
                    E[:, c * T + tc * 512:c * T + (tc + 1) * 512], self.scrA[:, :],
                    self.scrB[:, :], ALU.add), inc="c_dve")
                self.rreg("scrA", fin)
                self.rreg("scrB", fin)
                asm_toks.append(fin)
        em.waits("sp", tmax(asm_toks))
        self.rreg(("tabr", 0), ks_done)
        self.rreg(("tabr", 1), ks_done)
        wp_tok = em.op("sp", lambda ep: ep.dma_start(
            wT2, reap(self.s_wT[3], [(CM, 128), (128 * CM, NCH), (1, CM)])), inc="cwt2", amt=16)
        self.xt_tok = []
        for tt in range(NTT):
            em.waits("pe", tmax(wp_tok, asm_toks))
            em.waits("pe", self.rgate("PA0", "PA1"))
            mm = None
            for g2 in range(2):
                for sc in range(NCH):
                    em.op("pe", lambda ep, sc=sc, g2=g2, tt=tt: ep.matmul(
                        self.PA[:, g2 * 512:(g2 + 1) * 512],
                        E[:, sc * T + tt * 128: sc * T + tt * 128 + 128],
                        wT2[:, sc * CM + g2 * 512: sc * CM + (g2 + 1) * 512],
                        start=(sc == 0), stop=False, skip_group_check=True))
                mm = em.op("pe", lambda ep, g2=g2: ep.matmul(
                    self.PA[:, g2 * 512:(g2 + 1) * 512], self.ones1[:, :],
                    bp_row[:, g2 * 512:(g2 + 1) * 512],
                    start=False, stop=True, skip_group_check=True),
                    inc="c_pe" if g2 == 1 else None)
            em.waits("dve", tmax(mm, self.rgate(("xt", tt))))
            up = em.op("dve", lambda ep, tt=tt: ep.tensor_tensor(
                self.xt3()[:, tt, :].rearrange("p (c n) -> p c n", n=128),
                self.xt3()[:, tt, :].rearrange("p (c n) -> p c n", n=128),
                self.PA[:, 0:1024].rearrange("p (c n) -> p c n", n=128), ALU.add), inc="c_dve")
            self.rreg("PA0", up)
            self.rreg("PA1", up)
            self.xt_tok.append(up)
        self.rreg("big1", tmax(asm_toks, self.xt_tok))
        self.rreg("big3", tmax(asm_toks, self.xt_tok))



_CACHE = {}


def _build(dbg_cut=None):
    key = ("nc", dbg_cut)
    if key not in _CACHE:
        nc = bass.Bass("TRN2", target_bir_lowering=False)
        kb = KB(nc, dbg_cut=dbg_cut)
        n = kb.build()
        nc._pe_tags = kb.em.pe_tags
        _CACHE[key] = (nc, n)
    return _CACHE[key]


def make_inputs(inputs):
    consts = host_constants()
    lay = host_layouts(inputs)
    x = np.ascontiguousarray(np.asarray(inputs["L0_attn_x"], np.float32))
    base = {
        "wqkv": np.ascontiguousarray(np.asarray(inputs["L0_ffn_Wqkv"], np.float32)),
        "wo3": np.ascontiguousarray(np.asarray(inputs["L0_ffn_Wo"], np.float32)),
        "w1": np.ascontiguousarray(np.asarray(inputs["L0_ffn_W1"], np.float32)),
        "w2": np.ascontiguousarray(np.asarray(inputs["L0_ffn_W2"], np.float32)),
        "awq": np.ascontiguousarray(np.asarray(inputs["L0_attn_Wq"], np.float32)),
        "awk": np.ascontiguousarray(np.asarray(inputs["L0_attn_Wk"], np.float32)),
        "awv": np.ascontiguousarray(np.asarray(inputs["L0_attn_Wv"], np.float32)),
        "awp": np.ascontiguousarray(np.asarray(inputs["L0_attn_Wp"], np.float32)),
    }
    base.update(consts)
    base.update(lay)
    in_maps = []
    for core in range(8):
        m = dict(base)
        m["x"] = np.ascontiguousarray(x[2 * core:2 * core + 2].reshape(NSEQ * T, CM))
        in_maps.append(m)
    return in_maps


def kernel(**inputs):
    nc, _ = _build()
    in_maps = make_inputs(inputs)
    res = run_bass_kernel_spmd(nc, in_maps, core_ids=list(range(8)))
    out = np.empty((16, T, CM), np.float32)
    for core in range(8):
        out[2 * core:2 * core + 2] = res.results[core]["out"].reshape(NSEQ, T, CM)
    return out


# version 07
# speedup vs baseline: 1.0484x; 1.0063x over previous
import os
import numpy as np
import ml_dtypes
from contextlib import ExitStack

os.environ.setdefault("GAUGE_NTFF_IGNORE_DMA", "1")

import concourse.bass as bass
import concourse.mybir as mybir
from concourse.ap import AP
from concourse.bass_utils import run_bass_kernel_spmd

F32 = mybir.dt.float32
BF16 = mybir.dt.bfloat16
AF = mybir.ActivationFunctionType
ALU = mybir.AluOpType
AX = mybir.AxisListType
BF = ml_dtypes.bfloat16

T, CM, H, DH, WIN = 2048, 1024, 16, 64, 256
NCH, TC, NL, DI, NJ = 8, 512, 6, 2730, 22
NTC = T // TC
WJL = DI - 21 * 128
NSEQ = 2
NTT = T // 128
DBG_NW = 2
DBG_NH = H
DBG_SNAP = None
DBG_CUTOP = None



def host_constants():
    c = {}
    half = np.arange(0, DH, 2, dtype=np.float64)
    inv_freq = 1.0 / 10000.0 ** (half / DH)
    t = np.arange(T, dtype=np.float64)
    freqs = t[:, None] * inv_freq[None, :]
    freqs = np.concatenate([freqs, freqs], -1)
    base_scale = (half + 0.4 * DH) / (1.4 * DH)
    power = (t - T // 2) / (WIN // 2)
    scale = base_scale[None, :] ** power[:, None]
    scale = np.concatenate([scale, scale], -1)

    def fm(x):
        x = np.ascontiguousarray(x.T.astype(np.float32))
        return np.concatenate([x, x], 0).astype(BF)

    c["c_tabs"] = np.concatenate([fm(np.cos(freqs) * scale * 8.0), fm(np.sin(freqs) * scale * 8.0),
                                  fm(np.cos(freqs) / scale), fm(np.sin(freqs) / scale)], axis=1)
    c["c_identb"] = np.eye(128, dtype=BF)
    c["c_identf"] = np.eye(128, dtype=np.float32)
    pr = np.zeros((128, 128), np.float32)
    for b0 in (0, 64):
        for d in range(64):
            if d < 32:
                pr[b0 + d + 32, b0 + d] = -1.0
            else:
                pr[b0 + d - 32, b0 + d] = 1.0
    c["c_prot"] = pr.astype(BF)
    bq = np.zeros((128, NCH * 16), np.float32)
    bk = np.zeros((128, NCH * 48), np.float32)
    for ch in range(NCH):
        bq[0:64, ch * 16 + 2 * ch] = 1.0
        bq[64:128, ch * 16 + 2 * ch + 1] = 1.0
        bk[0:64, ch * 48 + 32 + 2 * ch] = 1.0
        bk[64:128, ch * 48 + 32 + 2 * ch + 1] = 1.0
    c["c_bselq"] = bq.astype(BF)
    c["c_bselk"] = bk.astype(BF)
    sp = np.zeros((16, NCH * 128), np.float32)
    for ch in range(NCH):
        sp[2 * ch, ch * 128:ch * 128 + 64] = 1.0
        sp[2 * ch + 1, ch * 128 + 64:ch * 128 + 128] = 1.0
    c["c_selpat"] = sp.astype(BF)
    u = np.zeros((1, 16 * 16), np.float32)
    for h in range(16):
        u[0, h * 16 + h] = 1.0
    c["c_unit16"] = u.astype(BF)
    c["c_ones1"] = np.ones((1, 128), np.float32).astype(BF)
    c["c_onescol"] = np.ones((128, 1), np.float32).astype(BF)
    c["c_onesrow"] = np.ones((1, TC), np.float32).astype(BF)
    p = np.arange(128)[:, None]
    i = np.arange(256)[None, :]

    def keep(jc):
        d = (i + 256) - (jc * 128 + p)
        return ((d >= 0) & (d <= 256)).astype(np.float32)

    k4 = np.hstack([keep(0), keep(1), keep(2), keep(3)])
    c["c_maskb"] = ((1.0 - k4) * -300.0).astype(BF)
    k0 = np.hstack([keep(2), keep(3)])
    c["c_maskb0"] = ((1.0 - k0) * -300.0).astype(BF)
    return c


def host_layouts(ins):
    d = {}
    qs = np.asarray(ins["L0_ffn_q_scale"], np.float32)
    ks = np.asarray(ins["L0_ffn_k_scale"], np.float32)
    d["qs_pp"] = np.ascontiguousarray(np.concatenate([qs.T, qs.T], 0))
    d["ks_pp"] = np.ascontiguousarray(np.concatenate([ks.T, ks.T], 0))
    g = np.asarray(ins["L0_ffn_ln_g"], np.float32)
    b = np.asarray(ins["L0_ffn_ln_b"], np.float32)
    d["g_pp"] = np.ascontiguousarray(g.reshape(NL, NCH, 128).transpose(2, 0, 1).reshape(128, NL * NCH))
    d["gb_host"] = np.ascontiguousarray(np.stack([g.reshape(-1), b.reshape(-1)], 0))
    b1 = np.asarray(ins["L0_ffn_b1"], np.float32)
    b1a = np.zeros((128, NL * NJ), np.float32)
    b1g = np.zeros((128, NL * NJ), np.float32)
    for l in range(NL):
        for j in range(NJ):
            w = 128 if j < NJ - 1 else WJL
            b1a[0:w, l * NJ + j] = b1[l, j * 128:j * 128 + w]
            b1g[0:w, l * NJ + j] = b1[l, DI + j * 128:DI + j * 128 + w]
    d["b1a_pp"] = b1a
    d["b1g_pp"] = b1g
    b2 = np.asarray(ins["L0_ffn_b2"], np.float32)
    d["b2_pp"] = np.ascontiguousarray(b2.reshape(NL, NCH, 128).transpose(2, 0, 1).reshape(128, NL * NCH))
    d["bq_pp"] = np.ascontiguousarray(np.asarray(ins["L0_attn_bq"], np.float32).reshape(NCH, 128).T)
    d["bk_r"] = np.asarray(ins["L0_attn_bk"], np.float32).reshape(1, CM).copy()
    d["bv_r"] = np.asarray(ins["L0_attn_bv"], np.float32).reshape(1, CM).copy()
    d["bp_r"] = np.asarray(ins["L0_attn_bp"], np.float32).reshape(1, CM).copy()
    return d



ENG2ATTR = {"pe": "tensor", "act": "scalar", "dve": "vector", "pool": "gpsimd", "sp": "sync"}


AUTO_SEM = {"dve": "c_dve", "act": "c_act", "pool": "c_gp"}
DMA_LOOKAHEAD = 8


class Em:
    def __init__(self):
        self.ops = {e: [] for e in ENG2ATTR}
        self.sems = {}
        self.cnt = {}
        self.waited = {e: {} for e in ENG2ATTR}
        self.nops = 0
        self.pe_tags = []

    def add_sem(self, name, handle):
        self.sems[name] = handle
        self.cnt[name] = 0

    def _dummy(self, eng, amt):
        sb = self.dummy_sb
        ps = self.dummy_ps
        bb = self.dummy_bb
        if amt == 16:
            return lambda ep: ep.dma_start(sb[0:1, 4:8], sb[0:1, 0:4])
        if eng == "pe":
            return lambda ep: ep.matmul(ps[0:1, 0:1], bb[0:1, 0:1], bb[0:1, 0:1],
                                        start=True, stop=True, skip_group_check=True)
        if eng == "act":
            return lambda ep: ep.activation(sb[0:1, 0:1], sb[0:1, 1:2], AF.Copy)
        return lambda ep: ep.engine_nop()

    def op(self, eng, fn, inc=None, amt=1):
        if eng == "pe":
            import sys as _s
            self.pe_tags.append(_s._getframe(1).f_lineno)
        if DBG_CUTOP is not None and self.nops >= DBG_CUTOP:
            fn = self._dummy(eng, amt)
        if eng in AUTO_SEM and amt == 1:
            own = AUTO_SEM[eng]
            if inc is None:
                inc = own
            if inc == own:
                self.wait(eng, (own, self.cnt[own]))
        self.nops += 1
        if inc is not None:
            self.cnt[inc] += amt
            tok = (inc, self.cnt[inc])
            sem_h = self.sems[inc]

            def wrapped(ep, fn=fn, sem_h=sem_h, amt=amt):
                fn(ep).then_inc(sem_h, amt)

            self.ops[eng].append(wrapped)
            return tok
        self.ops[eng].append(fn)
        return None

    def wait(self, eng, tok):
        if tok is None:
            return
        s, v = tok
        if v <= 0 or self.waited[eng].get(s, 0) >= v:
            return
        self.waited[eng][s] = v
        sem_h = self.sems[s]
        self.ops[eng].append(lambda ep, sem_h=sem_h, v=v: ep.wait_ge(sem_h, v))

    def waits(self, eng, toks):
        for t in toks:
            if t is not None:
                self.wait(eng, t)

    def flush(self, blk):
        for eng, attr in ENG2ATTR.items():
            ops = self.ops[eng]
            self.ops[eng] = []

            def run(ep, ops=ops):
                for f in ops:
                    f(ep)

            getattr(blk, attr)(run)


def tmax(*toks):
    best = {}
    for t in toks:
        if t is None:
            continue
        for u in (t if isinstance(t, list) else [t]):
            if u is not None:
                best[u[0]] = max(best.get(u[0], 0), u[1])
    return [(s, v) for s, v in best.items()]



def reap(src_ap, dims):
    return AP(src_ap.tensor, src_ap.offset, [list(d) for d in dims])



class KB:
    def __init__(self, nc, dbg_cut=None):
        self.nc = nc
        self.dbg_cut = dbg_cut
        self.es = ExitStack()
        self.y_tok = None
        self.dbg_done = False
        self.rd = {}
        self.wr = {}

    def rreg(self, key, tok):
        if tok is not None:
            self.rd.setdefault(key, []).append(tok)

    def rgate(self, *keys):
        out = []
        for k in keys:
            out += self.rd.pop(k, [])
        return tmax(*out)

    def sb(self, name, shape, dt):
        return self.es.enter_context(self.nc.sbuf_tensor(name, shape, dt))

    def sem(self, name):
        h = self.es.enter_context(self.nc.semaphore(name))
        self.em.add_sem(name, h)

    def build(self):
        nc = self.nc
        self.x = nc.dram_tensor("x", [NSEQ * T, CM], F32, kind="ExternalInput").ap()
        self.out = nc.dram_tensor("out", [NSEQ * T, CM], F32, kind="ExternalOutput").ap()
        for n, s in [("wqkv", [NL, CM, 3 * CM]), ("wo3", [NL, CM, CM]),
                     ("w1", [NL, CM, 2 * DI]), ("w2", [NL, DI, CM]),
                     ("awq", [CM, CM]), ("awk", [CM, CM]), ("awv", [CM, CM]), ("awp", [CM, CM]),
                     ("qs_pp", [128, NL]), ("ks_pp", [128, NL]), ("g_pp", [128, NL * NCH]),
                     ("gb_host", [2, NL * CM]), ("b1a_pp", [128, NL * NJ]), ("b1g_pp", [128, NL * NJ]),
                     ("b2_pp", [128, NL * NCH]), ("bq_pp", [128, NCH]),
                     ("bk_r", [1, CM]), ("bv_r", [1, CM]), ("bp_r", [1, CM]),
                     ("c_identf", [128, 128])]:
            setattr(self, "d_" + n, nc.dram_tensor(n, s, F32, kind="ExternalInput").ap())
        for n, s in [("c_tabs", [128, 4 * T]), ("c_identb", [128, 128]), ("c_prot", [128, 128]),
                     ("c_bselq", [128, NCH * 16]), ("c_bselk", [128, NCH * 48]),
                     ("c_selpat", [16, NCH * 128]), ("c_unit16", [1, 256]),
                     ("c_ones1", [1, 128]), ("c_onescol", [128, 1]), ("c_onesrow", [1, TC]),
                     ("c_maskb", [128, 1024]), ("c_maskb0", [128, 512])]:
            setattr(self, "d_" + n, nc.dram_tensor(n, s, BF16, kind="ExternalInput").ap())
        self.s_wqkv = nc.dram_tensor("s_wqkv", [NL, 24, NCH, 128, 128], BF16, kind="Internal").ap()
        self.s_wo = nc.dram_tensor("s_wo", [NL, NCH, NCH, 128, 128], BF16, kind="Internal").ap()
        self.s_w1 = nc.dram_tensor("s_w1", [NL, NJ, 2, NCH, 128, 128], BF16, kind="Internal").ap()
        self.s_w2 = nc.dram_tensor("s_w2", [NL, DI, CM], BF16, kind="Internal").ap()
        self.s_wT = nc.dram_tensor("s_wT", [4, CM, CM], BF16, kind="Internal").ap()
        self.s_x2 = nc.dram_tensor("s_x2", [NSEQ * T, CM], F32, kind="Internal").ap()
        if self.dbg_cut is not None:
            self.dbg = nc.dram_tensor("dbg", [128, NCH * (T + 1)], F32, kind="ExternalOutput").ap()
            self.dbg_snap = nc.dram_tensor("dbg_snap", [128, 1536], BF16, kind="ExternalOutput").ap()
            self.dbg_snap2 = nc.dram_tensor("dbg_snap2", [128, 6 * 1040], BF16, kind="ExternalOutput").ap()
            self.dbg_snap3 = nc.dram_tensor("dbg_snap3", [65, 256], F32, kind="ExternalOutput").ap()

        with self.es:
            self.em = Em()
            for s in (["d_sp", "d_gp", "c_pe", "c_dve", "c_act", "c_gp", "cbr", "cws",
                       "ccst", "cvw", "cx0", "cpk", "cwt1", "cwt2"]
                      + [f"cv{i}" for i in range(NL)] + [f"cw{i}" for i in range(4)]
                      + [f"cgb{i}" for i in range(4)] + [f"qkp{i}" for i in range(4)]
                      + [f"w1p{i}" for i in range(4)] + [f"w2p{i}" for i in range(4)]
                      + [f"wop{i}" for i in range(2)] + [f"tab{i}" for i in range(2)]):
                self.sem(s)
            self._alloc()
            self._emit()
            blk = self.es.enter_context(nc.Block())
            self.em.flush(blk)
        return self.em.nops

    def _alloc(self):
        sb = self.sb
        self.big1 = sb("big1", [128, NCH * (T + 1)], F32)
        self.big2 = sb("big2", [128, 16 * CM], F32)
        self.big3 = sb("big3", [128, 20480], BF16)
        self.tabs = sb("tabs", [128, 2 * 4 * TC], BF16)
        self.identb = sb("identb", [128, 128], BF16)
        self.identf = sb("identf", [128, 128], F32)
        self.prot = sb("prot", [128, 128], BF16)
        self.bselq = sb("bselq", [128, NCH * 16], BF16)
        self.bselk = sb("bselk", [128, NCH * 48], BF16)
        self.selpat = sb("selpat", [16, NCH * 128], BF16)
        self.unit16 = sb("unit16", [1, 256], BF16)
        self.ones1 = sb("ones1", [1, 128], BF16)
        self.onescol = sb("onescol", [128, 1], BF16)
        self.maskb = sb("maskb", [128, 1024], BF16)
        self.maskb0 = sb("maskb0", [128, 512], BF16)
        self.qs_pp = sb("qs_pp_s", [128, NL], F32)
        self.ks_pp = sb("ks_pp_s", [128, NL], F32)
        self.g_pp = sb("g_pp_s", [128, NL * NCH], F32)
        self.b1a_pp = sb("b1a_s", [128, NL * NJ], F32)
        self.b1g_pp = sb("b1g_s", [128, NL * NJ], F32)
        self.b2_pp = sb("b2_s", [128, NL * NCH], F32)
        self.bq_pp = sb("bq_s", [128, NCH], F32)
        self.gb2 = sb("gb2", [2, NCH * 128], BF16)
        self.x0t = sb("x0t", [128, CM], F32)
        self.tmu = sb("tmu", [128, CM], F32)
        self.sqtm = self.tmu
        self.ytm = self.tmu
        self.ytmb = self.tmu[:, :].bitcast(BF16)[:, 0:CM]

        self.arhs = sb("arhs", [1, TC], BF16)
        self.c2rhs = sb("c2rhs", [2, TC], BF16)
        self.col1 = sb("col1", [128, 8], F32)
        self.nstat_s = sb("nstat_s", [16, TC], F32)
        self.qss = self.nstat_s
        self.invden16 = sb("invden16", [16, TC], BF16)
        self.denrow = sb("denrow", [1, 2 * 256], BF16)
        self.bcq_sb = sb("bcq_sb", [128, 2 * TC], BF16)
        self.scrA = sb("scrA", [128, TC], BF16)
        self.scrB = sb("scrB", [128, TC], BF16)
        self.t1s = sb("t1s", [128, TC], F32)

        self.ksumc = sb("ksumc", [128, 8], F32)
        self.kwsel = sb("kwsel", [128, NCH * 16], BF16)
        self.denk = sb("denk", [128, 16], F32)
        self.bsave = sb("bsave", [128, NCH], BF16)
        self.epsc = sb("epsc", [128, 1], F32)
        self.knraw_v = self.tmu[:, :].bitcast(BF16)[:, 0:CM]
        self.ctxacc = self.tmu[:, 512:1024]
        self.mus = self.x0t[32:33, 0:TC]
        self.statv = self.x0t[0:1, 0:TC]
        self.statl = self.x0t[0:1, TC:2 * TC]
        self.qinv16 = self.big3[0:16, 19456:19968]
        self.kinv16 = self.big3[0:16, 19968:20480]
        psum = lambda n, s: self.es.enter_context(self.nc.psum_tensor(n, s, F32))
        self.PA = psum("PA", [128, 1024])
        self.PB = psum("PB", [128, 1024])
        self.PC = psum("PC", [128, 1024])
        self.PD = psum("PD", [128, 512])
        self.PE_ = psum("PE", [128, 512])
        self.em.dummy_sb = self.col1
        self.em.dummy_bb = self.onescol
        self.em.dummy_ps = self.PD

    def y3(self):
        return self.big1[:, :].rearrange("p (c n) -> p c n", n=T + 1)

    def xt3(self):
        return self.big2[:, :].rearrange("p (c n) -> p c n", n=CM)

    def b2bf(self, lo, hi):
        return self.big2[:, :].bitcast(BF16)[:, lo:hi]

    def b1bf(self, lo, hi):
        return self.big1[:, :].bitcast(BF16)[:, lo:hi]

    def _emit(self):
        em = self.em
        if self.dbg_cut is not None:
            em.op("dve", lambda ep: ep.memset(self.big2[:, :], 0.125))
            dm = em.op("dve", lambda ep: ep.memset(self.big3[:, :], 0.125), inc="c_dve")
            em.op("dve", lambda ep: ep.memset(self.denrow[:, :], 0.125))
            self.rreg("big2", dm)
            self.rreg("big3", dm)
        self._const_loads()
        self._conversion()
        for p in [self.PA, self.PB, self.PC]:
            em.op("dve", lambda ep, p=p: ep.memset(p[:, :], 0.0))
        em.op("dve", lambda ep: ep.memset(self.PD[:, :], 0.0))
        em.op("dve", lambda ep: ep.memset(self.PE_[:, :], 0.0))
        z = em.op("dve", lambda ep: ep.memset(self.epsc[:, :], 1e-5), inc="c_dve")
        self.eps_tok = z
        for k in ["PA0", "PA1", "PB0", "PB1", "PC0", "PC1", "PD", "PE"]:
            self.rreg(k, z)
        for s in range(NSEQ):
            self.seq = s
            self._seq(s)
            if self.dbg_done:
                break
        em.waits("sp", [(n, em.cnt[n]) for n in em.cnt if em.cnt[n] > 0 and not n.startswith("c_")])

    def _const_loads(self):
        em = self.em
        for name in ["c_identb identb", "c_identf identf", "c_prot prot",
                     "c_bselq bselq", "c_bselk bselk", "c_selpat selpat", "c_unit16 unit16",
                     "c_ones1 ones1", "c_onescol onescol", "c_maskb maskb", "c_maskb0 maskb0",
                     "qs_pp qs_pp", "ks_pp ks_pp", "g_pp g_pp", "b1a_pp b1a_pp",
                     "b1g_pp b1g_pp", "b2_pp b2_pp", "bq_pp bq_pp"]:
            dn, sn = name.split()
            em.op("sp", lambda ep, d=getattr(self, sn), s=getattr(self, "d_" + dn):
                  ep.dma_start(d[:, :], s), inc="ccst", amt=16)
        em.op("sp", lambda ep: ep.dma_start(self.c2rhs[1:2, :], self.d_c_onesrow), inc="ccst", amt=16)
        self.const_tok = ("ccst", em.cnt["ccst"])
        self.const_gp = ("d_gp", em.cnt["d_gp"])

    def _conversion(self):
        em = self.em
        self.conv_tok = {}
        for l in range(NL):
            cs = f"cv{l}"
            src = self.d_wqkv[l].rearrange("(s p) (o c) -> o s p c", p=128, c=128)
            for oc in range(24):
                em.op("pool", lambda ep, s=src[oc], d=self.s_wqkv[l, oc]:
                      ep.dma_start(d, s), inc=cs, amt=16)
            srco = self.d_wo3[l].rearrange("(s p) (o c) -> o s p c", p=128, c=128)
            for oc in range(NCH):
                em.op("pool", lambda ep, s=srco[oc], d=self.s_wo[l, oc]:
                      ep.dma_start(d, s), inc=cs, amt=16)
            for ag in range(2):
                src1 = self.d_w1[l][:, ag * DI:(ag + 1) * DI]
                for j in range(NJ):
                    w = 128 if j < NJ - 1 else WJL
                    sj = src1[:, j * 128:j * 128 + w].rearrange("(s p) c -> s p c", p=128)
                    em.op("pool", lambda ep, s=sj, d=self.s_w1[l, j, ag][:, :, 0:w]:
                          ep.dma_start(d, s), inc=cs, amt=16)
            em.op("pool", lambda ep, s=self.d_w2[l], d=self.s_w2[l]: ep.dma_start(d, s),
                  inc=cs, amt=16)
            self.conv_tok[l] = (cs, em.cnt[cs])
        em.waits("sp", [self.const_tok])
        em.waits("pool", tmax(self.rgate("big2"), self.rgate("big3")))
        wstage = self.b2bf(0, 8 * CM)
        wtst = self.big3[:, 0:NCH * 128]
        trpb = self.PC[:, :].bitcast(BF16)
        gate = None
        ev = None
        for widx, name in enumerate(["awq", "awk", "awv", "awp"]):
            src = reap(getattr(self, "d_" + name), [(CM, 128), (128 * CM, NCH), (1, CM)])
            em.waits("pool", tmax(gate))
            ld = em.op("pool", lambda ep, s=src, d=wstage: ep.dma_start(d, s), inc=f"cw{widx}", amt=16)
            for ic in range(NCH):
                em.waits("pe", tmax(ld, ev, self.const_tok))
                tk = None
                for oc in range(NCH):
                    tk = em.op("pe", lambda ep, oc=oc, ic=ic:
                               ep.transpose(trpb[:, oc * 128:oc * 128 + 128],
                                            wstage[:, oc * CM + ic * 128: oc * CM + ic * 128 + 128],
                                            self.identb[:, :]),
                               inc="c_pe" if oc == NCH - 1 else None)
                em.waits("dve", tmax(tk, gate))
                ev = em.op("dve", lambda ep: ep.tensor_copy(wtst[:, :], trpb[:, 0:1024]), inc="c_dve")
                em.waits("sp", [ev])
                gate = em.op("sp", lambda ep, d=self.s_wT[widx, ic * 128:ic * 128 + 128, :]:
                             ep.dma_start(d, wtst[:, :]), inc="cws", amt=16)
        self.wT_tok = ("cws", em.cnt["cws"])
        self.rreg("big2", self.wT_tok)
        self.rreg("big3", self.wT_tok)
        self.rreg("PC0", ev)
        self.rreg("PC1", ev)

    def _seq(self, s):
        self._zero_guards()
        self._ln_stream(self.x[s * T:(s + 1) * T, :], None)
        if self._dbg(0):
            return
        for l in range(NL):
            self._attn(l)
            if self._dbg(1 + 2 * l):
                return
            self._ffn(l)
            if self._dbg(2 + 2 * l):
                return
        self._assemble_x(self.x[s * T:(s + 1) * T, :], True)
        self._outer()
        if self._dbg(100):
            return
        self._park_x2()
        self._zero_guards()
        self._ln_stream(None, self.xt3())
        if self._dbg(101):
            return
        for l in range(NL):
            self._attn(l)
            if self._dbg(102 + 2 * l):
                return
            self._ffn(l)
            if self._dbg(103 + 2 * l):
                return
        self._assemble_x(self.s_x2[s * T:(s + 1) * T, :], False)

    def _dbg(self, sid):
        if self.dbg_cut is None:
            return False
        if self.seq == self.dbg_cut[0] and sid == self.dbg_cut[1]:
            em = self.em
            if sid == 100:
                em.waits("sp", tmax(self.xt_tok))
                em.op("sp", lambda ep: ep.dma_start(self.dbg[:, 0:16 * CM], self.big2[:, :]),
                      inc="d_sp", amt=16)
            else:
                em.waits("sp", tmax(self.y_tok))
                em.op("sp", lambda ep: ep.dma_start(self.dbg, self.big1[:, :]), inc="d_sp", amt=16)
            self.dbg_done = True
            return True
        return False

    def _zero_guards(self):
        em = self.em
        y3 = self.y3()
        em.waits("dve", tmax(self.y_tok, self.rgate("big1")))
        self.y_tok = [em.op("dve", lambda ep: ep.memset(y3[:, :, 0:1], 0.0), inc="c_dve")]

    def _ln_stream(self, hbm_src, xt3):
        em = self.em
        y3 = self.y3()
        prev = None
        for tt in range(NTT):
            if hbm_src is not None:
                em.waits("sp", self.rgate("x0t"))
                ld = em.op("sp", lambda ep, s=hbm_src[tt * 128:(tt + 1) * 128, :]:
                           ep.dma_start(self.x0t[:, :], s), inc="cx0", amt=16)
                xin = self.x0t[:, :]
                gate = [ld]
            else:
                xin = xt3[:, tt, :]
                gate = tmax(self.xt_tok)
            rstd, nm = self._tm_stats(xin, gate)
            t7 = em.op("dve", lambda ep, xin=xin, rstd=rstd, nm=nm: ep.tensor_scalar(
                self.ytm[:, :], xin, rstd, nm, ALU.mult, ALU.add), inc="c_dve")
            self.rreg("col1", t7)
            if hbm_src is not None:
                self.rreg("x0t", t7)
            em.waits("pe", tmax(t7, self.rgate("PC0", "PC1")))
            tk = None
            for ch in range(NCH):
                tk = em.op("pe", lambda ep, ch=ch: ep.transpose(
                    self.PC[:, ch * 128:(ch + 1) * 128],
                    self.ytm[:, ch * 128:(ch + 1) * 128], self.identf[:, :]),
                    inc="c_pe" if ch == NCH - 1 else None)
            self.rreg("ytm", tk)
            self.rreg("big1", tk)
            em.waits("dve", [tk])
            dst = y3[:, :, 1 + tt * 128: 1 + (tt + 1) * 128]
            prev = em.op("dve", lambda ep, dst=dst: ep.tensor_copy(
                dst, self.PC[:, 0:1024].rearrange("p (c n) -> p c n", n=128)), inc="c_dve")
            self.rreg("PC0", prev)
            self.rreg("PC1", prev)
        self.y_tok = tmax(self.y_tok, prev)

    def _tm_stats(self, xin, gate):
        em = self.em
        c = self.col1
        yg = self.rgate("ytm")
        em.waits("dve", tmax(gate, yg, self.rgate("col1", "sqtm")))
        em.waits("act", tmax(gate, yg))
        rs = em.op("dve", lambda ep: ep.reduce_sum(c[:, 0:1], xin, axis=AX.X))
        t_sq = em.op("act", lambda ep: ep.activation(self.sqtm[:, :], xin, AF.Square), inc="c_act")
        em.waits("dve", [t_sq])
        rr = em.op("dve", lambda ep: ep.reduce_sum(c[:, 1:2], self.sqtm[:, :], axis=AX.X), inc="c_dve")
        self.rreg("sqtm", rr)
        em.op("dve", lambda ep: ep.tensor_scalar_mul(c[:, 2:3], c[:, 0:1], 1.0 / CM))
        em.op("dve", lambda ep: ep.tensor_tensor(c[:, 3:4], c[:, 2:3], c[:, 2:3], ALU.mult))
        t3 = em.op("dve", lambda ep: ep.scalar_tensor_tensor(c[:, 4:5], c[:, 1:2], 1.0 / CM, c[:, 3:4],
                                                             ALU.mult, ALU.subtract), inc="c_dve")
        em.waits("act", [t3])
        em.waits("act", [self.eps_tok])
        em.op("act", lambda ep: ep.activation(c[:, 5:6], c[:, 4:5], AF.Ln, bias=self.epsc[:, 0:1]))
        t5 = em.op("act", lambda ep: ep.activation(c[:, 6:7], c[:, 5:6], AF.Exp, scale=-0.5), inc="c_act")
        em.waits("dve", [t5])
        em.op("dve", lambda ep: ep.tensor_scalar(c[:, 7:8], c[:, 2:3], c[:, 6:7], -1.0,
                                                 ALU.mult, ALU.mult))
        return c[:, 6:7], c[:, 7:8]

    def _assemble_x(self, base_hbm, into_xt):
        em = self.em
        y3 = self.y3()
        s = self.seq
        xt3 = self.xt3()
        self.xt_tok = []
        if not into_xt:
            em.waits("sp", [("cpk", em.cnt["cpk"])])
        for tt in range(NTT):
            em.waits("sp", self.rgate("x0t"))
            ld = em.op("sp", lambda ep, src=base_hbm[tt * 128:(tt + 1) * 128, :]:
                       ep.dma_start(self.x0t[:, :], src), inc="cx0", amt=16)
            em.waits("pe", tmax(self.y_tok, self.rgate("PC0", "PC1")))
            tk = None
            for ch in range(NCH):
                src = y3[:, ch, 1 + tt * 128:1 + (tt + 1) * 128]
                tk = em.op("pe", lambda ep, src=src, ch=ch: ep.transpose(
                    self.PC[:, ch * 128:(ch + 1) * 128], src, self.identf[:, :]),
                    inc="c_pe" if ch == NCH - 1 else None)
            self.rreg("big1", tk)
            em.waits("dve", [tk, ld])
            if into_xt:
                dst = xt3[:, tt, :].rearrange("p (c n) -> p c n", n=128)
                em.waits("dve", self.rgate(("xt", tt)))
            else:
                dst = self.x0t[:, :].rearrange("p (c n) -> p c n", n=128)
            tadd = em.op("dve", lambda ep, dst=dst: ep.scalar_tensor_tensor(
                dst, self.PC[:, 0:1024].rearrange("p (c n) -> p c n", n=128), 0.5,
                self.x0t[:, :].rearrange("p (c n) -> p c n", n=128), ALU.mult, ALU.add), inc="c_dve")
            self.rreg("PC0", tadd)
            self.rreg("PC1", tadd)
            self.rreg("x0t", tadd)
            if into_xt:
                self.xt_tok.append(tadd)
            else:
                em.waits("sp", [tadd])
                st = em.op("sp", lambda ep, d=self.out[s * T + tt * 128:s * T + (tt + 1) * 128, :]:
                           ep.dma_start(d, self.x0t[:, :]), inc="d_sp", amt=16)
                self.rreg("x0t", st)

    def _park_x2(self):
        em = self.em
        s = self.seq
        xt3 = self.xt3()
        em.waits("sp", tmax(self.xt_tok))
        self.park_tok = []
        for tt in range(NTT):
            src = xt3[:, tt, :]
            st = em.op("sp", lambda ep, src=src, d=self.s_x2[s * T + tt * 128:s * T + (tt + 1) * 128, :]:
                       ep.dma_start(d, src), inc="cpk", amt=16)
            self.park_tok.append(st)
            self.rreg(("xt", tt), st)
            self.rreg("big2", st)

    def _attn(self, l):
        em = self.em
        y3 = self.y3()
        zb3 = self.b2bf(0, NCH * 513).rearrange("p (c n) -> p c n", n=513)
        o = NCH * 513
        k2r = self.b2bf(o, o + NCH * 1024); o += NCH * 1024
        q2r = self.b2bf(o, o + NCH * 512); o += NCH * 512
        qraw = self.b2bf(o, o + NCH * 512); o += NCH * 512
        kraw = self.b2bf(o, o + NCH * 512); o += NCH * 512
        vtm = self.b2bf(o, o + 6 * 1040); o += 6 * 1040
        qp = self.b2bf(o, o + 512); o += 512
        kp = self.b2bf(o, o + 512); o += 512
        sqs = self.b2bf(o, o + 512); o += 512
        assert o <= 32768, o
        v65 = vtm.rearrange("p (sh n) -> p sh n", n=65)
        qkvp = self.big3[:, 0:4 * NCH * 128]
        wop = self.big3[:, 4 * NCH * 128:6 * NCH * 128]
        vw = self.big3[:, 6 * NCH * 128:6 * NCH * 128 + NCH * CM]
        PT = self.big3[:, 14 * 1024:15 * 1024]
        attnU = self.big3[:, 15 * 1024:15 * 1024 + NCH * 512]
        assert 15 * 1024 + NCH * 512 <= 20480

        big2_gate = self.rgate("big2")
        big3_gate = self.rgate("big3")
        em.waits("sp", tmax(self.conv_tok[l], big3_gate, self.rgate("vw")))
        for vcc in range(NCH):
            vsrc = reap(self.s_wqkv[l, 16 + vcc], [(128, 128), (16384, NCH), (1, 128)])
            vwdst = AP(vw.tensor, vw.offset + vcc * 128, [[vw.ap[0][0], 128], [CM, NCH], [1, 128]])
            vw_tok = em.op("sp", lambda ep, vsrc=vsrc, vwdst=vwdst: ep.dma_start(vwdst, vsrc),
                           inc="cvw", amt=16)
        em.waits("dve", tmax(big2_gate, self.rgate("vtm")))
        ones_tok = em.op("dve", lambda ep: ep.memset(v65[:, :, 64:65], 1.0), inc="c_dve")

        y_upd = []
        pidx = 0
        for tc in range(NTC):
            t0 = tc * TC
            em.waits("dve", tmax(self.y_tok, self.rgate("zb"), big2_gate))
            big2_gate = None
            if tc == 0:
                zb_tok = em.op("dve", lambda ep: ep.tensor_copy(
                    zb3[:, :, 0:513], y3[:, :, 0:513]), inc="c_dve")
            else:
                em.op("dve", lambda ep, t0=t0: ep.tensor_copy(
                    zb3[:, :, 1:513], y3[:, :, t0 + 1:t0 + 513]))
                zb_tok = em.op("dve", lambda ep: ep.tensor_copy(
                    zb3[:, :, 0:1], self.bsave[:, :].rearrange("p (c n) -> p c n", n=1)), inc="c_dve")
            if tc < NTC - 1:
                bs = em.op("dve", lambda ep, t0=t0: ep.tensor_copy(
                    self.bsave[:, :].rearrange("p (c n) -> p c n", n=1),
                    y3[:, :, t0 + TC:t0 + TC + 1]), inc="c_dve")

            em.waits("sp", self.rgate(("tabr", tc % 2)))
            tsrc = self.d_c_tabs.rearrange("p (k t) -> p k t", t=T)[:, :, t0:t0 + TC]
            tslot = (tc % 2) * 4 * TC
            tab_tok = em.op("sp", lambda ep, tsrc=tsrc, tslot=tslot: ep.dma_start(
                self.tabs[:, tslot:tslot + 4 * TC].rearrange("p (k t) -> p k t", t=TC), tsrc),
                inc=f"tab{tc % 2}", amt=16)

            def zsl(c):
                off = 1 if c < 4 else 0
                return zb3[:, c, off:off + TC]

            def ztile(c, ti):
                off = 1 if c < 4 else 0
                return zb3[:, c, off + ti * 128:off + ti * 128 + 128]

            qk_done = {}
            sq_by = {}
            sqslots = [sqs, qp]
            for oc in range(17):
                if oc < 16:
                    slot = pidx % 4
                    pidx += 1
                    em.waits("sp", self.rgate(("pc", slot)))
                    src = reap(self.s_wqkv[l, oc], [(128, 128), (128 * 128, NCH), (1, 128)])
                    dst = qkvp[:, slot * NCH * 128:(slot + 1) * NCH * 128]
                    ptok = em.op("sp", lambda ep, s=src, d=dst: ep.dma_start(d, s),
                                 inc=f"qkp{slot}", amt=16)
                    reg = oc % 2
                    pk = "PA" + str(reg)
                    em.waits("pe", tmax(ptok, zb_tok, self.rgate(pk)))
                    mm = None
                    for sc in range(NCH):
                        mm = em.op("pe", lambda ep, sc=sc, dst=dst, reg=reg: ep.matmul(
                            self.PA[:, reg * 512:reg * 512 + 512], dst[:, sc * 128:(sc + 1) * 128],
                            zsl(sc), start=(sc == 0), stop=(sc == NCH - 1), skip_group_check=True),
                            inc="c_pe" if sc == NCH - 1 else None)
                    self.rreg(("pc", slot), mm)
                    self.rreg("zb", mm)
                    raw = qraw if oc < 8 else kraw
                    c = oc % 8
                    em.waits("dve", tmax(mm, self.rgate(("raw", oc < 8, c))))
                    ev = em.op("dve", lambda ep, raw=raw, c=c, reg=reg: ep.tensor_copy(
                        raw[:, c * 512:(c + 1) * 512], self.PA[:, reg * 512:reg * 512 + 512]), inc="c_dve")
                    self.rreg(pk, ev)
                    sslot = oc % 2
                    em.waits("act", tmax(ev, self.rgate(("sqs", sslot))))
                    sq = em.op("act", lambda ep, raw=raw, c=c, sslot=sslot: ep.activation(
                        sqslots[sslot][:, 0:512], raw[:, c * 512:(c + 1) * 512],
                        AF.Square), inc="c_act")
                    self.rreg(("raw", oc < 8, c), sq)
                    sq_by[oc] = (sq, sslot)
                po = oc - 1
                if po >= 0:
                    sqtok, sslot = sq_by.pop(po)
                    if po == 0:
                        em.waits("pe", tmax(sqtok, self.rgate("PD")))
                    elif po == 8:
                        em.waits("pe", tmax(sqtok, self.rgate("PE")))
                    else:
                        em.waits("pe", [sqtok])
                    pdst = self.PD if po < 8 else self.PE_
                    g = em.op("pe", lambda ep, po=po, pdst=pdst, sslot=sslot: ep.matmul(
                        pdst[0:16, :], self.bselq[:, (po % 8) * 16:(po % 8 + 1) * 16],
                        sqslots[sslot][:, 0:512],
                        start=(po % 8 == 0), stop=(po % 8 == 7), skip_group_check=True),
                        inc="c_pe")
                    self.rreg(("sqs", sslot), g)
                    if po == 7:
                        qk_done["q"] = g
                    if po == 15:
                        qk_done["k"] = g
            em.waits("dve", tmax(qk_done["q"], qk_done["k"], self.rgate("nss")))
            c1 = em.op("dve", lambda ep: ep.tensor_scalar_max(self.nstat_s[:, :], self.PD[0:16, :],
                                                              1e-24), inc="c_dve")
            em.waits("act", [c1])
            em.op("act", lambda ep: ep.activation(self.nstat_s[:, :], self.nstat_s[:, :], AF.Ln))
            c4 = em.op("act", lambda ep: ep.activation(self.qinv16, self.nstat_s[:, :], AF.Exp,
                                                       scale=-0.5), inc="c_act")
            em.waits("dve", [c4])
            c2 = em.op("dve", lambda ep: ep.tensor_scalar_max(self.nstat_s[:, :], self.PE_[0:16, :],
                                                              1e-24), inc="c_dve")
            self.rreg("PD", c1)
            self.rreg("PE", c2)
            em.waits("act", [c2])
            em.op("act", lambda ep: ep.activation(self.nstat_s[:, :], self.nstat_s[:, :], AF.Ln))
            c6 = em.op("act", lambda ep: ep.activation(self.kinv16, self.nstat_s[:, :], AF.Exp,
                                                       scale=-0.5), inc="c_act")
            em.waits("dve", self.rgate(("sqs", 0), ("sqs", 1)))
            rot_pend = None

            def emit_rot(c, qk, st):
                pbuf = qp if qk == 0 else kp
                tab_c = tslot + (0 if qk == 0 else 2 * TC)
                tab_s = tslot + (TC if qk == 0 else 3 * TC)
                pbb = qk * 512
                em.waits("pe", tmax(st, self.rgate("PB" + str(qk))))
                pm = em.op("pe", lambda ep, pbuf=pbuf, pbb=pbb: ep.matmul(
                    self.PB[:, pbb:pbb + 512], self.prot[:, :], pbuf[:, 0:512],
                    start=True, stop=True), inc="c_pe")
                self.rreg(("pp", qk), pm)
                if qk == 0:
                    dsl = q2r[:, c * 512:(c + 1) * 512]
                    wkey = ("q2", c)
                    scr, skey = self.scrA, "scrA"
                else:
                    dsl = k2r[:, c * 1024 + (tc % 2) * 512: c * 1024 + (tc % 2) * 512 + 512]
                    wkey = ("k2", c, tc % 2)
                    scr, skey = self.scrB, "scrB"
                em.waits("pool", tmax(st, tab_tok, self.rgate(wkey)))
                tt1 = em.op("pool", lambda ep, dsl=dsl, pbuf=pbuf, tab_c=tab_c:
                      ep.tensor_tensor(dsl, pbuf[:, 0:512],
                                       self.tabs[:, tab_c:tab_c + TC], ALU.mult))
                em.waits("dve", tmax(pm, tab_tok, self.rgate(skey)))
                stt = em.op("dve", lambda ep, tab_s=tab_s, scr=scr, pbb=pbb: ep.scalar_tensor_tensor(
                    scr[:, :], self.PB[:, pbb:pbb + 512], 1.0,
                    self.tabs[:, tab_s:tab_s + TC], ALU.mult, ALU.mult))
                em.waits("pool", [stt])
                a3 = em.op("pool", lambda ep, dsl=dsl, scr=scr: ep.tensor_tensor(
                    dsl, dsl, scr[:, :], ALU.add), inc="c_gp")
                self.rreg(("tabr", tc % 2), a3)
                self.rreg(("pp", qk), a3)
                self.rreg("PB" + str(qk), stt)
                self.rreg(skey, a3)
                if qk == 0:
                    self.rreg(("sqs", 1), tt1)
                    self.rreg(("sqs", 1), pm)
                self.wr[wkey] = a3

            for c in range(NCH):
                for qk in range(2):
                    raw = qraw if qk == 0 else kraw
                    inv16 = self.qinv16 if qk == 0 else self.kinv16
                    gain = self.qs_pp if qk == 0 else self.ks_pp
                    ntok = c4 if qk == 0 else c6
                    pbuf = qp if qk == 0 else kp
                    bcdst = self.PE_ if qk == 0 else self.PD
                    bkey = "PE" if qk == 0 else "PD"
                    em.waits("pe", tmax(ntok, self.rgate(bkey)))
                    bc = em.op("pe", lambda ep, c=c, inv16=inv16, bcdst=bcdst: ep.matmul(
                        bcdst[:, :], self.selpat[:, c * 128:(c + 1) * 128], inv16,
                        start=True, stop=True), inc="c_pe")
                    em.waits("act", tmax(bc, self.rgate(("bcs", qk))))
                    bce = em.op("act", lambda ep, b=qk, bcdst=bcdst: ep.activation(
                        self.bcq_sb[:, b * TC:(b + 1) * TC], bcdst[:, :], AF.Identity), inc="c_act")
                    self.rreg(bkey, bce)
                    self.rreg("nss", bce)
                    em.waits("dve", tmax(bce, self.rgate(("pp", qk)),
                                         self.rgate(("raw", qk == 0, c))))
                    st = em.op("dve", lambda ep, c=c, raw=raw, gain=gain, pbuf=pbuf, b=qk:
                               ep.scalar_tensor_tensor(pbuf[:, 0:512],
                                                       raw[:, c * 512:(c + 1) * 512],
                                                       gain[:, l:l + 1],
                                                       self.bcq_sb[:, b * TC:(b + 1) * TC],
                                                       ALU.mult, ALU.mult), inc="c_dve")
                    self.rreg(("bcs", qk), st)
                    if rot_pend is not None:
                        emit_rot(*rot_pend)
                    rot_pend = (c, qk, st)
            emit_rot(*rot_pend)
            rot_pend = None
            for ti in range(4):
                tile = tc * 4 + ti
                slot = tile % 6
                em.waits("pe", tmax(vw_tok, zb_tok, ones_tok, self.rgate("PB0", "PB1")))
                mm = None
                for sc in range(NCH):
                    for g2 in range(2):
                        mm = em.op("pe", lambda ep, sc=sc, g2=g2, ti=ti: ep.matmul(
                            self.PB[:, g2 * 512:(g2 + 1) * 512], ztile(sc, ti),
                            vw[:, sc * CM + g2 * 512: sc * CM + (g2 + 1) * 512],
                            start=(sc == 0), stop=(sc == NCH - 1), skip_group_check=True),
                            inc="c_pe" if (sc == NCH - 1 and g2 == 1) else None)
                self.rreg("zb", mm)
                self.rreg("vw", mm)
                em.waits("dve", tmax(mm, self.rgate(("vt", slot))))
                dstv = v65[:, slot * 16:(slot + 1) * 16, 0:64]
                ev = em.op("dve", lambda ep, dstv=dstv: ep.tensor_copy(
                    dstv, self.PB[:, 0:1024].rearrange("p (h n) -> p h n", n=64)), inc="c_dve")
                self.rreg("PB0", ev)
                self.rreg("PB1", ev)
                self.wr[("vt", slot)] = ev
            em.waits("pe", self.rgate("PB0", "PB1"))
            PT1 = self.tmu[:, :].bitcast(BF16)[:, 0:1024]
            ptbufs = [PT, PT1]
            if tc == 0:
                self.tmu_gate = tmax(self.rgate("sqtm"), self.rgate("ytm"))
            its = []
            for wloc in range(DBG_NW):
                w = 2 * tc + wloc
                w0 = (w == 0)
                jts = [2 * w - 2, 2 * w - 1, 2 * w, 2 * w + 1]
                live = [(idx, jt) for idx, jt in enumerate(jts) if not (w0 and idx < 2)]
                for h in range(DBG_NH):
                    its.append((wloc, w0, live, h))
            st_mk = {}
            st_dr = {}
            den_done = None

            def emit_sim(i):
                wloc, w0, live, h = its[i]
                par = i % 2
                ptb = ptbufs[par]
                ch, hf = h // 2, h % 2
                simp = self.PC if par == 0 else self.PA
                sk0, sk1 = ("PC0", "PC1") if par == 0 else ("PA0", "PA1")
                em.waits("pe", tmax(self.wr.get(("q2", ch)), self.const_tok,
                                    *(self.wr.get(("k2", ch, (jt // 4) % 2)) for _, jt in live)))
                em.waits("pe", self.rgate(sk0, sk1))
                qrhs = q2r[hf * 64:hf * 64 + 64, ch * 512 + wloc * 256: ch * 512 + (wloc + 1) * 256]
                if w0:
                    em.op("pe", lambda ep, simp=simp: ep.matmul(
                        simp[:, 512:1024], self.identb[:, :], self.maskb0[:, :],
                        start=True, stop=False, skip_group_check=True))
                else:
                    for half in range(2):
                        em.op("pe", lambda ep, simp=simp, half=half: ep.matmul(
                            simp[:, half * 512:half * 512 + 512], self.identb[:, :],
                            self.maskb[:, half * 512:half * 512 + 512],
                            start=True, stop=False, skip_group_check=True))
                mmt = None
                for idx, jt in live:
                    ksl = k2r[hf * 64:hf * 64 + 64,
                              ch * 1024 + ((jt // 4) % 2) * 512 + (jt % 4) * 128:
                              ch * 1024 + ((jt // 4) % 2) * 512 + (jt % 4) * 128 + 128]
                    if idx == 0:
                        i0, i1 = 0, 128
                    elif idx == 3:
                        i0, i1 = 128, 256
                    else:
                        i0, i1 = 0, 256
                    mmt = em.op("pe", lambda ep, ksl=ksl, qrhs=qrhs, idx=idx, i0=i0, i1=i1,
                                simp=simp: ep.matmul(
                                    simp[:, idx * 256 + i0: idx * 256 + i1],
                                    ksl, qrhs[:, i0:i1], start=False, stop=True,
                                    skip_group_check=True),
                                inc="c_pe" if (idx, jt) == live[-1] else None)
                    self.rreg(("q2", ch), mmt)
                    self.rreg(("k2", ch, (jt // 4) % 2), mmt)
                em.waits("act", tmax(mmt, self.rgate(("PTb", par)), self.tmu_gate))
                self.tmu_gate = None
                if w0:
                    ex = em.op("act", lambda ep, simp=simp, ptb=ptb: ep.activation(
                        ptb[:, 512:1024], simp[:, 512:1024], AF.Exp), inc="c_act")
                else:
                    ex = em.op("act", lambda ep, simp=simp, ptb=ptb: ep.activation(
                        ptb[:, :], simp[:, :], AF.Exp), inc="c_act")
                self.rreg(sk0, ex)
                self.rreg(sk1, ex)
                st_mk[i] = ex

            def emit_pv(i):
                wloc, w0, live, h = its[i]
                par = i % 2
                ptb = ptbufs[par]
                ch, hf = h // 2, h % 2
                mk = st_mk.pop(i)
                em.waits("pe", tmax(mk, self.rgate(("pv", par)),
                                    *(self.wr.get(("vt", jt % 6)) for _, jt in live)))
                pvr = par * 256
                mmv = None
                for n, (idx, jt) in enumerate(live):
                    mmv = em.op("pe", lambda ep, jt=jt, idx=idx, pvr=pvr, h=h, ptb=ptb,
                                st=(n == 0), sp=(n == len(live) - 1): ep.matmul(
                        self.PB[0:65, pvr:pvr + 256], v65[:, (jt % 6) * 16 + h, :],
                        ptb[:, idx * 256:(idx + 1) * 256],
                        start=st, stop=sp, skip_group_check=True),
                        inc="c_pe" if n == len(live) - 1 else None)
                    self.rreg(("vt", jt % 6), mmv)
                self.rreg(("PTb", par), mmv)
                if par == 1:
                    self.rreg("ytm", mmv)
                em.waits("dve", tmax(mmv, self.rgate(("aU", ch, hf, wloc))))
                du = em.op("dve", lambda ep, hf=hf, ch=ch, wloc=wloc, pvr=pvr: ep.tensor_copy(
                    attnU[hf * 64:hf * 64 + 64,
                          ch * 512 + wloc * 256: ch * 512 + (wloc + 1) * 256],
                    self.PB[0:64, pvr:pvr + 256]), inc="c_dve")
                em.waits("dve", self.rgate(("dr", par)))
                dr = em.op("dve", lambda ep, pvr=pvr, drslot=par: ep.tensor_copy(
                    self.denrow[0:1, drslot * 256:(drslot + 1) * 256],
                    self.PB[64:65, pvr:pvr + 256]), inc="c_dve")
                self.rreg(("pv", par), du)
                self.rreg(("pv", par), dr)
                self.wr[("aU", ch, hf, wloc)] = du
                st_dr[i] = dr

            def emit_den(i):
                wloc, w0, live, h = its[i]
                par = i % 2
                dr = st_dr.pop(i)
                em.waits("pe", [dr])
                dn = em.op("pe", lambda ep, h=h, wloc=wloc, drslot=par: ep.matmul(
                    self.PB[0:16, 512 + wloc * 256:512 + (wloc + 1) * 256],
                    self.unit16[:, h * 16:(h + 1) * 16],
                    self.denrow[0:1, drslot * 256:(drslot + 1) * 256],
                    start=(h == 0), stop=(i == len(its) - 1), skip_group_check=True),
                    inc="c_pe")
                self.rreg(("dr", par), dn)
                return dn

            for i in range(len(its) + 1):
                if i < len(its):
                    emit_sim(i)
                if i >= 1:
                    den_done = emit_den(i - 1)
                if i < len(its):
                    emit_pv(i)
            em.waits("act", [den_done])
            em.op("act", lambda ep: ep.activation(self.nstat_s[:, :], self.PB[0:16, 512:1024], AF.Ln))
            iv = em.op("act", lambda ep: ep.activation(self.invden16[:, :], self.nstat_s[:, :],
                                                       AF.Exp, scale=-1.0), inc="c_act")
            self.rreg("PB1", iv)
            last_nm = None
            for c in range(NCH):
                bdst = self.PE_ if c % 2 == 0 else self.PD
                bkey = "PE" if c % 2 == 0 else "PD"
                em.waits("pe", tmax(iv, self.rgate(bkey)))
                bc = em.op("pe", lambda ep, c=c, bdst=bdst: ep.matmul(
                    bdst[:, :], self.selpat[:, c * 128:(c + 1) * 128], self.invden16[:, :],
                    start=True, stop=True), inc="c_pe")
                em.waits("dve", tmax(bc, self.wr.get(("aU", c, 0, 0)), self.wr.get(("aU", c, 0, 1)),
                                     self.wr.get(("aU", c, 1, 0)), self.wr.get(("aU", c, 1, 1))))
                nm = em.op("dve", lambda ep, c=c, bdst=bdst: ep.tensor_tensor(
                    attnU[:, c * 512:(c + 1) * 512], attnU[:, c * 512:(c + 1) * 512],
                    bdst[:, :], ALU.mult), inc="c_dve")
                self.rreg(bkey, nm)
                self.wr[("aN", c)] = nm
                last_nm = nm
            for oc in range(NCH):
                wslot = oc % 2
                em.waits("sp", tmax(self.conv_tok[l], self.rgate(("wo", wslot))))
                wsrc = reap(self.s_wo[l, oc], [(128, 128), (128 * 128, NCH), (1, 128)])
                wdst = wop[:, wslot * NCH * 128:(wslot + 1) * NCH * 128]
                wtok = em.op("sp", lambda ep, s=wsrc, d=wdst: ep.dma_start(d, s),
                             inc=f"wop{wslot}", amt=16)
                reg = oc % 2
                pk = "PA" + str(reg)
                em.waits("pe", tmax(wtok, self.rgate(pk),
                                    *(self.wr.get(("aN", sc)) for sc in range(NCH))))
                mm = None
                for sc in range(NCH):
                    mm = em.op("pe", lambda ep, sc=sc, wdst=wdst, reg=reg: ep.matmul(
                        self.PA[:, reg * 512:reg * 512 + 512], wdst[:, sc * 128:(sc + 1) * 128],
                        attnU[:, sc * 512:(sc + 1) * 512],
                        start=(sc == 0), stop=(sc == NCH - 1), skip_group_check=True),
                        inc="c_pe" if sc == NCH - 1 else None)
                self.rreg(("wo", wslot), mm)
                for sc in range(NCH):
                    self.rreg(("aU", sc, 0, 0), mm)
                em.waits("dve", [mm])
                dsty = y3[:, oc, 1 + t0:1 + t0 + TC]
                up = em.op("dve", lambda ep, dsty=dsty, reg=reg: ep.tensor_tensor(
                    dsty, dsty, self.PA[:, reg * 512:reg * 512 + 512], ALU.add), inc="c_dve")
                self.rreg(pk, up)
                y_upd.append(up)
        self.y_tok = tmax(y_upd)
        self.rreg("big2", tmax(y_upd))
        self.rreg("big3", tmax(y_upd))
        for c in range(NCH):
            for hf in range(2):
                for wl in range(2):
                    self.rd.pop(("aU", c, hf, wl), None)
            self.rd.pop(("q2", c), None)
            for sl in range(2):
                self.rd.pop(("k2", c, sl), None)
        for s2 in range(6):
            for t in self.rd.pop(("vt", s2), []):
                self.rreg("vtm", t)
        self.rreg("vtm", tmax(y_upd))

    def _ffn(self, l):
        em = self.em
        y3 = self.y3()
        zb3 = self.b2bf(0, NCH * 513).rearrange("p (c n) -> p c n", n=513)
        o = NCH * 513
        xn = self.b2bf(o, o + NCH * 512); o += NCH * 512
        hsb = self.b2bf(o, o + NJ * 512); o += NJ * 512
        gel = self.b2bf(o, o + 2 * 512); o += 2 * 512
        assert o <= 32768
        w1p = self.big3[:, 0:4 * 2 * NCH * 128]
        w2p = self.big3[:, 4 * 2 * NCH * 128:4 * 2 * NCH * 128 + 4 * CM]

        big2_gate = self.rgate("big2", "vtm")
        big3_gate = self.rgate("big3", "vw")
        y_upd = []
        self._gbi = getattr(self, "_gbi", 0)
        gbsem = f"cgb{self._gbi % 4}"
        self._gbi += 1
        em.waits("pool", tmax(self.rgate("gb2"), self.const_gp, self.conv_tok[NL - 1]))
        gb = em.op("pool", lambda ep: ep.dma_start(
            self.gb2[:, :], self.d_gb_host[:, l * CM:(l + 1) * CM]), inc=gbsem, amt=16)

        for tc in range(NTC):
            t0 = tc * TC
            em.waits("dve", tmax(self.y_tok, self.rgate("zb"), big2_gate))
            big2_gate = None
            if tc == 0:
                zb_tok = em.op("dve", lambda ep: ep.tensor_copy(
                    zb3[:, :, 0:513], y3[:, :, 0:513]), inc="c_dve")
            else:
                em.op("dve", lambda ep, t0=t0: ep.tensor_copy(
                    zb3[:, :, 1:513], y3[:, :, t0 + 1:t0 + 513]))
                zb_tok = em.op("dve", lambda ep: ep.tensor_copy(
                    zb3[:, :, 0:1], self.bsave[:, :].rearrange("p (c n) -> p c n", n=1)), inc="c_dve")
            if tc < NTC - 1:
                em.op("dve", lambda ep, t0=t0: ep.tensor_copy(
                    self.bsave[:, :].rearrange("p (c n) -> p c n", n=1),
                    y3[:, :, t0 + TC:t0 + TC + 1]))

            def zsl(c):
                off = 1 if c < 4 else 0
                return zb3[:, c, off:off + TC]

            em.waits("pe", tmax(zb_tok, self.rgate("PD")))
            sq_t = None
            for c in range(NCH):
                em.op("pe", lambda ep, c=c: ep.matmul(
                    self.PD[0:1, :], self.onescol[:, :], zsl(c),
                    start=(c == 0), stop=(c == NCH - 1), skip_group_check=True))
                em.waits("act", tmax(zb_tok, self.rgate(("bcs", c % 2))))
                sq = em.op("act", lambda ep, c=c: ep.activation(
                    self.bcq_sb[:, (c % 2) * TC:(c % 2 + 1) * TC], zsl(c), AF.Square), inc="c_act")
                self.rreg("zb", sq)
                if c == 0:
                    em.waits("pe", tmax(sq, self.rgate("PE")))
                else:
                    em.waits("pe", [sq])
                sq_t = em.op("pe", lambda ep, c=c: ep.matmul(
                    self.PE_[0:1, :], self.onescol[:, :],
                    self.bcq_sb[:, (c % 2) * TC:(c % 2 + 1) * TC],
                    start=(c == 0), stop=(c == NCH - 1), skip_group_check=True),
                    inc="c_pe")
                self.rreg(("bcs", c % 2), sq_t)
                self.rreg("zb", sq_t)
            em.waits("dve", tmax(sq_t, self.rgate("x0t")))
            em.op("dve", lambda ep: ep.tensor_scalar_mul(self.mus, self.PD[0:1, :], 1.0 / CM))
            t_sv = em.op("dve", lambda ep: ep.tensor_tensor(self.statv, self.mus, self.mus,
                                                            ALU.mult), inc="c_dve")
            v1 = em.op("dve", lambda ep: ep.scalar_tensor_tensor(
                self.statv, self.PE_[0:1, :], 1.0 / CM, self.statv,
                ALU.mult, ALU.subtract), inc="c_dve")
            self.rreg("PD", v1)
            self.rreg("PE", v1)
            em.waits("act", [v1])
            em.waits("act", [self.eps_tok])
            em.op("act", lambda ep: ep.activation(self.statl, self.statv, AF.Ln,
                                                  bias=self.epsc[0:1, 0:1]))
            r1 = em.op("act", lambda ep: ep.activation(self.arhs[:, :], self.statl,
                                                       AF.Exp, scale=-0.5), inc="c_act")
            em.waits("dve", [r1])
            cr = em.op("dve", lambda ep: ep.scalar_tensor_tensor(
                self.c2rhs[0:1, :], self.PD[0:1, :], -1.0 / CM, self.arhs[:, :],
                ALU.mult, ALU.mult), inc="c_dve")
            self.rreg("PD", cr)
            self.rreg("x0t", t_sv)
            em.waits("pe", tmax(r1, cr, gb, self.rgate("PE")))
            at = em.op("pe", lambda ep: ep.matmul(self.PE_[:, :], self.ones1[:, :],
                                                  self.arhs[:, :], start=True, stop=True), inc="c_pe")
            xn_tok = []
            for c in range(NCH):
                pk = "PB" + str(c % 2)
                em.waits("pe", self.rgate(pk))
                c2 = em.op("pe", lambda ep, c=c: ep.matmul(
                    self.PB[:, (c % 2) * 512:(c % 2) * 512 + 512],
                    self.gb2[:, c * 128:(c + 1) * 128], self.c2rhs[:, :],
                    start=True, stop=True), inc="c_pe")
                em.waits("dve", tmax(at, c2, self.rgate(("xn", c))))
                p1 = em.op("dve", lambda ep, c=c: ep.tensor_tensor(
                    self.t1s[:, :], zsl(c), self.PE_[:, :], ALU.mult),
                    inc="c_dve")
                self.rreg("zb", p1)
                p2 = em.op("dve", lambda ep, c=c: ep.scalar_tensor_tensor(
                    xn[:, c * 512:(c + 1) * 512], self.t1s[:, :],
                    self.g_pp[:, l * NCH + c:l * NCH + c + 1],
                    self.PB[:, (c % 2) * 512:(c % 2) * 512 + 512],
                    ALU.mult, ALU.add), inc="c_dve")
                self.rreg(pk, p2)
                self.rreg("PE", p2)
                xn_tok.append(p2)
                self.wr[("xn", c)] = p2
            self.rreg("gb2", xn_tok[-1])
            h_tok = [None] * NJ
            for j in range(NJ):
                wj = 128 if j < NJ - 1 else WJL
                slot = j % 4
                em.waits("sp", tmax(self.conv_tok[l], self.rgate(("w1", slot)), big3_gate))
                big3_gate = None
                wsrc = reap(self.s_w1[l, j], [(128, 128), (NCH * 128 * 128, 2), (128 * 128, NCH), (1, wj)])
                wdst = w1p[:, slot * 2 * NCH * 128:(slot + 1) * 2 * NCH * 128]
                wldst = reap(wdst, [(wdst.ap[0][0], 128), (NCH * 128, 2), (128, NCH), (1, wj)])
                wtok = em.op("sp", lambda ep, s=wsrc, d=wldst: ep.dma_start(d, s),
                             inc=f"w1p{slot}", amt=16)
                em.waits("pe", tmax(wtok, *xn_tok))
                em.waits("pe", self.rgate("PC" + str(j % 2), "PA" + str(j % 2)))
                mma = mmg = None
                for sc in range(NCH):
                    mma = em.op("pe", lambda ep, sc=sc, wdst=wdst, wj=wj, j=j: ep.matmul(
                        self.PC[0:wj, (j % 2) * 512:(j % 2) * 512 + 512],
                        wdst[:, sc * 128:sc * 128 + wj], xn[:, sc * 512:(sc + 1) * 512],
                        start=(sc == 0), stop=(sc == NCH - 1), skip_group_check=True))
                for sc in range(NCH):
                    mmg = em.op("pe", lambda ep, sc=sc, wdst=wdst, wj=wj, j=j: ep.matmul(
                        self.PA[0:wj, (j % 2) * 512:(j % 2) * 512 + 512],
                        wdst[:, NCH * 128 + sc * 128: NCH * 128 + sc * 128 + wj],
                        xn[:, sc * 512:(sc + 1) * 512],
                        start=(sc == 0), stop=(sc == NCH - 1), skip_group_check=True),
                        inc="c_pe" if sc == NCH - 1 else None)
                self.rreg(("w1", slot), mmg)
                for c in range(NCH):
                    self.rreg(("xn", c), mmg)
                em.waits("act", tmax(mmg, self.rgate(("gel", j % 2))))
                gl = em.op("act", lambda ep, j=j, wj=wj: ep.activation(
                    gel[0:wj, (j % 2) * 512:(j % 2) * 512 + 512],
                    self.PA[0:wj, (j % 2) * 512:(j % 2) * 512 + 512],
                    AF.Gelu, bias=self.b1g_pp[0:wj, l * NJ + j:l * NJ + j + 1]), inc="c_act")
                self.rreg("PA" + str(j % 2), gl)
                em.waits("dve", tmax(gl, self.rgate(("h", j))))
                ht = em.op("dve", lambda ep, j=j, wj=wj: ep.scalar_tensor_tensor(
                    hsb[0:wj, j * 512:j * 512 + 512],
                    self.PC[0:wj, (j % 2) * 512:(j % 2) * 512 + 512],
                    self.b1a_pp[0:wj, l * NJ + j:l * NJ + j + 1],
                    gel[0:wj, (j % 2) * 512:(j % 2) * 512 + 512], ALU.add, ALU.mult), inc="c_dve")
                h_tok[j] = ht
                self.rreg("PC" + str(j % 2), ht)
                self.rreg(("gel", j % 2), ht)
                self.wr[("h", j)] = ht
            w2reg = [(self.PA, 0, "PA0"), (self.PA, 512, "PA1"),
                     (self.PB, 0, "PB0"), (self.PB, 512, "PB1"),
                     (self.PC, 0, "PC0"), (self.PC, 512, "PC1"),
                     (self.PD, 0, "PD"), (self.PE_, 0, "PE")]
            em.waits("pe", tmax(h_tok))
            em.waits("pe", self.rgate(*[k for _, _, k in w2reg]))
            w2_toks = {}

            def w2_load(jj):
                wjx = 128 if jj < NJ - 1 else WJL
                slot = jj % 4
                em.waits("sp", tmax(self.conv_tok[l], self.rgate(("w2", slot))))
                wsrc = self.s_w2[l, jj * 128:jj * 128 + wjx, :]
                wdst = w2p[:, slot * CM:(slot + 1) * CM]
                w2_toks[jj] = em.op("sp", lambda ep, s=wsrc, d=wdst[0:wjx, :]: ep.dma_start(d, s),
                                    inc=f"w2p{slot}", amt=16)

            for jj in range(min(3, NJ)):
                w2_load(jj)
            mmlast = None
            for j in range(NJ):
                if j + 3 < NJ:
                    w2_load(j + 3)
                wj = 128 if j < NJ - 1 else WJL
                wdst = w2p[:, (j % 4) * CM:(j % 4 + 1) * CM]
                em.waits("pe", tmax(w2_toks.pop(j)))
                for oc in range(NCH):
                    pdst, po, pk = w2reg[oc]
                    mmlast = em.op("pe", lambda ep, j=j, wj=wj, oc=oc, po=po, wdst=wdst,
                                   pdst=pdst: ep.matmul(
                                       pdst[:, po:po + 512],
                                       wdst[0:wj, oc * 128:oc * 128 + 128],
                                       hsb[0:wj, j * 512:j * 512 + 512],
                                       start=(j == 0), stop=(j == NJ - 1),
                                       skip_group_check=True),
                                   inc="c_pe" if oc == NCH - 1 else None)
                self.rreg(("w2", j % 4), mmlast)
                if j == NJ - 1:
                    for jj in range(NJ):
                        self.rreg(("h", jj), mmlast)
            em.waits("dve", [mmlast])
            for oc in range(NCH):
                pdst, po, pk = w2reg[oc]
                dsty = y3[:, oc, 1 + t0:1 + t0 + TC]
                up = em.op("dve", lambda ep, dsty=dsty, pdst=pdst, po=po, oc=oc:
                           ep.scalar_tensor_tensor(dsty, pdst[:, po:po + 512],
                                                   self.b2_pp[:, l * NCH + oc:l * NCH + oc + 1],
                                                   dsty, ALU.add, ALU.add), inc="c_dve")
                self.rreg(pk, up)
                y_upd.append(up)
        self.y_tok = tmax(y_upd)
        self.rreg("big2", tmax(y_upd))
        self.rreg("big3", tmax(y_upd))
        for c in range(NCH):
            self.rd.pop(("xn", c), None)
        for j in range(NJ):
            self.rd.pop(("h", j), None)

    def _outer(self):
        em = self.em
        xt3 = self.xt3()
        vFM = self.b1bf(16384, 32768)
        E = self.b1bf(0, 16384)
        wT1 = self.big3[:, 0:NCH * CM]
        wT2 = self.big3[:, NCH * CM:2 * NCH * CM]
        knb = self.big3[:, 2 * NCH * CM:2 * NCH * CM + 2 * CM]
        vtb = self.big3[:, 2 * NCH * CM + 2 * CM:2 * NCH * CM + 4 * CM]
        ctxs = self.t1s[:, :].bitcast(BF16)[:, 0:512]
        knraw = self.knraw_v
        ksumr = self.knraw_v[0:1, :]
        bk_row = self.tabs[0:1, 0:CM]
        bv_row = self.tabs[0:1, CM:2 * CM]
        bp_row = self.tabs[0:1, 2 * CM:3 * CM]
        em.waits("pool", tmax(self.rgate(("tabr", 0)), self.rgate(("tabr", 1)), self.const_gp))
        em.op("pool", lambda ep: ep.dma_start(bk_row, self.d_bk_r), inc="cbr", amt=16)
        em.op("pool", lambda ep: ep.dma_start(bv_row, self.d_bv_r), inc="cbr", amt=16)
        em.op("pool", lambda ep: ep.dma_start(bp_row, self.d_bp_r), inc="cbr", amt=16)
        br_tok = ("cbr", em.cnt["cbr"])

        big3_gate = self.rgate("big3", "vw", "vtm")
        big1_gate = tmax(self.rgate("big1"), self.y_tok)

        trpb = self.PC[:, :].bitcast(BF16)
        vfm_tok = None
        for tt in range(NTT):
            xin = xt3[:, tt, :]
            rstd, nm = self._tm_stats(xin, tmax(self.xt_tok))
            tv = em.op("dve", lambda ep, xin=xin, rstd=rstd, nm=nm: ep.tensor_scalar(
                self.ytmb, xin, rstd, nm, ALU.mult, ALU.add), inc="c_dve")
            self.rreg("col1", tv)
            em.waits("pe", tmax(tv, self.rgate("PC0", "PC1")))
            tk = None
            for ch in range(NCH):
                tk = em.op("pe", lambda ep, ch=ch: ep.transpose(
                    trpb[:, ch * 128:(ch + 1) * 128], self.ytmb[:, ch * 128:(ch + 1) * 128],
                    self.identb[:, :]), inc="c_pe" if ch == NCH - 1 else None)
            self.rreg("ytm", tk)
            em.waits("dve", tmax(tk, big1_gate))
            big1_gate = None
            dst = AP(vFM.tensor, vFM.offset + tt * 128,
                     [[vFM.ap[0][0], 128], [T, NCH], [1, 128]])
            vfm_tok = em.op("dve", lambda ep, dst=dst: ep.tensor_copy(
                dst, trpb[:, 0:1024].rearrange("p (c n) -> p c n", n=128)), inc="c_dve")
            self.rreg("PC0", vfm_tok)
            self.rreg("PC1", vfm_tok)
        em.waits("sp", tmax(self.wT_tok, big3_gate))
        w1tok = em.op("sp", lambda ep: ep.dma_start(
            wT1, reap(self.s_wT[0], [(CM, 128), (128 * CM, NCH), (1, CM)])), inc="cwt1", amt=16)
        em.waits("act", tmax(big1_gate, vfm_tok))
        e_toks = []
        for tc in range(NTC):
            for oc in range(NCH):
                pq = self.PA if oc % 4 < 2 else self.PB
                pk = ("PA" if oc % 4 < 2 else "PB") + str(oc % 2)
                em.waits("pe", tmax(w1tok, vfm_tok, self.rgate(pk)))
                mm = None
                for sc in range(NCH):
                    mm = em.op("pe", lambda ep, sc=sc, oc=oc, tc=tc, pq=pq: ep.matmul(
                        pq[:, (oc % 2) * 512:(oc % 2) * 512 + 512],
                        wT1[:, sc * CM + oc * 128: sc * CM + oc * 128 + 128],
                        vFM[:, sc * T + tc * 512: sc * T + (tc + 1) * 512],
                        start=(sc == 0), stop=(sc == NCH - 1), skip_group_check=True),
                        inc="c_pe" if sc == NCH - 1 else None)
                em.waits("act", [mm])
                ex = em.op("act", lambda ep, oc=oc, tc=tc, pq=pq: ep.activation(
                    E[:, oc * T + tc * 512: oc * T + (tc + 1) * 512],
                    pq[:, (oc % 2) * 512:(oc % 2) * 512 + 512],
                    AF.Exp, bias=self.bq_pp[:, oc:oc + 1]), inc="c_act")
                self.rreg(pk, ex)
                e_toks.append(ex)
        em.waits("sp", tmax(e_toks[-1], self.rgate("wT1rd")))
        w2tok = em.op("sp", lambda ep: ep.dma_start(
            wT2, reap(self.s_wT[1], [(CM, 128), (128 * CM, NCH), (1, CM)])), inc="cwt2", amt=16)
        w3tok = em.op("sp", lambda ep: ep.dma_start(
            wT1, reap(self.s_wT[2], [(CM, 128), (128 * CM, NCH), (1, CM)])), inc="cwt1", amt=16)
        em.waits("pe", self.rgate("PC0", "PC1", "PD"))
        ctx_done = None
        ks_done = None
        kn_tok = vt_tok = None
        for tt in range(NTT):
            for which in range(2):
                wt = wT2 if which == 0 else wT1
                wtk = w2tok if which == 0 else w3tok
                brow = bk_row if which == 0 else bv_row
                pq = self.PA if which == 0 else self.PB
                pk0 = ("PA0", "PA1") if which == 0 else ("PB0", "PB1")
                em.waits("pe", tmax(wtk, vfm_tok, br_tok))
                em.waits("pe", self.rgate(*pk0))
                mm = None
                for g2 in range(2):
                    for sc in range(NCH):
                        em.op("pe", lambda ep, sc=sc, g2=g2, tt=tt, wt=wt, pq=pq: ep.matmul(
                            pq[:, g2 * 512:(g2 + 1) * 512],
                            vFM[:, sc * T + tt * 128: sc * T + tt * 128 + 128],
                            wt[:, sc * CM + g2 * 512: sc * CM + (g2 + 1) * 512],
                            start=(sc == 0), stop=False, skip_group_check=True))
                    mm = em.op("pe", lambda ep, g2=g2, brow=brow, pq=pq: ep.matmul(
                        pq[:, g2 * 512:(g2 + 1) * 512], self.ones1[:, :],
                        brow[:, g2 * 512:(g2 + 1) * 512],
                        start=False, stop=True, skip_group_check=True),
                        inc="c_pe" if g2 == 1 else None)
                if which == 0:
                    em.waits("act", tmax(mm, self.rgate("knraw")))
                    ek = em.op("act", lambda ep: ep.activation(knraw[:, :], self.PA[:, 0:1024],
                                                               AF.Exp), inc="c_act")
                    self.rreg("PA0", ek)
                    self.rreg("PA1", ek)
                    em.waits("dve", [ek])
                    dk = em.op("dve", lambda ep: ep.reduce_sum(
                        self.denk[:, :], knraw.rearrange("p (h n) -> p h n", n=64), axis=AX.X),
                        inc="c_dve")
                    em.waits("act", [dk])
                    em.op("act", lambda ep: ep.activation(self.denk[:, :], self.denk[:, :], AF.Ln))
                    ik = em.op("act", lambda ep: ep.activation(self.denk[:, :], self.denk[:, :],
                                                               AF.Exp, scale=-1.0), inc="c_act")
                    em.waits("dve", tmax(ik, self.rgate(("knb", tt % 2))))
                    kt = None
                    for h in range(H):
                        kt = em.op("dve", lambda ep, h=h, tt=tt: ep.tensor_scalar_mul(
                            knb[:, (tt % 2) * CM + h * 64:(tt % 2) * CM + (h + 1) * 64],
                            knraw[:, h * 64:(h + 1) * 64], self.denk[:, h:h + 1]),
                            inc="c_dve" if h == H - 1 else None)
                    self.rreg("knraw", kt)
                    kn_tok = kt
                else:
                    em.waits("dve", tmax(mm, self.rgate(("vtb", tt % 2))))
                    vt_tok = em.op("dve", lambda ep, tt=tt: ep.tensor_copy(
                        vtb[:, (tt % 2) * CM:(tt % 2 + 1) * CM], self.PB[:, 0:1024]), inc="c_dve")
                    self.rreg("PB0", vt_tok)
                    self.rreg("PB1", vt_tok)
            em.waits("pe", tmax(kn_tok, vt_tok, self.rgate("PC0", "PC1")))
            for h in range(H):
                ctx_done = em.op("pe", lambda ep, h=h, tt=tt: ep.matmul(
                    self.PC[(h % 2) * 64:(h % 2) * 64 + 64, (h // 2) * 64:(h // 2) * 64 + 64],
                    knb[:, (tt % 2) * CM + h * 64:(tt % 2) * CM + (h + 1) * 64],
                    vtb[:, (tt % 2) * CM + h * 64:(tt % 2) * CM + (h + 1) * 64],
                    start=True, stop=True, skip_group_check=True),
                    inc="c_pe" if h == H - 1 else None)
            em.waits("dve", [ctx_done])
            if tt == 0:
                ctx_done = em.op("dve", lambda ep: ep.tensor_copy(
                    self.ctxacc[:, :], self.PC[:, 0:512]), inc="c_dve")
            else:
                ctx_done = em.op("dve", lambda ep: ep.tensor_tensor(
                    self.ctxacc[:, :], self.ctxacc[:, :], self.PC[:, 0:512], ALU.add), inc="c_dve")
            self.rreg("PC0", ctx_done)
            for g2 in range(2):
                ks_done = em.op("pe", lambda ep, g2=g2, tt=tt: ep.matmul(
                    self.PD[0:1, :] if g2 == 0 else self.PE_[0:1, :],
                    self.onescol[:, :],
                    knb[:, (tt % 2) * CM + g2 * 512:(tt % 2) * CM + (g2 + 1) * 512],
                    start=(tt == 0), stop=(tt == NTT - 1), skip_group_check=True),
                    inc="c_pe" if (g2 == 1 and tt == NTT - 1) else None)
            self.rreg(("knb", tt % 2), ctx_done)
            self.rreg(("vtb", tt % 2), ctx_done)
        em.waits("dve", [ctx_done, ks_done])
        ctx_ev = em.op("dve", lambda ep: ep.tensor_copy(ctxs[:, :], self.ctxacc[:, :]), inc="c_dve")
        em.op("dve", lambda ep: ep.tensor_copy(ksumr[0:1, 0:512], self.PD[0:1, :]))
        ks_ev = em.op("dve", lambda ep: ep.tensor_copy(ksumr[0:1, 512:1024], self.PE_[0:1, :]),
                      inc="c_dve")
        self.rreg("knraw", ks_ev)
        self.rreg("PD", ks_ev)
        self.rreg("PE", ks_ev)
        em.waits("pe", tmax(ks_ev, self.rgate("PE")))
        tk = None
        for c in range(NCH):
            tk = em.op("pe", lambda ep, c=c, ksumr=ksumr: ep.matmul(
                self.PE_[:, c:c + 1], ksumr[0:1, c * 128:(c + 1) * 128],
                self.onescol[0:1, 0:1], start=True, stop=True, skip_group_check=True),
                inc="c_pe" if c == NCH - 1 else None)
        em.waits("dve", [tk])
        kc = em.op("dve", lambda ep: ep.tensor_copy(self.ksumc[:, :], self.PE_[:, 0:8]), inc="c_dve")
        self.rreg("PE", kc)
        em.waits("dve", [kc])
        kw = None
        for c in range(NCH):
            kw = em.op("dve", lambda ep, c=c: ep.tensor_scalar_mul(
                self.kwsel[:, c * 16:(c + 1) * 16], self.bselq[:, c * 16:(c + 1) * 16],
                self.ksumc[:, c:c + 1]), inc="c_dve" if c == NCH - 1 else None)
        collq = AP(vFM.tensor, vFM.offset, [[vFM.ap[0][0], 16], [1, 2048]])
        collD = AP(vFM.tensor, vFM.offset + 2048, [[vFM.ap[0][0], 16], [1, 2048]])
        dq_toks = []
        for tc in range(NTC):
            for which in range(2):
                sel = self.bselq if which == 0 else self.kwsel
                coll = collq if which == 0 else collD
                em.waits("pe", tmax(kw, e_toks[-1], self.rgate("PD")))
                mm = None
                for c in range(NCH):
                    mm = em.op("pe", lambda ep, c=c, sel=sel, tc=tc: ep.matmul(
                        self.PD[0:16, :], sel[:, c * 16:(c + 1) * 16],
                        E[:, c * T + tc * 512:c * T + (tc + 1) * 512],
                        start=(c == 0), stop=(c == NCH - 1), skip_group_check=True),
                        inc="c_pe" if c == NCH - 1 else None)
                em.waits("dve", [mm])
                ev = em.op("dve", lambda ep, coll=coll, tc=tc: ep.tensor_copy(
                    coll[:, tc * 512:(tc + 1) * 512], self.PD[0:16, :]), inc="c_dve")
                self.rreg("PD", ev)
                dq_toks.append(ev)
        em.waits("act", tmax(dq_toks))
        em.op("act", lambda ep: ep.activation(collq, collq, AF.Ln))
        iq = em.op("act", lambda ep: ep.activation(collq, collq, AF.Exp, scale=-1.0), inc="c_act")
        em.op("act", lambda ep: ep.activation(collD, collD, AF.Ln))
        iD = em.op("act", lambda ep: ep.activation(collD, collD, AF.Exp, scale=-1.0), inc="c_act")
        asm_toks = []
        for tc in range(NTC):
            for c in range(NCH):
                em.waits("pe", tmax(ctx_ev, iq, iD, self.rgate("PB0"), self.rgate("PE")))
                mp = None
                for hh in range(2):
                    h = 2 * c + hh
                    mp = em.op("pe", lambda ep, h=h, hh=hh, c=c, tc=tc: ep.matmul(
                        self.PB[hh * 64:hh * 64 + 64, 0:512],
                        ctxs[hh * 64:hh * 64 + 64, (h // 2) * 64:(h // 2) * 64 + 64],
                        E[hh * 64:hh * 64 + 64, c * T + tc * 512:c * T + (tc + 1) * 512],
                        start=True, stop=True), inc="c_pe" if hh == 1 else None)
                bq1 = em.op("pe", lambda ep, c=c, tc=tc: ep.matmul(
                    self.PE_[:, :], self.selpat[:, c * 128:(c + 1) * 128],
                    collq[:, tc * 512:(tc + 1) * 512],
                    start=True, stop=True), inc="c_pe")
                em.waits("dve", tmax(bq1, self.rgate("scrA")))
                t1 = em.op("dve", lambda ep, c=c, tc=tc: ep.tensor_tensor(
                    self.scrA[:, :], E[:, c * T + tc * 512:c * T + (tc + 1) * 512],
                    self.PE_[:, :], ALU.mult), inc="c_dve")
                em.waits("pe", [t1])
                bD = em.op("pe", lambda ep, c=c, tc=tc: ep.matmul(
                    self.PE_[:, :], self.selpat[:, c * 128:(c + 1) * 128],
                    collD[:, tc * 512:(tc + 1) * 512],
                    start=True, stop=True), inc="c_pe")
                em.waits("act", tmax(bD, self.rgate("bcsb")))
                bDs = em.op("act", lambda ep: ep.activation(self.bcq_sb[:, 0:TC], self.PE_[:, :],
                                                            AF.Identity), inc="c_act")
                self.rreg("PE", bDs)
                em.waits("dve", tmax(bDs, mp, self.rgate("scrB")))
                t2 = em.op("dve", lambda ep: ep.tensor_tensor(
                    self.scrB[:, :], self.PB[:, 0:512], self.bcq_sb[:, 0:TC], ALU.mult), inc="c_dve")
                self.rreg("PB0", t2)
                self.rreg("bcsb", t2)
                fin = em.op("dve", lambda ep, c=c, tc=tc: ep.tensor_tensor(
                    E[:, c * T + tc * 512:c * T + (tc + 1) * 512], self.scrA[:, :],
                    self.scrB[:, :], ALU.add), inc="c_dve")
                self.rreg("scrA", fin)
                self.rreg("scrB", fin)
                asm_toks.append(fin)
        em.waits("sp", tmax(asm_toks))
        self.rreg(("tabr", 0), ks_done)
        self.rreg(("tabr", 1), ks_done)
        wp_tok = em.op("sp", lambda ep: ep.dma_start(
            wT2, reap(self.s_wT[3], [(CM, 128), (128 * CM, NCH), (1, CM)])), inc="cwt2", amt=16)
        self.xt_tok = []
        for tt in range(NTT):
            em.waits("pe", tmax(wp_tok, asm_toks))
            em.waits("pe", self.rgate("PA0", "PA1"))
            mm = None
            for g2 in range(2):
                for sc in range(NCH):
                    em.op("pe", lambda ep, sc=sc, g2=g2, tt=tt: ep.matmul(
                        self.PA[:, g2 * 512:(g2 + 1) * 512],
                        E[:, sc * T + tt * 128: sc * T + tt * 128 + 128],
                        wT2[:, sc * CM + g2 * 512: sc * CM + (g2 + 1) * 512],
                        start=(sc == 0), stop=False, skip_group_check=True))
                mm = em.op("pe", lambda ep, g2=g2: ep.matmul(
                    self.PA[:, g2 * 512:(g2 + 1) * 512], self.ones1[:, :],
                    bp_row[:, g2 * 512:(g2 + 1) * 512],
                    start=False, stop=True, skip_group_check=True),
                    inc="c_pe" if g2 == 1 else None)
            em.waits("dve", tmax(mm, self.rgate(("xt", tt))))
            up = em.op("dve", lambda ep, tt=tt: ep.tensor_tensor(
                self.xt3()[:, tt, :].rearrange("p (c n) -> p c n", n=128),
                self.xt3()[:, tt, :].rearrange("p (c n) -> p c n", n=128),
                self.PA[:, 0:1024].rearrange("p (c n) -> p c n", n=128), ALU.add), inc="c_dve")
            self.rreg("PA0", up)
            self.rreg("PA1", up)
            self.xt_tok.append(up)
        self.rreg("big1", tmax(asm_toks, self.xt_tok))
        self.rreg("big3", tmax(asm_toks, self.xt_tok))



_CACHE = {}


def _build(dbg_cut=None):
    key = ("nc", dbg_cut)
    if key not in _CACHE:
        nc = bass.Bass("TRN2", target_bir_lowering=False)
        kb = KB(nc, dbg_cut=dbg_cut)
        n = kb.build()
        nc._pe_tags = kb.em.pe_tags
        _CACHE[key] = (nc, n)
    return _CACHE[key]


def make_inputs(inputs):
    consts = host_constants()
    lay = host_layouts(inputs)
    x = np.ascontiguousarray(np.asarray(inputs["L0_attn_x"], np.float32))
    base = {
        "wqkv": np.ascontiguousarray(np.asarray(inputs["L0_ffn_Wqkv"], np.float32)),
        "wo3": np.ascontiguousarray(np.asarray(inputs["L0_ffn_Wo"], np.float32)),
        "w1": np.ascontiguousarray(np.asarray(inputs["L0_ffn_W1"], np.float32)),
        "w2": np.ascontiguousarray(np.asarray(inputs["L0_ffn_W2"], np.float32)),
        "awq": np.ascontiguousarray(np.asarray(inputs["L0_attn_Wq"], np.float32)),
        "awk": np.ascontiguousarray(np.asarray(inputs["L0_attn_Wk"], np.float32)),
        "awv": np.ascontiguousarray(np.asarray(inputs["L0_attn_Wv"], np.float32)),
        "awp": np.ascontiguousarray(np.asarray(inputs["L0_attn_Wp"], np.float32)),
    }
    base.update(consts)
    base.update(lay)
    in_maps = []
    for core in range(8):
        m = dict(base)
        m["x"] = np.ascontiguousarray(x[2 * core:2 * core + 2].reshape(NSEQ * T, CM))
        in_maps.append(m)
    return in_maps


def kernel(**inputs):
    nc, _ = _build()
    in_maps = make_inputs(inputs)
    res = run_bass_kernel_spmd(nc, in_maps, core_ids=list(range(8)))
    out = np.empty((16, T, CM), np.float32)
    for core in range(8):
        out[2 * core:2 * core + 2] = res.results[core]["out"].reshape(NSEQ, T, CM)
    return out


# version 08
# speedup vs baseline: 1.0762x; 1.0265x over previous
import os
import numpy as np
import ml_dtypes
from contextlib import ExitStack

os.environ.setdefault("GAUGE_NTFF_IGNORE_DMA", "1")

import concourse.bass as bass
import concourse.mybir as mybir
from concourse.ap import AP
from concourse.bass_utils import run_bass_kernel_spmd

F32 = mybir.dt.float32
BF16 = mybir.dt.bfloat16
AF = mybir.ActivationFunctionType
ALU = mybir.AluOpType
AX = mybir.AxisListType
BF = ml_dtypes.bfloat16

T, CM, H, DH, WIN = 2048, 1024, 16, 64, 256
NCH, TC, NL, DI, NJ = 8, 512, 6, 2730, 22
NTC = T // TC
WJL = DI - 21 * 128
NSEQ = 2
NTT = T // 128
DBG_NW = 2
DBG_NH = H
DBG_SNAP = None
DBG_CUTOP = None



def host_constants():
    c = {}
    half = np.arange(0, DH, 2, dtype=np.float64)
    inv_freq = 1.0 / 10000.0 ** (half / DH)
    t = np.arange(T, dtype=np.float64)
    freqs = t[:, None] * inv_freq[None, :]
    freqs = np.concatenate([freqs, freqs], -1)
    base_scale = (half + 0.4 * DH) / (1.4 * DH)
    power = (t - T // 2) / (WIN // 2)
    scale = base_scale[None, :] ** power[:, None]
    scale = np.concatenate([scale, scale], -1)

    def fm(x):
        x = np.ascontiguousarray(x.T.astype(np.float32))
        return np.concatenate([x, x], 0).astype(BF)

    c["c_tabs"] = np.concatenate([fm(np.cos(freqs) * scale * 8.0), fm(np.sin(freqs) * scale * 8.0),
                                  fm(np.cos(freqs) / scale), fm(np.sin(freqs) / scale)], axis=1)
    c["c_identb"] = np.eye(128, dtype=BF)
    c["c_identf"] = np.eye(128, dtype=np.float32)
    pr = np.zeros((128, 128), np.float32)
    for b0 in (0, 64):
        for d in range(64):
            if d < 32:
                pr[b0 + d + 32, b0 + d] = -1.0
            else:
                pr[b0 + d - 32, b0 + d] = 1.0
    c["c_prot"] = pr.astype(BF)
    bq = np.zeros((128, NCH * 16), np.float32)
    bk = np.zeros((128, NCH * 48), np.float32)
    for ch in range(NCH):
        bq[0:64, ch * 16 + 2 * ch] = 1.0
        bq[64:128, ch * 16 + 2 * ch + 1] = 1.0
        bk[0:64, ch * 48 + 32 + 2 * ch] = 1.0
        bk[64:128, ch * 48 + 32 + 2 * ch + 1] = 1.0
    c["c_bselq"] = bq.astype(BF)
    c["c_bselk"] = bk.astype(BF)
    sp = np.zeros((16, NCH * 128), np.float32)
    for ch in range(NCH):
        sp[2 * ch, ch * 128:ch * 128 + 64] = 1.0
        sp[2 * ch + 1, ch * 128 + 64:ch * 128 + 128] = 1.0
    c["c_selpat"] = sp.astype(BF)
    u = np.zeros((1, 16 * 16), np.float32)
    for h in range(16):
        u[0, h * 16 + h] = 1.0
    c["c_unit16"] = u.astype(BF)
    c["c_ones1"] = np.ones((1, 128), np.float32).astype(BF)
    c["c_onescol"] = np.ones((128, 1), np.float32).astype(BF)
    c["c_onesrow"] = np.ones((1, TC), np.float32).astype(BF)
    p = np.arange(128)[:, None]
    i = np.arange(256)[None, :]

    def keep(jc):
        d = (i + 256) - (jc * 128 + p)
        return ((d >= 0) & (d <= 256)).astype(np.float32)

    k4 = np.hstack([keep(0), keep(1), keep(2), keep(3)])
    c["c_maskb"] = ((1.0 - k4) * -300.0).astype(BF)
    k0 = np.hstack([keep(2), keep(3)])
    c["c_maskb0"] = ((1.0 - k0) * -300.0).astype(BF)
    return c


def host_layouts(ins):
    d = {}
    qs = np.asarray(ins["L0_ffn_q_scale"], np.float32)
    ks = np.asarray(ins["L0_ffn_k_scale"], np.float32)
    d["qs_pp"] = np.ascontiguousarray(np.concatenate([qs.T, qs.T], 0))
    d["ks_pp"] = np.ascontiguousarray(np.concatenate([ks.T, ks.T], 0))
    g = np.asarray(ins["L0_ffn_ln_g"], np.float32)
    b = np.asarray(ins["L0_ffn_ln_b"], np.float32)
    d["g_pp"] = np.ascontiguousarray(g.reshape(NL, NCH, 128).transpose(2, 0, 1).reshape(128, NL * NCH))
    d["gb_host"] = np.ascontiguousarray(np.stack([g.reshape(-1), b.reshape(-1)], 0))
    b1 = np.asarray(ins["L0_ffn_b1"], np.float32)
    b1a = np.zeros((128, NL * NJ), np.float32)
    b1g = np.zeros((128, NL * NJ), np.float32)
    for l in range(NL):
        for j in range(NJ):
            w = 128 if j < NJ - 1 else WJL
            b1a[0:w, l * NJ + j] = b1[l, j * 128:j * 128 + w]
            b1g[0:w, l * NJ + j] = b1[l, DI + j * 128:DI + j * 128 + w]
    d["b1a_pp"] = b1a
    d["b1g_pp"] = b1g
    b2 = np.asarray(ins["L0_ffn_b2"], np.float32)
    d["b2_pp"] = np.ascontiguousarray(b2.reshape(NL, NCH, 128).transpose(2, 0, 1).reshape(128, NL * NCH))
    d["bq_pp"] = np.ascontiguousarray(np.asarray(ins["L0_attn_bq"], np.float32).reshape(NCH, 128).T)
    d["bk_r"] = np.asarray(ins["L0_attn_bk"], np.float32).reshape(1, CM).copy()
    d["bv_r"] = np.asarray(ins["L0_attn_bv"], np.float32).reshape(1, CM).copy()
    d["bp_r"] = np.asarray(ins["L0_attn_bp"], np.float32).reshape(1, CM).copy()
    return d



ENG2ATTR = {"pe": "tensor", "act": "scalar", "dve": "vector", "pool": "gpsimd", "sp": "sync"}


AUTO_SEM = {"dve": "c_dve", "act": "c_act", "pool": "c_gp"}
DMA_LOOKAHEAD = 8


class Em:
    def __init__(self):
        self.ops = {e: [] for e in ENG2ATTR}
        self.sems = {}
        self.cnt = {}
        self.waited = {e: {} for e in ENG2ATTR}
        self.nops = 0
        self.pe_tags = []

    def add_sem(self, name, handle):
        self.sems[name] = handle
        self.cnt[name] = 0

    def _dummy(self, eng, amt):
        sb = self.dummy_sb
        ps = self.dummy_ps
        bb = self.dummy_bb
        if amt == 16:
            return lambda ep: ep.dma_start(sb[0:1, 4:8], sb[0:1, 0:4])
        if eng == "pe":
            return lambda ep: ep.matmul(ps[0:1, 0:1], bb[0:1, 0:1], bb[0:1, 0:1],
                                        start=True, stop=True, skip_group_check=True)
        if eng == "act":
            return lambda ep: ep.activation(sb[0:1, 0:1], sb[0:1, 1:2], AF.Copy)
        return lambda ep: ep.engine_nop()

    def op(self, eng, fn, inc=None, amt=1):
        if eng == "pe":
            import sys as _s
            self.pe_tags.append(_s._getframe(1).f_lineno)
        if DBG_CUTOP is not None and self.nops >= DBG_CUTOP:
            fn = self._dummy(eng, amt)
        if eng in AUTO_SEM and amt == 1:
            own = AUTO_SEM[eng]
            if inc is None:
                inc = own
            if inc == own:
                self.wait(eng, (own, self.cnt[own]))
        self.nops += 1
        if inc is not None:
            self.cnt[inc] += amt
            tok = (inc, self.cnt[inc])
            sem_h = self.sems[inc]

            def wrapped(ep, fn=fn, sem_h=sem_h, amt=amt):
                fn(ep).then_inc(sem_h, amt)

            self.ops[eng].append(wrapped)
            return tok
        self.ops[eng].append(fn)
        return None

    def wait(self, eng, tok):
        if tok is None:
            return
        s, v = tok
        if v <= 0 or self.waited[eng].get(s, 0) >= v:
            return
        self.waited[eng][s] = v
        sem_h = self.sems[s]
        self.ops[eng].append(lambda ep, sem_h=sem_h, v=v: ep.wait_ge(sem_h, v))

    def waits(self, eng, toks):
        for t in toks:
            if t is not None:
                self.wait(eng, t)

    def flush(self, blk):
        for eng, attr in ENG2ATTR.items():
            ops = self.ops[eng]
            self.ops[eng] = []

            def run(ep, ops=ops):
                for f in ops:
                    f(ep)

            getattr(blk, attr)(run)


def tmax(*toks):
    best = {}
    for t in toks:
        if t is None:
            continue
        for u in (t if isinstance(t, list) else [t]):
            if u is not None:
                best[u[0]] = max(best.get(u[0], 0), u[1])
    return [(s, v) for s, v in best.items()]



def reap(src_ap, dims):
    return AP(src_ap.tensor, src_ap.offset, [list(d) for d in dims])



class KB:
    def __init__(self, nc, dbg_cut=None):
        self.nc = nc
        self.dbg_cut = dbg_cut
        self.es = ExitStack()
        self.y_tok = None
        self.dbg_done = False
        self.rd = {}
        self.wr = {}

    def rreg(self, key, tok):
        if tok is not None:
            self.rd.setdefault(key, []).append(tok)

    def rgate(self, *keys):
        out = []
        for k in keys:
            out += self.rd.pop(k, [])
        return tmax(*out)

    def sb(self, name, shape, dt):
        return self.es.enter_context(self.nc.sbuf_tensor(name, shape, dt))

    def sem(self, name):
        h = self.es.enter_context(self.nc.semaphore(name))
        self.em.add_sem(name, h)

    def build(self):
        nc = self.nc
        self.x = nc.dram_tensor("x", [NSEQ * T, CM], F32, kind="ExternalInput").ap()
        self.out = nc.dram_tensor("out", [NSEQ * T, CM], F32, kind="ExternalOutput").ap()
        for n, s in [("wqkv", [NL, CM, 3 * CM]), ("wo3", [NL, CM, CM]),
                     ("w1", [NL, CM, 2 * DI]), ("w2", [NL, DI, CM]),
                     ("awq", [CM, CM]), ("awk", [CM, CM]), ("awv", [CM, CM]), ("awp", [CM, CM]),
                     ("qs_pp", [128, NL]), ("ks_pp", [128, NL]), ("g_pp", [128, NL * NCH]),
                     ("gb_host", [2, NL * CM]), ("b1a_pp", [128, NL * NJ]), ("b1g_pp", [128, NL * NJ]),
                     ("b2_pp", [128, NL * NCH]), ("bq_pp", [128, NCH]),
                     ("bk_r", [1, CM]), ("bv_r", [1, CM]), ("bp_r", [1, CM]),
                     ("c_identf", [128, 128])]:
            setattr(self, "d_" + n, nc.dram_tensor(n, s, F32, kind="ExternalInput").ap())
        for n, s in [("c_tabs", [128, 4 * T]), ("c_identb", [128, 128]), ("c_prot", [128, 128]),
                     ("c_bselq", [128, NCH * 16]), ("c_bselk", [128, NCH * 48]),
                     ("c_selpat", [16, NCH * 128]), ("c_unit16", [1, 256]),
                     ("c_ones1", [1, 128]), ("c_onescol", [128, 1]), ("c_onesrow", [1, TC]),
                     ("c_maskb", [128, 1024]), ("c_maskb0", [128, 512])]:
            setattr(self, "d_" + n, nc.dram_tensor(n, s, BF16, kind="ExternalInput").ap())
        self.s_wqkv = nc.dram_tensor("s_wqkv", [NL, 24, NCH, 128, 128], BF16, kind="Internal").ap()
        self.s_wo = nc.dram_tensor("s_wo", [NL, NCH, NCH, 128, 128], BF16, kind="Internal").ap()
        self.s_w1 = nc.dram_tensor("s_w1", [NL, NJ, 2, NCH, 128, 128], BF16, kind="Internal").ap()
        self.s_w2 = nc.dram_tensor("s_w2", [NL, DI, CM], BF16, kind="Internal").ap()
        self.s_wT = nc.dram_tensor("s_wT", [4, CM, CM], BF16, kind="Internal").ap()
        self.s_x2 = nc.dram_tensor("s_x2", [NSEQ * T, CM], F32, kind="Internal").ap()
        if self.dbg_cut is not None:
            self.dbg = nc.dram_tensor("dbg", [128, NCH * (T + 1)], F32, kind="ExternalOutput").ap()
            self.dbg_snap = nc.dram_tensor("dbg_snap", [128, 1536], BF16, kind="ExternalOutput").ap()
            self.dbg_snap2 = nc.dram_tensor("dbg_snap2", [128, 6 * 1040], BF16, kind="ExternalOutput").ap()
            self.dbg_snap3 = nc.dram_tensor("dbg_snap3", [65, 256], F32, kind="ExternalOutput").ap()

        with self.es:
            self.em = Em()
            for s in (["d_sp", "d_gp", "c_pe", "c_dve", "c_act", "c_gp", "cbr", "cws",
                       "ccst", "cvw", "cx0", "cpk", "cwt1", "cwt2"]
                      + [f"cv{i}" for i in range(NL)] + [f"cw{i}" for i in range(4)]
                      + [f"cgb{i}" for i in range(4)] + [f"qkp{i}" for i in range(4)]
                      + [f"w1p{i}" for i in range(4)] + [f"w2p{i}" for i in range(4)]
                      + [f"wop{i}" for i in range(2)] + [f"tab{i}" for i in range(2)]):
                self.sem(s)
            self._alloc()
            self._emit()
            blk = self.es.enter_context(nc.Block())
            self.em.flush(blk)
        return self.em.nops

    def _alloc(self):
        sb = self.sb
        self.big1 = sb("big1", [128, NCH * (T + 1)], F32)
        self.big2 = sb("big2", [128, 16 * CM], F32)
        self.big3 = sb("big3", [128, 20480], BF16)
        self.tabs = sb("tabs", [128, 2 * 4 * TC], BF16)
        self.identb = sb("identb", [128, 128], BF16)
        self.identf = sb("identf", [128, 128], F32)
        self.prot = sb("prot", [128, 128], BF16)
        self.bselq = sb("bselq", [128, NCH * 16], BF16)
        self.bselk = sb("bselk", [128, NCH * 48], BF16)
        self.selpat = sb("selpat", [16, NCH * 128], BF16)
        self.unit16 = sb("unit16", [1, 256], BF16)
        self.ones1 = sb("ones1", [1, 128], BF16)
        self.onescol = sb("onescol", [128, 1], BF16)
        self.maskb = sb("maskb", [128, 1024], BF16)
        self.maskb0 = sb("maskb0", [128, 512], BF16)
        self.qs_pp = sb("qs_pp_s", [128, NL], F32)
        self.ks_pp = sb("ks_pp_s", [128, NL], F32)
        self.g_pp = sb("g_pp_s", [128, NL * NCH], F32)
        self.b1a_pp = sb("b1a_s", [128, NL * NJ], F32)
        self.b1g_pp = sb("b1g_s", [128, NL * NJ], F32)
        self.b2_pp = sb("b2_s", [128, NL * NCH], F32)
        self.bq_pp = sb("bq_s", [128, NCH], F32)
        self.gb2 = sb("gb2", [2, NCH * 128], BF16)
        self.x0t = sb("x0t", [128, CM], F32)
        self.tmu = sb("tmu", [128, CM], F32)
        self.sqtm = self.tmu
        self.ytm = self.tmu
        self.ytmb = self.tmu[:, :].bitcast(BF16)[:, 0:CM]

        self.arhs = sb("arhs", [1, TC], BF16)
        self.c2rhs = sb("c2rhs", [2, TC], BF16)
        self.col1 = sb("col1", [128, 8], F32)
        self.nstat_s = sb("nstat_s", [16, TC], F32)
        self.qss = self.nstat_s
        self.invden16 = sb("invden16", [16, TC], BF16)
        self.denrow = sb("denrow", [1, 2 * 256], BF16)
        self.bcq_sb = sb("bcq_sb", [128, 2 * TC], BF16)
        self.scrA = sb("scrA", [128, TC], BF16)
        self.scrB = sb("scrB", [128, TC], BF16)
        self.t1s = sb("t1s", [128, TC], F32)

        self.ksumc = sb("ksumc", [128, 8], F32)
        self.kwsel = sb("kwsel", [128, NCH * 16], BF16)
        self.denk = sb("denk", [128, 16], F32)
        self.bsave = sb("bsave", [128, NCH], BF16)
        self.epsc = sb("epsc", [128, 1], F32)
        self.knraw_v = self.tmu[:, :].bitcast(BF16)[:, 0:CM]
        self.ctxacc = self.tmu[:, 512:1024]
        self.mus = self.x0t[32:33, 0:TC]
        self.statv = self.x0t[0:1, 0:TC]
        self.statl = self.x0t[0:1, TC:2 * TC]
        self.qinv16 = self.big3[0:16, 19456:19968]
        self.kinv16 = self.big3[0:16, 19968:20480]
        psum = lambda n, s: self.es.enter_context(self.nc.psum_tensor(n, s, F32))
        self.PA = psum("PA", [128, 1024])
        self.PB = psum("PB", [128, 1024])
        self.PC = psum("PC", [128, 1024])
        self.PD = psum("PD", [128, 512])
        self.PE_ = psum("PE", [128, 512])
        self.em.dummy_sb = self.col1
        self.em.dummy_bb = self.onescol
        self.em.dummy_ps = self.PD

    def y3(self):
        return self.big1[:, :].rearrange("p (c n) -> p c n", n=T + 1)

    def xt3(self):
        return self.big2[:, :].rearrange("p (c n) -> p c n", n=CM)

    def b2bf(self, lo, hi):
        return self.big2[:, :].bitcast(BF16)[:, lo:hi]

    def b1bf(self, lo, hi):
        return self.big1[:, :].bitcast(BF16)[:, lo:hi]

    def _emit(self):
        em = self.em
        if self.dbg_cut is not None:
            em.op("dve", lambda ep: ep.memset(self.big2[:, :], 0.125))
            dm = em.op("dve", lambda ep: ep.memset(self.big3[:, :], 0.125), inc="c_dve")
            em.op("dve", lambda ep: ep.memset(self.denrow[:, :], 0.125))
            self.rreg("big2", dm)
            self.rreg("big3", dm)
        self._const_loads()
        self._conversion()
        for p in [self.PA, self.PB, self.PC]:
            em.op("dve", lambda ep, p=p: ep.memset(p[:, :], 0.0))
        em.op("dve", lambda ep: ep.memset(self.PD[:, :], 0.0))
        em.op("dve", lambda ep: ep.memset(self.PE_[:, :], 0.0))
        z = em.op("dve", lambda ep: ep.memset(self.epsc[:, :], 1e-5), inc="c_dve")
        self.eps_tok = z
        for k in ["PA0", "PA1", "PB0", "PB1", "PC0", "PC1", "PD", "PE"]:
            self.rreg(k, z)
        for s in range(NSEQ):
            self.seq = s
            self._seq(s)
            if self.dbg_done:
                break
        em.waits("sp", [(n, em.cnt[n]) for n in em.cnt if em.cnt[n] > 0 and not n.startswith("c_")])

    def _const_loads(self):
        em = self.em
        for name in ["c_identb identb", "c_identf identf", "c_prot prot",
                     "c_bselq bselq", "c_bselk bselk", "c_selpat selpat", "c_unit16 unit16",
                     "c_ones1 ones1", "c_onescol onescol", "c_maskb maskb", "c_maskb0 maskb0",
                     "qs_pp qs_pp", "ks_pp ks_pp", "g_pp g_pp", "b1a_pp b1a_pp",
                     "b1g_pp b1g_pp", "b2_pp b2_pp", "bq_pp bq_pp"]:
            dn, sn = name.split()
            em.op("sp", lambda ep, d=getattr(self, sn), s=getattr(self, "d_" + dn):
                  ep.dma_start(d[:, :], s), inc="ccst", amt=16)
        em.op("sp", lambda ep: ep.dma_start(self.c2rhs[1:2, :], self.d_c_onesrow), inc="ccst", amt=16)
        self.const_tok = ("ccst", em.cnt["ccst"])
        self.const_gp = ("d_gp", em.cnt["d_gp"])

    def _conversion(self):
        em = self.em
        self.conv_tok = {}
        for l in range(NL):
            cs = f"cv{l}"
            src = self.d_wqkv[l].rearrange("(s p) (o c) -> o s p c", p=128, c=128)
            for oc in range(24):
                em.op("pool", lambda ep, s=src[oc], d=self.s_wqkv[l, oc]:
                      ep.dma_start(d, s), inc=cs, amt=16)
            srco = self.d_wo3[l].rearrange("(s p) (o c) -> o s p c", p=128, c=128)
            for oc in range(NCH):
                em.op("pool", lambda ep, s=srco[oc], d=self.s_wo[l, oc]:
                      ep.dma_start(d, s), inc=cs, amt=16)
            for ag in range(2):
                src1 = self.d_w1[l][:, ag * DI:(ag + 1) * DI]
                for j in range(NJ):
                    w = 128 if j < NJ - 1 else WJL
                    sj = src1[:, j * 128:j * 128 + w].rearrange("(s p) c -> s p c", p=128)
                    em.op("pool", lambda ep, s=sj, d=self.s_w1[l, j, ag][:, :, 0:w]:
                          ep.dma_start(d, s), inc=cs, amt=16)
            em.op("pool", lambda ep, s=self.d_w2[l], d=self.s_w2[l]: ep.dma_start(d, s),
                  inc=cs, amt=16)
            self.conv_tok[l] = (cs, em.cnt[cs])
        em.waits("sp", [self.const_tok])
        em.waits("pool", tmax(self.rgate("big2"), self.rgate("big3")))
        wstage = self.b2bf(0, 8 * CM)
        wtst = self.big3[:, 0:NCH * 128]
        trpb = self.PC[:, :].bitcast(BF16)
        gate = None
        ev = None
        for widx, name in enumerate(["awq", "awk", "awv", "awp"]):
            src = reap(getattr(self, "d_" + name), [(CM, 128), (128 * CM, NCH), (1, CM)])
            em.waits("pool", tmax(gate))
            ld = em.op("pool", lambda ep, s=src, d=wstage: ep.dma_start(d, s), inc=f"cw{widx}", amt=16)
            for ic in range(NCH):
                em.waits("pe", tmax(ld, ev, self.const_tok))
                tk = None
                for oc in range(NCH):
                    tk = em.op("pe", lambda ep, oc=oc, ic=ic:
                               ep.transpose(trpb[:, oc * 128:oc * 128 + 128],
                                            wstage[:, oc * CM + ic * 128: oc * CM + ic * 128 + 128],
                                            self.identb[:, :]),
                               inc="c_pe" if oc == NCH - 1 else None)
                em.waits("dve", tmax(tk, gate))
                ev = em.op("dve", lambda ep: ep.tensor_copy(wtst[:, :], trpb[:, 0:1024]), inc="c_dve")
                em.waits("sp", [ev])
                gate = em.op("sp", lambda ep, d=self.s_wT[widx, ic * 128:ic * 128 + 128, :]:
                             ep.dma_start(d, wtst[:, :]), inc="cws", amt=16)
        self.wT_tok = ("cws", em.cnt["cws"])
        self.rreg("big2", self.wT_tok)
        self.rreg("big3", self.wT_tok)
        self.rreg("PC0", ev)
        self.rreg("PC1", ev)

    def _seq(self, s):
        self._zero_guards()
        self._ln_stream(self.x[s * T:(s + 1) * T, :], None)
        if self._dbg(0):
            return
        for l in range(NL):
            self._attn(l)
            if self._dbg(1 + 2 * l):
                return
            self._ffn(l)
            if self._dbg(2 + 2 * l):
                return
        self._assemble_x(self.x[s * T:(s + 1) * T, :], True)
        self._outer()
        if self._dbg(100):
            return
        self._park_x2()
        self._zero_guards()
        self._ln_stream(None, self.xt3())
        if self._dbg(101):
            return
        for l in range(NL):
            self._attn(l)
            if self._dbg(102 + 2 * l):
                return
            self._ffn(l)
            if self._dbg(103 + 2 * l):
                return
        self._assemble_x(self.s_x2[s * T:(s + 1) * T, :], False)

    def _dbg(self, sid):
        if self.dbg_cut is None:
            return False
        if self.seq == self.dbg_cut[0] and sid == self.dbg_cut[1]:
            em = self.em
            if sid == 100:
                em.waits("sp", tmax(self.xt_tok))
                em.op("sp", lambda ep: ep.dma_start(self.dbg[:, 0:16 * CM], self.big2[:, :]),
                      inc="d_sp", amt=16)
            else:
                em.waits("sp", tmax(self.y_tok))
                em.op("sp", lambda ep: ep.dma_start(self.dbg, self.big1[:, :]), inc="d_sp", amt=16)
            self.dbg_done = True
            return True
        return False

    def _zero_guards(self):
        em = self.em
        y3 = self.y3()
        em.waits("dve", tmax(self.y_tok, self.rgate("big1")))
        self.y_tok = [em.op("dve", lambda ep: ep.memset(y3[:, :, 0:1], 0.0), inc="c_dve")]

    def _ln_stream(self, hbm_src, xt3):
        em = self.em
        y3 = self.y3()
        prev = None
        for tt in range(NTT):
            if hbm_src is not None:
                em.waits("sp", self.rgate("x0t"))
                ld = em.op("sp", lambda ep, s=hbm_src[tt * 128:(tt + 1) * 128, :]:
                           ep.dma_start(self.x0t[:, :], s), inc="cx0", amt=16)
                xin = self.x0t[:, :]
                gate = [ld]
            else:
                xin = xt3[:, tt, :]
                gate = tmax(self.xt_tok)
            rstd, nm = self._tm_stats(xin, gate)
            t7 = em.op("dve", lambda ep, xin=xin, rstd=rstd, nm=nm: ep.tensor_scalar(
                self.ytm[:, :], xin, rstd, nm, ALU.mult, ALU.add), inc="c_dve")
            self.rreg("col1", t7)
            if hbm_src is not None:
                self.rreg("x0t", t7)
            em.waits("pe", tmax(t7, self.rgate("PC0", "PC1")))
            tk = None
            for ch in range(NCH):
                tk = em.op("pe", lambda ep, ch=ch: ep.transpose(
                    self.PC[:, ch * 128:(ch + 1) * 128],
                    self.ytm[:, ch * 128:(ch + 1) * 128], self.identf[:, :]),
                    inc="c_pe" if ch == NCH - 1 else None)
            self.rreg("ytm", tk)
            self.rreg("big1", tk)
            em.waits("dve", [tk])
            dst = y3[:, :, 1 + tt * 128: 1 + (tt + 1) * 128]
            prev = em.op("dve", lambda ep, dst=dst: ep.tensor_copy(
                dst, self.PC[:, 0:1024].rearrange("p (c n) -> p c n", n=128)), inc="c_dve")
            self.rreg("PC0", prev)
            self.rreg("PC1", prev)
        self.y_tok = tmax(self.y_tok, prev)

    def _tm_stats(self, xin, gate):
        em = self.em
        c = self.col1
        yg = self.rgate("ytm")
        em.waits("dve", tmax(gate, yg, self.rgate("col1", "sqtm")))
        em.waits("act", tmax(gate, yg))
        rs = em.op("dve", lambda ep: ep.reduce_sum(c[:, 0:1], xin, axis=AX.X))
        t_sq = em.op("act", lambda ep: ep.activation(self.sqtm[:, :], xin, AF.Square), inc="c_act")
        em.waits("dve", [t_sq])
        rr = em.op("dve", lambda ep: ep.reduce_sum(c[:, 1:2], self.sqtm[:, :], axis=AX.X), inc="c_dve")
        self.rreg("sqtm", rr)
        em.op("dve", lambda ep: ep.tensor_scalar_mul(c[:, 2:3], c[:, 0:1], 1.0 / CM))
        em.op("dve", lambda ep: ep.tensor_tensor(c[:, 3:4], c[:, 2:3], c[:, 2:3], ALU.mult))
        t3 = em.op("dve", lambda ep: ep.scalar_tensor_tensor(c[:, 4:5], c[:, 1:2], 1.0 / CM, c[:, 3:4],
                                                             ALU.mult, ALU.subtract), inc="c_dve")
        em.waits("act", [t3])
        em.waits("act", [self.eps_tok])
        em.op("act", lambda ep: ep.activation(c[:, 5:6], c[:, 4:5], AF.Ln, bias=self.epsc[:, 0:1]))
        t5 = em.op("act", lambda ep: ep.activation(c[:, 6:7], c[:, 5:6], AF.Exp, scale=-0.5), inc="c_act")
        em.waits("dve", [t5])
        em.op("dve", lambda ep: ep.tensor_scalar(c[:, 7:8], c[:, 2:3], c[:, 6:7], -1.0,
                                                 ALU.mult, ALU.mult))
        return c[:, 6:7], c[:, 7:8]

    def _assemble_x(self, base_hbm, into_xt):
        em = self.em
        y3 = self.y3()
        s = self.seq
        xt3 = self.xt3()
        self.xt_tok = []
        if not into_xt:
            em.waits("sp", [("cpk", em.cnt["cpk"])])
        for tt in range(NTT):
            em.waits("sp", self.rgate("x0t"))
            ld = em.op("sp", lambda ep, src=base_hbm[tt * 128:(tt + 1) * 128, :]:
                       ep.dma_start(self.x0t[:, :], src), inc="cx0", amt=16)
            em.waits("pe", tmax(self.y_tok, self.rgate("PC0", "PC1")))
            tk = None
            for ch in range(NCH):
                src = y3[:, ch, 1 + tt * 128:1 + (tt + 1) * 128]
                tk = em.op("pe", lambda ep, src=src, ch=ch: ep.transpose(
                    self.PC[:, ch * 128:(ch + 1) * 128], src, self.identf[:, :]),
                    inc="c_pe" if ch == NCH - 1 else None)
            self.rreg("big1", tk)
            em.waits("dve", [tk, ld])
            if into_xt:
                dst = xt3[:, tt, :].rearrange("p (c n) -> p c n", n=128)
                em.waits("dve", self.rgate(("xt", tt)))
            else:
                dst = self.x0t[:, :].rearrange("p (c n) -> p c n", n=128)
            tadd = em.op("dve", lambda ep, dst=dst: ep.scalar_tensor_tensor(
                dst, self.PC[:, 0:1024].rearrange("p (c n) -> p c n", n=128), 0.5,
                self.x0t[:, :].rearrange("p (c n) -> p c n", n=128), ALU.mult, ALU.add), inc="c_dve")
            self.rreg("PC0", tadd)
            self.rreg("PC1", tadd)
            self.rreg("x0t", tadd)
            if into_xt:
                self.xt_tok.append(tadd)
            else:
                em.waits("sp", [tadd])
                st = em.op("sp", lambda ep, d=self.out[s * T + tt * 128:s * T + (tt + 1) * 128, :]:
                           ep.dma_start(d, self.x0t[:, :]), inc="d_sp", amt=16)
                self.rreg("x0t", st)

    def _park_x2(self):
        em = self.em
        s = self.seq
        xt3 = self.xt3()
        em.waits("sp", tmax(self.xt_tok))
        self.park_tok = []
        for tt in range(NTT):
            src = xt3[:, tt, :]
            st = em.op("sp", lambda ep, src=src, d=self.s_x2[s * T + tt * 128:s * T + (tt + 1) * 128, :]:
                       ep.dma_start(d, src), inc="cpk", amt=16)
            self.park_tok.append(st)
            self.rreg(("xt", tt), st)
            self.rreg("big2", st)

    def _attn(self, l):
        em = self.em
        y3 = self.y3()
        zb3 = self.b2bf(0, NCH * 513).rearrange("p (c n) -> p c n", n=513)
        o = NCH * 513
        k2r = self.b2bf(o, o + NCH * 1024); o += NCH * 1024
        q2r = self.b2bf(o, o + NCH * 512); o += NCH * 512
        qraw = self.b2bf(o, o + NCH * 512); o += NCH * 512
        kraw = self.b2bf(o, o + NCH * 512); o += NCH * 512
        vtm = self.b2bf(o, o + 6 * 1040); o += 6 * 1040
        qp = self.b2bf(o, o + 512); o += 512
        kp = self.b2bf(o, o + 512); o += 512
        sqs = self.b2bf(o, o + 512); o += 512
        assert o <= 32768, o
        v65 = vtm.rearrange("p (sh n) -> p sh n", n=65)
        qkvp = self.big3[:, 0:4 * NCH * 128]
        wop = self.big3[:, 4 * NCH * 128:6 * NCH * 128]
        vw = self.big3[:, 6 * NCH * 128:6 * NCH * 128 + NCH * CM]
        PT = self.big3[:, 14 * 1024:15 * 1024]
        attnU = self.big3[:, 15 * 1024:15 * 1024 + NCH * 512]
        assert 15 * 1024 + NCH * 512 <= 20480

        big2_gate = self.rgate("big2")
        big3_gate = self.rgate("big3")
        em.waits("sp", tmax(self.conv_tok[l], big3_gate, self.rgate("vw")))
        for vcc in range(NCH):
            vsrc = reap(self.s_wqkv[l, 16 + vcc], [(128, 128), (16384, NCH), (1, 128)])
            vwdst = AP(vw.tensor, vw.offset + vcc * 128, [[vw.ap[0][0], 128], [CM, NCH], [1, 128]])
            vw_tok = em.op("sp", lambda ep, vsrc=vsrc, vwdst=vwdst: ep.dma_start(vwdst, vsrc),
                           inc="cvw", amt=16)
        em.waits("dve", tmax(big2_gate, self.rgate("vtm")))
        ones_tok = em.op("dve", lambda ep: ep.memset(v65[:, :, 64:65], 1.0), inc="c_dve")

        y_upd = []
        pidx = 0
        for tc in range(NTC):
            t0 = tc * TC
            em.waits("dve", tmax(self.y_tok, self.rgate("zb"), big2_gate))
            big2_gate = None
            if tc == 0:
                zb_tok = em.op("dve", lambda ep: ep.tensor_copy(
                    zb3[:, :, 0:513], y3[:, :, 0:513]), inc="c_dve")
            else:
                em.op("dve", lambda ep, t0=t0: ep.tensor_copy(
                    zb3[:, :, 1:513], y3[:, :, t0 + 1:t0 + 513]))
                zb_tok = em.op("dve", lambda ep: ep.tensor_copy(
                    zb3[:, :, 0:1], self.bsave[:, :].rearrange("p (c n) -> p c n", n=1)), inc="c_dve")
            if tc < NTC - 1:
                bs = em.op("dve", lambda ep, t0=t0: ep.tensor_copy(
                    self.bsave[:, :].rearrange("p (c n) -> p c n", n=1),
                    y3[:, :, t0 + TC:t0 + TC + 1]), inc="c_dve")

            em.waits("sp", self.rgate(("tabr", tc % 2)))
            tsrc = self.d_c_tabs.rearrange("p (k t) -> p k t", t=T)[:, :, t0:t0 + TC]
            tslot = (tc % 2) * 4 * TC
            tab_tok = em.op("sp", lambda ep, tsrc=tsrc, tslot=tslot: ep.dma_start(
                self.tabs[:, tslot:tslot + 4 * TC].rearrange("p (k t) -> p k t", t=TC), tsrc),
                inc=f"tab{tc % 2}", amt=16)

            def zsl(c):
                off = 1 if c < 4 else 0
                return zb3[:, c, off:off + TC]

            def ztile(c, ti):
                off = 1 if c < 4 else 0
                return zb3[:, c, off + ti * 128:off + ti * 128 + 128]

            qk_done = {}
            sq_by = {}
            sqslots = [sqs, qp]
            for oc in range(17):
                if oc < 16:
                    slot = pidx % 4
                    pidx += 1
                    em.waits("sp", self.rgate(("pc", slot)))
                    src = reap(self.s_wqkv[l, oc], [(128, 128), (128 * 128, NCH), (1, 128)])
                    dst = qkvp[:, slot * NCH * 128:(slot + 1) * NCH * 128]
                    ptok = em.op("sp", lambda ep, s=src, d=dst: ep.dma_start(d, s),
                                 inc=f"qkp{slot}", amt=16)
                    reg = oc % 2
                    pk = "PA" + str(reg)
                    em.waits("pe", tmax(ptok, zb_tok, self.rgate(pk)))
                    mm = None
                    for sc in range(NCH):
                        mm = em.op("pe", lambda ep, sc=sc, dst=dst, reg=reg: ep.matmul(
                            self.PA[:, reg * 512:reg * 512 + 512], dst[:, sc * 128:(sc + 1) * 128],
                            zsl(sc), start=(sc == 0), stop=(sc == NCH - 1), skip_group_check=True),
                            inc="c_pe" if sc == NCH - 1 else None)
                    self.rreg(("pc", slot), mm)
                    self.rreg("zb", mm)
                    raw = qraw if oc < 8 else kraw
                    c = oc % 8
                    em.waits("dve", tmax(mm, self.rgate(("raw", oc < 8, c))))
                    ev = em.op("dve", lambda ep, raw=raw, c=c, reg=reg: ep.tensor_copy(
                        raw[:, c * 512:(c + 1) * 512], self.PA[:, reg * 512:reg * 512 + 512]), inc="c_dve")
                    self.rreg(pk, ev)
                    sslot = oc % 2
                    em.waits("act", tmax(ev, self.rgate(("sqs", sslot))))
                    sq = em.op("act", lambda ep, raw=raw, c=c, sslot=sslot: ep.activation(
                        sqslots[sslot][:, 0:512], raw[:, c * 512:(c + 1) * 512],
                        AF.Square), inc="c_act")
                    self.rreg(("raw", oc < 8, c), sq)
                    sq_by[oc] = (sq, sslot)
                po = oc - 1
                if po >= 0:
                    sqtok, sslot = sq_by.pop(po)
                    if po == 0:
                        em.waits("pe", tmax(sqtok, self.rgate("PD")))
                    elif po == 8:
                        em.waits("pe", tmax(sqtok, self.rgate("PE")))
                    else:
                        em.waits("pe", [sqtok])
                    pdst = self.PD if po < 8 else self.PE_
                    g = em.op("pe", lambda ep, po=po, pdst=pdst, sslot=sslot: ep.matmul(
                        pdst[0:16, :], self.bselq[:, (po % 8) * 16:(po % 8 + 1) * 16],
                        sqslots[sslot][:, 0:512],
                        start=(po % 8 == 0), stop=(po % 8 == 7), skip_group_check=True),
                        inc="c_pe")
                    self.rreg(("sqs", sslot), g)
                    if po == 7:
                        qk_done["q"] = g
                    if po == 15:
                        qk_done["k"] = g
            em.waits("dve", tmax(qk_done["q"], qk_done["k"], self.rgate("nss")))
            c1 = em.op("dve", lambda ep: ep.tensor_scalar_max(self.nstat_s[:, :], self.PD[0:16, :],
                                                              1e-24), inc="c_dve")
            em.waits("act", [c1])
            em.op("act", lambda ep: ep.activation(self.nstat_s[:, :], self.nstat_s[:, :], AF.Ln))
            c4 = em.op("act", lambda ep: ep.activation(self.qinv16, self.nstat_s[:, :], AF.Exp,
                                                       scale=-0.5), inc="c_act")
            em.waits("dve", [c4])
            c2 = em.op("dve", lambda ep: ep.tensor_scalar_max(self.nstat_s[:, :], self.PE_[0:16, :],
                                                              1e-24), inc="c_dve")
            self.rreg("PD", c1)
            self.rreg("PE", c2)
            em.waits("act", [c2])
            em.op("act", lambda ep: ep.activation(self.nstat_s[:, :], self.nstat_s[:, :], AF.Ln))
            c6 = em.op("act", lambda ep: ep.activation(self.kinv16, self.nstat_s[:, :], AF.Exp,
                                                       scale=-0.5), inc="c_act")
            self.rreg("nss", c6)
            em.waits("dve", self.rgate(("sqs", 0), ("sqs", 1)))
            rot_pend = None

            def emit_rot(c, qk, st):
                pbuf = qp if qk == 0 else kp
                tab_c = tslot + (0 if qk == 0 else 2 * TC)
                tab_s = tslot + (TC if qk == 0 else 3 * TC)
                pbb = qk * 512
                em.waits("pe", tmax(st, self.rgate("PB" + str(qk))))
                pm = em.op("pe", lambda ep, pbuf=pbuf, pbb=pbb: ep.matmul(
                    self.PB[:, pbb:pbb + 512], self.prot[:, :], pbuf[:, 0:512],
                    start=True, stop=True), inc="c_pe")
                self.rreg(("pp", qk), pm)
                if qk == 0:
                    dsl = q2r[:, c * 512:(c + 1) * 512]
                    wkey = ("q2", c)
                    scr, skey = self.scrA, "scrA"
                else:
                    dsl = k2r[:, c * 1024 + (tc % 2) * 512: c * 1024 + (tc % 2) * 512 + 512]
                    wkey = ("k2", c, tc % 2)
                    scr, skey = self.scrB, "scrB"
                em.waits("pool", tmax(st, tab_tok, self.rgate(wkey)))
                tt1 = em.op("pool", lambda ep, dsl=dsl, pbuf=pbuf, tab_c=tab_c:
                      ep.tensor_tensor(dsl, pbuf[:, 0:512],
                                       self.tabs[:, tab_c:tab_c + TC], ALU.mult))
                em.waits("dve", tmax(pm, tab_tok, self.rgate(skey)))
                stt = em.op("dve", lambda ep, tab_s=tab_s, scr=scr, pbb=pbb: ep.scalar_tensor_tensor(
                    scr[:, :], self.PB[:, pbb:pbb + 512], 1.0,
                    self.tabs[:, tab_s:tab_s + TC], ALU.mult, ALU.mult))
                em.waits("pool", [stt])
                a3 = em.op("pool", lambda ep, dsl=dsl, scr=scr: ep.tensor_tensor(
                    dsl, dsl, scr[:, :], ALU.add), inc="c_gp")
                self.rreg(("tabr", tc % 2), a3)
                self.rreg(("pp", qk), a3)
                self.rreg("PB" + str(qk), stt)
                self.rreg(skey, a3)
                if qk == 0:
                    self.rreg(("sqs", 1), tt1)
                    self.rreg(("sqs", 1), pm)
                self.wr[wkey] = a3

            for c in range(NCH):
                for qk in range(2):
                    raw = qraw if qk == 0 else kraw
                    inv16 = self.qinv16 if qk == 0 else self.kinv16
                    gain = self.qs_pp if qk == 0 else self.ks_pp
                    ntok = c4 if qk == 0 else c6
                    pbuf = qp if qk == 0 else kp
                    bcdst = self.PE_ if qk == 0 else self.PD
                    bkey = "PE" if qk == 0 else "PD"
                    em.waits("pe", tmax(ntok, self.rgate(bkey)))
                    bc = em.op("pe", lambda ep, c=c, inv16=inv16, bcdst=bcdst: ep.matmul(
                        bcdst[:, :], self.selpat[:, c * 128:(c + 1) * 128], inv16,
                        start=True, stop=True), inc="c_pe")
                    self.rreg("nss", bc)
                    em.waits("dve", tmax(bc, self.rgate(("pp", qk)),
                                         self.rgate(("raw", qk == 0, c))))
                    st = em.op("dve", lambda ep, c=c, raw=raw, gain=gain, pbuf=pbuf, bcdst=bcdst:
                               ep.scalar_tensor_tensor(pbuf[:, 0:512],
                                                       raw[:, c * 512:(c + 1) * 512],
                                                       gain[:, l:l + 1],
                                                       bcdst[:, :],
                                                       ALU.mult, ALU.mult), inc="c_dve")
                    self.rreg(bkey, st)
                    if rot_pend is not None:
                        emit_rot(*rot_pend)
                    rot_pend = (c, qk, st)
            emit_rot(*rot_pend)
            rot_pend = None
            for ti in range(4):
                tile = tc * 4 + ti
                slot = tile % 6
                em.waits("pe", tmax(vw_tok, zb_tok, ones_tok, self.rgate("PB0", "PB1")))
                mm = None
                for sc in range(NCH):
                    for g2 in range(2):
                        mm = em.op("pe", lambda ep, sc=sc, g2=g2, ti=ti: ep.matmul(
                            self.PB[:, g2 * 512:(g2 + 1) * 512], ztile(sc, ti),
                            vw[:, sc * CM + g2 * 512: sc * CM + (g2 + 1) * 512],
                            start=(sc == 0), stop=(sc == NCH - 1), skip_group_check=True),
                            inc="c_pe" if (sc == NCH - 1 and g2 == 1) else None)
                self.rreg("zb", mm)
                self.rreg("vw", mm)
                em.waits("dve", tmax(mm, self.rgate(("vt", slot))))
                dstv = v65[:, slot * 16:(slot + 1) * 16, 0:64]
                ev = em.op("dve", lambda ep, dstv=dstv: ep.tensor_copy(
                    dstv, self.PB[:, 0:1024].rearrange("p (h n) -> p h n", n=64)), inc="c_dve")
                self.rreg("PB0", ev)
                self.rreg("PB1", ev)
                self.wr[("vt", slot)] = ev
            em.waits("pe", self.rgate("PB0", "PB1"))
            PT1 = self.tmu[:, :].bitcast(BF16)[:, 0:1024]
            ptbufs = [PT, PT1]
            if tc == 0:
                self.tmu_gate = tmax(self.rgate("sqtm"), self.rgate("ytm"))
            its = []
            for wloc in range(DBG_NW):
                w = 2 * tc + wloc
                w0 = (w == 0)
                jts = [2 * w - 2, 2 * w - 1, 2 * w, 2 * w + 1]
                live = [(idx, jt) for idx, jt in enumerate(jts) if not (w0 and idx < 2)]
                for h in range(DBG_NH):
                    its.append((wloc, w0, live, h))
            st_mk = {}
            st_dr = {}
            den_done = None

            def emit_sim(i):
                wloc, w0, live, h = its[i]
                par = i % 2
                ptb = ptbufs[par]
                ch, hf = h // 2, h % 2
                simp = self.PC if par == 0 else self.PA
                sk0, sk1 = ("PC0", "PC1") if par == 0 else ("PA0", "PA1")
                em.waits("pe", tmax(self.wr.get(("q2", ch)), self.const_tok,
                                    *(self.wr.get(("k2", ch, (jt // 4) % 2)) for _, jt in live)))
                em.waits("pe", self.rgate(sk0, sk1))
                qrhs = q2r[hf * 64:hf * 64 + 64, ch * 512 + wloc * 256: ch * 512 + (wloc + 1) * 256]
                if w0:
                    em.op("pe", lambda ep, simp=simp: ep.matmul(
                        simp[:, 512:1024], self.identb[:, :], self.maskb0[:, :],
                        start=True, stop=False, skip_group_check=True))
                else:
                    for half in range(2):
                        em.op("pe", lambda ep, simp=simp, half=half: ep.matmul(
                            simp[:, half * 512:half * 512 + 512], self.identb[:, :],
                            self.maskb[:, half * 512:half * 512 + 512],
                            start=True, stop=False, skip_group_check=True))
                mmt = None
                for idx, jt in live:
                    ksl = k2r[hf * 64:hf * 64 + 64,
                              ch * 1024 + ((jt // 4) % 2) * 512 + (jt % 4) * 128:
                              ch * 1024 + ((jt // 4) % 2) * 512 + (jt % 4) * 128 + 128]
                    if idx == 0:
                        i0, i1 = 0, 128
                    elif idx == 3:
                        i0, i1 = 128, 256
                    else:
                        i0, i1 = 0, 256
                    mmt = em.op("pe", lambda ep, ksl=ksl, qrhs=qrhs, idx=idx, i0=i0, i1=i1,
                                simp=simp: ep.matmul(
                                    simp[:, idx * 256 + i0: idx * 256 + i1],
                                    ksl, qrhs[:, i0:i1], start=False, stop=True,
                                    skip_group_check=True),
                                inc="c_pe" if (idx, jt) == live[-1] else None)
                    self.rreg(("q2", ch), mmt)
                    self.rreg(("k2", ch, (jt // 4) % 2), mmt)
                em.waits("act", tmax(mmt, self.rgate(("PTb", par)), self.tmu_gate))
                self.tmu_gate = None
                if w0:
                    exa = None
                    exb = em.op("act", lambda ep, simp=simp, ptb=ptb: ep.activation(
                        ptb[:, 512:1024], simp[:, 512:1024], AF.Exp), inc="c_act")
                else:
                    exa = em.op("act", lambda ep, simp=simp, ptb=ptb: ep.activation(
                        ptb[:, 0:512], simp[:, 0:512], AF.Exp), inc="c_act")
                    exb = em.op("act", lambda ep, simp=simp, ptb=ptb: ep.activation(
                        ptb[:, 512:1024], simp[:, 512:1024], AF.Exp), inc="c_act")
                self.rreg(sk0, exb)
                self.rreg(sk1, exb)
                st_mk[i] = (exa, exb)

            def emit_pv(i):
                wloc, w0, live, h = its[i]
                par = i % 2
                ptb = ptbufs[par]
                ch, hf = h // 2, h % 2
                exa, exb = st_mk.pop(i)
                em.waits("pe", tmax(exa if exa is not None else exb,
                                    self.rgate(("pv", par)),
                                    *(self.wr.get(("vt", jt % 6)) for _, jt in live)))
                pvr = par * 256
                mmv = None
                for n, (idx, jt) in enumerate(live):
                    if idx >= 2:
                        em.waits("pe", [exb])
                    mmv = em.op("pe", lambda ep, jt=jt, idx=idx, pvr=pvr, h=h, ptb=ptb,
                                st=(n == 0), sp=(n == len(live) - 1): ep.matmul(
                        self.PB[0:65, pvr:pvr + 256], v65[:, (jt % 6) * 16 + h, :],
                        ptb[:, idx * 256:(idx + 1) * 256],
                        start=st, stop=sp, skip_group_check=True),
                        inc="c_pe" if n == len(live) - 1 else None)
                    self.rreg(("vt", jt % 6), mmv)
                self.rreg(("PTb", par), mmv)
                if par == 1:
                    self.rreg("ytm", mmv)
                em.waits("dve", tmax(mmv, self.rgate(("aU", ch, hf, wloc))))
                du = em.op("dve", lambda ep, hf=hf, ch=ch, wloc=wloc, pvr=pvr: ep.tensor_copy(
                    attnU[hf * 64:hf * 64 + 64,
                          ch * 512 + wloc * 256: ch * 512 + (wloc + 1) * 256],
                    self.PB[0:64, pvr:pvr + 256]), inc="c_dve")
                em.waits("dve", self.rgate(("dr", par)))
                dr = em.op("dve", lambda ep, pvr=pvr, drslot=par: ep.tensor_copy(
                    self.denrow[0:1, drslot * 256:(drslot + 1) * 256],
                    self.PB[64:65, pvr:pvr + 256]), inc="c_dve")
                self.rreg(("pv", par), du)
                self.rreg(("pv", par), dr)
                self.wr[("aU", ch, hf, wloc)] = du
                st_dr[i] = dr

            def emit_den(i):
                wloc, w0, live, h = its[i]
                par = i % 2
                dr = st_dr.pop(i)
                em.waits("pe", [dr])
                dn = em.op("pe", lambda ep, h=h, wloc=wloc, drslot=par: ep.matmul(
                    self.PB[0:16, 512 + wloc * 256:512 + (wloc + 1) * 256],
                    self.unit16[:, h * 16:(h + 1) * 16],
                    self.denrow[0:1, drslot * 256:(drslot + 1) * 256],
                    start=(h == 0), stop=(i == len(its) - 1), skip_group_check=True),
                    inc="c_pe")
                self.rreg(("dr", par), dn)
                return dn

            for i in range(len(its) + 1):
                if i < len(its):
                    emit_sim(i)
                if i >= 1:
                    den_done = emit_den(i - 1)
                if i < len(its):
                    emit_pv(i)
            em.waits("act", [den_done])
            em.op("act", lambda ep: ep.activation(self.nstat_s[:, :], self.PB[0:16, 512:1024], AF.Ln))
            iv = em.op("act", lambda ep: ep.activation(self.invden16[:, :], self.nstat_s[:, :],
                                                       AF.Exp, scale=-1.0), inc="c_act")
            self.rreg("PB1", iv)
            last_nm = None
            for c in range(NCH):
                bdst = self.PE_ if c % 2 == 0 else self.PD
                bkey = "PE" if c % 2 == 0 else "PD"
                em.waits("pe", tmax(iv, self.rgate(bkey)))
                bc = em.op("pe", lambda ep, c=c, bdst=bdst: ep.matmul(
                    bdst[:, :], self.selpat[:, c * 128:(c + 1) * 128], self.invden16[:, :],
                    start=True, stop=True), inc="c_pe")
                em.waits("dve", tmax(bc, self.wr.get(("aU", c, 0, 0)), self.wr.get(("aU", c, 0, 1)),
                                     self.wr.get(("aU", c, 1, 0)), self.wr.get(("aU", c, 1, 1))))
                nm = em.op("dve", lambda ep, c=c, bdst=bdst: ep.tensor_tensor(
                    attnU[:, c * 512:(c + 1) * 512], attnU[:, c * 512:(c + 1) * 512],
                    bdst[:, :], ALU.mult), inc="c_dve")
                self.rreg(bkey, nm)
                self.wr[("aN", c)] = nm
                last_nm = nm
            for oc in range(NCH):
                wslot = oc % 2
                em.waits("sp", tmax(self.conv_tok[l], self.rgate(("wo", wslot))))
                wsrc = reap(self.s_wo[l, oc], [(128, 128), (128 * 128, NCH), (1, 128)])
                wdst = wop[:, wslot * NCH * 128:(wslot + 1) * NCH * 128]
                wtok = em.op("sp", lambda ep, s=wsrc, d=wdst: ep.dma_start(d, s),
                             inc=f"wop{wslot}", amt=16)
                reg = oc % 2
                pk = "PA" + str(reg)
                em.waits("pe", tmax(wtok, self.rgate(pk),
                                    *(self.wr.get(("aN", sc)) for sc in range(NCH))))
                mm = None
                for sc in range(NCH):
                    mm = em.op("pe", lambda ep, sc=sc, wdst=wdst, reg=reg: ep.matmul(
                        self.PA[:, reg * 512:reg * 512 + 512], wdst[:, sc * 128:(sc + 1) * 128],
                        attnU[:, sc * 512:(sc + 1) * 512],
                        start=(sc == 0), stop=(sc == NCH - 1), skip_group_check=True),
                        inc="c_pe" if sc == NCH - 1 else None)
                self.rreg(("wo", wslot), mm)
                for sc in range(NCH):
                    self.rreg(("aU", sc, 0, 0), mm)
                em.waits("dve", [mm])
                dsty = y3[:, oc, 1 + t0:1 + t0 + TC]
                up = em.op("dve", lambda ep, dsty=dsty, reg=reg: ep.tensor_tensor(
                    dsty, dsty, self.PA[:, reg * 512:reg * 512 + 512], ALU.add), inc="c_dve")
                self.rreg(pk, up)
                y_upd.append(up)
        self.y_tok = tmax(y_upd)
        self.rreg("big2", tmax(y_upd))
        self.rreg("big3", tmax(y_upd))
        for c in range(NCH):
            for hf in range(2):
                for wl in range(2):
                    self.rd.pop(("aU", c, hf, wl), None)
            self.rd.pop(("q2", c), None)
            for sl in range(2):
                self.rd.pop(("k2", c, sl), None)
        for s2 in range(6):
            for t in self.rd.pop(("vt", s2), []):
                self.rreg("vtm", t)
        self.rreg("vtm", tmax(y_upd))

    def _ffn(self, l):
        em = self.em
        y3 = self.y3()
        zb3 = self.b2bf(0, NCH * 513).rearrange("p (c n) -> p c n", n=513)
        o = NCH * 513
        xn = self.b2bf(o, o + NCH * 512); o += NCH * 512
        hsb = self.b2bf(o, o + NJ * 512); o += NJ * 512
        gel = self.b2bf(o, o + 2 * 512); o += 2 * 512
        assert o <= 32768
        w1p = self.big3[:, 0:4 * 2 * NCH * 128]
        w2p = self.big3[:, 4 * 2 * NCH * 128:4 * 2 * NCH * 128 + 4 * CM]

        big2_gate = self.rgate("big2", "vtm")
        big3_gate = self.rgate("big3", "vw")
        y_upd = []
        self._gbi = getattr(self, "_gbi", 0)
        gbsem = f"cgb{self._gbi % 4}"
        self._gbi += 1
        em.waits("pool", tmax(self.rgate("gb2"), self.const_gp, self.conv_tok[NL - 1]))
        gb = em.op("pool", lambda ep: ep.dma_start(
            self.gb2[:, :], self.d_gb_host[:, l * CM:(l + 1) * CM]), inc=gbsem, amt=16)

        for tc in range(NTC):
            t0 = tc * TC
            em.waits("dve", tmax(self.y_tok, self.rgate("zb"), big2_gate))
            big2_gate = None
            if tc == 0:
                zb_tok = em.op("dve", lambda ep: ep.tensor_copy(
                    zb3[:, :, 0:513], y3[:, :, 0:513]), inc="c_dve")
            else:
                em.op("dve", lambda ep, t0=t0: ep.tensor_copy(
                    zb3[:, :, 1:513], y3[:, :, t0 + 1:t0 + 513]))
                zb_tok = em.op("dve", lambda ep: ep.tensor_copy(
                    zb3[:, :, 0:1], self.bsave[:, :].rearrange("p (c n) -> p c n", n=1)), inc="c_dve")
            if tc < NTC - 1:
                em.op("dve", lambda ep, t0=t0: ep.tensor_copy(
                    self.bsave[:, :].rearrange("p (c n) -> p c n", n=1),
                    y3[:, :, t0 + TC:t0 + TC + 1]))

            def zsl(c):
                off = 1 if c < 4 else 0
                return zb3[:, c, off:off + TC]

            em.waits("pe", tmax(zb_tok, self.rgate("PD")))
            sq_t = None
            for c in range(NCH):
                em.op("pe", lambda ep, c=c: ep.matmul(
                    self.PD[0:1, :], self.onescol[:, :], zsl(c),
                    start=(c == 0), stop=(c == NCH - 1), skip_group_check=True))
                em.waits("act", tmax(zb_tok, self.rgate(("bcs", c % 2))))
                sq = em.op("act", lambda ep, c=c: ep.activation(
                    self.bcq_sb[:, (c % 2) * TC:(c % 2 + 1) * TC], zsl(c), AF.Square), inc="c_act")
                self.rreg("zb", sq)
                if c == 0:
                    em.waits("pe", tmax(sq, self.rgate("PE")))
                else:
                    em.waits("pe", [sq])
                sq_t = em.op("pe", lambda ep, c=c: ep.matmul(
                    self.PE_[0:1, :], self.onescol[:, :],
                    self.bcq_sb[:, (c % 2) * TC:(c % 2 + 1) * TC],
                    start=(c == 0), stop=(c == NCH - 1), skip_group_check=True),
                    inc="c_pe")
                self.rreg(("bcs", c % 2), sq_t)
                self.rreg("zb", sq_t)
            em.waits("dve", tmax(sq_t, self.rgate("x0t")))
            em.op("dve", lambda ep: ep.tensor_scalar_mul(self.mus, self.PD[0:1, :], 1.0 / CM))
            t_sv = em.op("dve", lambda ep: ep.tensor_tensor(self.statv, self.mus, self.mus,
                                                            ALU.mult), inc="c_dve")
            v1 = em.op("dve", lambda ep: ep.scalar_tensor_tensor(
                self.statv, self.PE_[0:1, :], 1.0 / CM, self.statv,
                ALU.mult, ALU.subtract), inc="c_dve")
            self.rreg("PD", v1)
            self.rreg("PE", v1)
            em.waits("act", [v1])
            em.waits("act", [self.eps_tok])
            em.op("act", lambda ep: ep.activation(self.statl, self.statv, AF.Ln,
                                                  bias=self.epsc[0:1, 0:1]))
            r1 = em.op("act", lambda ep: ep.activation(self.arhs[:, :], self.statl,
                                                       AF.Exp, scale=-0.5), inc="c_act")
            em.waits("dve", [r1])
            cr = em.op("dve", lambda ep: ep.scalar_tensor_tensor(
                self.c2rhs[0:1, :], self.PD[0:1, :], -1.0 / CM, self.arhs[:, :],
                ALU.mult, ALU.mult), inc="c_dve")
            self.rreg("PD", cr)
            self.rreg("x0t", t_sv)
            em.waits("pe", tmax(r1, cr, gb, self.rgate("PE")))
            at = em.op("pe", lambda ep: ep.matmul(self.PE_[:, :], self.ones1[:, :],
                                                  self.arhs[:, :], start=True, stop=True), inc="c_pe")
            xn_tok = []
            for c in range(NCH):
                pk = "PB" + str(c % 2)
                em.waits("pe", self.rgate(pk))
                c2 = em.op("pe", lambda ep, c=c: ep.matmul(
                    self.PB[:, (c % 2) * 512:(c % 2) * 512 + 512],
                    self.gb2[:, c * 128:(c + 1) * 128], self.c2rhs[:, :],
                    start=True, stop=True), inc="c_pe")
                em.waits("dve", tmax(at, c2, self.rgate(("xn", c))))
                p1 = em.op("dve", lambda ep, c=c: ep.tensor_tensor(
                    self.t1s[:, :], zsl(c), self.PE_[:, :], ALU.mult),
                    inc="c_dve")
                self.rreg("zb", p1)
                p2 = em.op("dve", lambda ep, c=c: ep.scalar_tensor_tensor(
                    xn[:, c * 512:(c + 1) * 512], self.t1s[:, :],
                    self.g_pp[:, l * NCH + c:l * NCH + c + 1],
                    self.PB[:, (c % 2) * 512:(c % 2) * 512 + 512],
                    ALU.mult, ALU.add), inc="c_dve")
                self.rreg(pk, p2)
                self.rreg("PE", p2)
                xn_tok.append(p2)
                self.wr[("xn", c)] = p2
            self.rreg("gb2", xn_tok[-1])
            h_tok = [None] * NJ
            for j in range(NJ):
                wj = 128 if j < NJ - 1 else WJL
                slot = j % 4
                em.waits("sp", tmax(self.conv_tok[l], self.rgate(("w1", slot)), big3_gate))
                big3_gate = None
                wsrc = reap(self.s_w1[l, j], [(128, 128), (NCH * 128 * 128, 2), (128 * 128, NCH), (1, wj)])
                wdst = w1p[:, slot * 2 * NCH * 128:(slot + 1) * 2 * NCH * 128]
                wldst = reap(wdst, [(wdst.ap[0][0], 128), (NCH * 128, 2), (128, NCH), (1, wj)])
                wtok = em.op("sp", lambda ep, s=wsrc, d=wldst: ep.dma_start(d, s),
                             inc=f"w1p{slot}", amt=16)
                em.waits("pe", tmax(wtok, *xn_tok))
                em.waits("pe", self.rgate("PC" + str(j % 2), "PA" + str(j % 2)))
                mma = mmg = None
                for sc in range(NCH):
                    mma = em.op("pe", lambda ep, sc=sc, wdst=wdst, wj=wj, j=j: ep.matmul(
                        self.PC[0:wj, (j % 2) * 512:(j % 2) * 512 + 512],
                        wdst[:, sc * 128:sc * 128 + wj], xn[:, sc * 512:(sc + 1) * 512],
                        start=(sc == 0), stop=(sc == NCH - 1), skip_group_check=True))
                for sc in range(NCH):
                    mmg = em.op("pe", lambda ep, sc=sc, wdst=wdst, wj=wj, j=j: ep.matmul(
                        self.PA[0:wj, (j % 2) * 512:(j % 2) * 512 + 512],
                        wdst[:, NCH * 128 + sc * 128: NCH * 128 + sc * 128 + wj],
                        xn[:, sc * 512:(sc + 1) * 512],
                        start=(sc == 0), stop=(sc == NCH - 1), skip_group_check=True),
                        inc="c_pe" if sc == NCH - 1 else None)
                self.rreg(("w1", slot), mmg)
                for c in range(NCH):
                    self.rreg(("xn", c), mmg)
                em.waits("act", tmax(mmg, self.rgate(("gel", j % 2))))
                gl = em.op("act", lambda ep, j=j, wj=wj: ep.activation(
                    gel[0:wj, (j % 2) * 512:(j % 2) * 512 + 512],
                    self.PA[0:wj, (j % 2) * 512:(j % 2) * 512 + 512],
                    AF.Gelu, bias=self.b1g_pp[0:wj, l * NJ + j:l * NJ + j + 1]), inc="c_act")
                self.rreg("PA" + str(j % 2), gl)
                em.waits("dve", tmax(gl, self.rgate(("h", j))))
                ht = em.op("dve", lambda ep, j=j, wj=wj: ep.scalar_tensor_tensor(
                    hsb[0:wj, j * 512:j * 512 + 512],
                    self.PC[0:wj, (j % 2) * 512:(j % 2) * 512 + 512],
                    self.b1a_pp[0:wj, l * NJ + j:l * NJ + j + 1],
                    gel[0:wj, (j % 2) * 512:(j % 2) * 512 + 512], ALU.add, ALU.mult), inc="c_dve")
                h_tok[j] = ht
                self.rreg("PC" + str(j % 2), ht)
                self.rreg(("gel", j % 2), ht)
                self.wr[("h", j)] = ht
            w2reg = [(self.PA, 0, "PA0"), (self.PA, 512, "PA1"),
                     (self.PB, 0, "PB0"), (self.PB, 512, "PB1"),
                     (self.PC, 0, "PC0"), (self.PC, 512, "PC1"),
                     (self.PD, 0, "PD"), (self.PE_, 0, "PE")]
            em.waits("pe", tmax(h_tok))
            em.waits("pe", self.rgate(*[k for _, _, k in w2reg]))
            w2_toks = {}

            def w2_load(jj):
                wjx = 128 if jj < NJ - 1 else WJL
                slot = jj % 4
                em.waits("sp", tmax(self.conv_tok[l], self.rgate(("w2", slot))))
                wsrc = self.s_w2[l, jj * 128:jj * 128 + wjx, :]
                wdst = w2p[:, slot * CM:(slot + 1) * CM]
                w2_toks[jj] = em.op("sp", lambda ep, s=wsrc, d=wdst[0:wjx, :]: ep.dma_start(d, s),
                                    inc=f"w2p{slot}", amt=16)

            for jj in range(min(3, NJ)):
                w2_load(jj)
            mmlast = None
            for j in range(NJ):
                if j + 3 < NJ:
                    w2_load(j + 3)
                wj = 128 if j < NJ - 1 else WJL
                wdst = w2p[:, (j % 4) * CM:(j % 4 + 1) * CM]
                em.waits("pe", tmax(w2_toks.pop(j)))
                for oc in range(NCH):
                    pdst, po, pk = w2reg[oc]
                    mmlast = em.op("pe", lambda ep, j=j, wj=wj, oc=oc, po=po, wdst=wdst,
                                   pdst=pdst: ep.matmul(
                                       pdst[:, po:po + 512],
                                       wdst[0:wj, oc * 128:oc * 128 + 128],
                                       hsb[0:wj, j * 512:j * 512 + 512],
                                       start=(j == 0), stop=(j == NJ - 1),
                                       skip_group_check=True),
                                   inc="c_pe" if oc == NCH - 1 else None)
                self.rreg(("w2", j % 4), mmlast)
                if j == NJ - 1:
                    for jj in range(NJ):
                        self.rreg(("h", jj), mmlast)
            em.waits("dve", [mmlast])
            for oc in range(NCH):
                pdst, po, pk = w2reg[oc]
                dsty = y3[:, oc, 1 + t0:1 + t0 + TC]
                up = em.op("dve", lambda ep, dsty=dsty, pdst=pdst, po=po, oc=oc:
                           ep.scalar_tensor_tensor(dsty, pdst[:, po:po + 512],
                                                   self.b2_pp[:, l * NCH + oc:l * NCH + oc + 1],
                                                   dsty, ALU.add, ALU.add), inc="c_dve")
                self.rreg(pk, up)
                y_upd.append(up)
        self.y_tok = tmax(y_upd)
        self.rreg("big2", tmax(y_upd))
        self.rreg("big3", tmax(y_upd))
        for c in range(NCH):
            self.rd.pop(("xn", c), None)
        for j in range(NJ):
            self.rd.pop(("h", j), None)

    def _outer(self):
        em = self.em
        xt3 = self.xt3()
        vFM = self.b1bf(16384, 32768)
        E = self.b1bf(0, 16384)
        wT1 = self.big3[:, 0:NCH * CM]
        wT2 = self.big3[:, NCH * CM:2 * NCH * CM]
        knb = self.big3[:, 2 * NCH * CM:2 * NCH * CM + 2 * CM]
        vtb = self.big3[:, 2 * NCH * CM + 2 * CM:2 * NCH * CM + 4 * CM]
        ctxs = self.t1s[:, :].bitcast(BF16)[:, 0:512]
        knraw = self.knraw_v
        ksumr = self.knraw_v[0:1, :]
        bk_row = self.tabs[0:1, 0:CM]
        bv_row = self.tabs[0:1, CM:2 * CM]
        bp_row = self.tabs[0:1, 2 * CM:3 * CM]
        em.waits("pool", tmax(self.rgate(("tabr", 0)), self.rgate(("tabr", 1)), self.const_gp))
        em.op("pool", lambda ep: ep.dma_start(bk_row, self.d_bk_r), inc="cbr", amt=16)
        em.op("pool", lambda ep: ep.dma_start(bv_row, self.d_bv_r), inc="cbr", amt=16)
        em.op("pool", lambda ep: ep.dma_start(bp_row, self.d_bp_r), inc="cbr", amt=16)
        br_tok = ("cbr", em.cnt["cbr"])

        big3_gate = self.rgate("big3", "vw", "vtm")
        big1_gate = tmax(self.rgate("big1"), self.y_tok)

        trpb = self.PC[:, :].bitcast(BF16)
        vfm_tok = None
        for tt in range(NTT):
            xin = xt3[:, tt, :]
            rstd, nm = self._tm_stats(xin, tmax(self.xt_tok))
            tv = em.op("dve", lambda ep, xin=xin, rstd=rstd, nm=nm: ep.tensor_scalar(
                self.ytmb, xin, rstd, nm, ALU.mult, ALU.add), inc="c_dve")
            self.rreg("col1", tv)
            em.waits("pe", tmax(tv, self.rgate("PC0", "PC1")))
            tk = None
            for ch in range(NCH):
                tk = em.op("pe", lambda ep, ch=ch: ep.transpose(
                    trpb[:, ch * 128:(ch + 1) * 128], self.ytmb[:, ch * 128:(ch + 1) * 128],
                    self.identb[:, :]), inc="c_pe" if ch == NCH - 1 else None)
            self.rreg("ytm", tk)
            em.waits("dve", tmax(tk, big1_gate))
            big1_gate = None
            dst = AP(vFM.tensor, vFM.offset + tt * 128,
                     [[vFM.ap[0][0], 128], [T, NCH], [1, 128]])
            vfm_tok = em.op("dve", lambda ep, dst=dst: ep.tensor_copy(
                dst, trpb[:, 0:1024].rearrange("p (c n) -> p c n", n=128)), inc="c_dve")
            self.rreg("PC0", vfm_tok)
            self.rreg("PC1", vfm_tok)
        em.waits("sp", tmax(self.wT_tok, big3_gate))
        w1tok = em.op("sp", lambda ep: ep.dma_start(
            wT1, reap(self.s_wT[0], [(CM, 128), (128 * CM, NCH), (1, CM)])), inc="cwt1", amt=16)
        em.waits("act", tmax(big1_gate, vfm_tok))
        e_toks = []
        for tc in range(NTC):
            for oc in range(NCH):
                pq = self.PA if oc % 4 < 2 else self.PB
                pk = ("PA" if oc % 4 < 2 else "PB") + str(oc % 2)
                em.waits("pe", tmax(w1tok, vfm_tok, self.rgate(pk)))
                mm = None
                for sc in range(NCH):
                    mm = em.op("pe", lambda ep, sc=sc, oc=oc, tc=tc, pq=pq: ep.matmul(
                        pq[:, (oc % 2) * 512:(oc % 2) * 512 + 512],
                        wT1[:, sc * CM + oc * 128: sc * CM + oc * 128 + 128],
                        vFM[:, sc * T + tc * 512: sc * T + (tc + 1) * 512],
                        start=(sc == 0), stop=(sc == NCH - 1), skip_group_check=True),
                        inc="c_pe" if sc == NCH - 1 else None)
                em.waits("act", [mm])
                ex = em.op("act", lambda ep, oc=oc, tc=tc, pq=pq: ep.activation(
                    E[:, oc * T + tc * 512: oc * T + (tc + 1) * 512],
                    pq[:, (oc % 2) * 512:(oc % 2) * 512 + 512],
                    AF.Exp, bias=self.bq_pp[:, oc:oc + 1]), inc="c_act")
                self.rreg(pk, ex)
                e_toks.append(ex)
        em.waits("sp", tmax(e_toks[-1], self.rgate("wT1rd")))
        w2tok = em.op("sp", lambda ep: ep.dma_start(
            wT2, reap(self.s_wT[1], [(CM, 128), (128 * CM, NCH), (1, CM)])), inc="cwt2", amt=16)
        w3tok = em.op("sp", lambda ep: ep.dma_start(
            wT1, reap(self.s_wT[2], [(CM, 128), (128 * CM, NCH), (1, CM)])), inc="cwt1", amt=16)
        em.waits("pe", self.rgate("PC0", "PC1", "PD"))
        ctx_done = None
        ks_done = None
        kn_tok = vt_tok = None
        for tt in range(NTT):
            for which in range(2):
                wt = wT2 if which == 0 else wT1
                wtk = w2tok if which == 0 else w3tok
                brow = bk_row if which == 0 else bv_row
                pq = self.PA if which == 0 else self.PB
                pk0 = ("PA0", "PA1") if which == 0 else ("PB0", "PB1")
                em.waits("pe", tmax(wtk, vfm_tok, br_tok))
                em.waits("pe", self.rgate(*pk0))
                mm = None
                for g2 in range(2):
                    for sc in range(NCH):
                        em.op("pe", lambda ep, sc=sc, g2=g2, tt=tt, wt=wt, pq=pq: ep.matmul(
                            pq[:, g2 * 512:(g2 + 1) * 512],
                            vFM[:, sc * T + tt * 128: sc * T + tt * 128 + 128],
                            wt[:, sc * CM + g2 * 512: sc * CM + (g2 + 1) * 512],
                            start=(sc == 0), stop=False, skip_group_check=True))
                    mm = em.op("pe", lambda ep, g2=g2, brow=brow, pq=pq: ep.matmul(
                        pq[:, g2 * 512:(g2 + 1) * 512], self.ones1[:, :],
                        brow[:, g2 * 512:(g2 + 1) * 512],
                        start=False, stop=True, skip_group_check=True),
                        inc="c_pe" if g2 == 1 else None)
                if which == 0:
                    em.waits("act", tmax(mm, self.rgate("knraw")))
                    ek = em.op("act", lambda ep: ep.activation(knraw[:, :], self.PA[:, 0:1024],
                                                               AF.Exp), inc="c_act")
                    self.rreg("PA0", ek)
                    self.rreg("PA1", ek)
                    em.waits("dve", [ek])
                    dk = em.op("dve", lambda ep: ep.reduce_sum(
                        self.denk[:, :], knraw.rearrange("p (h n) -> p h n", n=64), axis=AX.X),
                        inc="c_dve")
                    em.waits("act", [dk])
                    em.op("act", lambda ep: ep.activation(self.denk[:, :], self.denk[:, :], AF.Ln))
                    ik = em.op("act", lambda ep: ep.activation(self.denk[:, :], self.denk[:, :],
                                                               AF.Exp, scale=-1.0), inc="c_act")
                    em.waits("dve", tmax(ik, self.rgate(("knb", tt % 2))))
                    kt = None
                    for h in range(H):
                        kt = em.op("dve", lambda ep, h=h, tt=tt: ep.tensor_scalar_mul(
                            knb[:, (tt % 2) * CM + h * 64:(tt % 2) * CM + (h + 1) * 64],
                            knraw[:, h * 64:(h + 1) * 64], self.denk[:, h:h + 1]),
                            inc="c_dve" if h == H - 1 else None)
                    self.rreg("knraw", kt)
                    kn_tok = kt
                else:
                    em.waits("dve", tmax(mm, self.rgate(("vtb", tt % 2))))
                    vt_tok = em.op("dve", lambda ep, tt=tt: ep.tensor_copy(
                        vtb[:, (tt % 2) * CM:(tt % 2 + 1) * CM], self.PB[:, 0:1024]), inc="c_dve")
                    self.rreg("PB0", vt_tok)
                    self.rreg("PB1", vt_tok)
            em.waits("pe", tmax(kn_tok, vt_tok, self.rgate("PC0", "PC1")))
            for h in range(H):
                ctx_done = em.op("pe", lambda ep, h=h, tt=tt: ep.matmul(
                    self.PC[(h % 2) * 64:(h % 2) * 64 + 64, (h // 2) * 64:(h // 2) * 64 + 64],
                    knb[:, (tt % 2) * CM + h * 64:(tt % 2) * CM + (h + 1) * 64],
                    vtb[:, (tt % 2) * CM + h * 64:(tt % 2) * CM + (h + 1) * 64],
                    start=True, stop=True, skip_group_check=True),
                    inc="c_pe" if h == H - 1 else None)
            em.waits("dve", [ctx_done])
            if tt == 0:
                ctx_done = em.op("dve", lambda ep: ep.tensor_copy(
                    self.ctxacc[:, :], self.PC[:, 0:512]), inc="c_dve")
            else:
                ctx_done = em.op("dve", lambda ep: ep.tensor_tensor(
                    self.ctxacc[:, :], self.ctxacc[:, :], self.PC[:, 0:512], ALU.add), inc="c_dve")
            self.rreg("PC0", ctx_done)
            for g2 in range(2):
                ks_done = em.op("pe", lambda ep, g2=g2, tt=tt: ep.matmul(
                    self.PD[0:1, :] if g2 == 0 else self.PE_[0:1, :],
                    self.onescol[:, :],
                    knb[:, (tt % 2) * CM + g2 * 512:(tt % 2) * CM + (g2 + 1) * 512],
                    start=(tt == 0), stop=(tt == NTT - 1), skip_group_check=True),
                    inc="c_pe" if (g2 == 1 and tt == NTT - 1) else None)
            self.rreg(("knb", tt % 2), ctx_done)
            self.rreg(("vtb", tt % 2), ctx_done)
        em.waits("dve", [ctx_done, ks_done])
        ctx_ev = em.op("dve", lambda ep: ep.tensor_copy(ctxs[:, :], self.ctxacc[:, :]), inc="c_dve")
        em.op("dve", lambda ep: ep.tensor_copy(ksumr[0:1, 0:512], self.PD[0:1, :]))
        ks_ev = em.op("dve", lambda ep: ep.tensor_copy(ksumr[0:1, 512:1024], self.PE_[0:1, :]),
                      inc="c_dve")
        self.rreg("knraw", ks_ev)
        self.rreg("PD", ks_ev)
        self.rreg("PE", ks_ev)
        em.waits("pe", tmax(ks_ev, self.rgate("PE")))
        tk = None
        for c in range(NCH):
            tk = em.op("pe", lambda ep, c=c, ksumr=ksumr: ep.matmul(
                self.PE_[:, c:c + 1], ksumr[0:1, c * 128:(c + 1) * 128],
                self.onescol[0:1, 0:1], start=True, stop=True, skip_group_check=True),
                inc="c_pe" if c == NCH - 1 else None)
        em.waits("dve", [tk])
        kc = em.op("dve", lambda ep: ep.tensor_copy(self.ksumc[:, :], self.PE_[:, 0:8]), inc="c_dve")
        self.rreg("PE", kc)
        em.waits("dve", [kc])
        kw = None
        for c in range(NCH):
            kw = em.op("dve", lambda ep, c=c: ep.tensor_scalar_mul(
                self.kwsel[:, c * 16:(c + 1) * 16], self.bselq[:, c * 16:(c + 1) * 16],
                self.ksumc[:, c:c + 1]), inc="c_dve" if c == NCH - 1 else None)
        collq = AP(vFM.tensor, vFM.offset, [[vFM.ap[0][0], 16], [1, 2048]])
        collD = AP(vFM.tensor, vFM.offset + 2048, [[vFM.ap[0][0], 16], [1, 2048]])
        dq_toks = []
        for tc in range(NTC):
            for which in range(2):
                sel = self.bselq if which == 0 else self.kwsel
                coll = collq if which == 0 else collD
                em.waits("pe", tmax(kw, e_toks[-1], self.rgate("PD")))
                mm = None
                for c in range(NCH):
                    mm = em.op("pe", lambda ep, c=c, sel=sel, tc=tc: ep.matmul(
                        self.PD[0:16, :], sel[:, c * 16:(c + 1) * 16],
                        E[:, c * T + tc * 512:c * T + (tc + 1) * 512],
                        start=(c == 0), stop=(c == NCH - 1), skip_group_check=True),
                        inc="c_pe" if c == NCH - 1 else None)
                em.waits("dve", [mm])
                ev = em.op("dve", lambda ep, coll=coll, tc=tc: ep.tensor_copy(
                    coll[:, tc * 512:(tc + 1) * 512], self.PD[0:16, :]), inc="c_dve")
                self.rreg("PD", ev)
                dq_toks.append(ev)
        em.waits("act", tmax(dq_toks))
        em.op("act", lambda ep: ep.activation(collq, collq, AF.Ln))
        iq = em.op("act", lambda ep: ep.activation(collq, collq, AF.Exp, scale=-1.0), inc="c_act")
        em.op("act", lambda ep: ep.activation(collD, collD, AF.Ln))
        iD = em.op("act", lambda ep: ep.activation(collD, collD, AF.Exp, scale=-1.0), inc="c_act")
        asm_toks = []
        for tc in range(NTC):
            for c in range(NCH):
                em.waits("pe", tmax(ctx_ev, iq, iD, self.rgate("PB0"), self.rgate("PE")))
                mp = None
                for hh in range(2):
                    h = 2 * c + hh
                    mp = em.op("pe", lambda ep, h=h, hh=hh, c=c, tc=tc: ep.matmul(
                        self.PB[hh * 64:hh * 64 + 64, 0:512],
                        ctxs[hh * 64:hh * 64 + 64, (h // 2) * 64:(h // 2) * 64 + 64],
                        E[hh * 64:hh * 64 + 64, c * T + tc * 512:c * T + (tc + 1) * 512],
                        start=True, stop=True), inc="c_pe" if hh == 1 else None)
                bq1 = em.op("pe", lambda ep, c=c, tc=tc: ep.matmul(
                    self.PE_[:, :], self.selpat[:, c * 128:(c + 1) * 128],
                    collq[:, tc * 512:(tc + 1) * 512],
                    start=True, stop=True), inc="c_pe")
                em.waits("dve", tmax(bq1, self.rgate("scrA")))
                t1 = em.op("dve", lambda ep, c=c, tc=tc: ep.tensor_tensor(
                    self.scrA[:, :], E[:, c * T + tc * 512:c * T + (tc + 1) * 512],
                    self.PE_[:, :], ALU.mult), inc="c_dve")
                em.waits("pe", [t1])
                bD = em.op("pe", lambda ep, c=c, tc=tc: ep.matmul(
                    self.PE_[:, :], self.selpat[:, c * 128:(c + 1) * 128],
                    collD[:, tc * 512:(tc + 1) * 512],
                    start=True, stop=True), inc="c_pe")
                em.waits("act", tmax(bD, self.rgate("bcsb")))
                bDs = em.op("act", lambda ep: ep.activation(self.bcq_sb[:, 0:TC], self.PE_[:, :],
                                                            AF.Identity), inc="c_act")
                self.rreg("PE", bDs)
                em.waits("dve", tmax(bDs, mp, self.rgate("scrB")))
                t2 = em.op("dve", lambda ep: ep.tensor_tensor(
                    self.scrB[:, :], self.PB[:, 0:512], self.bcq_sb[:, 0:TC], ALU.mult), inc="c_dve")
                self.rreg("PB0", t2)
                self.rreg("bcsb", t2)
                fin = em.op("dve", lambda ep, c=c, tc=tc: ep.tensor_tensor(
                    E[:, c * T + tc * 512:c * T + (tc + 1) * 512], self.scrA[:, :],
                    self.scrB[:, :], ALU.add), inc="c_dve")
                self.rreg("scrA", fin)
                self.rreg("scrB", fin)
                asm_toks.append(fin)
        em.waits("sp", tmax(asm_toks))
        self.rreg(("tabr", 0), ks_done)
        self.rreg(("tabr", 1), ks_done)
        wp_tok = em.op("sp", lambda ep: ep.dma_start(
            wT2, reap(self.s_wT[3], [(CM, 128), (128 * CM, NCH), (1, CM)])), inc="cwt2", amt=16)
        self.xt_tok = []
        for tt in range(NTT):
            em.waits("pe", tmax(wp_tok, asm_toks))
            em.waits("pe", self.rgate("PA0", "PA1"))
            mm = None
            for g2 in range(2):
                for sc in range(NCH):
                    em.op("pe", lambda ep, sc=sc, g2=g2, tt=tt: ep.matmul(
                        self.PA[:, g2 * 512:(g2 + 1) * 512],
                        E[:, sc * T + tt * 128: sc * T + tt * 128 + 128],
                        wT2[:, sc * CM + g2 * 512: sc * CM + (g2 + 1) * 512],
                        start=(sc == 0), stop=False, skip_group_check=True))
                mm = em.op("pe", lambda ep, g2=g2: ep.matmul(
                    self.PA[:, g2 * 512:(g2 + 1) * 512], self.ones1[:, :],
                    bp_row[:, g2 * 512:(g2 + 1) * 512],
                    start=False, stop=True, skip_group_check=True),
                    inc="c_pe" if g2 == 1 else None)
            em.waits("dve", tmax(mm, self.rgate(("xt", tt))))
            up = em.op("dve", lambda ep, tt=tt: ep.tensor_tensor(
                self.xt3()[:, tt, :].rearrange("p (c n) -> p c n", n=128),
                self.xt3()[:, tt, :].rearrange("p (c n) -> p c n", n=128),
                self.PA[:, 0:1024].rearrange("p (c n) -> p c n", n=128), ALU.add), inc="c_dve")
            self.rreg("PA0", up)
            self.rreg("PA1", up)
            self.xt_tok.append(up)
        self.rreg("big1", tmax(asm_toks, self.xt_tok))
        self.rreg("big3", tmax(asm_toks, self.xt_tok))



_CACHE = {}


def _build(dbg_cut=None):
    key = ("nc", dbg_cut)
    if key not in _CACHE:
        nc = bass.Bass("TRN2", target_bir_lowering=False)
        kb = KB(nc, dbg_cut=dbg_cut)
        n = kb.build()
        nc._pe_tags = kb.em.pe_tags
        _CACHE[key] = (nc, n)
    return _CACHE[key]


def make_inputs(inputs):
    consts = host_constants()
    lay = host_layouts(inputs)
    x = np.ascontiguousarray(np.asarray(inputs["L0_attn_x"], np.float32))
    base = {
        "wqkv": np.ascontiguousarray(np.asarray(inputs["L0_ffn_Wqkv"], np.float32)),
        "wo3": np.ascontiguousarray(np.asarray(inputs["L0_ffn_Wo"], np.float32)),
        "w1": np.ascontiguousarray(np.asarray(inputs["L0_ffn_W1"], np.float32)),
        "w2": np.ascontiguousarray(np.asarray(inputs["L0_ffn_W2"], np.float32)),
        "awq": np.ascontiguousarray(np.asarray(inputs["L0_attn_Wq"], np.float32)),
        "awk": np.ascontiguousarray(np.asarray(inputs["L0_attn_Wk"], np.float32)),
        "awv": np.ascontiguousarray(np.asarray(inputs["L0_attn_Wv"], np.float32)),
        "awp": np.ascontiguousarray(np.asarray(inputs["L0_attn_Wp"], np.float32)),
    }
    base.update(consts)
    base.update(lay)
    in_maps = []
    for core in range(8):
        m = dict(base)
        m["x"] = np.ascontiguousarray(x[2 * core:2 * core + 2].reshape(NSEQ * T, CM))
        in_maps.append(m)
    return in_maps


def kernel(**inputs):
    nc, _ = _build()
    in_maps = make_inputs(inputs)
    res = run_bass_kernel_spmd(nc, in_maps, core_ids=list(range(8)))
    out = np.empty((16, T, CM), np.float32)
    for core in range(8):
        out[2 * core:2 * core + 2] = res.results[core]["out"].reshape(NSEQ, T, CM)
    return out


# version 09
# speedup vs baseline: 1.1220x; 1.0426x over previous
import os
import numpy as np
import ml_dtypes
from contextlib import ExitStack

os.environ.setdefault("GAUGE_NTFF_IGNORE_DMA", "1")

import concourse.bass as bass
import concourse.mybir as mybir
from concourse.ap import AP
from concourse.bass_utils import run_bass_kernel_spmd

F32 = mybir.dt.float32
BF16 = mybir.dt.bfloat16
AF = mybir.ActivationFunctionType
ALU = mybir.AluOpType
AX = mybir.AxisListType
BF = ml_dtypes.bfloat16

T, CM, H, DH, WIN = 2048, 1024, 16, 64, 256
NCH, TC, NL, DI, NJ = 8, 512, 6, 2730, 22
NTC = T // TC
WJL = DI - 21 * 128
NSEQ = 2
NTT = T // 128
DBG_NW = 2
DBG_NH = H
DBG_SNAP = None
DBG_CUTOP = None



def host_constants():
    c = {}
    half = np.arange(0, DH, 2, dtype=np.float64)
    inv_freq = 1.0 / 10000.0 ** (half / DH)
    t = np.arange(T, dtype=np.float64)
    freqs = t[:, None] * inv_freq[None, :]
    freqs = np.concatenate([freqs, freqs], -1)
    base_scale = (half + 0.4 * DH) / (1.4 * DH)
    power = (t - T // 2) / (WIN // 2)
    scale = base_scale[None, :] ** power[:, None]
    scale = np.concatenate([scale, scale], -1)

    def fm(x):
        x = np.ascontiguousarray(x.T.astype(np.float32))
        return np.concatenate([x, x], 0).astype(BF)

    c["c_tabs"] = np.concatenate([fm(np.cos(freqs) * scale * 8.0), fm(np.sin(freqs) * scale * 8.0),
                                  fm(np.cos(freqs) / scale), fm(np.sin(freqs) / scale)], axis=1)
    c["c_identb"] = np.eye(128, dtype=BF)
    c["c_identf"] = np.eye(128, dtype=np.float32)
    pr = np.zeros((128, 128), np.float32)
    for b0 in (0, 64):
        for d in range(64):
            if d < 32:
                pr[b0 + d + 32, b0 + d] = -1.0
            else:
                pr[b0 + d - 32, b0 + d] = 1.0
    c["c_prot"] = pr.astype(BF)
    bq = np.zeros((128, NCH * 16), np.float32)
    bk = np.zeros((128, NCH * 48), np.float32)
    for ch in range(NCH):
        bq[0:64, ch * 16 + 2 * ch] = 1.0
        bq[64:128, ch * 16 + 2 * ch + 1] = 1.0
        bk[0:64, ch * 48 + 32 + 2 * ch] = 1.0
        bk[64:128, ch * 48 + 32 + 2 * ch + 1] = 1.0
    c["c_bselq"] = bq.astype(BF)
    c["c_bselk"] = bk.astype(BF)
    sp = np.zeros((16, NCH * 128), np.float32)
    for ch in range(NCH):
        sp[2 * ch, ch * 128:ch * 128 + 64] = 1.0
        sp[2 * ch + 1, ch * 128 + 64:ch * 128 + 128] = 1.0
    c["c_selpat"] = sp.astype(BF)
    u = np.zeros((1, 16 * 16), np.float32)
    for h in range(16):
        u[0, h * 16 + h] = 1.0
    c["c_unit16"] = u.astype(BF)
    c["c_ones1"] = np.ones((1, 128), np.float32).astype(BF)
    c["c_onescol"] = np.ones((128, 1), np.float32).astype(BF)
    c["c_onesrow"] = np.ones((1, TC), np.float32).astype(BF)
    p = np.arange(128)[:, None]
    i = np.arange(256)[None, :]

    def keep(jc):
        d = (i + 256) - (jc * 128 + p)
        return ((d >= 0) & (d <= 256)).astype(np.float32)

    k4 = np.hstack([keep(0), keep(1), keep(2), keep(3)])
    c["c_maskb"] = ((1.0 - k4) * -300.0).astype(BF)
    k0 = np.hstack([keep(2), keep(3)])
    c["c_maskb0"] = ((1.0 - k0) * -300.0).astype(BF)
    return c


def host_layouts(ins):
    d = {}
    qs = np.asarray(ins["L0_ffn_q_scale"], np.float32)
    ks = np.asarray(ins["L0_ffn_k_scale"], np.float32)
    d["qs_pp"] = np.ascontiguousarray(np.concatenate([qs.T, qs.T], 0))
    d["ks_pp"] = np.ascontiguousarray(np.concatenate([ks.T, ks.T], 0))
    g = np.asarray(ins["L0_ffn_ln_g"], np.float32)
    b = np.asarray(ins["L0_ffn_ln_b"], np.float32)
    d["g_pp"] = np.ascontiguousarray(g.reshape(NL, NCH, 128).transpose(2, 0, 1).reshape(128, NL * NCH))
    d["gb_host"] = np.ascontiguousarray(np.stack([g.reshape(-1), b.reshape(-1)], 0))
    b1 = np.asarray(ins["L0_ffn_b1"], np.float32)
    b1a = np.zeros((128, NL * NJ), np.float32)
    b1g = np.zeros((128, NL * NJ), np.float32)
    for l in range(NL):
        for j in range(NJ):
            w = 128 if j < NJ - 1 else WJL
            b1a[0:w, l * NJ + j] = b1[l, j * 128:j * 128 + w]
            b1g[0:w, l * NJ + j] = b1[l, DI + j * 128:DI + j * 128 + w]
    d["b1a_pp"] = b1a
    d["b1g_pp"] = b1g
    b2 = np.asarray(ins["L0_ffn_b2"], np.float32)
    d["b2_pp"] = np.ascontiguousarray(b2.reshape(NL, NCH, 128).transpose(2, 0, 1).reshape(128, NL * NCH))
    d["bq_pp"] = np.ascontiguousarray(np.asarray(ins["L0_attn_bq"], np.float32).reshape(NCH, 128).T)
    d["bk_r"] = np.asarray(ins["L0_attn_bk"], np.float32).reshape(1, CM).copy()
    d["bv_r"] = np.asarray(ins["L0_attn_bv"], np.float32).reshape(1, CM).copy()
    d["bp_r"] = np.asarray(ins["L0_attn_bp"], np.float32).reshape(1, CM).copy()
    return d



ENG2ATTR = {"pe": "tensor", "act": "scalar", "dve": "vector", "pool": "gpsimd", "sp": "sync"}


AUTO_SEM = {"dve": "c_dve", "act": "c_act", "pool": "c_gp"}
DMA_LOOKAHEAD = 8


class Em:
    def __init__(self):
        self.ops = {e: [] for e in ENG2ATTR}
        self.sems = {}
        self.cnt = {}
        self.waited = {e: {} for e in ENG2ATTR}
        self.nops = 0
        self.pe_tags = []

    def add_sem(self, name, handle):
        self.sems[name] = handle
        self.cnt[name] = 0

    def _dummy(self, eng, amt):
        sb = self.dummy_sb
        ps = self.dummy_ps
        bb = self.dummy_bb
        if amt == 16:
            return lambda ep: ep.dma_start(sb[0:1, 4:8], sb[0:1, 0:4])
        if eng == "pe":
            return lambda ep: ep.matmul(ps[0:1, 0:1], bb[0:1, 0:1], bb[0:1, 0:1],
                                        start=True, stop=True, skip_group_check=True)
        if eng == "act":
            return lambda ep: ep.activation(sb[0:1, 0:1], sb[0:1, 1:2], AF.Copy)
        return lambda ep: ep.engine_nop()

    def op(self, eng, fn, inc=None, amt=1):
        if eng == "pe":
            import sys as _s
            self.pe_tags.append(_s._getframe(1).f_lineno)
        if DBG_CUTOP is not None and self.nops >= DBG_CUTOP:
            fn = self._dummy(eng, amt)
        if eng in AUTO_SEM and amt == 1:
            own = AUTO_SEM[eng]
            if inc is None:
                inc = own
            if inc == own:
                self.wait(eng, (own, self.cnt[own]))
        self.nops += 1
        if inc is not None:
            self.cnt[inc] += amt
            tok = (inc, self.cnt[inc])
            sem_h = self.sems[inc]

            def wrapped(ep, fn=fn, sem_h=sem_h, amt=amt):
                fn(ep).then_inc(sem_h, amt)

            self.ops[eng].append(wrapped)
            return tok
        self.ops[eng].append(fn)
        return None

    def wait(self, eng, tok):
        if tok is None:
            return
        s, v = tok
        if v <= 0 or self.waited[eng].get(s, 0) >= v:
            return
        self.waited[eng][s] = v
        sem_h = self.sems[s]
        self.ops[eng].append(lambda ep, sem_h=sem_h, v=v: ep.wait_ge(sem_h, v))

    def waits(self, eng, toks):
        for t in toks:
            if t is not None:
                self.wait(eng, t)

    def flush(self, blk):
        for eng, attr in ENG2ATTR.items():
            ops = self.ops[eng]
            self.ops[eng] = []

            def run(ep, ops=ops):
                for f in ops:
                    f(ep)

            getattr(blk, attr)(run)


def tmax(*toks):
    best = {}
    for t in toks:
        if t is None:
            continue
        for u in (t if isinstance(t, list) else [t]):
            if u is not None:
                best[u[0]] = max(best.get(u[0], 0), u[1])
    return [(s, v) for s, v in best.items()]



def reap(src_ap, dims):
    return AP(src_ap.tensor, src_ap.offset, [list(d) for d in dims])



class KB:
    def __init__(self, nc, dbg_cut=None):
        self.nc = nc
        self.dbg_cut = dbg_cut
        self.es = ExitStack()
        self.y_tok = None
        self.dbg_done = False
        self.rd = {}
        self.wr = {}

    def rreg(self, key, tok):
        if tok is not None:
            self.rd.setdefault(key, []).append(tok)

    def rgate(self, *keys):
        out = []
        for k in keys:
            out += self.rd.pop(k, [])
        return tmax(*out)

    def sb(self, name, shape, dt):
        return self.es.enter_context(self.nc.sbuf_tensor(name, shape, dt))

    def sem(self, name):
        h = self.es.enter_context(self.nc.semaphore(name))
        self.em.add_sem(name, h)

    def build(self):
        nc = self.nc
        self.x = nc.dram_tensor("x", [NSEQ * T, CM], F32, kind="ExternalInput").ap()
        self.out = nc.dram_tensor("out", [NSEQ * T, CM], F32, kind="ExternalOutput").ap()
        for n, s in [("wqkv", [NL, CM, 3 * CM]), ("wo3", [NL, CM, CM]),
                     ("w1", [NL, CM, 2 * DI]), ("w2", [NL, DI, CM]),
                     ("awq", [CM, CM]), ("awk", [CM, CM]), ("awv", [CM, CM]), ("awp", [CM, CM]),
                     ("qs_pp", [128, NL]), ("ks_pp", [128, NL]), ("g_pp", [128, NL * NCH]),
                     ("gb_host", [2, NL * CM]), ("b1a_pp", [128, NL * NJ]), ("b1g_pp", [128, NL * NJ]),
                     ("b2_pp", [128, NL * NCH]), ("bq_pp", [128, NCH]),
                     ("bk_r", [1, CM]), ("bv_r", [1, CM]), ("bp_r", [1, CM]),
                     ("c_identf", [128, 128])]:
            setattr(self, "d_" + n, nc.dram_tensor(n, s, F32, kind="ExternalInput").ap())
        for n, s in [("c_tabs", [128, 4 * T]), ("c_identb", [128, 128]), ("c_prot", [128, 128]),
                     ("c_bselq", [128, NCH * 16]), ("c_bselk", [128, NCH * 48]),
                     ("c_selpat", [16, NCH * 128]), ("c_unit16", [1, 256]),
                     ("c_ones1", [1, 128]), ("c_onescol", [128, 1]), ("c_onesrow", [1, TC]),
                     ("c_maskb", [128, 1024]), ("c_maskb0", [128, 512])]:
            setattr(self, "d_" + n, nc.dram_tensor(n, s, BF16, kind="ExternalInput").ap())
        self.s_wqkv = nc.dram_tensor("s_wqkv", [NL, 24, NCH, 128, 128], BF16, kind="Internal").ap()
        self.s_wo = nc.dram_tensor("s_wo", [NL, NCH, NCH, 128, 128], BF16, kind="Internal").ap()
        self.s_w1 = nc.dram_tensor("s_w1", [NL, NJ, 2, NCH, 128, 128], BF16, kind="Internal").ap()
        self.s_w2 = nc.dram_tensor("s_w2", [NL, DI, CM], BF16, kind="Internal").ap()
        self.s_wT = nc.dram_tensor("s_wT", [4, CM, CM], BF16, kind="Internal").ap()
        self.s_x2 = nc.dram_tensor("s_x2", [NSEQ * T, CM], F32, kind="Internal").ap()
        if self.dbg_cut is not None:
            self.dbg = nc.dram_tensor("dbg", [128, NCH * (T + 1)], F32, kind="ExternalOutput").ap()
            self.dbg_snap = nc.dram_tensor("dbg_snap", [128, 1536], BF16, kind="ExternalOutput").ap()
            self.dbg_snap2 = nc.dram_tensor("dbg_snap2", [128, 6 * 1040], BF16, kind="ExternalOutput").ap()
            self.dbg_snap3 = nc.dram_tensor("dbg_snap3", [65, 256], F32, kind="ExternalOutput").ap()

        with self.es:
            self.em = Em()
            for s in (["d_sp", "d_gp", "c_pe", "c_dve", "c_act", "c_gp", "cbr", "cws",
                       "ccst", "cvw", "cx0", "cpk", "cwt1", "cwt2"]
                      + [f"cv{i}" for i in range(NL)] + [f"cw{i}" for i in range(4)]
                      + [f"cgb{i}" for i in range(4)] + [f"qkp{i}" for i in range(4)]
                      + [f"w1p{i}" for i in range(4)] + [f"w2p{i}" for i in range(4)]
                      + [f"wop{i}" for i in range(2)] + [f"tab{i}" for i in range(2)]):
                self.sem(s)
            self._alloc()
            self._emit()
            blk = self.es.enter_context(nc.Block())
            self.em.flush(blk)
        return self.em.nops

    def _alloc(self):
        sb = self.sb
        self.big1 = sb("big1", [128, NCH * (T + 1)], F32)
        self.big2 = sb("big2", [128, 16 * CM], F32)
        self.big3 = sb("big3", [128, 20480], BF16)
        self.tabs = sb("tabs", [128, 2 * 4 * TC], BF16)
        self.identb = sb("identb", [128, 128], BF16)
        self.identf = sb("identf", [128, 128], F32)
        self.prot = sb("prot", [128, 128], BF16)
        self.bselq = sb("bselq", [128, NCH * 16], BF16)
        self.bselk = sb("bselk", [128, NCH * 48], BF16)
        self.selpat = sb("selpat", [16, NCH * 128], BF16)
        self.unit16 = sb("unit16", [1, 256], BF16)
        self.ones1 = sb("ones1", [1, 128], BF16)
        self.onescol = sb("onescol", [128, 1], BF16)
        self.maskb = sb("maskb", [128, 1024], BF16)
        self.maskb0 = sb("maskb0", [128, 512], BF16)
        self.qs_pp = sb("qs_pp_s", [128, NL], F32)
        self.ks_pp = sb("ks_pp_s", [128, NL], F32)
        self.g_pp = sb("g_pp_s", [128, NL * NCH], F32)
        self.b1a_pp = sb("b1a_s", [128, NL * NJ], F32)
        self.b1g_pp = sb("b1g_s", [128, NL * NJ], F32)
        self.b2_pp = sb("b2_s", [128, NL * NCH], F32)
        self.bq_pp = sb("bq_s", [128, NCH], F32)
        self.gb2 = sb("gb2", [2, NCH * 128], BF16)
        self.x0t = sb("x0t", [128, CM], F32)
        self.tmu = sb("tmu", [128, CM], F32)
        self.sqtm = self.tmu
        self.ytm = self.tmu
        self.ytmb = self.tmu[:, :].bitcast(BF16)[:, 0:CM]

        self.arhs = sb("arhs", [1, TC], BF16)
        self.c2rhs = sb("c2rhs", [2, TC], BF16)
        self.col1 = sb("col1", [128, 8], F32)
        self.nstat_s = sb("nstat_s", [16, TC], F32)
        self.qss = self.nstat_s
        self.invden16 = sb("invden16", [16, TC], BF16)
        self.denrow = sb("denrow", [1, 2 * 256], BF16)
        self.bcq_sb = sb("bcq_sb", [128, 2 * TC], BF16)
        self.scrA = sb("scrA", [128, TC], BF16)
        self.scrB = sb("scrB", [128, TC], BF16)
        self.t1s = sb("t1s", [128, TC], F32)

        self.ksumc = sb("ksumc", [128, 8], F32)
        self.kwsel = sb("kwsel", [128, NCH * 16], BF16)
        self.denk = sb("denk", [128, 16], F32)
        self.bsave = sb("bsave", [128, NCH], BF16)
        self.epsc = sb("epsc", [128, 1], F32)
        self.knraw_v = self.tmu[:, :].bitcast(BF16)[:, 0:CM]
        self.ctxacc = self.tmu[:, 512:1024]
        self.mus = self.x0t[32:33, 0:TC]
        self.statv = self.x0t[0:1, 0:TC]
        self.statl = self.x0t[0:1, TC:2 * TC]
        self.qinv16 = self.big3[0:16, 19456:19968]
        self.kinv16 = self.big3[0:16, 19968:20480]
        psum = lambda n, s: self.es.enter_context(self.nc.psum_tensor(n, s, F32))
        self.PA = psum("PA", [128, 1024])
        self.PB = psum("PB", [128, 1024])
        self.PC = psum("PC", [128, 1024])
        self.PD = psum("PD", [128, 512])
        self.PE_ = psum("PE", [128, 512])
        self.em.dummy_sb = self.col1
        self.em.dummy_bb = self.onescol
        self.em.dummy_ps = self.PD

    def y3(self):
        return self.big1[:, :].rearrange("p (c n) -> p c n", n=T + 1)

    def xt3(self):
        return self.big2[:, :].rearrange("p (c n) -> p c n", n=CM)

    def b2bf(self, lo, hi):
        return self.big2[:, :].bitcast(BF16)[:, lo:hi]

    def b1bf(self, lo, hi):
        return self.big1[:, :].bitcast(BF16)[:, lo:hi]

    def _emit(self):
        em = self.em
        if self.dbg_cut is not None:
            em.op("dve", lambda ep: ep.memset(self.big2[:, :], 0.125))
            dm = em.op("dve", lambda ep: ep.memset(self.big3[:, :], 0.125), inc="c_dve")
            em.op("dve", lambda ep: ep.memset(self.denrow[:, :], 0.125))
            self.rreg("big2", dm)
            self.rreg("big3", dm)
        self._const_loads()
        self._conversion()
        for p in [self.PA, self.PB, self.PC]:
            em.op("dve", lambda ep, p=p: ep.memset(p[:, :], 0.0))
        em.op("dve", lambda ep: ep.memset(self.PD[:, :], 0.0))
        em.op("dve", lambda ep: ep.memset(self.PE_[:, :], 0.0))
        z = em.op("dve", lambda ep: ep.memset(self.epsc[:, :], 1e-5), inc="c_dve")
        self.eps_tok = z
        for k in ["PA0", "PA1", "PB0", "PB1", "PC0", "PC1", "PD", "PE"]:
            self.rreg(k, z)
        for s in range(NSEQ):
            self.seq = s
            self._seq(s)
            if self.dbg_done:
                break
        em.waits("sp", [(n, em.cnt[n]) for n in em.cnt if em.cnt[n] > 0 and not n.startswith("c_")])

    def _const_loads(self):
        em = self.em
        for name in ["c_identb identb", "c_identf identf", "c_prot prot",
                     "c_bselq bselq", "c_bselk bselk", "c_selpat selpat", "c_unit16 unit16",
                     "c_ones1 ones1", "c_onescol onescol", "c_maskb maskb", "c_maskb0 maskb0",
                     "qs_pp qs_pp", "ks_pp ks_pp", "g_pp g_pp", "b1a_pp b1a_pp",
                     "b1g_pp b1g_pp", "b2_pp b2_pp", "bq_pp bq_pp"]:
            dn, sn = name.split()
            em.op("sp", lambda ep, d=getattr(self, sn), s=getattr(self, "d_" + dn):
                  ep.dma_start(d[:, :], s), inc="ccst", amt=16)
        em.op("sp", lambda ep: ep.dma_start(self.c2rhs[1:2, :], self.d_c_onesrow), inc="ccst", amt=16)
        self.const_tok = ("ccst", em.cnt["ccst"])
        self.const_gp = ("d_gp", em.cnt["d_gp"])

    def _conversion(self):
        em = self.em
        self.conv_tok = {}
        for l in range(NL):
            cs = f"cv{l}"
            src = self.d_wqkv[l].rearrange("(s p) (o c) -> o s p c", p=128, c=128)
            for oc in range(24):
                em.op("pool", lambda ep, s=src[oc], d=self.s_wqkv[l, oc]:
                      ep.dma_start(d, s), inc=cs, amt=16)
            srco = self.d_wo3[l].rearrange("(s p) (o c) -> o s p c", p=128, c=128)
            for oc in range(NCH):
                em.op("pool", lambda ep, s=srco[oc], d=self.s_wo[l, oc]:
                      ep.dma_start(d, s), inc=cs, amt=16)
            for ag in range(2):
                src1 = self.d_w1[l][:, ag * DI:(ag + 1) * DI]
                for j in range(NJ):
                    w = 128 if j < NJ - 1 else WJL
                    sj = src1[:, j * 128:j * 128 + w].rearrange("(s p) c -> s p c", p=128)
                    em.op("pool", lambda ep, s=sj, d=self.s_w1[l, j, ag][:, :, 0:w]:
                          ep.dma_start(d, s), inc=cs, amt=16)
            em.op("pool", lambda ep, s=self.d_w2[l], d=self.s_w2[l]: ep.dma_start(d, s),
                  inc=cs, amt=16)
            self.conv_tok[l] = (cs, em.cnt[cs])
        em.waits("sp", [self.const_tok])
        em.waits("pool", tmax(self.rgate("big2"), self.rgate("big3")))
        wstage = self.b2bf(0, 8 * CM)
        wtst = self.big3[:, 0:NCH * 128]
        trpb = self.PC[:, :].bitcast(BF16)
        gate = None
        ev = None
        for widx, name in enumerate(["awq", "awk", "awv", "awp"]):
            src = reap(getattr(self, "d_" + name), [(CM, 128), (128 * CM, NCH), (1, CM)])
            em.waits("pool", tmax(gate))
            ld = em.op("pool", lambda ep, s=src, d=wstage: ep.dma_start(d, s), inc=f"cw{widx}", amt=16)
            for ic in range(NCH):
                em.waits("pe", tmax(ld, ev, self.const_tok))
                tk = None
                for oc in range(NCH):
                    tk = em.op("pe", lambda ep, oc=oc, ic=ic:
                               ep.transpose(trpb[:, oc * 128:oc * 128 + 128],
                                            wstage[:, oc * CM + ic * 128: oc * CM + ic * 128 + 128],
                                            self.identb[:, :]),
                               inc="c_pe" if oc == NCH - 1 else None)
                em.waits("dve", tmax(tk, gate))
                ev = em.op("dve", lambda ep: ep.tensor_copy(wtst[:, :], trpb[:, 0:1024]), inc="c_dve")
                em.waits("sp", [ev])
                gate = em.op("sp", lambda ep, d=self.s_wT[widx, ic * 128:ic * 128 + 128, :]:
                             ep.dma_start(d, wtst[:, :]), inc="cws", amt=16)
        self.wT_tok = ("cws", em.cnt["cws"])
        self.rreg("big2", self.wT_tok)
        self.rreg("big3", self.wT_tok)
        self.rreg("PC0", ev)
        self.rreg("PC1", ev)

    def _seq(self, s):
        self._zero_guards()
        self._ln_stream(self.x[s * T:(s + 1) * T, :], None)
        if self._dbg(0):
            return
        for l in range(NL):
            self._attn(l)
            if self._dbg(1 + 2 * l):
                return
            self._ffn(l)
            if self._dbg(2 + 2 * l):
                return
        self._assemble_x(self.x[s * T:(s + 1) * T, :], True)
        self._outer()
        if self._dbg(100):
            return
        self._park_x2()
        self._zero_guards()
        self._ln_stream(None, self.xt3())
        if self._dbg(101):
            return
        for l in range(NL):
            self._attn(l)
            if self._dbg(102 + 2 * l):
                return
            self._ffn(l)
            if self._dbg(103 + 2 * l):
                return
        self._assemble_x(self.s_x2[s * T:(s + 1) * T, :], False)

    def _dbg(self, sid):
        if self.dbg_cut is None:
            return False
        if self.seq == self.dbg_cut[0] and sid == self.dbg_cut[1]:
            em = self.em
            if sid == 100:
                em.waits("sp", tmax(self.xt_tok))
                em.op("sp", lambda ep: ep.dma_start(self.dbg[:, 0:16 * CM], self.big2[:, :]),
                      inc="d_sp", amt=16)
            else:
                em.waits("sp", tmax(self.y_tok))
                em.op("sp", lambda ep: ep.dma_start(self.dbg, self.big1[:, :]), inc="d_sp", amt=16)
            self.dbg_done = True
            return True
        return False

    def _zero_guards(self):
        em = self.em
        y3 = self.y3()
        em.waits("dve", tmax(self.y_tok, self.rgate("big1")))
        self.y_tok = [em.op("dve", lambda ep: ep.memset(y3[:, :, 0:1], 0.0), inc="c_dve")]

    def _ln_stream(self, hbm_src, xt3):
        em = self.em
        y3 = self.y3()
        prev = None
        for tt in range(NTT):
            if hbm_src is not None:
                em.waits("sp", self.rgate("x0t"))
                ld = em.op("sp", lambda ep, s=hbm_src[tt * 128:(tt + 1) * 128, :]:
                           ep.dma_start(self.x0t[:, :], s), inc="cx0", amt=16)
                xin = self.x0t[:, :]
                gate = [ld]
            else:
                xin = xt3[:, tt, :]
                gate = tmax(self.xt_tok)
            rstd, nm = self._tm_stats(xin, gate)
            t7 = em.op("dve", lambda ep, xin=xin, rstd=rstd, nm=nm: ep.tensor_scalar(
                self.ytm[:, :], xin, rstd, nm, ALU.mult, ALU.add), inc="c_dve")
            self.rreg("col1", t7)
            if hbm_src is not None:
                self.rreg("x0t", t7)
            em.waits("pe", tmax(t7, self.rgate("PC0", "PC1")))
            tk = None
            for ch in range(NCH):
                tk = em.op("pe", lambda ep, ch=ch: ep.transpose(
                    self.PC[:, ch * 128:(ch + 1) * 128],
                    self.ytm[:, ch * 128:(ch + 1) * 128], self.identf[:, :]),
                    inc="c_pe" if ch == NCH - 1 else None)
            self.rreg("ytm", tk)
            self.rreg("big1", tk)
            em.waits("dve", [tk])
            dst = y3[:, :, 1 + tt * 128: 1 + (tt + 1) * 128]
            prev = em.op("dve", lambda ep, dst=dst: ep.tensor_copy(
                dst, self.PC[:, 0:1024].rearrange("p (c n) -> p c n", n=128)), inc="c_dve")
            self.rreg("PC0", prev)
            self.rreg("PC1", prev)
        self.y_tok = tmax(self.y_tok, prev)

    def _tm_stats(self, xin, gate):
        em = self.em
        c = self.col1
        yg = self.rgate("ytm")
        em.waits("dve", tmax(gate, yg, self.rgate("col1", "sqtm")))
        em.waits("act", tmax(gate, yg))
        rs = em.op("dve", lambda ep: ep.reduce_sum(c[:, 0:1], xin, axis=AX.X))
        t_sq = em.op("act", lambda ep: ep.activation(self.sqtm[:, :], xin, AF.Square), inc="c_act")
        em.waits("dve", [t_sq])
        rr = em.op("dve", lambda ep: ep.reduce_sum(c[:, 1:2], self.sqtm[:, :], axis=AX.X), inc="c_dve")
        self.rreg("sqtm", rr)
        em.op("dve", lambda ep: ep.tensor_scalar_mul(c[:, 2:3], c[:, 0:1], 1.0 / CM))
        em.op("dve", lambda ep: ep.tensor_tensor(c[:, 3:4], c[:, 2:3], c[:, 2:3], ALU.mult))
        t3 = em.op("dve", lambda ep: ep.scalar_tensor_tensor(c[:, 4:5], c[:, 1:2], 1.0 / CM, c[:, 3:4],
                                                             ALU.mult, ALU.subtract), inc="c_dve")
        em.waits("act", [t3])
        em.waits("act", [self.eps_tok])
        em.op("act", lambda ep: ep.activation(c[:, 5:6], c[:, 4:5], AF.Ln, bias=self.epsc[:, 0:1]))
        t5 = em.op("act", lambda ep: ep.activation(c[:, 6:7], c[:, 5:6], AF.Exp, scale=-0.5), inc="c_act")
        em.waits("dve", [t5])
        em.op("dve", lambda ep: ep.tensor_scalar(c[:, 7:8], c[:, 2:3], c[:, 6:7], -1.0,
                                                 ALU.mult, ALU.mult))
        return c[:, 6:7], c[:, 7:8]

    def _assemble_x(self, base_hbm, into_xt):
        em = self.em
        y3 = self.y3()
        s = self.seq
        xt3 = self.xt3()
        self.xt_tok = []
        if not into_xt:
            em.waits("sp", [("cpk", em.cnt["cpk"])])
        for tt in range(NTT):
            em.waits("sp", self.rgate("x0t"))
            ld = em.op("sp", lambda ep, src=base_hbm[tt * 128:(tt + 1) * 128, :]:
                       ep.dma_start(self.x0t[:, :], src), inc="cx0", amt=16)
            em.waits("pe", tmax(self.y_tok, self.rgate("PC0", "PC1")))
            tk = None
            for ch in range(NCH):
                src = y3[:, ch, 1 + tt * 128:1 + (tt + 1) * 128]
                tk = em.op("pe", lambda ep, src=src, ch=ch: ep.transpose(
                    self.PC[:, ch * 128:(ch + 1) * 128], src, self.identf[:, :]),
                    inc="c_pe" if ch == NCH - 1 else None)
            self.rreg("big1", tk)
            em.waits("dve", [tk, ld])
            if into_xt:
                dst = xt3[:, tt, :].rearrange("p (c n) -> p c n", n=128)
                em.waits("dve", self.rgate(("xt", tt)))
            else:
                dst = self.x0t[:, :].rearrange("p (c n) -> p c n", n=128)
            tadd = em.op("dve", lambda ep, dst=dst: ep.scalar_tensor_tensor(
                dst, self.PC[:, 0:1024].rearrange("p (c n) -> p c n", n=128), 0.5,
                self.x0t[:, :].rearrange("p (c n) -> p c n", n=128), ALU.mult, ALU.add), inc="c_dve")
            self.rreg("PC0", tadd)
            self.rreg("PC1", tadd)
            self.rreg("x0t", tadd)
            if into_xt:
                self.xt_tok.append(tadd)
            else:
                em.waits("sp", [tadd])
                st = em.op("sp", lambda ep, d=self.out[s * T + tt * 128:s * T + (tt + 1) * 128, :]:
                           ep.dma_start(d, self.x0t[:, :]), inc="d_sp", amt=16)
                self.rreg("x0t", st)

    def _park_x2(self):
        em = self.em
        s = self.seq
        xt3 = self.xt3()
        em.waits("sp", tmax(self.xt_tok))
        self.park_tok = []
        for tt in range(NTT):
            src = xt3[:, tt, :]
            st = em.op("sp", lambda ep, src=src, d=self.s_x2[s * T + tt * 128:s * T + (tt + 1) * 128, :]:
                       ep.dma_start(d, src), inc="cpk", amt=16)
            self.park_tok.append(st)
            self.rreg(("xt", tt), st)
            self.rreg("big2", st)

    def _attn(self, l):
        em = self.em
        y3 = self.y3()
        zb3 = self.b2bf(0, NCH * 513).rearrange("p (c n) -> p c n", n=513)
        o = NCH * 513
        k2r = self.b2bf(o, o + NCH * 1024); o += NCH * 1024
        q2r = self.b2bf(o, o + NCH * 512); o += NCH * 512
        qraw = self.b2bf(o, o + NCH * 512); o += NCH * 512
        kraw = self.b2bf(o, o + NCH * 512); o += NCH * 512
        vtm = self.b2bf(o, o + 6 * 1040); o += 6 * 1040
        qp = self.b2bf(o, o + 512); o += 512
        kp = self.b2bf(o, o + 512); o += 512
        sqs = self.b2bf(o, o + 512); o += 512
        assert o <= 32768, o
        v65 = vtm.rearrange("p (sh n) -> p sh n", n=65)
        qkvp = self.big3[:, 0:4 * NCH * 128]
        wop = self.big3[:, 4 * NCH * 128:6 * NCH * 128]
        vw = self.big3[:, 6 * NCH * 128:6 * NCH * 128 + NCH * CM]
        PT = self.big3[:, 14 * 1024:15 * 1024]
        attnU = self.big3[:, 15 * 1024:15 * 1024 + NCH * 512]
        assert 15 * 1024 + NCH * 512 <= 20480

        big2_gate = self.rgate("big2")
        big3_gate = self.rgate("big3")
        em.waits("sp", tmax(self.conv_tok[l], big3_gate, self.rgate("vw")))
        for vcc in range(NCH):
            vsrc = reap(self.s_wqkv[l, 16 + vcc], [(128, 128), (16384, NCH), (1, 128)])
            vwdst = AP(vw.tensor, vw.offset + vcc * 128, [[vw.ap[0][0], 128], [CM, NCH], [1, 128]])
            vw_tok = em.op("sp", lambda ep, vsrc=vsrc, vwdst=vwdst: ep.dma_start(vwdst, vsrc),
                           inc="cvw", amt=16)
        em.waits("dve", tmax(big2_gate, self.rgate("vtm")))
        ones_tok = em.op("dve", lambda ep: ep.memset(v65[:, :, 64:65], 1.0), inc="c_dve")

        y_upd = []
        pidx = 0
        for tc in range(NTC):
            t0 = tc * TC
            em.waits("dve", tmax(self.y_tok, self.rgate("zb"), big2_gate))
            big2_gate = None
            if tc == 0:
                zb_tok = em.op("dve", lambda ep: ep.tensor_copy(
                    zb3[:, :, 0:513], y3[:, :, 0:513]), inc="c_dve")
            else:
                em.op("dve", lambda ep, t0=t0: ep.tensor_copy(
                    zb3[:, :, 1:513], y3[:, :, t0 + 1:t0 + 513]))
                zb_tok = em.op("dve", lambda ep: ep.tensor_copy(
                    zb3[:, :, 0:1], self.bsave[:, :].rearrange("p (c n) -> p c n", n=1)), inc="c_dve")
            if tc < NTC - 1:
                bs = em.op("dve", lambda ep, t0=t0: ep.tensor_copy(
                    self.bsave[:, :].rearrange("p (c n) -> p c n", n=1),
                    y3[:, :, t0 + TC:t0 + TC + 1]), inc="c_dve")

            em.waits("sp", self.rgate(("tabr", tc % 2)))
            tsrc = self.d_c_tabs.rearrange("p (k t) -> p k t", t=T)[:, :, t0:t0 + TC]
            tslot = (tc % 2) * 4 * TC
            tab_tok = em.op("sp", lambda ep, tsrc=tsrc, tslot=tslot: ep.dma_start(
                self.tabs[:, tslot:tslot + 4 * TC].rearrange("p (k t) -> p k t", t=TC), tsrc),
                inc=f"tab{tc % 2}", amt=16)

            def zsl(c):
                off = 1 if c < 4 else 0
                return zb3[:, c, off:off + TC]

            def ztile(c, ti):
                off = 1 if c < 4 else 0
                return zb3[:, c, off + ti * 128:off + ti * 128 + 128]

            qk_done = {}
            sq_by = {}
            sqslots = [sqs, qp]
            for oc in range(17):
                if oc < 16:
                    slot = pidx % 4
                    pidx += 1
                    em.waits("sp", self.rgate(("pc", slot)))
                    src = reap(self.s_wqkv[l, oc], [(128, 128), (128 * 128, NCH), (1, 128)])
                    dst = qkvp[:, slot * NCH * 128:(slot + 1) * NCH * 128]
                    ptok = em.op("sp", lambda ep, s=src, d=dst: ep.dma_start(d, s),
                                 inc=f"qkp{slot}", amt=16)
                    reg = oc % 2
                    pk = "PA" + str(reg)
                    em.waits("pe", tmax(ptok, zb_tok, self.rgate(pk)))
                    mm = None
                    for sc in range(NCH):
                        mm = em.op("pe", lambda ep, sc=sc, dst=dst, reg=reg: ep.matmul(
                            self.PA[:, reg * 512:reg * 512 + 512], dst[:, sc * 128:(sc + 1) * 128],
                            zsl(sc), start=(sc == 0), stop=(sc == NCH - 1), skip_group_check=True),
                            inc="c_pe" if sc == NCH - 1 else None)
                    self.rreg(("pc", slot), mm)
                    self.rreg("zb", mm)
                    raw = qraw if oc < 8 else kraw
                    c = oc % 8
                    em.waits("dve", tmax(mm, self.rgate(("raw", oc < 8, c))))
                    ev = em.op("dve", lambda ep, raw=raw, c=c, reg=reg: ep.tensor_copy(
                        raw[:, c * 512:(c + 1) * 512], self.PA[:, reg * 512:reg * 512 + 512]), inc="c_dve")
                    self.rreg(pk, ev)
                    sslot = oc % 2
                    em.waits("act", tmax(ev, self.rgate(("sqs", sslot))))
                    sq = em.op("act", lambda ep, raw=raw, c=c, sslot=sslot: ep.activation(
                        sqslots[sslot][:, 0:512], raw[:, c * 512:(c + 1) * 512],
                        AF.Square), inc="c_act")
                    self.rreg(("raw", oc < 8, c), sq)
                    sq_by[oc] = (sq, sslot)
                po = oc - 1
                if po >= 0:
                    sqtok, sslot = sq_by.pop(po)
                    if po == 0:
                        em.waits("pe", tmax(sqtok, self.rgate("PD")))
                    elif po == 8:
                        em.waits("pe", tmax(sqtok, self.rgate("PE")))
                    else:
                        em.waits("pe", [sqtok])
                    pdst = self.PD if po < 8 else self.PE_
                    g = em.op("pe", lambda ep, po=po, pdst=pdst, sslot=sslot: ep.matmul(
                        pdst[0:16, :], self.bselq[:, (po % 8) * 16:(po % 8 + 1) * 16],
                        sqslots[sslot][:, 0:512],
                        start=(po % 8 == 0), stop=(po % 8 == 7), skip_group_check=True),
                        inc="c_pe")
                    self.rreg(("sqs", sslot), g)
                    if po == 7:
                        qk_done["q"] = g
                    if po == 15:
                        qk_done["k"] = g
            em.waits("dve", tmax(qk_done["q"], qk_done["k"], self.rgate("nss")))
            c1 = em.op("dve", lambda ep: ep.tensor_scalar_max(self.nstat_s[:, :], self.PD[0:16, :],
                                                              1e-24), inc="c_dve")
            em.waits("act", [c1])
            em.op("act", lambda ep: ep.activation(self.nstat_s[:, :], self.nstat_s[:, :], AF.Ln))
            c4 = em.op("act", lambda ep: ep.activation(self.qinv16, self.nstat_s[:, :], AF.Exp,
                                                       scale=-0.5), inc="c_act")
            em.waits("dve", [c4])
            c2 = em.op("dve", lambda ep: ep.tensor_scalar_max(self.nstat_s[:, :], self.PE_[0:16, :],
                                                              1e-24), inc="c_dve")
            self.rreg("PD", c1)
            self.rreg("PE", c2)
            em.waits("act", [c2])
            em.op("act", lambda ep: ep.activation(self.nstat_s[:, :], self.nstat_s[:, :], AF.Ln))
            c6 = em.op("act", lambda ep: ep.activation(self.kinv16, self.nstat_s[:, :], AF.Exp,
                                                       scale=-0.5), inc="c_act")
            self.rreg("nss", c6)
            em.waits("dve", self.rgate(("sqs", 0), ("sqs", 1)))
            rot_pend = None

            def emit_vtm(ti):
                tile = tc * 4 + ti
                slot = tile % 6
                em.waits("pe", tmax(vw_tok, zb_tok, ones_tok, self.rgate("PC0", "PC1")))
                mm = None
                for sc in range(NCH):
                    for g2 in range(2):
                        mm = em.op("pe", lambda ep, sc=sc, g2=g2, ti=ti: ep.matmul(
                            self.PC[:, g2 * 512:(g2 + 1) * 512], ztile(sc, ti),
                            vw[:, sc * CM + g2 * 512: sc * CM + (g2 + 1) * 512],
                            start=(sc == 0), stop=(sc == NCH - 1), skip_group_check=True),
                            inc="c_pe" if (sc == NCH - 1 and g2 == 1) else None)
                self.rreg("zb", mm)
                self.rreg("vw", mm)
                em.waits("dve", tmax(mm, self.rgate(("vt", slot))))
                dstv = v65[:, slot * 16:(slot + 1) * 16, 0:64]
                ev = em.op("dve", lambda ep, dstv=dstv: ep.tensor_copy(
                    dstv, self.PC[:, 0:1024].rearrange("p (h n) -> p h n", n=64)), inc="c_dve")
                self.rreg("PC0", ev)
                self.rreg("PC1", ev)
                self.wr[("vt", slot)] = ev

            def emit_rot(c, qk, st):
                pbuf = qp if qk == 0 else kp
                tab_c = tslot + (0 if qk == 0 else 2 * TC)
                tab_s = tslot + (TC if qk == 0 else 3 * TC)
                pbb = qk * 512
                em.waits("pe", tmax(st, self.rgate("PB" + str(qk))))
                pm = em.op("pe", lambda ep, pbuf=pbuf, pbb=pbb: ep.matmul(
                    self.PB[:, pbb:pbb + 512], self.prot[:, :], pbuf[:, 0:512],
                    start=True, stop=True), inc="c_pe")
                self.rreg(("pp", qk), pm)
                if qk == 0:
                    dsl = q2r[:, c * 512:(c + 1) * 512]
                    wkey = ("q2", c)
                    scr, skey = self.scrA, "scrA"
                else:
                    dsl = k2r[:, c * 1024 + (tc % 2) * 512: c * 1024 + (tc % 2) * 512 + 512]
                    wkey = ("k2", c, tc % 2)
                    scr, skey = self.scrB, "scrB"
                em.waits("pool", tmax(st, tab_tok, self.rgate(wkey)))
                tt1 = em.op("pool", lambda ep, dsl=dsl, pbuf=pbuf, tab_c=tab_c:
                      ep.tensor_tensor(dsl, pbuf[:, 0:512],
                                       self.tabs[:, tab_c:tab_c + TC], ALU.mult))
                em.waits("dve", tmax(pm, tab_tok, self.rgate(skey)))
                stt = em.op("dve", lambda ep, tab_s=tab_s, scr=scr, pbb=pbb: ep.scalar_tensor_tensor(
                    scr[:, :], self.PB[:, pbb:pbb + 512], 1.0,
                    self.tabs[:, tab_s:tab_s + TC], ALU.mult, ALU.mult))
                em.waits("pool", [stt])
                a3 = em.op("pool", lambda ep, dsl=dsl, scr=scr: ep.tensor_tensor(
                    dsl, dsl, scr[:, :], ALU.add), inc="c_gp")
                self.rreg(("tabr", tc % 2), a3)
                self.rreg(("pp", qk), a3)
                self.rreg("PB" + str(qk), stt)
                self.rreg(skey, a3)
                if qk == 0:
                    self.rreg(("sqs", 1), tt1)
                    self.rreg(("sqs", 1), pm)
                self.wr[wkey] = a3

            for c in range(NCH):
                for qk in range(2):
                    raw = qraw if qk == 0 else kraw
                    inv16 = self.qinv16 if qk == 0 else self.kinv16
                    gain = self.qs_pp if qk == 0 else self.ks_pp
                    ntok = c4 if qk == 0 else c6
                    pbuf = qp if qk == 0 else kp
                    bcdst = self.PE_ if qk == 0 else self.PD
                    bkey = "PE" if qk == 0 else "PD"
                    em.waits("pe", tmax(ntok, self.rgate(bkey)))
                    bc = em.op("pe", lambda ep, c=c, inv16=inv16, bcdst=bcdst: ep.matmul(
                        bcdst[:, :], self.selpat[:, c * 128:(c + 1) * 128], inv16,
                        start=True, stop=True), inc="c_pe")
                    self.rreg("nss", bc)
                    em.waits("dve", tmax(bc, self.rgate(("pp", qk)),
                                         self.rgate(("raw", qk == 0, c))))
                    st = em.op("dve", lambda ep, c=c, raw=raw, gain=gain, pbuf=pbuf, bcdst=bcdst:
                               ep.scalar_tensor_tensor(pbuf[:, 0:512],
                                                       raw[:, c * 512:(c + 1) * 512],
                                                       gain[:, l:l + 1],
                                                       bcdst[:, :],
                                                       ALU.mult, ALU.mult), inc="c_dve")
                    self.rreg(bkey, st)
                    if rot_pend is not None:
                        emit_rot(*rot_pend)
                    rot_pend = (c, qk, st)
                    if (2 * c + qk) % 4 == 3:
                        emit_vtm((2 * c + qk) // 4)
            emit_rot(*rot_pend)
            rot_pend = None
            em.waits("pe", self.rgate("PB0", "PB1"))
            PT1 = self.tmu[:, :].bitcast(BF16)[:, 0:1024]
            ptbufs = [PT, PT1]
            if tc == 0:
                self.tmu_gate = tmax(self.rgate("sqtm"), self.rgate("ytm"))
            its = []
            for wloc in range(DBG_NW):
                w = 2 * tc + wloc
                w0 = (w == 0)
                jts = [2 * w - 2, 2 * w - 1, 2 * w, 2 * w + 1]
                live = [(idx, jt) for idx, jt in enumerate(jts) if not (w0 and idx < 2)]
                for h in range(DBG_NH):
                    its.append((wloc, w0, live, h))
            st_mk = {}
            st_dr = {}
            den_done = None

            def emit_sim(i):
                wloc, w0, live, h = its[i]
                par = i % 2
                ptb = ptbufs[par]
                ch, hf = h // 2, h % 2
                simp = self.PC if par == 0 else self.PA
                sk0, sk1 = ("PC0", "PC1") if par == 0 else ("PA0", "PA1")
                em.waits("pe", tmax(self.wr.get(("q2", ch)), self.const_tok,
                                    *(self.wr.get(("k2", ch, (jt // 4) % 2)) for _, jt in live)))
                em.waits("pe", self.rgate(sk0, sk1))
                qrhs = q2r[hf * 64:hf * 64 + 64, ch * 512 + wloc * 256: ch * 512 + (wloc + 1) * 256]
                if w0:
                    em.op("pe", lambda ep, simp=simp: ep.matmul(
                        simp[:, 512:1024], self.identb[:, :], self.maskb0[:, :],
                        start=True, stop=False, skip_group_check=True))
                else:
                    for half in range(2):
                        em.op("pe", lambda ep, simp=simp, half=half: ep.matmul(
                            simp[:, half * 512:half * 512 + 512], self.identb[:, :],
                            self.maskb[:, half * 512:half * 512 + 512],
                            start=True, stop=False, skip_group_check=True))
                mmt = None
                for idx, jt in live:
                    ksl = k2r[hf * 64:hf * 64 + 64,
                              ch * 1024 + ((jt // 4) % 2) * 512 + (jt % 4) * 128:
                              ch * 1024 + ((jt // 4) % 2) * 512 + (jt % 4) * 128 + 128]
                    if idx == 0:
                        i0, i1 = 0, 128
                    elif idx == 3:
                        i0, i1 = 128, 256
                    else:
                        i0, i1 = 0, 256
                    mmt = em.op("pe", lambda ep, ksl=ksl, qrhs=qrhs, idx=idx, i0=i0, i1=i1,
                                simp=simp: ep.matmul(
                                    simp[:, idx * 256 + i0: idx * 256 + i1],
                                    ksl, qrhs[:, i0:i1], start=False, stop=True,
                                    skip_group_check=True),
                                inc="c_pe" if (idx, jt) == live[-1] else None)
                    self.rreg(("q2", ch), mmt)
                    self.rreg(("k2", ch, (jt // 4) % 2), mmt)
                em.waits("act", tmax(mmt, self.rgate(("PTb", par)), self.tmu_gate))
                self.tmu_gate = None
                if w0:
                    exa = None
                    exb = em.op("act", lambda ep, simp=simp, ptb=ptb: ep.activation(
                        ptb[:, 512:1024], simp[:, 512:1024], AF.Exp), inc="c_act")
                else:
                    exa = em.op("act", lambda ep, simp=simp, ptb=ptb: ep.activation(
                        ptb[:, 0:512], simp[:, 0:512], AF.Exp), inc="c_act")
                    exb = em.op("act", lambda ep, simp=simp, ptb=ptb: ep.activation(
                        ptb[:, 512:1024], simp[:, 512:1024], AF.Exp), inc="c_act")
                self.rreg(sk0, exb)
                self.rreg(sk1, exb)
                st_mk[i] = (exa, exb)

            def emit_pv(i):
                wloc, w0, live, h = its[i]
                par = i % 2
                ptb = ptbufs[par]
                ch, hf = h // 2, h % 2
                exa, exb = st_mk.pop(i)
                em.waits("pe", tmax(exa if exa is not None else exb,
                                    self.rgate(("pv", par)),
                                    *(self.wr.get(("vt", jt % 6)) for _, jt in live)))
                pvr = par * 256
                mmv = None
                for n, (idx, jt) in enumerate(live):
                    if idx >= 2:
                        em.waits("pe", [exb])
                    mmv = em.op("pe", lambda ep, jt=jt, idx=idx, pvr=pvr, h=h, ptb=ptb,
                                st=(n == 0), sp=(n == len(live) - 1): ep.matmul(
                        self.PB[0:65, pvr:pvr + 256], v65[:, (jt % 6) * 16 + h, :],
                        ptb[:, idx * 256:(idx + 1) * 256],
                        start=st, stop=sp, skip_group_check=True),
                        inc="c_pe" if n == len(live) - 1 else None)
                    self.rreg(("vt", jt % 6), mmv)
                self.rreg(("PTb", par), mmv)
                if par == 1:
                    self.rreg("ytm", mmv)
                em.waits("dve", tmax(mmv, self.rgate(("aU", ch, hf, wloc))))
                du = em.op("dve", lambda ep, hf=hf, ch=ch, wloc=wloc, pvr=pvr: ep.tensor_copy(
                    attnU[hf * 64:hf * 64 + 64,
                          ch * 512 + wloc * 256: ch * 512 + (wloc + 1) * 256],
                    self.PB[0:64, pvr:pvr + 256]), inc="c_dve")
                em.waits("dve", self.rgate(("dr", par)))
                dr = em.op("dve", lambda ep, pvr=pvr, drslot=par: ep.tensor_copy(
                    self.denrow[0:1, drslot * 256:(drslot + 1) * 256],
                    self.PB[64:65, pvr:pvr + 256]), inc="c_dve")
                self.rreg(("pv", par), du)
                self.rreg(("pv", par), dr)
                self.wr[("aU", ch, hf, wloc)] = du
                st_dr[i] = dr

            def emit_den(i):
                wloc, w0, live, h = its[i]
                par = i % 2
                dr = st_dr.pop(i)
                em.waits("pe", [dr])
                dn = em.op("pe", lambda ep, h=h, wloc=wloc, drslot=par: ep.matmul(
                    self.PB[0:16, 512 + wloc * 256:512 + (wloc + 1) * 256],
                    self.unit16[:, h * 16:(h + 1) * 16],
                    self.denrow[0:1, drslot * 256:(drslot + 1) * 256],
                    start=(h == 0), stop=(i == len(its) - 1), skip_group_check=True),
                    inc="c_pe")
                self.rreg(("dr", par), dn)
                return dn

            for i in range(len(its) + 1):
                if i < len(its):
                    emit_sim(i)
                if i >= 1:
                    den_done = emit_den(i - 1)
                if i < len(its):
                    emit_pv(i)
            em.waits("act", [den_done])
            em.op("act", lambda ep: ep.activation(self.nstat_s[:, :], self.PB[0:16, 512:1024], AF.Ln))
            iv = em.op("act", lambda ep: ep.activation(self.invden16[:, :], self.nstat_s[:, :],
                                                       AF.Exp, scale=-1.0), inc="c_act")
            self.rreg("PB1", iv)
            last_nm = None
            for c in range(NCH):
                bdst = self.PE_ if c % 2 == 0 else self.PD
                bkey = "PE" if c % 2 == 0 else "PD"
                em.waits("pe", tmax(iv, self.rgate(bkey)))
                bc = em.op("pe", lambda ep, c=c, bdst=bdst: ep.matmul(
                    bdst[:, :], self.selpat[:, c * 128:(c + 1) * 128], self.invden16[:, :],
                    start=True, stop=True), inc="c_pe")
                em.waits("dve", tmax(bc, self.wr.get(("aU", c, 0, 0)), self.wr.get(("aU", c, 0, 1)),
                                     self.wr.get(("aU", c, 1, 0)), self.wr.get(("aU", c, 1, 1))))
                nm = em.op("dve", lambda ep, c=c, bdst=bdst: ep.tensor_tensor(
                    attnU[:, c * 512:(c + 1) * 512], attnU[:, c * 512:(c + 1) * 512],
                    bdst[:, :], ALU.mult), inc="c_dve")
                self.rreg(bkey, nm)
                self.wr[("aN", c)] = nm
                last_nm = nm
            for oc in range(NCH):
                wslot = oc % 2
                em.waits("sp", tmax(self.conv_tok[l], self.rgate(("wo", wslot))))
                wsrc = reap(self.s_wo[l, oc], [(128, 128), (128 * 128, NCH), (1, 128)])
                wdst = wop[:, wslot * NCH * 128:(wslot + 1) * NCH * 128]
                wtok = em.op("sp", lambda ep, s=wsrc, d=wdst: ep.dma_start(d, s),
                             inc=f"wop{wslot}", amt=16)
                reg = oc % 2
                pk = "PA" + str(reg)
                em.waits("pe", tmax(wtok, self.rgate(pk),
                                    *(self.wr.get(("aN", sc)) for sc in range(NCH))))
                mm = None
                for sc in range(NCH):
                    mm = em.op("pe", lambda ep, sc=sc, wdst=wdst, reg=reg: ep.matmul(
                        self.PA[:, reg * 512:reg * 512 + 512], wdst[:, sc * 128:(sc + 1) * 128],
                        attnU[:, sc * 512:(sc + 1) * 512],
                        start=(sc == 0), stop=(sc == NCH - 1), skip_group_check=True),
                        inc="c_pe" if sc == NCH - 1 else None)
                self.rreg(("wo", wslot), mm)
                for sc in range(NCH):
                    self.rreg(("aU", sc, 0, 0), mm)
                em.waits("dve", [mm])
                dsty = y3[:, oc, 1 + t0:1 + t0 + TC]
                up = em.op("dve", lambda ep, dsty=dsty, reg=reg: ep.tensor_tensor(
                    dsty, dsty, self.PA[:, reg * 512:reg * 512 + 512], ALU.add), inc="c_dve")
                self.rreg(pk, up)
                y_upd.append(up)
        self.y_tok = tmax(y_upd)
        self.rreg("big2", tmax(y_upd))
        self.rreg("big3", tmax(y_upd))
        for c in range(NCH):
            for hf in range(2):
                for wl in range(2):
                    self.rd.pop(("aU", c, hf, wl), None)
            self.rd.pop(("q2", c), None)
            for sl in range(2):
                self.rd.pop(("k2", c, sl), None)
        for s2 in range(6):
            for t in self.rd.pop(("vt", s2), []):
                self.rreg("vtm", t)
        self.rreg("vtm", tmax(y_upd))

    def _ffn(self, l):
        em = self.em
        y3 = self.y3()
        zb3 = self.b2bf(0, NCH * 513).rearrange("p (c n) -> p c n", n=513)
        o = NCH * 513
        xn = self.b2bf(o, o + NCH * 512); o += NCH * 512
        hsb = self.b2bf(o, o + NJ * 512); o += NJ * 512
        gel = self.b2bf(o, o + 2 * 512); o += 2 * 512
        assert o <= 32768
        w1p = self.big3[:, 0:4 * 2 * NCH * 128]
        w2p = self.big3[:, 4 * 2 * NCH * 128:4 * 2 * NCH * 128 + 4 * CM]

        big2_gate = self.rgate("big2", "vtm")
        big3_gate = self.rgate("big3", "vw")
        y_upd = []
        self._gbi = getattr(self, "_gbi", 0)
        gbsem = f"cgb{self._gbi % 4}"
        self._gbi += 1
        em.waits("pool", tmax(self.rgate("gb2"), self.const_gp, self.conv_tok[NL - 1]))
        gb = em.op("pool", lambda ep: ep.dma_start(
            self.gb2[:, :], self.d_gb_host[:, l * CM:(l + 1) * CM]), inc=gbsem, amt=16)

        for tc in range(NTC):
            t0 = tc * TC
            em.waits("dve", tmax(self.y_tok, self.rgate("zb"), big2_gate))
            big2_gate = None
            if tc == 0:
                zb_tok = em.op("dve", lambda ep: ep.tensor_copy(
                    zb3[:, :, 0:513], y3[:, :, 0:513]), inc="c_dve")
            else:
                em.op("dve", lambda ep, t0=t0: ep.tensor_copy(
                    zb3[:, :, 1:513], y3[:, :, t0 + 1:t0 + 513]))
                zb_tok = em.op("dve", lambda ep: ep.tensor_copy(
                    zb3[:, :, 0:1], self.bsave[:, :].rearrange("p (c n) -> p c n", n=1)), inc="c_dve")
            if tc < NTC - 1:
                em.op("dve", lambda ep, t0=t0: ep.tensor_copy(
                    self.bsave[:, :].rearrange("p (c n) -> p c n", n=1),
                    y3[:, :, t0 + TC:t0 + TC + 1]))

            def zsl(c):
                off = 1 if c < 4 else 0
                return zb3[:, c, off:off + TC]

            em.waits("pe", tmax(zb_tok, self.rgate("PD")))
            sq_t = None
            for c in range(NCH):
                em.op("pe", lambda ep, c=c: ep.matmul(
                    self.PD[0:1, :], self.onescol[:, :], zsl(c),
                    start=(c == 0), stop=(c == NCH - 1), skip_group_check=True))
                em.waits("act", tmax(zb_tok, self.rgate(("bcs", c % 2))))
                sq = em.op("act", lambda ep, c=c: ep.activation(
                    self.bcq_sb[:, (c % 2) * TC:(c % 2 + 1) * TC], zsl(c), AF.Square), inc="c_act")
                self.rreg("zb", sq)
                if c == 0:
                    em.waits("pe", tmax(sq, self.rgate("PE")))
                else:
                    em.waits("pe", [sq])
                sq_t = em.op("pe", lambda ep, c=c: ep.matmul(
                    self.PE_[0:1, :], self.onescol[:, :],
                    self.bcq_sb[:, (c % 2) * TC:(c % 2 + 1) * TC],
                    start=(c == 0), stop=(c == NCH - 1), skip_group_check=True),
                    inc="c_pe")
                self.rreg(("bcs", c % 2), sq_t)
                self.rreg("zb", sq_t)
            em.waits("dve", tmax(sq_t, self.rgate("x0t")))
            em.op("dve", lambda ep: ep.tensor_scalar_mul(self.mus, self.PD[0:1, :], 1.0 / CM))
            t_sv = em.op("dve", lambda ep: ep.tensor_tensor(self.statv, self.mus, self.mus,
                                                            ALU.mult), inc="c_dve")
            v1 = em.op("dve", lambda ep: ep.scalar_tensor_tensor(
                self.statv, self.PE_[0:1, :], 1.0 / CM, self.statv,
                ALU.mult, ALU.subtract), inc="c_dve")
            self.rreg("PD", v1)
            self.rreg("PE", v1)
            em.waits("act", [v1])
            em.waits("act", [self.eps_tok])
            em.op("act", lambda ep: ep.activation(self.statl, self.statv, AF.Ln,
                                                  bias=self.epsc[0:1, 0:1]))
            r1 = em.op("act", lambda ep: ep.activation(self.arhs[:, :], self.statl,
                                                       AF.Exp, scale=-0.5), inc="c_act")
            em.waits("dve", [r1])
            cr = em.op("dve", lambda ep: ep.scalar_tensor_tensor(
                self.c2rhs[0:1, :], self.PD[0:1, :], -1.0 / CM, self.arhs[:, :],
                ALU.mult, ALU.mult), inc="c_dve")
            self.rreg("PD", cr)
            self.rreg("x0t", t_sv)
            em.waits("pe", tmax(r1, cr, gb, self.rgate("PE")))
            at = em.op("pe", lambda ep: ep.matmul(self.PE_[:, :], self.ones1[:, :],
                                                  self.arhs[:, :], start=True, stop=True), inc="c_pe")
            xn_tok = []
            for c in range(NCH):
                pk = "PB" + str(c % 2)
                em.waits("pe", self.rgate(pk))
                c2 = em.op("pe", lambda ep, c=c: ep.matmul(
                    self.PB[:, (c % 2) * 512:(c % 2) * 512 + 512],
                    self.gb2[:, c * 128:(c + 1) * 128], self.c2rhs[:, :],
                    start=True, stop=True), inc="c_pe")
                em.waits("dve", tmax(at, c2, self.rgate(("xn", c))))
                p1 = em.op("dve", lambda ep, c=c: ep.tensor_tensor(
                    self.t1s[:, :], zsl(c), self.PE_[:, :], ALU.mult),
                    inc="c_dve")
                self.rreg("zb", p1)
                p2 = em.op("dve", lambda ep, c=c: ep.scalar_tensor_tensor(
                    xn[:, c * 512:(c + 1) * 512], self.t1s[:, :],
                    self.g_pp[:, l * NCH + c:l * NCH + c + 1],
                    self.PB[:, (c % 2) * 512:(c % 2) * 512 + 512],
                    ALU.mult, ALU.add), inc="c_dve")
                self.rreg(pk, p2)
                self.rreg("PE", p2)
                xn_tok.append(p2)
                self.wr[("xn", c)] = p2
            self.rreg("gb2", xn_tok[-1])
            h_tok = [None] * NJ
            for j in range(NJ):
                wj = 128 if j < NJ - 1 else WJL
                slot = j % 4
                em.waits("sp", tmax(self.conv_tok[l], self.rgate(("w1", slot)), big3_gate))
                big3_gate = None
                wsrc = reap(self.s_w1[l, j], [(128, 128), (NCH * 128 * 128, 2), (128 * 128, NCH), (1, wj)])
                wdst = w1p[:, slot * 2 * NCH * 128:(slot + 1) * 2 * NCH * 128]
                wldst = reap(wdst, [(wdst.ap[0][0], 128), (NCH * 128, 2), (128, NCH), (1, wj)])
                wtok = em.op("sp", lambda ep, s=wsrc, d=wldst: ep.dma_start(d, s),
                             inc=f"w1p{slot}", amt=16)
                em.waits("pe", tmax(wtok, *xn_tok))
                em.waits("pe", self.rgate("PC" + str(j % 2), "PA" + str(j % 2)))
                mma = mmg = None
                for sc in range(NCH):
                    mma = em.op("pe", lambda ep, sc=sc, wdst=wdst, wj=wj, j=j: ep.matmul(
                        self.PC[0:wj, (j % 2) * 512:(j % 2) * 512 + 512],
                        wdst[:, sc * 128:sc * 128 + wj], xn[:, sc * 512:(sc + 1) * 512],
                        start=(sc == 0), stop=(sc == NCH - 1), skip_group_check=True))
                for sc in range(NCH):
                    mmg = em.op("pe", lambda ep, sc=sc, wdst=wdst, wj=wj, j=j: ep.matmul(
                        self.PA[0:wj, (j % 2) * 512:(j % 2) * 512 + 512],
                        wdst[:, NCH * 128 + sc * 128: NCH * 128 + sc * 128 + wj],
                        xn[:, sc * 512:(sc + 1) * 512],
                        start=(sc == 0), stop=(sc == NCH - 1), skip_group_check=True),
                        inc="c_pe" if sc == NCH - 1 else None)
                self.rreg(("w1", slot), mmg)
                for c in range(NCH):
                    self.rreg(("xn", c), mmg)
                em.waits("act", tmax(mmg, self.rgate(("gel", j % 2))))
                gl = em.op("act", lambda ep, j=j, wj=wj: ep.activation(
                    gel[0:wj, (j % 2) * 512:(j % 2) * 512 + 512],
                    self.PA[0:wj, (j % 2) * 512:(j % 2) * 512 + 512],
                    AF.Gelu, bias=self.b1g_pp[0:wj, l * NJ + j:l * NJ + j + 1]), inc="c_act")
                self.rreg("PA" + str(j % 2), gl)
                em.waits("dve", tmax(gl, self.rgate(("h", j))))
                ht = em.op("dve", lambda ep, j=j, wj=wj: ep.scalar_tensor_tensor(
                    hsb[0:wj, j * 512:j * 512 + 512],
                    self.PC[0:wj, (j % 2) * 512:(j % 2) * 512 + 512],
                    self.b1a_pp[0:wj, l * NJ + j:l * NJ + j + 1],
                    gel[0:wj, (j % 2) * 512:(j % 2) * 512 + 512], ALU.add, ALU.mult), inc="c_dve")
                h_tok[j] = ht
                self.rreg("PC" + str(j % 2), ht)
                self.rreg(("gel", j % 2), ht)
                self.wr[("h", j)] = ht
            w2reg = [(self.PA, 0, "PA0"), (self.PA, 512, "PA1"),
                     (self.PB, 0, "PB0"), (self.PB, 512, "PB1"),
                     (self.PC, 0, "PC0"), (self.PC, 512, "PC1"),
                     (self.PD, 0, "PD"), (self.PE_, 0, "PE")]
            em.waits("pe", tmax(h_tok))
            em.waits("pe", self.rgate(*[k for _, _, k in w2reg]))
            w2_toks = {}

            def w2_load(jj):
                wjx = 128 if jj < NJ - 1 else WJL
                slot = jj % 4
                em.waits("sp", tmax(self.conv_tok[l], self.rgate(("w2", slot))))
                wsrc = self.s_w2[l, jj * 128:jj * 128 + wjx, :]
                wdst = w2p[:, slot * CM:(slot + 1) * CM]
                w2_toks[jj] = em.op("sp", lambda ep, s=wsrc, d=wdst[0:wjx, :]: ep.dma_start(d, s),
                                    inc=f"w2p{slot}", amt=16)

            for jj in range(min(3, NJ)):
                w2_load(jj)
            mmlast = None
            for j in range(NJ):
                if j + 3 < NJ:
                    w2_load(j + 3)
                wj = 128 if j < NJ - 1 else WJL
                wdst = w2p[:, (j % 4) * CM:(j % 4 + 1) * CM]
                em.waits("pe", tmax(w2_toks.pop(j)))
                for oc in range(NCH):
                    pdst, po, pk = w2reg[oc]
                    mmlast = em.op("pe", lambda ep, j=j, wj=wj, oc=oc, po=po, wdst=wdst,
                                   pdst=pdst: ep.matmul(
                                       pdst[:, po:po + 512],
                                       wdst[0:wj, oc * 128:oc * 128 + 128],
                                       hsb[0:wj, j * 512:j * 512 + 512],
                                       start=(j == 0), stop=(j == NJ - 1),
                                       skip_group_check=True),
                                   inc="c_pe" if oc == NCH - 1 else None)
                self.rreg(("w2", j % 4), mmlast)
                if j == NJ - 1:
                    for jj in range(NJ):
                        self.rreg(("h", jj), mmlast)
            em.waits("dve", [mmlast])
            for oc in range(NCH):
                pdst, po, pk = w2reg[oc]
                dsty = y3[:, oc, 1 + t0:1 + t0 + TC]
                up = em.op("dve", lambda ep, dsty=dsty, pdst=pdst, po=po, oc=oc:
                           ep.scalar_tensor_tensor(dsty, pdst[:, po:po + 512],
                                                   self.b2_pp[:, l * NCH + oc:l * NCH + oc + 1],
                                                   dsty, ALU.add, ALU.add), inc="c_dve")
                self.rreg(pk, up)
                y_upd.append(up)
        self.y_tok = tmax(y_upd)
        self.rreg("big2", tmax(y_upd))
        self.rreg("big3", tmax(y_upd))
        for c in range(NCH):
            self.rd.pop(("xn", c), None)
        for j in range(NJ):
            self.rd.pop(("h", j), None)

    def _outer(self):
        em = self.em
        xt3 = self.xt3()
        vFM = self.b1bf(16384, 32768)
        E = self.b1bf(0, 16384)
        wT1 = self.big3[:, 0:NCH * CM]
        wT2 = self.big3[:, NCH * CM:2 * NCH * CM]
        knb = self.big3[:, 2 * NCH * CM:2 * NCH * CM + 2 * CM]
        vtb = self.big3[:, 2 * NCH * CM + 2 * CM:2 * NCH * CM + 4 * CM]
        ctxs = self.t1s[:, :].bitcast(BF16)[:, 0:512]
        knraw = self.knraw_v
        ksumr = self.knraw_v[0:1, :]
        bk_row = self.tabs[0:1, 0:CM]
        bv_row = self.tabs[0:1, CM:2 * CM]
        bp_row = self.tabs[0:1, 2 * CM:3 * CM]
        em.waits("pool", tmax(self.rgate(("tabr", 0)), self.rgate(("tabr", 1)), self.const_gp))
        em.op("pool", lambda ep: ep.dma_start(bk_row, self.d_bk_r), inc="cbr", amt=16)
        em.op("pool", lambda ep: ep.dma_start(bv_row, self.d_bv_r), inc="cbr", amt=16)
        em.op("pool", lambda ep: ep.dma_start(bp_row, self.d_bp_r), inc="cbr", amt=16)
        br_tok = ("cbr", em.cnt["cbr"])

        big3_gate = self.rgate("big3", "vw", "vtm")
        big1_gate = tmax(self.rgate("big1"), self.y_tok)

        trpb = self.PC[:, :].bitcast(BF16)
        vfm_tok = None
        for tt in range(NTT):
            xin = xt3[:, tt, :]
            rstd, nm = self._tm_stats(xin, tmax(self.xt_tok))
            tv = em.op("dve", lambda ep, xin=xin, rstd=rstd, nm=nm: ep.tensor_scalar(
                self.ytmb, xin, rstd, nm, ALU.mult, ALU.add), inc="c_dve")
            self.rreg("col1", tv)
            em.waits("pe", tmax(tv, self.rgate("PC0", "PC1")))
            tk = None
            for ch in range(NCH):
                tk = em.op("pe", lambda ep, ch=ch: ep.transpose(
                    trpb[:, ch * 128:(ch + 1) * 128], self.ytmb[:, ch * 128:(ch + 1) * 128],
                    self.identb[:, :]), inc="c_pe" if ch == NCH - 1 else None)
            self.rreg("ytm", tk)
            em.waits("dve", tmax(tk, big1_gate))
            big1_gate = None
            dst = AP(vFM.tensor, vFM.offset + tt * 128,
                     [[vFM.ap[0][0], 128], [T, NCH], [1, 128]])
            vfm_tok = em.op("dve", lambda ep, dst=dst: ep.tensor_copy(
                dst, trpb[:, 0:1024].rearrange("p (c n) -> p c n", n=128)), inc="c_dve")
            self.rreg("PC0", vfm_tok)
            self.rreg("PC1", vfm_tok)
        em.waits("sp", tmax(self.wT_tok, big3_gate))
        w1tok = em.op("sp", lambda ep: ep.dma_start(
            wT1, reap(self.s_wT[0], [(CM, 128), (128 * CM, NCH), (1, CM)])), inc="cwt1", amt=16)
        em.waits("act", tmax(big1_gate, vfm_tok))
        e_toks = []
        for tc in range(NTC):
            for oc in range(NCH):
                pq = self.PA if oc % 4 < 2 else self.PB
                pk = ("PA" if oc % 4 < 2 else "PB") + str(oc % 2)
                em.waits("pe", tmax(w1tok, vfm_tok, self.rgate(pk)))
                mm = None
                for sc in range(NCH):
                    mm = em.op("pe", lambda ep, sc=sc, oc=oc, tc=tc, pq=pq: ep.matmul(
                        pq[:, (oc % 2) * 512:(oc % 2) * 512 + 512],
                        wT1[:, sc * CM + oc * 128: sc * CM + oc * 128 + 128],
                        vFM[:, sc * T + tc * 512: sc * T + (tc + 1) * 512],
                        start=(sc == 0), stop=(sc == NCH - 1), skip_group_check=True),
                        inc="c_pe" if sc == NCH - 1 else None)
                em.waits("act", [mm])
                ex = em.op("act", lambda ep, oc=oc, tc=tc, pq=pq: ep.activation(
                    E[:, oc * T + tc * 512: oc * T + (tc + 1) * 512],
                    pq[:, (oc % 2) * 512:(oc % 2) * 512 + 512],
                    AF.Exp, bias=self.bq_pp[:, oc:oc + 1]), inc="c_act")
                self.rreg(pk, ex)
                e_toks.append(ex)
        em.waits("sp", tmax(e_toks[-1], self.rgate("wT1rd")))
        w2tok = em.op("sp", lambda ep: ep.dma_start(
            wT2, reap(self.s_wT[1], [(CM, 128), (128 * CM, NCH), (1, CM)])), inc="cwt2", amt=16)
        w3tok = em.op("sp", lambda ep: ep.dma_start(
            wT1, reap(self.s_wT[2], [(CM, 128), (128 * CM, NCH), (1, CM)])), inc="cwt1", amt=16)
        em.waits("pe", self.rgate("PC0", "PC1", "PD"))
        ctx_done = None
        ks_done = None
        kn_tok = vt_tok = None
        for tt in range(NTT):
            for which in range(2):
                wt = wT2 if which == 0 else wT1
                wtk = w2tok if which == 0 else w3tok
                brow = bk_row if which == 0 else bv_row
                pq = self.PA if which == 0 else self.PB
                pk0 = ("PA0", "PA1") if which == 0 else ("PB0", "PB1")
                em.waits("pe", tmax(wtk, vfm_tok, br_tok))
                em.waits("pe", self.rgate(*pk0))
                mm = None
                for g2 in range(2):
                    for sc in range(NCH):
                        em.op("pe", lambda ep, sc=sc, g2=g2, tt=tt, wt=wt, pq=pq: ep.matmul(
                            pq[:, g2 * 512:(g2 + 1) * 512],
                            vFM[:, sc * T + tt * 128: sc * T + tt * 128 + 128],
                            wt[:, sc * CM + g2 * 512: sc * CM + (g2 + 1) * 512],
                            start=(sc == 0), stop=False, skip_group_check=True))
                    mm = em.op("pe", lambda ep, g2=g2, brow=brow, pq=pq: ep.matmul(
                        pq[:, g2 * 512:(g2 + 1) * 512], self.ones1[:, :],
                        brow[:, g2 * 512:(g2 + 1) * 512],
                        start=False, stop=True, skip_group_check=True),
                        inc="c_pe" if g2 == 1 else None)
                if which == 0:
                    em.waits("act", tmax(mm, self.rgate("knraw")))
                    ek = em.op("act", lambda ep: ep.activation(knraw[:, :], self.PA[:, 0:1024],
                                                               AF.Exp), inc="c_act")
                    self.rreg("PA0", ek)
                    self.rreg("PA1", ek)
                    em.waits("dve", [ek])
                    dk = em.op("dve", lambda ep: ep.reduce_sum(
                        self.denk[:, :], knraw.rearrange("p (h n) -> p h n", n=64), axis=AX.X),
                        inc="c_dve")
                    em.waits("act", [dk])
                    em.op("act", lambda ep: ep.activation(self.denk[:, :], self.denk[:, :], AF.Ln))
                    ik = em.op("act", lambda ep: ep.activation(self.denk[:, :], self.denk[:, :],
                                                               AF.Exp, scale=-1.0), inc="c_act")
                    em.waits("dve", tmax(ik, self.rgate(("knb", tt % 2))))
                    kt = None
                    for h in range(H):
                        kt = em.op("dve", lambda ep, h=h, tt=tt: ep.tensor_scalar_mul(
                            knb[:, (tt % 2) * CM + h * 64:(tt % 2) * CM + (h + 1) * 64],
                            knraw[:, h * 64:(h + 1) * 64], self.denk[:, h:h + 1]),
                            inc="c_dve" if h == H - 1 else None)
                    self.rreg("knraw", kt)
                    kn_tok = kt
                else:
                    em.waits("dve", tmax(mm, self.rgate(("vtb", tt % 2))))
                    vt_tok = em.op("dve", lambda ep, tt=tt: ep.tensor_copy(
                        vtb[:, (tt % 2) * CM:(tt % 2 + 1) * CM], self.PB[:, 0:1024]), inc="c_dve")
                    self.rreg("PB0", vt_tok)
                    self.rreg("PB1", vt_tok)
            em.waits("pe", tmax(kn_tok, vt_tok, self.rgate("PC0", "PC1")))
            for h in range(H):
                ctx_done = em.op("pe", lambda ep, h=h, tt=tt: ep.matmul(
                    self.PC[(h % 2) * 64:(h % 2) * 64 + 64, (h // 2) * 64:(h // 2) * 64 + 64],
                    knb[:, (tt % 2) * CM + h * 64:(tt % 2) * CM + (h + 1) * 64],
                    vtb[:, (tt % 2) * CM + h * 64:(tt % 2) * CM + (h + 1) * 64],
                    start=True, stop=True, skip_group_check=True),
                    inc="c_pe" if h == H - 1 else None)
            em.waits("dve", [ctx_done])
            if tt == 0:
                ctx_done = em.op("dve", lambda ep: ep.tensor_copy(
                    self.ctxacc[:, :], self.PC[:, 0:512]), inc="c_dve")
            else:
                ctx_done = em.op("dve", lambda ep: ep.tensor_tensor(
                    self.ctxacc[:, :], self.ctxacc[:, :], self.PC[:, 0:512], ALU.add), inc="c_dve")
            self.rreg("PC0", ctx_done)
            for g2 in range(2):
                ks_done = em.op("pe", lambda ep, g2=g2, tt=tt: ep.matmul(
                    self.PD[0:1, :] if g2 == 0 else self.PE_[0:1, :],
                    self.onescol[:, :],
                    knb[:, (tt % 2) * CM + g2 * 512:(tt % 2) * CM + (g2 + 1) * 512],
                    start=(tt == 0), stop=(tt == NTT - 1), skip_group_check=True),
                    inc="c_pe" if (g2 == 1 and tt == NTT - 1) else None)
            self.rreg(("knb", tt % 2), ctx_done)
            self.rreg(("vtb", tt % 2), ctx_done)
        em.waits("dve", [ctx_done, ks_done])
        ctx_ev = em.op("dve", lambda ep: ep.tensor_copy(ctxs[:, :], self.ctxacc[:, :]), inc="c_dve")
        em.op("dve", lambda ep: ep.tensor_copy(ksumr[0:1, 0:512], self.PD[0:1, :]))
        ks_ev = em.op("dve", lambda ep: ep.tensor_copy(ksumr[0:1, 512:1024], self.PE_[0:1, :]),
                      inc="c_dve")
        self.rreg("knraw", ks_ev)
        self.rreg("PD", ks_ev)
        self.rreg("PE", ks_ev)
        em.waits("pe", tmax(ks_ev, self.rgate("PE")))
        tk = None
        for c in range(NCH):
            tk = em.op("pe", lambda ep, c=c, ksumr=ksumr: ep.matmul(
                self.PE_[:, c:c + 1], ksumr[0:1, c * 128:(c + 1) * 128],
                self.onescol[0:1, 0:1], start=True, stop=True, skip_group_check=True),
                inc="c_pe" if c == NCH - 1 else None)
        em.waits("dve", [tk])
        kc = em.op("dve", lambda ep: ep.tensor_copy(self.ksumc[:, :], self.PE_[:, 0:8]), inc="c_dve")
        self.rreg("PE", kc)
        em.waits("dve", [kc])
        kw = None
        for c in range(NCH):
            kw = em.op("dve", lambda ep, c=c: ep.tensor_scalar_mul(
                self.kwsel[:, c * 16:(c + 1) * 16], self.bselq[:, c * 16:(c + 1) * 16],
                self.ksumc[:, c:c + 1]), inc="c_dve" if c == NCH - 1 else None)
        collq = AP(vFM.tensor, vFM.offset, [[vFM.ap[0][0], 16], [1, 2048]])
        collD = AP(vFM.tensor, vFM.offset + 2048, [[vFM.ap[0][0], 16], [1, 2048]])
        dq_toks = []
        for tc in range(NTC):
            for which in range(2):
                sel = self.bselq if which == 0 else self.kwsel
                coll = collq if which == 0 else collD
                em.waits("pe", tmax(kw, e_toks[-1], self.rgate("PD")))
                mm = None
                for c in range(NCH):
                    mm = em.op("pe", lambda ep, c=c, sel=sel, tc=tc: ep.matmul(
                        self.PD[0:16, :], sel[:, c * 16:(c + 1) * 16],
                        E[:, c * T + tc * 512:c * T + (tc + 1) * 512],
                        start=(c == 0), stop=(c == NCH - 1), skip_group_check=True),
                        inc="c_pe" if c == NCH - 1 else None)
                em.waits("dve", [mm])
                ev = em.op("dve", lambda ep, coll=coll, tc=tc: ep.tensor_copy(
                    coll[:, tc * 512:(tc + 1) * 512], self.PD[0:16, :]), inc="c_dve")
                self.rreg("PD", ev)
                dq_toks.append(ev)
        em.waits("act", tmax(dq_toks))
        em.op("act", lambda ep: ep.activation(collq, collq, AF.Ln))
        iq = em.op("act", lambda ep: ep.activation(collq, collq, AF.Exp, scale=-1.0), inc="c_act")
        em.op("act", lambda ep: ep.activation(collD, collD, AF.Ln))
        iD = em.op("act", lambda ep: ep.activation(collD, collD, AF.Exp, scale=-1.0), inc="c_act")
        asm_toks = []
        for tc in range(NTC):
            for c in range(NCH):
                em.waits("pe", tmax(ctx_ev, iq, iD, self.rgate("PB0"), self.rgate("PE")))
                mp = None
                for hh in range(2):
                    h = 2 * c + hh
                    mp = em.op("pe", lambda ep, h=h, hh=hh, c=c, tc=tc: ep.matmul(
                        self.PB[hh * 64:hh * 64 + 64, 0:512],
                        ctxs[hh * 64:hh * 64 + 64, (h // 2) * 64:(h // 2) * 64 + 64],
                        E[hh * 64:hh * 64 + 64, c * T + tc * 512:c * T + (tc + 1) * 512],
                        start=True, stop=True), inc="c_pe" if hh == 1 else None)
                bq1 = em.op("pe", lambda ep, c=c, tc=tc: ep.matmul(
                    self.PE_[:, :], self.selpat[:, c * 128:(c + 1) * 128],
                    collq[:, tc * 512:(tc + 1) * 512],
                    start=True, stop=True), inc="c_pe")
                em.waits("dve", tmax(bq1, self.rgate("scrA")))
                t1 = em.op("dve", lambda ep, c=c, tc=tc: ep.tensor_tensor(
                    self.scrA[:, :], E[:, c * T + tc * 512:c * T + (tc + 1) * 512],
                    self.PE_[:, :], ALU.mult), inc="c_dve")
                em.waits("pe", [t1])
                bD = em.op("pe", lambda ep, c=c, tc=tc: ep.matmul(
                    self.PE_[:, :], self.selpat[:, c * 128:(c + 1) * 128],
                    collD[:, tc * 512:(tc + 1) * 512],
                    start=True, stop=True), inc="c_pe")
                em.waits("act", tmax(bD, self.rgate("bcsb")))
                bDs = em.op("act", lambda ep: ep.activation(self.bcq_sb[:, 0:TC], self.PE_[:, :],
                                                            AF.Identity), inc="c_act")
                self.rreg("PE", bDs)
                em.waits("dve", tmax(bDs, mp, self.rgate("scrB")))
                t2 = em.op("dve", lambda ep: ep.tensor_tensor(
                    self.scrB[:, :], self.PB[:, 0:512], self.bcq_sb[:, 0:TC], ALU.mult), inc="c_dve")
                self.rreg("PB0", t2)
                self.rreg("bcsb", t2)
                fin = em.op("dve", lambda ep, c=c, tc=tc: ep.tensor_tensor(
                    E[:, c * T + tc * 512:c * T + (tc + 1) * 512], self.scrA[:, :],
                    self.scrB[:, :], ALU.add), inc="c_dve")
                self.rreg("scrA", fin)
                self.rreg("scrB", fin)
                asm_toks.append(fin)
        em.waits("sp", tmax(asm_toks))
        self.rreg(("tabr", 0), ks_done)
        self.rreg(("tabr", 1), ks_done)
        wp_tok = em.op("sp", lambda ep: ep.dma_start(
            wT2, reap(self.s_wT[3], [(CM, 128), (128 * CM, NCH), (1, CM)])), inc="cwt2", amt=16)
        self.xt_tok = []
        for tt in range(NTT):
            em.waits("pe", tmax(wp_tok, asm_toks))
            em.waits("pe", self.rgate("PA0", "PA1"))
            mm = None
            for g2 in range(2):
                for sc in range(NCH):
                    em.op("pe", lambda ep, sc=sc, g2=g2, tt=tt: ep.matmul(
                        self.PA[:, g2 * 512:(g2 + 1) * 512],
                        E[:, sc * T + tt * 128: sc * T + tt * 128 + 128],
                        wT2[:, sc * CM + g2 * 512: sc * CM + (g2 + 1) * 512],
                        start=(sc == 0), stop=False, skip_group_check=True))
                mm = em.op("pe", lambda ep, g2=g2: ep.matmul(
                    self.PA[:, g2 * 512:(g2 + 1) * 512], self.ones1[:, :],
                    bp_row[:, g2 * 512:(g2 + 1) * 512],
                    start=False, stop=True, skip_group_check=True),
                    inc="c_pe" if g2 == 1 else None)
            em.waits("dve", tmax(mm, self.rgate(("xt", tt))))
            up = em.op("dve", lambda ep, tt=tt: ep.tensor_tensor(
                self.xt3()[:, tt, :].rearrange("p (c n) -> p c n", n=128),
                self.xt3()[:, tt, :].rearrange("p (c n) -> p c n", n=128),
                self.PA[:, 0:1024].rearrange("p (c n) -> p c n", n=128), ALU.add), inc="c_dve")
            self.rreg("PA0", up)
            self.rreg("PA1", up)
            self.xt_tok.append(up)
        self.rreg("big1", tmax(asm_toks, self.xt_tok))
        self.rreg("big3", tmax(asm_toks, self.xt_tok))



_CACHE = {}


def _build(dbg_cut=None):
    key = ("nc", dbg_cut)
    if key not in _CACHE:
        nc = bass.Bass("TRN2", target_bir_lowering=False)
        kb = KB(nc, dbg_cut=dbg_cut)
        n = kb.build()
        nc._pe_tags = kb.em.pe_tags
        _CACHE[key] = (nc, n)
    return _CACHE[key]


def make_inputs(inputs):
    consts = host_constants()
    lay = host_layouts(inputs)
    x = np.ascontiguousarray(np.asarray(inputs["L0_attn_x"], np.float32))
    base = {
        "wqkv": np.ascontiguousarray(np.asarray(inputs["L0_ffn_Wqkv"], np.float32)),
        "wo3": np.ascontiguousarray(np.asarray(inputs["L0_ffn_Wo"], np.float32)),
        "w1": np.ascontiguousarray(np.asarray(inputs["L0_ffn_W1"], np.float32)),
        "w2": np.ascontiguousarray(np.asarray(inputs["L0_ffn_W2"], np.float32)),
        "awq": np.ascontiguousarray(np.asarray(inputs["L0_attn_Wq"], np.float32)),
        "awk": np.ascontiguousarray(np.asarray(inputs["L0_attn_Wk"], np.float32)),
        "awv": np.ascontiguousarray(np.asarray(inputs["L0_attn_Wv"], np.float32)),
        "awp": np.ascontiguousarray(np.asarray(inputs["L0_attn_Wp"], np.float32)),
    }
    base.update(consts)
    base.update(lay)
    in_maps = []
    for core in range(8):
        m = dict(base)
        m["x"] = np.ascontiguousarray(x[2 * core:2 * core + 2].reshape(NSEQ * T, CM))
        in_maps.append(m)
    return in_maps


def kernel(**inputs):
    nc, _ = _build()
    in_maps = make_inputs(inputs)
    res = run_bass_kernel_spmd(nc, in_maps, core_ids=list(range(8)))
    out = np.empty((16, T, CM), np.float32)
    for core in range(8):
        out[2 * core:2 * core + 2] = res.results[core]["out"].reshape(NSEQ, T, CM)
    return out
